# version 12
# baseline (speedup 1.0000x reference)
import numpy as np
import ml_dtypes
from contextlib import ExitStack

import concourse.bass as bass
import concourse.mybir as mybir
from concourse.bass_utils import run_bass_kernel_spmd
from concourse.alu_op_type import AluOpType as OP

F32 = mybir.dt.float32
BF16 = mybir.dt.bfloat16
AF = mybir.ActivationFunctionType
AXX = mybir.AxisListType.X

P = 128
T = 2048
TC = 16
D = 1024
H = 16
NPAIR = 8
KC = 16
NS = 64
DQ = 256
DHF = 512
GOB = 24

_DEBUG = False
_X2OUT = False
_TAPS = False


def _np_consts():
    h = np.array([[1.0]])
    while h.shape[0] < 64:
        h = np.block([[h, h], [h, -h]])
    u = h[1:, :] / 8.0
    ut = np.zeros((64, 64), np.float32)
    ut[:, :63] = u.T
    ut2 = np.tile(ut, (2, 1))
    ones_pair = np.zeros((128, 2), np.float32)
    ones_pair[0:64, 0] = 1.0
    ones_pair[64:128, 1] = 1.0
    return {
        "c_ut2": ut2.astype(ml_dtypes.bfloat16),
        "c_id": np.eye(128, dtype=np.float32).astype(ml_dtypes.bfloat16),
        "c_ones_m": np.ones((1, 128), np.float32),
        "c_ones_pair": ones_pair.astype(ml_dtypes.bfloat16),
        "c_ones_n": np.ones((128, 1), np.float32).astype(ml_dtypes.bfloat16),
        "c_idf": np.eye(128, dtype=np.float32),
    }


class Prog:
    def __init__(self, nc, stack):
        self.nc = nc
        self.steps = []
        self.sems = {}
        self.cnt = {}
        for name in ("LD", "GL", "TE", "VE", "SE", "GP", "SO"):
            self.sems[name] = stack.enter_context(nc.semaphore(name))
            self.cnt[name] = 0

    def emit(self, engine, fn, waits=(), inc=None, n=None, dr=False):
        if n is None:
            n = 16 if inc in ("LD", "GL", "SO") else 1
        for s, v in waits:
            assert v <= self.cnt[s], (engine, s, v, self.cnt[s])
        if inc is not None:
            self.cnt[inc] += n
        self.steps.append((engine, fn, tuple(waits), inc, n, dr))
        return self.cnt[inc] if inc is not None else None

    def c(self, sem):
        return self.cnt[sem]

    def finalize(self):
        waited = {s: set() for s in self.sems}
        for _, _, waits, _, _, _ in self.steps:
            for s, v in waits:
                waited[s].add(v)
        cnt = {s: 0 for s in self.sems}
        newcnt = {s: 0 for s in self.sems}
        remap = {s: {0: 0} for s in self.sems}
        new_steps = []
        for e, fn, waits, inc, n, dr in self.steps:
            nw = tuple((s, remap[s][v]) for s, v in waits)
            if inc is not None:
                cnt[inc] += n
                if inc in ("GL", "LD", "SO") or cnt[inc] in waited[inc]:
                    newcnt[inc] += n
                    remap[inc][cnt[inc]] = newcnt[inc]
                    new_steps.append((e, fn, nw, inc, n, dr))
                else:
                    remap[inc][cnt[inc]] = newcnt[inc]
                    new_steps.append((e, fn, nw, None, n, dr))
            else:
                new_steps.append((e, fn, nw, None, n, dr))
        self.steps = new_steps

    def run_engine(self, name, eng):
        for e, fn, waits, inc, n, dr in self.steps:
            if e != name:
                continue
            for s, v in waits:
                eng.wait_ge(self.sems[s], v)
            if dr:
                eng.drain()
            instr = fn(eng)
            if inc is not None:
                instr.then_inc(self.sems[inc], n)


def build_nc(debug=False):
    nc = bass.Bass("TRN2", target_bir_lowering=False,
                   dynamic_dma_scratch_size=8192)

    dr = {}

    def din(name, shape, dt=F32):
        dr[name] = nc.dram_tensor(name, list(shape), dt, kind="ExternalInput").ap()

    din("x", (T, D))
    for nm, shp in [
        ("Wq", (D, D)), ("Wk", (D, D)), ("Wv", (D, D)), ("Wo", (D, D)),
        ("slot_init", (NS, D)), ("w1", (2 * D, DHF)), ("b1", (1, DHF)),
        ("w2", (DHF, NS)), ("b2", (1, NS)), ("tau", (1, 1)),
        ("Wih", (3 * D, D)), ("Whh", (3 * D, D)),
        ("bih", (1, 3 * D)), ("bhh", (1, 3 * D)),
        ("wq", (D, DQ)), ("bq", (1, DQ)), ("wk", (D, DQ)), ("bk", (1, DQ)),
        ("wv", (D, D)), ("bv", (1, D)), ("ln_g", (1, D)), ("ln_b", (1, D)),
        ("wvp", (D, D)), ("bvp", (1, D)), ("wo", (D, D)), ("bo", (1, D)),
    ]:
        din(nm, shp)
    din("c_ut2", (P, 64), BF16)
    din("c_id", (P, P), BF16)
    din("c_ones_m", (1, P))
    din("c_ones_pair", (P, 2), BF16)
    din("c_ones_n", (P, 1), BF16)
    din("c_idf", (P, P))

    out = nc.dram_tensor("out", [T, D], F32, kind="ExternalOutput").ap()
    x2s = nc.dram_tensor("x2s", [T, D], F32,
                         kind="ExternalOutput" if _X2OUT else "Internal").ap()
    wgT = nc.dram_tensor("wgT", [2, GOB, P, 8 * P], BF16, kind="Internal").ap()
    dbg = {}
    if _TAPS:
        for nm, shp, dt in [
            ("tap_hfm", (P, 4 * T), BF16), ("tap_alpha", (P, TC * NS), BF16),
            ("tap_sv", (NS, D), BF16), ("tap_snew", (NS, D), BF16),
            ("tap_yln", (NS, D), BF16), ("tap_vs", (NS, D), BF16),
            ("tap_sifm", (P, 8 * NS), BF16), ("tap_outfm", (P, 8 * T), BF16),
            ("tap_bt", (1, D), F32), ("tap_wvp", (P, 8 * D), BF16),
            ("tap_ylnfm", (P, 8 * NS), BF16),
        ]:
            dbg[nm] = nc.dram_tensor(nm, list(shp), dt, kind="ExternalOutput").ap()
    if debug:
        for nm, shp, dt in [
            ("dbg_ln1", (P, 4 * T), BF16), ("dbg_qfm", (P, 8 * T), BF16),
            ("dbg_kfm", (P, 8 * T), BF16), ("dbg_vhat", (P, KC * H * 65), BF16),
            ("dbg_attnfm", (P, 8 * T), BF16), ("dbg_x2", (T, D), F32),
            ("dbg_hfm", (P, 4 * T), BF16), ("dbg_alpha", (P, TC * NS), BF16),
            ("dbg_sv", (NS, D), BF16), ("dbg_ln2fm", (P, 4 * T), BF16),
            ("dbg_logits", (P, TC * NS), F32), ("dbg_e0", (P, D), BF16),
            ("dbg_ln1b", (P, 4 * T), BF16), ("dbg_lnb", (P, 4 * T), BF16),
            ("dbg_vps", (P, D), BF16),
            ("dbg_av0", (P, 8 * 128), BF16), ("dbg_pt0", (P, TC * P), BF16),
        ]:
            dbg[nm] = nc.dram_tensor(nm, list(shp), dt, kind="ExternalOutput").ap()

    with ExitStack() as stack:
        def sb(name, shape, dt):
            return stack.enter_context(nc.sbuf_tensor(name, shape, dt))

        r_lnfm_a = sb("r_lnfm_a", [P, 4, T], BF16)
        r_lnfm_b = sb("r_lnfm_b", [P, 4, T], BF16)
        r_q = sb("r_q", [P, NPAIR, T], BF16)
        r_k = sb("r_k", [P, NPAIR, T], BF16)
        r_v = sb("r_v", [P, KC, H, 65], BF16)
        r_e = sb("r_e", [P, 8, D], BF16)
        r_f1 = sb("r_f1", [P, 8, D], BF16)
        r_f2 = sb("r_f2", [P, 8, D], BF16)
        r_x = sb("r_x", [P, 2, D], F32)
        r_et = sb("r_et", [P, 2, D], BF16)
        r_pt = sb("r_pt", [P, TC, P], BF16)
        r_qv = sb("r_qv", [P, T], BF16)

        c_const = sb("c_const", [P, 2], F32)
        stats = sb("stats", [P, 8], F32)
        t8 = sb("t8", [P, 8], F32)
        negm1 = sb("negm1", [P, 1], F32)
        e8b = sb("e8b", [P, 8], F32)
        e64b = sb("e64b", [P, NS], F32)
        maskb = sb("maskb", [P, NS], F32)
        s8 = sb("s8", [P, 2], F32)
        logits = sb("logits", [P, NS], F32)
        recipC = sb("recipC", [P, 8], F32)
        inv_tau = sb("inv_tau", [P, 1], F32)
        tau_t = sb("tau_t", [1, 2], F32)
        id_sb = sb("id_sb", [P, P], BF16)
        ut2_sb = sb("ut2_sb", [P, 64], BF16)
        ones_m = sb("ones_m", [1, P], F32)
        ones_pr = sb("ones_pr", [P, 2], BF16)
        ones_n = sb("ones_n", [P, 1], BF16)
        idf_sb = sb("idf_sb", [P, P], F32)
        bias_g = sb("bias_g", [1, 2 * P], F32)
        b1_row = sb("b1_row", [1, DHF], F32)
        b2_row = sb("b2_row", [1, NS], F32)
        hb_row = sb("hb_row", [1, DHF], F32)
        hbiasT = sb("hbiasT", [P, 4], F32)
        gf = sb("gf", [P, 8], F32)
        bfm = sb("bfm", [P, 8], F32)
        smf = sb("smf", [P, 8], BF16)
        slotS = sb("slotS", [NS, D], BF16)
        slotFM = sb("slotFM", [P, 8, NS], BF16)
        qsFM = sb("qsFM", [P, 2, NS], BF16)
        ksFM = sb("ksFM", [P, 2, NS], BF16)
        a2 = sb("a2", [NS, NS], BF16)
        a2T = sb("a2T", [NS, NS], BF16)
        sst = sb("sst", [NS, 8], F32)
        denA = sb("denA", [NS, 2], F32)
        gsc = sb("gsc", [NS, 8, P], F32)
        dbgav_sb = sb("dbgav", [P, 1024], BF16) if debug else None
        _sfm3 = r_qv[:, 0:1536].rearrange("p (a b c) -> p a b c", a=3, b=8, c=NS)
        siFM = _sfm3[:, 0, :, :]
        snFM = _sfm3[:, 1, :, :]
        ylnFM = _sfm3[:, 2, :, :]

        wtv = r_pt[:, :, :].rearrange("p a b -> p (a b)").rearrange(
            "p (w i c) -> p w i c", w=2, i=8, c=P)
        alphaFM = r_et[:, :, :].rearrange("p a b -> p (a b)")[0:NS, :].rearrange(
            "p (a b) -> p a b", a=TC, b=P)
        etf = r_et[:, :, :].rearrange("p a b -> p (a b)")
        f1f2 = r_f1[:, :, :].rearrange("p a b -> p (a b)")
        rptf = r_pt[:, :, :].rearrange("p a b -> p (a b)")
        siB = etf[0:NS, 0:D]
        rrB = etf[0:NS, 0:D]
        zzB = etf[0:NS, D:2 * D]
        snewB = f1f2[0:NS, 0:D]
        yyB = f1f2[0:NS, D:2 * D]
        ylnB = f1f2[0:NS, 2 * D:3 * D]
        ysqd = f1f2[0:NS, 3 * D:4 * D]
        vsB = rptf[0:NS, 0:D]
        svB = rptf[0:NS, D:2 * D]
        rlnbf = r_lnfm_b[:, :, :].rearrange("p a b -> p (a b)")
        alphaT = rlnbf[:, 0:TC * NS].rearrange("p (a b) -> p a b", a=TC, b=NS)
        qsB = rlnbf[0:NS, 1024:1024 + DQ]
        ksB = rlnbf[0:NS, 1280:1280 + DQ]
        w2N = rlnbf[:, 1536:1792].rearrange("p (a b) -> p a b", a=4, b=NS)
        bias_scr = gsc[:, :, :].rearrange("p a b -> p (a b)")[0:1, 0:D]

        f1v = r_f1[:, :, :].rearrange("p a b -> p (a b)")
        w1a_a = f1v[:, 0:2048].rearrange("p (a b) -> p a b", a=4, b=DHF)
        w1a_b = f1v[:, 2048:4096].rearrange("p (a b) -> p a b", a=4, b=DHF)
        wqN = f1v[:, 4096:6144].rearrange("p (a b) -> p a b", a=8, b=DQ)
        wkN = f1v[:, 6144:8192].rearrange("p (a b) -> p a b", a=8, b=DQ)
        f2v = r_f2[:, :, :].rearrange("p a b -> p (a b)")
        wvN = f2v[:, 0:8192].rearrange("p (a b) -> p a b", a=8, b=D)
        rkv = r_k[:, :, :].rearrange("p a b -> p (a b)")
        wvpN = rkv[:, 0:8192].rearrange("p (a b) -> p a b", a=8, b=D)
        woN = rkv[:, 8192:16384].rearrange("p (a b) -> p a b", a=8, b=D)
        rev = r_e[:, :, :].rearrange("p a b -> p (a b)")
        wstage = rev[:, 0:4096].rearrange("p (a b) -> p a b", a=4, b=D)
        ystage = rev[:, 4096:8192].rearrange("p (a b) -> p a b", a=4, b=D)
        hFM = rev[:, 0:8192].rearrange("p (a b) -> p a b", a=4, b=T)
        ln2tok = r_v[:, :, :, :].rearrange("p a b c -> p (a b c)")[:, 0:TC * D]
        ln2tok = ln2tok.rearrange("p (a b) -> p a b", a=TC, b=D)
        sq = r_et[:, :, :].rearrange("p a b -> p (a b)")
        lnt = r_pt[:, :, :].rearrange("p a b -> p (a b)")

        def lnfm(ic):
            return (r_lnfm_a if ic < 4 else r_lnfm_b)[:, ic % 4, :]

        psA = stack.enter_context(nc.psum_tensor("psA", [P, 2048], F32))
        psB1 = stack.enter_context(nc.psum_tensor("psB1", [P, 1024], F32))
        ps_tr = stack.enter_context(nc.psum_tensor("ps_tr", [P, 512], BF16))
        psC = stack.enter_context(nc.psum_tensor("psC", [P, 512], F32))

        sc0 = psA[:, 0:1024]
        sc1 = psA[:, 1024:2048]
        avA = psB1[:, 0:512].rearrange("p (a b) -> p a b", a=4, b=128)
        avB = psB1[:, 512:1024].rearrange("p (a b) -> p a b", a=4, b=128)
        y0 = psA[0:NS, 1024:2048]
        y1 = psB1[0:NS, 0:1024]
        ps_trf = psC[0:P, 128:144]

        pr = Prog(nc, stack)
        E = pr.emit
        LD, GL, TE, VE, SE, GP, SO = "LD", "GL", "TE", "VE", "SE", "GP", "SO"

        for nm, dst in [("c_id", id_sb), ("c_ut2", ut2_sb), ("c_ones_m", ones_m),
                        ("c_ones_pair", ones_pr), ("c_ones_n", ones_n), ("c_idf", idf_sb),
                        ("b1", b1_row), ("b2", b2_row)]:
            E("sync", lambda s, nm=nm, dst=dst: s.dma_start(out=dst[:], in_=dr[nm]),
              inc=LD)
        E("sync", lambda s: s.dma_start(out=tau_t[0:1, 0:1], in_=dr["tau"]), inc=LD)
        ld_consts = pr.c(LD)
        E("gpsimd", lambda g: g.memset(c_const[:, 0:1], -20.0), inc=GP)
        E("gpsimd", lambda g: g.memset(c_const[:, 1:2], 1e-5), inc=GP)
        E("gpsimd", lambda g: g.memset(r_v[:], 1.0), inc=GP)
        gp_init = pr.c(GP)
        E("gpsimd", lambda g: g.dma_start(out=slotS[:], in_=dr["slot_init"]), inc=GL)
        gl_slot = pr.c(GL)

        E("scalar", lambda s: s.activation(out=tau_t[0:1, 1:2], in_=tau_t[0:1, 0:1],
                                           func=AF.Abs),
          waits=[(LD, ld_consts)], inc=SE)
        se_tau = pr.c(SE)
        E("vector", lambda v: v.tensor_scalar_add(tau_t[0:1, 1:2], tau_t[0:1, 1:2], 0.1),
          waits=[(SE, se_tau)], inc=VE)
        E("vector", lambda v: v.reciprocal(tau_t[0:1, 0:1], tau_t[0:1, 1:2]), inc=VE, dr=True)
        ve_tau = pr.c(VE)
        E("tensor", lambda t: t.matmul(psC[0:P, 0:1], ones_m[0:1, :], tau_t[0:1, 0:1],
                                       start=True, stop=True),
          waits=[(VE, ve_tau), (LD, ld_consts)], inc=TE)
        te_tau = pr.c(TE)
        E("vector", lambda v: v.tensor_copy(inv_tau[:], psC[0:P, 0:1]),
          waits=[(TE, te_tau)], inc=VE)
        ve_invtau = pr.c(VE)

        ld_x = [0] * TC
        ve_ln = [0] * TC
        te_lntr = [0]
        tr_ev = [0]
        ve_var_prev = [0]

        def x_load(tc, slot, src):
            return lambda s: s.dma_start(out=r_x[:, slot, :],
                                         in_=src[tc * P:(tc + 1) * P, :])

        E("sync", x_load(0, 0, dr["x"]), inc=LD)
        ld_x[0] = pr.c(LD)
        for tc in range(TC):
            sl = tc % 2
            if tc + 1 < TC:
                w = [(VE, ve_ln[tc - 1])] if tc >= 1 else []
                E("sync", x_load(tc + 1, (tc + 1) % 2, dr["x"]), waits=w, inc=LD)
                ld_x[tc + 1] = pr.c(LD)
            xb = r_x[:, sl, :]
            E("scalar", lambda s, xb=xb: s.activation(
                out=r_et[:, 0, :], in_=xb, func=AF.Square, accum_out=stats[:, 0:1]),
                waits=[(LD, ld_x[tc]), (VE, ve_var_prev[0])], inc=SE)
            se_sq = pr.c(SE)
            E("vector", lambda v, xb=xb: v.reduce_sum(stats[:, 1:2], xb, axis=AXX),
              waits=[(LD, ld_x[tc])], inc=VE)
            E("vector", lambda v: v.tensor_scalar_mul(stats[:, 2:3], stats[:, 1:2], 1.0 / D), inc=VE, dr=True)
            E("vector", lambda v: v.tensor_mul(stats[:, 3:4], stats[:, 2:3], stats[:, 2:3]), inc=VE, dr=True)
            E("vector", lambda v: v.tensor_scalar(stats[:, 4:5], stats[:, 0:1], 1.0 / D,
                                                  None, op0=OP.mult),
              waits=[(SE, se_sq)], inc=VE)
            E("vector", lambda v: v.tensor_sub(stats[:, 5:6], stats[:, 4:5], stats[:, 3:4]), inc=VE, dr=True)
            ve_var_prev[0] = pr.c(VE)
            E("scalar", lambda s: s.activation(out=stats[:, 6:7], in_=stats[:, 5:6],
                                               func=AF.Ln, bias=c_const[:, 1:2]),
              waits=[(VE, ve_var_prev[0]), (GP, gp_init)], inc=SE)
            E("scalar", lambda s: s.activation(out=stats[:, 7:8], in_=stats[:, 6:7],
                                               func=AF.Exp, scale=-0.5), inc=SE)
            se_rstd = pr.c(SE)
            w = [(SE, se_rstd)]
            if tc >= 1:
                w.append((TE, te_lntr[0]))
            E("vector", lambda v, xb=xb: v.tensor_scalar(
                lnt[:, 0:D], xb, stats[:, 2:3], stats[:, 7:8],
                op0=OP.subtract, op1=OP.mult), waits=w, inc=VE)
            ve_ln[tc] = pr.c(VE)
            for g in range(2):
                w = [(VE, ve_ln[tc]), (VE, tr_ev[0])]
                for i in range(4):
                    ic = 4 * g + i
                    E("tensor", lambda t, ic=ic, i=i: t.transpose(
                        ps_tr[:, i * 128:(i + 1) * 128],
                        lnt[:, ic * 128:(ic + 1) * 128], id_sb[:]),
                        waits=w if i == 0 else [], inc=TE)
                te_lntr[0] = pr.c(TE)
                tgt = r_lnfm_a if g == 0 else r_lnfm_b
                E("vector", lambda v, tgt=tgt, tc=tc: v.tensor_copy(
                    tgt[:, 0:4, tc * P:(tc + 1) * P],
                    ps_tr[:, :]), waits=[(TE, te_lntr[0])], inc=VE)
                tr_ev[0] = pr.c(VE)
        ve_ln1_done = pr.c(VE)

        gl_stage = [0]
        te_stage_rd = [0]
        se_yev = [0]

        def load_stage(name, half):
            return lambda g: g.dma_start(
                out=wstage[:, :, :],
                in_=dr[name][half * 512:(half + 1) * 512, :].rearrange(
                    "(a p) b -> p a b", p=P))

        def stage_to(dst, src, half, extra_w=()):
            for c in range(4):
                for icg in range(2):
                    w = [(VE, tr_ev[0])] + list(extra_w)
                    for i in range(4):
                        ic = icg * 4 + i
                        E("tensor", lambda t, c=c, ic=ic, i=i, src=src: t.transpose(
                            ps_tr[:, i * 128:(i + 1) * 128],
                            src[:, c, ic * 128:(ic + 1) * 128], id_sb[:]),
                            waits=w if i == 0 else [], inc=TE)
                    te_tr = pr.c(TE)
                    te_stage_rd[0] = te_tr
                    E("vector", lambda v, c=c, icg=icg, half=half, dst=dst: v.tensor_copy(
                        dst[:, icg * 4:(icg + 1) * 4,
                            half * 512 + c * 128: half * 512 + (c + 1) * 128],
                        ps_tr[:, :]), waits=[(TE, te_tr)], inc=VE)
                    tr_ev[0] = pr.c(VE)

        def prep_centered(name, dst):
            for half in range(2):
                E("gpsimd", load_stage(name, half),
                  waits=[(SE, se_yev[0]), (TE, te_stage_rd[0])], inc=GL)
                gl_stage[0] = pr.c(GL)
                yev_h = {}
                for hh in range(8):
                    base = 64 * (hh % 2)
                    chunk = hh // 2
                    yps = y0 if (hh % 2 == 0) else y1
                    for sln in range(2):
                        w = []
                        if sln == 0:
                            w = [(GL, gl_stage[0])]
                            w.append((SE, yev_h[hh - 2] if hh >= 2 else se_yev[0]))
                        E("tensor", lambda t, base=base, chunk=chunk, yps=yps, sx=sln: t.matmul(
                            yps[:, sx * 512:(sx + 1) * 512],
                            ut2_sb[base:base + 64, :],
                            wstage[base:base + 64, chunk, sx * 512:(sx + 1) * 512],
                            start=True, stop=True),
                            waits=w, inc=TE)
                    te_f = pr.c(TE)
                    te_stage_rd[0] = te_f
                    E("scalar", lambda s, base=base, chunk=chunk, yps=yps: s.copy(
                        out=ystage[base:base + 64, chunk, :], in_=yps[:, :]),
                        waits=[(TE, te_f)], inc=SE)
                    se_yev[0] = pr.c(SE)
                    yev_h[hh] = se_yev[0]
                stage_to(dst, ystage, half, extra_w=[(SE, se_yev[0])])

        def prep_plain(name, dst, extra_w=()):
            for half in range(2):
                E("gpsimd", load_stage(name, half),
                  waits=[(TE, te_stage_rd[0])] + list(extra_w), inc=GL)
                gl_stage[0] = pr.c(GL)
                stage_to(dst, wstage, half, extra_w=[(GL, gl_stage[0])])

        prep_centered("Wq", r_f1)
        te_wq_prep, ve_wq_prep = pr.c(TE), pr.c(VE)
        prep_centered("Wk", r_f2)
        te_wk_prep, ve_wk_prep = pr.c(TE), pr.c(VE)

        for icg in range(2):
            w = [(GL, gl_slot), (VE, tr_ev[0])]
            for i in range(4):
                ic = icg * 4 + i
                E("tensor", lambda t, ic=ic, i=i: t.transpose(
                    ps_tr[:, i * 128: i * 128 + NS],
                    slotS[:, ic * 128:(ic + 1) * 128], id_sb[0:NS, 0:NS]),
                    waits=w if i == 0 else [], inc=TE)
            te_str = pr.c(TE)
            E("vector", lambda v, icg=icg: v.tensor_copy(
                slotFM[:, icg * 4:(icg + 1) * 4, :],
                ps_tr[:, :].rearrange("p (a b) -> p a b", a=4, b=128)[:, :, 0:NS]),
                waits=[(TE, te_str)], inc=VE)
            tr_ev[0] = pr.c(VE)
        ve_slotfm = pr.c(VE)
        for ic in range(8):
            E("vector", lambda v, ic=ic: v.reduce_sum(stats[:, 0:1], slotFM[:, ic, :], axis=AXX),
              waits=[(VE, ve_slotfm), (VE, ve_ln1_done)] if ic == 0 else [], inc=VE)
            E("vector", lambda v, ic=ic: v.tensor_scalar(smf[:, ic:ic + 1], stats[:, 0:1],
                                                         1.0 / NS, None, op0=OP.mult), inc=VE, dr=True)
        ve_smf = pr.c(VE)
        E("gpsimd", lambda g: g.dma_start(
            out=ystage[:, :, 0:DHF],
            in_=dr["w1"][D:2 * D, :].rearrange("(a p) b -> p a b", p=P)[:, 0:4, :]),
          waits=[(TE, te_stage_rd[0]), (SE, se_yev[0]), (VE, tr_ev[0])], inc=GL)
        gl_w1b_a = pr.c(GL)
        E("gpsimd", lambda g: g.dma_start(
            out=wstage[:, :, 0:DHF],
            in_=dr["w1"][D:2 * D, :].rearrange("(a p) b -> p a b", p=P)[:, 4:8, :]),
          inc=GL)
        gl_w1b_b = pr.c(GL)
        for ic in range(8):
            src = ystage if ic < 4 else wstage
            E("tensor", lambda t, src=src, ic=ic: t.matmul(
                psC[0:1, 0:DHF], smf[:, ic:ic + 1], src[:, ic % 4, 0:DHF],
                start=(ic == 0), stop=(ic == 7)),
                waits=[(VE, ve_smf), (GL, gl_w1b_b if ic >= 4 else gl_w1b_a)] if ic in (0, 4) else [],
                inc=TE)
        te_c = pr.c(TE)
        te_stage_rd[0] = te_c
        E("vector", lambda v: v.tensor_add(hb_row[:], psC[0:1, 0:DHF], b1_row[:]),
          waits=[(TE, te_c), (LD, ld_consts)], inc=VE)
        ve_hb = pr.c(VE)
        for i in range(4):
            E("tensor", lambda t, i=i: t.transpose(
                ps_trf[:, i:i + 1], hb_row[0:1, i * 128:(i + 1) * 128],
                ones_m[0:1, 0:1]),
                waits=[(VE, ve_hb)] if i == 0 else [], inc=TE)
        te_hbt = pr.c(TE)
        E("vector", lambda v: v.tensor_copy(hbiasT[:, :], ps_trf[:, 0:4]),
          waits=[(TE, te_hbt)], inc=VE)
        ve_hbt = pr.c(VE)

        E("sync", lambda s: s.dma_start(out=bias_scr[0:1, :], in_=dr["ln_g"]),
          waits=[(LD, ld_consts)], inc=LD)
        ld_g = pr.c(LD)
        for i in range(8):
            E("tensor", lambda t, i=i: t.transpose(
                ps_trf[:, 8 + i:9 + i], bias_scr[0:1, i * 128:(i + 1) * 128],
                ones_m[0:1, 0:1]),
                waits=[(LD, ld_g), (VE, ve_hbt)] if i == 0 else [], inc=TE)
        te_gt = pr.c(TE)
        E("vector", lambda v: v.tensor_copy(gf[:, :], ps_trf[:, 8:16]),
          waits=[(TE, te_gt)], inc=VE)
        ve_gf = pr.c(VE)
        E("sync", lambda s: s.dma_start(out=bias_scr[0:1, :], in_=dr["ln_b"]),
          waits=[(VE, ve_gf)], inc=LD)
        ld_b = pr.c(LD)
        for i in range(8):
            E("tensor", lambda t, i=i: t.transpose(
                ps_trf[:, 8 + i:9 + i], bias_scr[0:1, i * 128:(i + 1) * 128],
                ones_m[0:1, 0:1]),
                waits=[(LD, ld_b), (VE, ve_gf)] if i == 0 else [], inc=TE)
        te_bt = pr.c(TE)
        E("vector", lambda v: v.tensor_copy(bfm[:, :], ps_trf[:, 8:16]),
          waits=[(TE, te_bt)], inc=VE)
        ve_bfm = pr.c(VE)

        te_proj = [0]
        ve_pev = [0]
        so_scat = [0]
        se_qv = [0]
        te_qv_last = [0]

        def qk_projection(wsrc, qdst, first_waits):
            for oc in range(NPAIR):
                for halft in range(2):
                    ps = sc0 if halft == 0 else sc1
                    for ic in range(8):
                        for sln in range(2):
                            w = []
                            if ic == 0 and sln == 0:
                                w = list(first_waits) + [(VE, ve_pev[0])]
                            E("tensor", lambda t, ps=ps, oc=oc, ic=ic, sln=sln, halft=halft,
                              wsrc=wsrc: t.matmul(
                                ps[:, sln * 512:(sln + 1) * 512],
                                wsrc[:, ic, oc * 128:(oc + 1) * 128],
                                lnfm(ic)[:, halft * 1024 + sln * 512: halft * 1024 + (sln + 1) * 512],
                                start=(ic == 0), stop=(ic == 7)), waits=w, inc=TE)
                    te_proj[0] = pr.c(TE)
                    E("vector", lambda v, ps=ps, oc=oc, halft=halft, qdst=qdst: v.tensor_copy(
                        qdst[:, oc, halft * 1024:(halft + 1) * 1024], ps[:, :]),
                        waits=[(TE, te_proj[0])], inc=VE)
                    ve_pev[0] = pr.c(VE)
                E("vector", lambda v, oc=oc, qdst=qdst: v.tensor_mul(
                    sq[:, :], qdst[:, oc, :], qdst[:, oc, :]),
                    waits=[(SO, so_scat[0])], inc=VE, dr=True)
                ve_sq = pr.c(VE)
                for q4 in range(4):
                    qvp = psC[0:2, 0:512] if q4 % 2 == 0 else psB1[0:2, 0:512]
                    E("tensor", lambda t, q4=q4, qvp=qvp: t.matmul(
                        qvp[:, :], ones_pr[:, :],
                        sq[:, q4 * 512:(q4 + 1) * 512],
                        start=True, stop=True),
                        waits=[(VE, ve_sq), (SE, se_qv[0])], inc=TE)
                    te_qv = pr.c(TE)
                    te_qv_last[0] = te_qv
                    E("scalar", lambda s, q4=q4, qvp=qvp: s.activation(
                        out=r_qv[0:2, q4 * 512:(q4 + 1) * 512], in_=qvp[:, :],
                        func=AF.Copy, scale=1.0 / 63.0),
                        waits=[(TE, te_qv)], inc=SE)
                    se_qv[0] = pr.c(SE)
                E("sync", lambda s, oc=oc, qdst=qdst: s.dma_start(
                    out=qdst[63:64, oc, :], in_=r_qv[0:1, :]),
                    waits=[(SE, se_qv[0]), (VE, ve_pev[0])], inc=SO)
                E("sync", lambda s, oc=oc, qdst=qdst: s.dma_start(
                    out=qdst[127:128, oc, :], in_=r_qv[1:2, :]), inc=SO)
                so_scat[0] = pr.c(SO)

        qk_projection(r_f1, r_q,
                      [(TE, te_wq_prep), (VE, ve_wq_prep), (VE, ve_ln1_done),
                       (VE, ve_hbt), (VE, ve_bfm), (VE, ve_hb)])
        so_q_scat = so_scat[0]
        qk_projection(r_f2, r_k, [(TE, te_wk_prep), (VE, ve_wk_prep)])
        so_k_scat = so_scat[0]

        prep_plain("Wv", r_f1, extra_w=[(VE, ve_pev[0]), (SE, se_qv[0])])
        te_wv_prep, ve_wv_prep = pr.c(TE), pr.c(VE)
        if debug:
            E("sync", lambda s: s.dma_start(
                out=dbg["dbg_ln1b"],
                in_=r_lnfm_a[:, :, :].rearrange("p a b -> p (a b)")),
                waits=[(TE, te_wv_prep), (VE, ve_wv_prep)], inc=SO)
            E("sync", lambda s: s.dma_start(
                out=dbg["dbg_lnb"],
                in_=r_lnfm_b[:, :, :].rearrange("p a b -> p (a b)")), inc=SO)
            so_lnb = pr.c(SO)
        for tc in range(TC):
            ps = sc0 if tc % 2 == 0 else sc1
            for ic in range(8):
                for sln in range(2):
                    w = []
                    if ic == 0 and sln == 0:
                        w = [(TE, te_wv_prep), (VE, ve_wv_prep), (VE, ve_pev[0]),
                             (GP, gp_init)]
                        if debug and tc == 0:
                            w.append((SO, so_lnb))
                    E("tensor", lambda t, ps=ps, tc=tc, ic=ic, sln=sln: t.matmul(
                        ps[:, sln * 512:(sln + 1) * 512],
                        lnfm(ic)[:, tc * 128:(tc + 1) * 128],
                        r_f1[:, ic, sln * 512:(sln + 1) * 512],
                        start=(ic == 0), stop=(ic == 7)), waits=w, inc=TE)
            te_proj[0] = pr.c(TE)
            if debug and tc == 0:
                E("scalar", lambda s: s.activation(
                    out=dbgav_sb[:, :], in_=sc0[:, :], func=AF.Copy),
                    waits=[(TE, te_proj[0])], inc=SE)
                se_dv = pr.c(SE)
                E("sync", lambda s: s.dma_start(out=dbg["dbg_vps"], in_=dbgav_sb[:, :]),
                  waits=[(SE, se_dv)], inc=SO)
                so_dv = pr.c(SO)
                E("vector", lambda v, tc=tc, ps=ps: v.tensor_copy(
                    r_v[:, tc, :, 0:64], ps[:, :]),
                    waits=[(TE, te_proj[0]), (SE, se_dv)], inc=VE)
            else:
                E("vector", lambda v, tc=tc, ps=ps: v.tensor_copy(
                    r_v[:, tc, :, 0:64], ps[:, :]),
                    waits=[(TE, te_proj[0])], inc=VE)
            ve_pev[0] = pr.c(VE)
        ve_vhat = pr.c(VE)

        prep_plain("Wo", r_f2, extra_w=[(VE, ve_pev[0])])
        te_wo_prep, ve_wo_prep = pr.c(TE), pr.c(VE)

        if debug:
            E("sync", lambda s: s.dma_start(
                out=dbg["dbg_ln1"],
                in_=r_lnfm_a[:, :, :].rearrange("p a b -> p (a b)")),
                waits=[(VE, ve_ln1_done)], inc=SO)
            E("sync", lambda s: s.dma_start(
                out=dbg["dbg_qfm"], in_=r_q[:, :, :].rearrange("p a b -> p (a b)")),
                waits=[(SO, so_q_scat)], inc=SO)
            E("sync", lambda s: s.dma_start(
                out=dbg["dbg_kfm"], in_=r_k[:, :, :].rearrange("p a b -> p (a b)")),
                waits=[(SO, so_k_scat)], inc=SO)
            E("sync", lambda s: s.dma_start(
                out=dbg["dbg_vhat"],
                in_=r_v[:, :, :, :].rearrange("p a b c -> p (a b c)")),
                waits=[(VE, ve_vhat)], inc=SO)

        E("gpsimd", lambda g: g.nop(), waits=[(VE, ve_vhat)], inc=GP)
        gp_f1_free = pr.c(GP)
        for fn in [
            lambda g: g.dma_start(out=w1a_a[:, :, :],
                                  in_=dr["w1"][0:D, :].rearrange("(a p) b -> p a b", p=P)[:, 0:4, :]),
            lambda g: g.dma_start(out=w1a_b[:, :, :],
                                  in_=dr["w1"][0:D, :].rearrange("(a p) b -> p a b", p=P)[:, 4:8, :]),
            lambda g: g.dma_start(out=wqN[:, :, :],
                                  in_=dr["wq"].rearrange("(a p) b -> p a b", p=P)),
            lambda g: g.dma_start(out=wkN[:, :, :],
                                  in_=dr["wk"].rearrange("(a p) b -> p a b", p=P)),
        ]:
            E("gpsimd", fn, waits=[(GP, gp_f1_free)], inc=GL)
        gl_ffn1 = pr.c(GL)

        ebuf = r_et
        se_exp = {}
        te_av = {}
        ve_avev = [0]
        def emit_qk(rnd, pair, qq, kc):
            par = kc % 2
            scA = psA[:, par * 1024: par * 1024 + 512]
            scB = psA[:, par * 1024 + 512: (par + 1) * 1024]
            w = []
            if kc >= 2:
                w.append((SE, se_exp[(rnd, kc - 2)]))
            elif rnd >= 1:
                w.append((SE, se_exp[(rnd - 1, KC - 2 + kc)]))
            else:
                w += [(SO, so_q_scat), (SO, so_k_scat), (VE, ve_vhat)]
                if debug:
                    w.append((SO, pr.c(SO)))
            for hh in range(2):
                base = 64 * hh
                scp = scA if hh == 0 else scB
                E("tensor", lambda t, scp=scp, base=base, pair=pair, kc=kc,
                  qq=qq: t.matmul(
                    scp[:, :],
                    r_k[base:base + 64, pair, kc * 128:(kc + 1) * 128],
                    r_q[base:base + 64, pair, qq * 512:(qq + 1) * 512],
                    start=True, stop=True, tile_position=(base, 0)),
                    waits=w if hh == 0 else [], inc=TE)
            te_qkv = pr.c(TE)
            w = [(TE, te_qkv)]
            if kc >= 2:
                w.append((TE, te_av[(rnd, kc - 2, 1)]))
            elif rnd >= 1:
                w.append((TE, te_av[(rnd - 1, KC - 2 + kc, 1)]))
            E("scalar", lambda s, kc=kc: s.activation(
                out=ebuf[:, kc % 2, :],
                in_=psA[:, (kc % 2) * 1024:(kc % 2 + 1) * 1024], func=AF.Exp,
                scale=1.0 / 64.0, bias=c_const[:, 0:1]),
                waits=w, inc=SE)
            se_exp[(rnd, kc)] = pr.c(SE)

        def emit_av(rnd, pair, kc):
            for hh in range(2):
                avp = avA if hh == 0 else avB
                hglob = 2 * pair + hh
                w = [(SE, se_exp[(rnd, kc)])]
                if kc == 0:
                    w.append((VE, ve_avev[0]))
                for qc in range(4):
                    st = (kc == 0 and qc == 0)
                    E("tensor", lambda t, kc=kc, qc=qc, hglob=hglob, st=st,
                      avp=avp, hh=hh: t.matmul(
                        avp[:, qc, 0:65],
                        ebuf[:, kc % 2, hh * 512 + qc * 128:
                             hh * 512 + (qc + 1) * 128],
                        r_v[:, kc, hglob, :],
                        start=st, stop=(kc == KC - 1),
                        skip_group_check=True),
                        waits=w if (qc == 0 and hh == 0) else [], inc=TE)
                te_av[(rnd, kc, hh)] = pr.c(TE)

        gwt_gl = [0]
        gwt_te = [0]
        gwt_ve = [0]
        gwt_so = [0]

        gwt_te_by_step = {}
        gwt_gl_by_step = {}
        gwt_so_by_step = {}

        gwt_ve_by_step = {}

        def emit_gru_load(step):
            which, ob = step % 2, step // 2
            nm = "Wih" if which == 0 else "Whh"
            slot = step % 2
            w = [(VE, ve_vhat)] if step < 2 else [(TE, gwt_te_by_step[step - 2])]
            E("sync", lambda s, nm=nm, ob=ob, slot=slot: s.dma_start(
                out=r_x[:, slot, :],
                in_=dr[nm][ob * 128:(ob + 1) * 128, :]),
                waits=w, inc=LD)
            gwt_gl_by_step[step] = pr.c(LD)

        def emit_gru_tr(step):
            which, ob = step % 2, step // 2
            slot = step % 2
            yslot = step % 4
            for icg in range(2):
                w = [(LD, gwt_gl_by_step[step]), (VE, gwt_ve[0])]
                if step >= 4:
                    w.append((SO, gwt_so_by_step[step - 4]))
                for i in range(4):
                    ic = icg * 4 + i
                    E("tensor", lambda t, ic=ic, i=i, slot=slot: t.transpose(
                        psC[:, i * 128:(i + 1) * 128],
                        r_x[:, slot, ic * 128:(ic + 1) * 128], idf_sb[:]),
                        waits=w if i == 0 else [], inc=TE)
                te_t = pr.c(TE)
                gwt_te_by_step[step] = te_t
                E("vector", lambda v, icg=icg, yslot=yslot: v.tensor_copy(
                    ystage[:, yslot, icg * 512:(icg + 1) * 512], psC[:, :]),
                    waits=[(TE, te_t)], inc=VE)
                gwt_ve[0] = pr.c(VE)
            gwt_ve_by_step[step] = pr.c(VE)
            E("sync", lambda s, which=which, ob=ob, yslot=yslot: s.dma_start(
                out=wgT[which, ob, :, :], in_=ystage[:, yslot, :]),
                waits=[(VE, gwt_ve_by_step[step])], inc=SO)
            gwt_so_by_step[step] = pr.c(SO)
            gwt_so[0] = gwt_so_by_step[step]

        se_exp_h = {}
        pending_ptr = [None]
        te_ptr_last = [0]

        def emit_pair_tr():
            if pending_ptr[0] is None:
                return
            ppair, ve_ready = pending_ptr[0]
            pending_ptr[0] = None
            for g4 in range(4):
                w = [(VE, ve_ready), (VE, tr_ev[0])]
                for i in range(4):
                    tcl = g4 * 4 + i
                    E("tensor", lambda t, tcl=tcl, i=i: t.transpose(
                        ps_tr[:, i * 128:(i + 1) * 128],
                        r_pt[:, tcl, :], id_sb[:]),
                        waits=w if i == 0 else [], inc=TE)
                te_ptr = pr.c(TE)
                te_ptr_last[0] = te_ptr
                E("vector", lambda v, g4=g4, ppair=ppair: v.tensor_copy(
                    r_q[:, ppair, g4 * 512:(g4 + 1) * 512], ps_tr[:, :]),
                    waits=[(TE, te_ptr)], inc=VE)
                tr_ev[0] = pr.c(VE)

        NSTEP = 2 * GOB
        loads_it = iter(range(NSTEP))
        trs_it = iter(range(NSTEP))
        for pair in range(NPAIR):
            for qq in range(4):
                rnd = pair * 4 + qq
                if rnd == 2:
                    for _ in range(2):
                        st_i = next(loads_it, None)
                        if st_i is not None:
                            emit_gru_load(st_i)
                if rnd >= 3:
                    for _ in range(2):
                        st_i = next(trs_it, None)
                        if st_i is not None:
                            emit_gru_tr(st_i)
                            st_l = next(loads_it, None)
                            if st_l is not None:
                                emit_gru_load(st_l)
                emit_qk(rnd, pair, qq, 0)
                emit_qk(rnd, pair, qq, 1)
                if qq == 0:
                    emit_pair_tr()
                for kc in range(KC):
                    emit_av(rnd, pair, kc)
                    if kc + 2 < KC:
                        emit_qk(rnd, pair, qq, kc + 2)
                te_av_done = pr.c(TE)
                E("vector", lambda v: v.reciprocal(recipC[:, 0:4], avA[:, :, 64]),
                  waits=[(TE, te_av_done), (TE, te_ptr_last[0])], inc=VE)
                E("vector", lambda v: v.reciprocal(recipC[:, 4:8], avB[:, :, 64]),
                  inc=VE)
                E("vector", lambda v: v.tensor_copy(stats[:, 0:1], recipC[:, 0:1]), inc=VE)
                for hh in range(2):
                    avp = avA if hh == 0 else avB
                    for qc in range(4):
                        E("vector", lambda v, qc=qc, qq=qq, hh=hh, avp=avp: v.tensor_scalar(
                            r_pt[:, qq * 4 + qc, hh * 64:hh * 64 + 64],
                            avp[:, qc, 0:64], recipC[:, hh * 4 + qc:hh * 4 + qc + 1],
                            None, op0=OP.mult),
                            inc=VE)
                ve_avev[0] = pr.c(VE)
            pending_ptr[0] = (pair, ve_avev[0])
        for st_i in trs_it:
            emit_gru_tr(st_i)
            st_l = next(loads_it, None)
            if st_l is not None:
                emit_gru_load(st_l)
        emit_pair_tr()
        ve_attnfm = pr.c(VE)
        te_headloop = pr.c(TE)

        if debug:
            E("sync", lambda s: s.dma_start(
                out=dbg["dbg_attnfm"], in_=r_q[:, :, :].rearrange("p a b -> p (a b)")),
                waits=[(VE, ve_attnfm)], inc=SO)

        E("gpsimd", lambda g: g.nop(), waits=[(TE, te_headloop)], inc=GP)
        gp_rk_free = pr.c(GP)
        E("gpsimd", lambda g: g.dma_start(
            out=wvpN[:, :, :], in_=dr["wvp"].rearrange("(a p) b -> p a b", p=P)),
          waits=[(GP, gp_rk_free)], inc=GL)
        gl_wvp = pr.c(GL)
        E("gpsimd", lambda g: g.dma_start(
            out=woN[:, :, :], in_=dr["wo"].rearrange("(a p) b -> p a b", p=P)),
          inc=GL)
        gl_wo = pr.c(GL)

        ld_x2 = [0] * TC
        ve_x2 = [0] * TC
        ve_norm2 = [0] * TC
        so_x2 = [0] * TC
        se_sq2 = [0]
        E("sync", x_load(0, 0, dr["x"]),
          waits=[(VE, ve_avev[0]), (SO, pr.c(SO))], inc=LD)
        ld_x2[0] = pr.c(LD)
        for tc in range(TC):
            sl = tc % 2
            if tc + 1 < TC:
                w = []
                if tc >= 1:
                    w = [(VE, ve_norm2[tc - 1]), (SO, so_x2[tc - 1])]
                E("sync", x_load(tc + 1, (tc + 1) % 2, dr["x"]), waits=w, inc=LD)
                ld_x2[tc + 1] = pr.c(LD)
            ps = sc0 if tc % 2 == 0 else sc1
            for ic in range(8):
                for sln in range(2):
                    w = []
                    if ic == 0 and sln == 0:
                        w = [(VE, ve_attnfm), (TE, te_wo_prep), (VE, ve_wo_prep)]
                        if tc >= 2:
                            w.append((VE, ve_x2[tc - 2]))
                    E("tensor", lambda t, ps=ps, tc=tc, ic=ic, sln=sln: t.matmul(
                        ps[:, sln * 512:(sln + 1) * 512],
                        r_q[:, ic, tc * 128:(tc + 1) * 128],
                        r_f2[:, ic, sln * 512:(sln + 1) * 512],
                        start=(ic == 0), stop=(ic == 7)), waits=w, inc=TE)
            te_wo = pr.c(TE)
            xb = r_x[:, sl, :]
            E("vector", lambda v, xb=xb, ps=ps: v.tensor_add(xb, ps[:, :], xb),
              waits=[(TE, te_wo), (LD, ld_x2[tc])], inc=VE)
            ve_x2[tc] = pr.c(VE)
            E("sync", lambda s, xb=xb, tc=tc: s.dma_start(
                out=x2s[tc * P:(tc + 1) * P, :], in_=xb),
                waits=[(VE, ve_x2[tc])], inc=SO)
            so_x2[tc] = pr.c(SO)
            E("scalar", lambda s, xb=xb: s.activation(
                out=sq[:, 0:D], in_=xb, func=AF.Square, accum_out=stats[:, 0:1]),
                waits=[(VE, ve_x2[tc]), (VE, ve_var_prev[0])], inc=SE)
            se_sq2[0] = pr.c(SE)
            E("vector", lambda v, xb=xb: v.reduce_sum(stats[:, 1:2], xb, axis=AXX), inc=VE, dr=True)
            E("vector", lambda v: v.tensor_scalar_mul(stats[:, 2:3], stats[:, 1:2], 1.0 / D), inc=VE, dr=True)
            E("vector", lambda v: v.tensor_mul(stats[:, 3:4], stats[:, 2:3], stats[:, 2:3]), inc=VE, dr=True)
            E("vector", lambda v: v.tensor_scalar(stats[:, 4:5], stats[:, 0:1], 1.0 / D,
                                                  None, op0=OP.mult),
              waits=[(SE, se_sq2[0])], inc=VE)
            E("vector", lambda v: v.tensor_sub(stats[:, 5:6], stats[:, 4:5], stats[:, 3:4]), inc=VE, dr=True)
            ve_var_prev[0] = pr.c(VE)
            E("scalar", lambda s: s.activation(out=stats[:, 6:7], in_=stats[:, 5:6],
                                               func=AF.Ln, bias=c_const[:, 1:2]),
              waits=[(VE, ve_var_prev[0])], inc=SE)
            E("scalar", lambda s: s.activation(out=stats[:, 7:8], in_=stats[:, 6:7],
                                               func=AF.Exp, scale=-0.5), inc=SE)
            se_rstd2 = pr.c(SE)
            E("vector", lambda v, xb=xb, tc=tc: v.tensor_scalar(
                ln2tok[:, tc, :], xb, stats[:, 2:3], stats[:, 7:8],
                op0=OP.subtract, op1=OP.mult),
                waits=[(SE, se_rstd2)], inc=VE)
            ve_norm2[tc] = pr.c(VE)
            for g in range(2):
                w = [(VE, ve_norm2[tc]), (VE, tr_ev[0])]
                for i in range(4):
                    ic = 4 * g + i
                    E("tensor", lambda t, tc=tc, ic=ic, i=i: t.transpose(
                        ps_tr[:, i * 128:(i + 1) * 128],
                        ln2tok[:, tc, ic * 128:(ic + 1) * 128], id_sb[:]),
                        waits=w if i == 0 else [], inc=TE)
                te_tr2 = pr.c(TE)
                tgt = r_lnfm_a if g == 0 else r_lnfm_b
                E("vector", lambda v, tgt=tgt, tc=tc: v.tensor_copy(
                    tgt[:, 0:4, tc * P:(tc + 1) * P],
                    ps_tr[:, :]), waits=[(TE, te_tr2)], inc=VE)
                tr_ev[0] = pr.c(VE)
        ve_ln2fm = pr.c(VE)
        te_p3 = pr.c(TE)

        if debug:
            E("sync", lambda s: s.dma_start(out=dbg["dbg_x2"], in_=x2s),
              waits=[(SO, so_x2[TC - 1])], inc=SO)
            E("sync", lambda s: s.dma_start(
                out=dbg["dbg_ln2fm"], in_=r_lnfm_a[:, :, :].rearrange("p a b -> p (a b)")),
                waits=[(VE, ve_ln2fm)], inc=SO)

        se_gelu = [0]
        for dhc in range(4):
            for halft in range(2):
                ps = sc0 if halft == 0 else sc1
                for ic in range(8):
                    wsrc = w1a_a if ic < 4 else w1a_b
                    for sln in range(2):
                        w = []
                        if ic == 0 and sln == 0:
                            w = [(VE, ve_ln2fm), (GL, gl_ffn1), (VE, ve_hbt)]
                            if dhc > 0 or halft > 0:
                                w.append((SE, se_gelu[0]))
                        E("tensor", lambda t, ps=ps, dhc=dhc, ic=ic, sln=sln, halft=halft,
                          wsrc=wsrc: t.matmul(
                            ps[:, sln * 512:(sln + 1) * 512],
                            wsrc[:, ic % 4, dhc * 128:(dhc + 1) * 128],
                            lnfm(ic)[:, halft * 1024 + sln * 512: halft * 1024 + (sln + 1) * 512],
                            start=(ic == 0), stop=(ic == 7)), waits=w, inc=TE)
                te_h = pr.c(TE)
                E("scalar", lambda s, ps=ps, dhc=dhc, halft=halft: s.activation(
                    out=hFM[:, dhc, halft * 1024:(halft + 1) * 1024], in_=ps[:, :],
                    func=AF.Gelu, bias=hbiasT[:, dhc:dhc + 1]),
                    waits=[(TE, te_h)], inc=SE)
                se_gelu[0] = pr.c(SE)
        se_hfm = pr.c(SE)
        if debug:
            E("sync", lambda s: s.dma_start(
                out=dbg["dbg_hfm"], in_=hFM[:, :, :].rearrange("p a b -> p (a b)")),
                waits=[(SE, se_hfm)], inc=SO)

        E("gpsimd", lambda g: g.dma_start(
            out=w2N[:, :, :], in_=dr["w2"].rearrange("(a p) b -> p a b", p=P)),
            waits=[(SE, se_hfm)], inc=GL)
        gl_w2 = pr.c(GL)
        ve_alpha = [0] * TC
        rlp = psC[0:P, 64:128]
        for tc in range(TC):
            for dhc in range(4):
                w = []
                if dhc == 0:
                    w = [(SE, se_hfm), (GL, gl_w2)]
                    if tc >= 1:
                        w.append((VE, ve_alpha[tc - 1]))
                E("tensor", lambda t, tc=tc, dhc=dhc: t.matmul(
                    rlp[:, :], hFM[:, dhc, tc * 128:(tc + 1) * 128],
                    w2N[:, dhc, :], start=(dhc == 0), stop=False,
                    skip_group_check=True), waits=w, inc=TE)
            E("tensor", lambda t: t.matmul(rlp[:, :], ones_m[0:1, :], b2_row[0:1, :],
                                           start=False, stop=True,
                                           skip_group_check=True), inc=TE)
            te_rl = pr.c(TE)
            E("vector", lambda v: v.tensor_scalar(logits[:, :], rlp[:, :], inv_tau[:, 0:1],
                                                  None, op0=OP.mult),
              waits=[(TE, te_rl), (VE, ve_invtau)], inc=VE)
            E("vector", lambda v: v.max(t8[:, :], logits[:, :]), inc=VE, dr=True)
            E("vector", lambda v: v.tensor_scalar(negm1[:, :], t8[:, 0:1], -1.0, None,
                                                  op0=OP.mult), inc=VE, dr=True)
            ve_t8 = pr.c(VE)
            E("scalar", lambda s: s.activation(out=e8b[:, :], in_=t8[:, :], func=AF.Exp,
                                               bias=negm1[:, 0:1]),
              waits=[(VE, ve_t8)], inc=SE)
            E("scalar", lambda s: s.activation(out=e64b[:, :], in_=logits[:, :],
                                               func=AF.Exp, bias=negm1[:, 0:1]), inc=SE)
            se_e = pr.c(SE)
            E("vector", lambda v: v.reduce_sum(s8[:, 0:1], e8b[:, :], axis=AXX),
              waits=[(SE, se_e)], inc=VE)
            E("vector", lambda v: v.reciprocal(s8[:, 1:2], s8[:, 0:1]), inc=VE, dr=True)
            E("vector", lambda v: v.tensor_scalar(maskb[:, :], logits[:, :], t8[:, 7:8],
                                                  None, op0=OP.is_ge), inc=VE)
            E("vector", lambda v, tc=tc: v.scalar_tensor_tensor(
                alphaT[:, tc, :], e64b[:, :], s8[:, 1:2], maskb[:, :],
                op0=OP.mult, op1=OP.mult), inc=VE)
            ve_alpha[tc] = pr.c(VE)
            if debug:
                E("sync", lambda s, tc=tc: s.dma_start(
                    out=dbg["dbg_logits"][:, tc * NS:(tc + 1) * NS], in_=logits[:, :]),
                    waits=[(VE, ve_alpha[tc])], inc=SO)
        ve_alpha_all = pr.c(VE)
        if debug:
            E("sync", lambda s: s.dma_start(
                out=dbg["dbg_alpha"], in_=alphaT[:, :, :].rearrange("p a b -> p (a b)")),
                waits=[(VE, ve_alpha_all)], inc=SO)

        si_ps = psA[0:NS, 1024:2048]
        for tc in range(TC):
            for sln in range(2):
                w = []
                if sln == 0:
                    w = [(VE, ve_alpha[tc])]
                    if tc == 0:
                        w.append((VE, ve_avev[0]))
                E("tensor", lambda t, tc=tc, sln=sln: t.matmul(
                    si_ps[:, sln * 512:(sln + 1) * 512],
                    alphaT[:, tc, :], ln2tok[:, tc, sln * 512:(sln + 1) * 512],
                    start=(tc == 0), stop=(tc == TC - 1), skip_group_check=True),
                    waits=w, inc=TE)
        te_si = pr.c(TE)
        dn_ps = psC[0:NS, 0:1]
        for tc in range(TC):
            E("tensor", lambda t, tc=tc: t.matmul(
                dn_ps[:, :], alphaT[:, tc, :], ones_n[:, :],
                start=(tc == 0), stop=(tc == TC - 1)),
                waits=[(VE, ve_alpha_all)] if tc == 0 else [], inc=TE)
        te_dn = pr.c(TE)
        E("vector", lambda v: v.tensor_scalar_add(denA[:, 0:1], dn_ps[:, :], 1e-8),
          waits=[(TE, te_dn), (TE, te_si)], inc=VE)
        E("vector", lambda v: v.reciprocal(denA[:, 1:2], denA[:, 0:1]), inc=VE, dr=True)
        E("vector", lambda v: v.tensor_copy(stats[0:NS, 0:1], denA[0:NS, 1:2]), inc=VE)
        E("vector", lambda v: v.tensor_scalar(siB[:, :], si_ps[:, :],
                                              denA[:, 1:2], None, op0=OP.mult), inc=VE)
        ve_si = pr.c(VE)


        def smajor_to_fm(srcap, dst, ready_w):
            for icg in range(2):
                w = list(ready_w) + [(VE, tr_ev[0])]
                for i in range(4):
                    ic = icg * 4 + i
                    E("tensor", lambda t, ic=ic, i=i, srcap=srcap: t.transpose(
                        ps_tr[:, i * 128: i * 128 + NS],
                        srcap[:, ic * 128:(ic + 1) * 128], id_sb[0:NS, 0:NS]),
                        waits=w if i == 0 else [], inc=TE)
                te_t = pr.c(TE)
                E("vector", lambda v, icg=icg, dst=dst: v.tensor_copy(
                    dst[:, icg * 4:(icg + 1) * 4, :],
                    ps_tr[:, :].rearrange("p (a b) -> p a b", a=4, b=128)[:, :, 0:NS]),
                    waits=[(TE, te_t)], inc=VE)
                tr_ev[0] = pr.c(VE)
            return pr.c(VE)

        ve_sifm = smajor_to_fm(siB, siFM, [(VE, ve_si)])

        gi_ps = psA[0:NS, 1024:1152]
        gh_ps = psA[0:NS, 1536:1664]
        te_gru = [0]
        ve_gru = [0]
        se_gru = [0]
        gru_tr_ev = [0]
        te_wt_done = [0, 0]
        for ob in range(GOB):
            w = []
            if ob == 0:
                w = [(VE, ve_sifm), (SE, se_hfm), (TE, te_si), (SO, gwt_so[0])]
            else:
                w = [(TE, te_gru[0])]
            E("sync", lambda s, ob=ob: s.dma_start(
                out=wtv[:, 0, :, :], in_=wgT[0, ob, :, :]), waits=w, inc=LD)
            E("sync", lambda s, ob=ob: s.dma_start(
                out=wtv[:, 1, :, :], in_=wgT[1, ob, :, :]), inc=LD)
            ld_wt = pr.c(LD)
            E("sync", lambda s, ob=ob: s.dma_start(
                out=bias_g[0:1, 0:P], in_=dr["bih"][0:1, ob * 128:(ob + 1) * 128]),
                waits=[(TE, te_gru[0])] if ob >= 1 else [], inc=LD)
            E("sync", lambda s, ob=ob: s.dma_start(
                out=bias_g[0:1, P:2 * P], in_=dr["bhh"][0:1, ob * 128:(ob + 1) * 128]), inc=LD)
            ld_gb = pr.c(LD)
            ve_wt = None
            for ic in range(8):
                w = []
                if ic == 0:
                    w = [(LD, ld_wt)]
                    if ob >= 1:
                        w += [(VE, ve_gru[0]), (SE, se_gru[0])]
                E("tensor", lambda t, ic=ic: t.matmul(
                    gi_ps[:, :], siFM[:, ic, :], wtv[:, 0, ic, :],
                    start=(ic == 0), stop=False), waits=w, inc=TE)
                E("tensor", lambda t, ic=ic: t.matmul(
                    gh_ps[:, :], slotFM[:, ic, :], wtv[:, 1, ic, :],
                    start=(ic == 0), stop=False), inc=TE)
            E("tensor", lambda t: t.matmul(gi_ps[:, :], ones_m[0:1, 0:NS], bias_g[0:1, 0:P],
                                           start=False, stop=True),
              waits=[(LD, ld_gb)], inc=TE)
            E("tensor", lambda t: t.matmul(gh_ps[:, :], ones_m[0:1, 0:NS], bias_g[0:1, P:2 * P],
                                           start=False, stop=True), inc=TE)
            te_gru[0] = pr.c(TE)
            gate = ob // 8
            dsl = (ob % 8) * 128
            if gate < 2:
                dst = rrB if gate == 0 else zzB
                E("vector", lambda v: v.tensor_copy(gsc[:, 0, :], gi_ps[:, :]),
                  waits=[(TE, te_gru[0])], inc=VE)
                E("vector", lambda v: v.tensor_add(gsc[:, 1, :],
                                                   gsc[:, 0, :], gh_ps[:, :]), inc=VE, dr=True)
                ve_g = pr.c(VE)
                E("scalar", lambda s: s.activation(out=gsc[:, 2, :],
                                                   in_=gsc[:, 1, :],
                                                   func=AF.Exp, scale=-1.0),
                  waits=[(VE, ve_g)], inc=SE)
                se_gru[0] = pr.c(SE)
                E("vector", lambda v: v.tensor_scalar_add(gsc[:, 3, :],
                                                          gsc[:, 2, :], 1.0),
                  waits=[(SE, se_gru[0])], inc=VE)
                E("vector", lambda v: v.reciprocal(gsc[:, 2, :], gsc[:, 3, :]), inc=VE, dr=True)
                E("vector", lambda v, dst=dst, dsl=dsl: v.tensor_copy(
                    dst[:, dsl:dsl + 128], gsc[:, 2, :]), inc=VE, dr=True)
                ve_gru[0] = pr.c(VE)
            else:
                E("vector", lambda v, dsl=dsl: v.tensor_mul(
                    gsc[:, 0, :], rrB[:, dsl:dsl + 128], gh_ps[:, :]),
                    waits=[(TE, te_gru[0])], inc=VE)
                E("vector", lambda v: v.tensor_add(gsc[:, 1, :],
                                                   gsc[:, 0, :], gi_ps[:, :]), inc=VE, dr=True)
                ve_g = pr.c(VE)
                E("scalar", lambda s: s.activation(out=gsc[:, 2, :],
                                                   in_=gsc[:, 1, :],
                                                   func=AF.Exp, scale=-2.0),
                  waits=[(VE, ve_g)], inc=SE)
                se_gru[0] = pr.c(SE)
                E("vector", lambda v: v.tensor_scalar_add(gsc[:, 3, :],
                                                          gsc[:, 2, :], 1.0),
                  waits=[(SE, se_gru[0])], inc=VE)
                E("vector", lambda v: v.reciprocal(gsc[:, 0, :], gsc[:, 3, :]), inc=VE, dr=True)
                E("vector", lambda v: v.tensor_scalar(gsc[:, 1, :], gsc[:, 2, :],
                                                      -1.0, 1.0, op0=OP.mult, op1=OP.add), inc=VE)
                E("vector", lambda v: v.tensor_mul(gsc[:, 1, :], gsc[:, 1, :], gsc[:, 0, :]), inc=VE, dr=True)
                E("vector", lambda v, dsl=dsl: v.tensor_sub(
                    gsc[:, 2, :], slotS[:, dsl:dsl + 128], gsc[:, 1, :]), inc=VE, dr=True)
                E("vector", lambda v, dsl=dsl: v.tensor_mul(
                    gsc[:, 3, :], zzB[:, dsl:dsl + 128], gsc[:, 2, :]), inc=VE, dr=True)
                E("vector", lambda v, dsl=dsl: v.tensor_add(
                    snewB[:, dsl:dsl + 128], gsc[:, 1, :], gsc[:, 3, :]), inc=VE, dr=True)
                ve_gru[0] = pr.c(VE)
        ve_snew = pr.c(VE)

        E("gpsimd", lambda g: g.nop(), waits=[(TE, te_p3)], inc=GP)
        gp_f2_free = pr.c(GP)
        E("gpsimd", lambda g: g.dma_start(
            out=wvN[:, :, :], in_=dr["wv"].rearrange("(a p) b -> p a b", p=P)),
          waits=[(GP, gp_f2_free)], inc=GL)
        gl_wv = pr.c(GL)

        ve_snb = ve_snew
        ve_snfm = smajor_to_fm(snewB, snFM, [(VE, ve_snew)])

        def slot_proj(dst_ps, wsrc, n_out, bias_name, first_w):
            E("sync", lambda s: s.dma_start(out=bias_scr[0:1, 0:n_out],
                                            in_=dr[bias_name]),
              waits=list(first_w) + [(TE, pr.c(TE))], inc=LD)
            ldb = pr.c(LD)
            nsl = (n_out + 511) // 512
            for ic in range(8):
                for sli in range(nsl):
                    lo, hi = sli * 512, min(n_out, (sli + 1) * 512)
                    E("tensor", lambda t, ic=ic, lo=lo, hi=hi, wsrc=wsrc: t.matmul(
                        dst_ps[:, lo:hi], snFM[:, ic, :], wsrc[:, ic, lo:hi],
                        start=(ic == 0), stop=False, skip_group_check=True),
                        waits=(first_w + [(VE, ve_snfm)]) if (ic == 0 and sli == 0) else [],
                        inc=TE)
            for sli in range(nsl):
                lo, hi = sli * 512, min(n_out, (sli + 1) * 512)
                E("tensor", lambda t, lo=lo, hi=hi: t.matmul(
                    dst_ps[:, lo:hi], ones_m[0:1, 0:NS], bias_scr[0:1, lo:hi],
                    start=False, stop=True, skip_group_check=True),
                    waits=[(LD, ldb)] if sli == 0 else [], inc=TE)
            return pr.c(TE)

        qs_ps = psC[0:NS, 0:DQ]
        ks_ps = psC[0:NS, DQ:2 * DQ]
        te_qs = slot_proj(qs_ps, wqN, DQ, "bq", [(VE, ve_snfm), (GL, gl_ffn1)])
        E("vector", lambda v: v.tensor_copy(qsB[:, :], qs_ps[:, :]),
          waits=[(TE, te_qs)], inc=VE)
        ve_qs = pr.c(VE)
        te_ks = slot_proj(ks_ps, wkN, DQ, "bk", [(VE, ve_qs)])
        E("vector", lambda v: v.tensor_copy(ksB[:, :], ks_ps[:, :]),
          waits=[(TE, te_ks)], inc=VE)
        ve_ks = pr.c(VE)
        vs_ps = psA[0:NS, 1024:2048]
        te_vs = slot_proj(vs_ps, wvN, D, "bv",
                          [(GL, gl_wv), (VE, ve_gru[0]), (SE, se_gru[0])])
        E("vector", lambda v: v.tensor_copy(vsB[:, :], vs_ps[:, :]),
          waits=[(TE, te_vs)], inc=VE)
        ve_vs = pr.c(VE)

        def tr256(src, dst, ready_w):
            w = list(ready_w) + [(VE, tr_ev[0])]
            for i in range(2):
                E("tensor", lambda t, i=i, src=src: t.transpose(
                    ps_tr[:, i * 128: i * 128 + NS],
                    src[:, i * 128:(i + 1) * 128], id_sb[0:NS, 0:NS]),
                    waits=w if i == 0 else [], inc=TE)
            te_t = pr.c(TE)
            E("vector", lambda v, dst=dst: v.tensor_copy(
                dst[:, :, :],
                ps_tr[:, 0:256].rearrange("p (a b) -> p a b", a=2, b=128)[:, :, 0:NS]),
                waits=[(TE, te_t)], inc=VE)
            tr_ev[0] = pr.c(VE)
            return pr.c(VE)

        ve_qsfm = tr256(qsB, qsFM, [(VE, ve_qs)])
        ve_ksfm = tr256(ksB, ksFM, [(VE, ve_ks)])

        s2_ps = psC[0:NS, 0:NS]
        for ic in range(2):
            E("tensor", lambda t, ic=ic: t.matmul(
                s2_ps[:, :], qsFM[:, ic, :], ksFM[:, ic, :],
                start=(ic == 0), stop=(ic == 1)),
                waits=[(VE, ve_qsfm), (VE, ve_ksfm)] if ic == 0 else [], inc=TE)
        te_s2 = pr.c(TE)
        E("vector", lambda v: v.reduce_max(sst[:, 0:1], s2_ps[:, :], axis=AXX),
          waits=[(TE, te_s2)], inc=VE)
        E("vector", lambda v: v.tensor_scalar(sst[:, 1:2], sst[:, 0:1], -0.0625, None,
                                              op0=OP.mult), inc=VE, dr=True)
        ve_m2 = pr.c(VE)
        E("scalar", lambda s: s.activation(out=a2[:, :], in_=s2_ps[:, :], func=AF.Exp,
                                           scale=0.0625, bias=sst[:, 1:2]),
          waits=[(VE, ve_m2)], inc=SE)
        se_a2 = pr.c(SE)
        E("vector", lambda v: v.reduce_sum(sst[:, 2:3], a2[:, :], axis=AXX),
          waits=[(SE, se_a2)], inc=VE)
        E("vector", lambda v: v.reciprocal(sst[:, 3:4], sst[:, 2:3]), inc=VE, dr=True)
        E("vector", lambda v: v.tensor_copy(sst[:, 4:5], sst[:, 3:4]), inc=VE)
        E("vector", lambda v: v.tensor_scalar(a2[:, :], a2[:, :], sst[:, 3:4], None,
                                              op0=OP.mult), inc=VE)
        ve_a2 = pr.c(VE)
        E("tensor", lambda t: t.transpose(ps_tr[0:NS, 0:NS], a2[:, :], id_sb[0:NS, 0:NS]),
          waits=[(VE, ve_a2), (VE, tr_ev[0])], inc=TE)
        te_a2t = pr.c(TE)
        E("vector", lambda v: v.tensor_copy(a2T[:, :], ps_tr[0:NS, 0:NS]),
          waits=[(TE, te_a2t)], inc=VE)
        ve_a2t = pr.c(VE)
        tr_ev[0] = ve_a2t
        scm_ps = psB1[0:NS, 0:1024]
        for sln in range(2):
            E("tensor", lambda t, sln=sln: t.matmul(
                scm_ps[:, sln * 512:(sln + 1) * 512], a2T[:, :],
                vsB[:, sln * 512:(sln + 1) * 512], start=True, stop=True),
                waits=[(VE, ve_a2t), (VE, ve_vs)] if sln == 0 else [], inc=TE)
        te_scm = pr.c(TE)
        E("vector", lambda v: v.tensor_add(yyB[:, :], scm_ps[:, :], snewB[:, :]),
          waits=[(TE, te_scm)], inc=VE)
        ve_y = pr.c(VE)
        E("scalar", lambda s: s.activation(out=ysqd[:, :], in_=yyB[:, :],
                                           func=AF.Square, accum_out=sst[:, 0:1]),
          waits=[(VE, ve_y)], inc=SE)
        se_ysq = pr.c(SE)
        E("vector", lambda v: v.reduce_sum(sst[:, 1:2], yyB[:, :], axis=AXX), inc=VE, dr=True)
        E("vector", lambda v: v.tensor_scalar_mul(sst[:, 2:3], sst[:, 1:2], 1.0 / D), inc=VE, dr=True)
        E("vector", lambda v: v.tensor_mul(sst[:, 3:4], sst[:, 2:3], sst[:, 2:3]), inc=VE, dr=True)
        E("vector", lambda v: v.tensor_scalar(sst[:, 4:5], sst[:, 0:1], 1.0 / D, None,
                                              op0=OP.mult),
          waits=[(SE, se_ysq)], inc=VE)
        E("vector", lambda v: v.tensor_sub(sst[:, 5:6], sst[:, 4:5], sst[:, 3:4]), inc=VE, dr=True)
        ve_yv = pr.c(VE)
        E("scalar", lambda s: s.activation(out=sst[:, 6:7], in_=sst[:, 5:6], func=AF.Ln,
                                           bias=c_const[0:NS, 1:2]),
          waits=[(VE, ve_yv)], inc=SE)
        E("scalar", lambda s: s.activation(out=sst[:, 7:8], in_=sst[:, 6:7], func=AF.Exp,
                                           scale=-0.5), inc=SE)
        se_yr = pr.c(SE)
        E("vector", lambda v: v.tensor_scalar(ylnB[:, :], yyB[:, :], sst[:, 2:3],
                                              sst[:, 7:8], op0=OP.subtract, op1=OP.mult),
          waits=[(SE, se_yr)], inc=VE)
        ve_yln = pr.c(VE)
        ve_ylnfm = smajor_to_fm(ylnB, ylnFM, [(VE, ve_yln)])

        ve_wvpg0 = [0]
        for ic in range(8):
            E("vector", lambda v, ic=ic: v.tensor_scalar(
                wvpN[:, ic, :], wvpN[:, ic, :], gf[:, ic:ic + 1], None, op0=OP.mult),
                waits=[(GL, gl_wvp), (VE, ve_gf)] if ic == 0 else [], inc=VE)
        ve_wvpg = pr.c(VE)

        ld_bw = [0]
        te_bt2 = [0]
        for half in range(2):
            for ic in range(8):
                slb = ic % 2
                E("sync", lambda s, ic=ic, slb=slb: s.dma_start(
                    out=r_x[:, slb, :], in_=dr["wvp"][ic * 128:(ic + 1) * 128, :]),
                    waits=([(VE, ve_snb), (VE, ve_yln)] if (half == 0 and ic == 0) else
                           [(TE, te_bt2[0])]), inc=LD)
                ld_bw[0] = pr.c(LD)
                E("tensor", lambda t, ic=ic, half=half, slb=slb: t.matmul(
                    psC[0:1, 0:512], bfm[:, ic:ic + 1],
                    r_x[:, slb, half * 512:(half + 1) * 512],
                    start=(ic == 0), stop=(ic == 7)),
                    waits=[(LD, ld_bw[0])] + ([(VE, ve_yv)] if (half == 0 and ic == 0) else []),
                    inc=TE)
                te_bt2[0] = pr.c(TE)
            dstrow = hb_row if half == 0 else b1_row
            E("vector", lambda v, dstrow=dstrow: v.tensor_copy(
                dstrow[0:1, 0:512], psC[0:1, 0:512]),
                waits=[(TE, te_bt2[0])], inc=VE)
        ve_btr = pr.c(VE)

        sv_ps = psA[0:NS, 1024:2048]
        E("sync", lambda s: s.dma_start(out=bias_scr[0:1, :], in_=dr["bvp"]),
          waits=[(VE, ve_btr), (TE, pr.c(TE))], inc=LD)
        ld_bvp = pr.c(LD)
        for ic in range(8):
            for sln in range(2):
                E("tensor", lambda t, ic=ic, sln=sln: t.matmul(
                    sv_ps[:, sln * 512:(sln + 1) * 512], ylnFM[:, ic, :],
                    wvpN[:, ic, sln * 512:(sln + 1) * 512],
                    start=(ic == 0), stop=False, skip_group_check=True),
                    waits=[(VE, ve_ylnfm), (VE, ve_wvpg), (VE, ve_vs)] if (ic == 0 and sln == 0) else [],
                    inc=TE)
        for sln in range(2):
            E("tensor", lambda t, sln=sln: t.matmul(
                sv_ps[:, sln * 512:(sln + 1) * 512], ones_m[0:1, 0:NS],
                bias_scr[0:1, sln * 512:(sln + 1) * 512],
                start=False, stop=False, skip_group_check=True),
                waits=[(LD, ld_bvp)] if sln == 0 else [], inc=TE)
            btsrc = hb_row if sln == 0 else b1_row
            E("tensor", lambda t, sln=sln, btsrc=btsrc: t.matmul(
                sv_ps[:, sln * 512:(sln + 1) * 512], ones_m[0:1, 0:NS],
                btsrc[0:1, 0:512],
                start=False, stop=True, skip_group_check=True),
                waits=[(VE, ve_btr)] if sln == 0 else [], inc=TE)
        te_sv = pr.c(TE)
        E("vector", lambda v: v.tensor_copy(svB[:, :], sv_ps[:, :]),
          waits=[(TE, te_sv)], inc=VE)
        ve_sv = pr.c(VE)


        for tcg in range(4):
            w = [(VE, ve_alpha_all), (VE, tr_ev[0])]
            for i in range(4):
                tc = tcg * 4 + i
                E("tensor", lambda t, tc=tc, i=i: t.transpose(
                    ps_tr[0:NS, i * 128:(i + 1) * 128],
                    alphaT[:, tc, :], id_sb[:]),
                    waits=w if i == 0 else [], inc=TE)
            te_t = pr.c(TE)
            E("vector", lambda v, tcg=tcg: v.tensor_copy(
                alphaFM[:, tcg * 4:(tcg + 1) * 4, :],
                ps_tr[0:NS, :]), waits=[(TE, te_t)], inc=VE)
            tr_ev[0] = pr.c(VE)
        ve_alphafm = pr.c(VE)
        if debug:
            E("sync", lambda s: s.dma_start(out=dbg["dbg_sv"], in_=svB[:, :]),
              waits=[(VE, ve_sv)], inc=SO)

        afm = alphaFM[:, :, :].rearrange("p a b -> p (a b)")
        ve_of = {}
        for dc in range(8):
            for sln in range(4):
                ps = sc0 if dc % 2 == 0 else sc1
                w = []
                if dc == 0 and sln == 0:
                    w = [(VE, ve_sv), (VE, ve_alphafm)]
                if (dc, sln) >= (0, 2) and (dc * 4 + sln) >= 2:
                    prevkey = (dc * 4 + sln) - 2
                    w.append((VE, ve_of[prevkey]))
                E("tensor", lambda t, dc=dc, sln=sln, ps=ps: t.matmul(
                    ps[:, (sln % 2) * 512:(sln % 2 + 1) * 512],
                    svB[:, dc * 128:(dc + 1) * 128],
                    afm[0:NS, sln * 512:(sln + 1) * 512],
                    start=True, stop=True), waits=w, inc=TE)
                te_o = pr.c(TE)
                E("vector", lambda v, dc=dc, sln=sln, ps=ps: v.tensor_copy(
                    r_q[:, dc, sln * 512:(sln + 1) * 512],
                    ps[:, (sln % 2) * 512:(sln % 2 + 1) * 512]),
                    waits=[(TE, te_o)], inc=VE)
                ve_of[dc * 4 + sln] = pr.c(VE)
        ve_outfm = pr.c(VE)

        E("sync", lambda s: s.dma_start(out=bias_scr[0:1, :], in_=dr["bo"]),
          waits=[(LD, ld_bvp), (TE, te_sv)], inc=LD)
        ld_bo = pr.c(LD)
        ld_xf = [0] * TC
        ve_fin = [0] * TC
        so_out = [0] * TC
        E("sync", x_load(0, 0, x2s), waits=[(TE, te_sv)], inc=LD)
        ld_xf[0] = pr.c(LD)
        for tc in range(TC):
            sl = tc % 2
            if tc + 1 < TC:
                w = [(TE, te_sv)]
                if tc >= 1:
                    w += [(VE, ve_fin[tc - 1]), (SO, so_out[tc - 1])]
                E("sync", x_load(tc + 1, (tc + 1) % 2, x2s), waits=w, inc=LD)
                ld_xf[tc + 1] = pr.c(LD)
            ps = sc0 if tc % 2 == 0 else sc1
            for ic in range(8):
                for sln in range(2):
                    w = []
                    if ic == 0 and sln == 0:
                        w = [(VE, ve_outfm), (GL, gl_wo)]
                        if tc >= 2:
                            w.append((VE, ve_fin[tc - 2]))
                    E("tensor", lambda t, ps=ps, tc=tc, ic=ic, sln=sln: t.matmul(
                        ps[:, sln * 512:(sln + 1) * 512],
                        r_q[:, ic, tc * 128:(tc + 1) * 128],
                        woN[:, ic, sln * 512:(sln + 1) * 512],
                        start=(ic == 0), stop=False), waits=w, inc=TE)
            for sln in range(2):
                E("tensor", lambda t, ps=ps, sln=sln: t.matmul(
                    ps[:, sln * 512:(sln + 1) * 512], ones_m[0:1, :],
                    bias_scr[0:1, sln * 512:(sln + 1) * 512],
                    start=False, stop=True),
                    waits=[(LD, ld_bo)] if sln == 0 else [], inc=TE)
            te_fin = pr.c(TE)
            xb = r_x[:, sl, :]
            E("vector", lambda v, xb=xb, ps=ps: v.tensor_add(xb, ps[:, :], xb),
              waits=[(TE, te_fin), (LD, ld_xf[tc])], inc=VE)
            ve_fin[tc] = pr.c(VE)
            E("sync", lambda s, xb=xb, tc=tc: s.dma_start(
                out=out[tc * P:(tc + 1) * P, :], in_=xb),
                waits=[(VE, ve_fin[tc])], inc=SO)
            so_out[tc] = pr.c(SO)
        so_done = pr.c(SO)
        if _TAPS:
            tap_pairs = [("tap_hfm", hFM[:, :, :].rearrange("p a b -> p (a b)")),
                           ("tap_alpha", alphaT[:, :, :].rearrange("p a b -> p (a b)")),
                           ("tap_sv", svB[:, :]), ("tap_snew", snewB[:, :]),
                           ("tap_yln", ylnB[:, :]), ("tap_vs", vsB[:, :]),
                           ("tap_sifm", r_qv[:, 0:512]),
                           ("tap_bt", None), ("tap_wvp", rkv[:, 0:8192]),
                           ("tap_ylnfm", r_qv[:, 1024:1536]),
                           ("tap_outfm", r_q[:, :, :].rearrange("p a b -> p (a b)"))]
            for nm, ap in tap_pairs:
                if ap is None:
                    continue
                E("sync", lambda s, nm=nm, ap=ap: s.dma_start(out=dbg[nm], in_=ap),
                  waits=[(SO, so_done)], inc=SO)
            E("sync", lambda s: s.dma_start(out=dbg["tap_bt"][0:1, 0:512],
                                            in_=hb_row[0:1, :]),
              waits=[(SO, so_done)], inc=SO)
            E("sync", lambda s: s.dma_start(out=dbg["tap_bt"][0:1, 512:1024],
                                            in_=b1_row[0:1, :]), inc=SO)
            so_done = pr.c(SO)
        E("sync", lambda s: s.nop(), waits=[(SO, so_done)], inc=None)

        pr.finalize()
        block = stack.enter_context(nc.Block())

        @block.sync
        def _(eng):
            pr.run_engine("sync", eng)

        @block.gpsimd
        def _(eng):
            pr.run_engine("gpsimd", eng)

        @block.tensor
        def _(eng):
            pr.run_engine("tensor", eng)

        @block.scalar
        def _(eng):
            pr.run_engine("scalar", eng)

        @block.vector
        def _(eng):
            pr.run_engine("vector", eng)

    return nc


_NC_CACHE = {}


def kernel(**inputs):
    consts = _np_consts()
    if "main" not in _NC_CACHE:
        _NC_CACHE["main"] = build_nc(debug=_DEBUG)
    nc = _NC_CACHE["main"]

    x = np.ascontiguousarray(np.asarray(inputs["L0_attn_x"], dtype=np.float32))
    names = {
        "Wq": "L0_attn_Wq", "Wk": "L0_attn_Wk", "Wv": "L0_attn_Wv",
        "Wo": "L0_attn_Wo", "slot_init": "L0_ffn_slot_init", "w1": "L0_ffn_w1",
        "w2": "L0_ffn_w2", "Wih": "L0_ffn_Wih", "Whh": "L0_ffn_Whh",
        "wq": "L0_ffn_wq", "wk": "L0_ffn_wk", "wv": "L0_ffn_wv",
        "wvp": "L0_ffn_wvp", "wo": "L0_ffn_wo",
    }
    rows = {
        "b1": "L0_ffn_b1", "b2": "L0_ffn_b2", "bih": "L0_ffn_bih",
        "bhh": "L0_ffn_bhh", "bq": "L0_ffn_bq", "bk": "L0_ffn_bk",
        "bv": "L0_ffn_bv", "ln_g": "L0_ffn_ln_g", "ln_b": "L0_ffn_ln_b",
        "bvp": "L0_ffn_bvp", "bo": "L0_ffn_bo",
    }
    shared = {}
    for k, src in names.items():
        shared[k] = np.ascontiguousarray(np.asarray(inputs[src], dtype=np.float32))
    for k, src in rows.items():
        shared[k] = np.ascontiguousarray(
            np.asarray(inputs[src], dtype=np.float32).reshape(1, -1))
    shared["tau"] = np.ascontiguousarray(
        np.asarray(inputs["L0_ffn_tau"], dtype=np.float32).reshape(1, 1))
    shared.update(consts)

    in_maps = []
    for b in range(8):
        m = dict(shared)
        m["x"] = np.ascontiguousarray(x[b])
        in_maps.append(m)

    res = run_bass_kernel_spmd(nc, in_maps, core_ids=list(range(8)))
    return np.stack([res.results[b]["out"] for b in range(8)], axis=0).astype(np.float32)


# version 13
# speedup vs baseline: 1.0254x; 1.0254x over previous
import numpy as np
import ml_dtypes
from contextlib import ExitStack

import concourse.bass as bass
import concourse.mybir as mybir
from concourse.bass_utils import run_bass_kernel_spmd
from concourse.alu_op_type import AluOpType as OP

F32 = mybir.dt.float32
BF16 = mybir.dt.bfloat16
AF = mybir.ActivationFunctionType
AXX = mybir.AxisListType.X

P = 128
T = 2048
TC = 16
D = 1024
H = 16
NPAIR = 8
KC = 16
NS = 64
DQ = 256
DHF = 512
GOB = 24

_DEBUG = False
_X2OUT = False
_TAPS = False


def _np_consts():
    h = np.array([[1.0]])
    while h.shape[0] < 64:
        h = np.block([[h, h], [h, -h]])
    u = h[1:, :] / 8.0
    ut = np.zeros((64, 64), np.float32)
    ut[:, :63] = u.T
    ut2 = np.tile(ut, (2, 1))
    ones_pair = np.zeros((128, 2), np.float32)
    ones_pair[0:64, 0] = 1.0
    ones_pair[64:128, 1] = 1.0
    return {
        "c_ut2": ut2.astype(ml_dtypes.bfloat16),
        "c_id": np.eye(128, dtype=np.float32).astype(ml_dtypes.bfloat16),
        "c_ones_m": np.ones((1, 128), np.float32),
        "c_ones_pair": ones_pair.astype(ml_dtypes.bfloat16),
        "c_ones_n": np.ones((128, 1), np.float32).astype(ml_dtypes.bfloat16),
        "c_idf": np.eye(128, dtype=np.float32),
    }


class Prog:
    def __init__(self, nc, stack):
        self.nc = nc
        self.steps = []
        self.sems = {}
        self.cnt = {}
        for name in ("LD", "GL", "TE", "VE", "SE", "GP", "SO"):
            self.sems[name] = stack.enter_context(nc.semaphore(name))
            self.cnt[name] = 0

    def emit(self, engine, fn, waits=(), inc=None, n=None, dr=False):
        if n is None:
            n = 16 if inc in ("LD", "GL", "SO") else 1
        for s, v in waits:
            assert v <= self.cnt[s], (engine, s, v, self.cnt[s])
        if inc is not None:
            self.cnt[inc] += n
        self.steps.append((engine, fn, tuple(waits), inc, n, dr))
        return self.cnt[inc] if inc is not None else None

    def c(self, sem):
        return self.cnt[sem]

    def finalize(self):
        waited = {s: set() for s in self.sems}
        for _, _, waits, _, _, _ in self.steps:
            for s, v in waits:
                waited[s].add(v)
        cnt = {s: 0 for s in self.sems}
        newcnt = {s: 0 for s in self.sems}
        remap = {s: {0: 0} for s in self.sems}
        new_steps = []
        for e, fn, waits, inc, n, dr in self.steps:
            nw = tuple((s, remap[s][v]) for s, v in waits)
            if inc is not None:
                cnt[inc] += n
                if inc in ("GL", "LD", "SO") or cnt[inc] in waited[inc]:
                    newcnt[inc] += n
                    remap[inc][cnt[inc]] = newcnt[inc]
                    new_steps.append((e, fn, nw, inc, n, dr))
                else:
                    remap[inc][cnt[inc]] = newcnt[inc]
                    new_steps.append((e, fn, nw, None, n, dr))
            else:
                new_steps.append((e, fn, nw, None, n, dr))
        self.steps = new_steps

    def run_engine(self, name, eng):
        for e, fn, waits, inc, n, dr in self.steps:
            if e != name:
                continue
            for s, v in waits:
                eng.wait_ge(self.sems[s], v)
            if dr:
                eng.drain()
            instr = fn(eng)
            if inc is not None:
                instr.then_inc(self.sems[inc], n)


def build_nc(debug=False):
    nc = bass.Bass("TRN2", target_bir_lowering=False,
                   dynamic_dma_scratch_size=8192)

    dr = {}

    def din(name, shape, dt=F32):
        dr[name] = nc.dram_tensor(name, list(shape), dt, kind="ExternalInput").ap()

    din("x", (T, D))
    for nm, shp in [
        ("Wq", (D, D)), ("Wk", (D, D)), ("Wv", (D, D)), ("Wo", (D, D)),
        ("slot_init", (NS, D)), ("w1", (2 * D, DHF)), ("b1", (1, DHF)),
        ("w2", (DHF, NS)), ("b2", (1, NS)), ("tau", (1, 1)),
        ("Wih", (3 * D, D)), ("Whh", (3 * D, D)),
        ("bih", (1, 3 * D)), ("bhh", (1, 3 * D)),
        ("wq", (D, DQ)), ("bq", (1, DQ)), ("wk", (D, DQ)), ("bk", (1, DQ)),
        ("wv", (D, D)), ("bv", (1, D)), ("ln_g", (1, D)), ("ln_b", (1, D)),
        ("wvp", (D, D)), ("bvp", (1, D)), ("wo", (D, D)), ("bo", (1, D)),
    ]:
        din(nm, shp)
    din("c_ut2", (P, 64), BF16)
    din("c_id", (P, P), BF16)
    din("c_ones_m", (1, P))
    din("c_ones_pair", (P, 2), BF16)
    din("c_ones_n", (P, 1), BF16)
    din("c_idf", (P, P))

    out = nc.dram_tensor("out", [T, D], F32, kind="ExternalOutput").ap()
    x2s = nc.dram_tensor("x2s", [T, D], F32,
                         kind="ExternalOutput" if _X2OUT else "Internal").ap()
    wgT = nc.dram_tensor("wgT", [2, GOB, P, 8 * P], BF16, kind="Internal").ap()
    dbg = {}
    if _TAPS:
        for nm, shp, dt in [
            ("tap_hfm", (P, 4 * T), BF16), ("tap_alpha", (P, TC * NS), BF16),
            ("tap_sv", (NS, D), BF16), ("tap_snew", (NS, D), BF16),
            ("tap_yln", (NS, D), BF16), ("tap_vs", (NS, D), BF16),
            ("tap_sifm", (P, 8 * NS), BF16), ("tap_outfm", (P, 8 * T), BF16),
            ("tap_bt", (1, D), F32), ("tap_wvp", (P, 8 * D), BF16),
            ("tap_ylnfm", (P, 8 * NS), BF16),
        ]:
            dbg[nm] = nc.dram_tensor(nm, list(shp), dt, kind="ExternalOutput").ap()
    if debug:
        for nm, shp, dt in [
            ("dbg_ln1", (P, 4 * T), BF16), ("dbg_qfm", (P, 8 * T), BF16),
            ("dbg_kfm", (P, 8 * T), BF16), ("dbg_vhat", (P, KC * H * 65), BF16),
            ("dbg_attnfm", (P, 8 * T), BF16), ("dbg_x2", (T, D), F32),
            ("dbg_hfm", (P, 4 * T), BF16), ("dbg_alpha", (P, TC * NS), BF16),
            ("dbg_sv", (NS, D), BF16), ("dbg_ln2fm", (P, 4 * T), BF16),
            ("dbg_logits", (P, TC * NS), F32), ("dbg_e0", (P, D), BF16),
            ("dbg_ln1b", (P, 4 * T), BF16), ("dbg_lnb", (P, 4 * T), BF16),
            ("dbg_vps", (P, D), BF16),
            ("dbg_av0", (P, 8 * 128), BF16), ("dbg_pt0", (P, TC * P), BF16),
        ]:
            dbg[nm] = nc.dram_tensor(nm, list(shp), dt, kind="ExternalOutput").ap()

    with ExitStack() as stack:
        def sb(name, shape, dt):
            return stack.enter_context(nc.sbuf_tensor(name, shape, dt))

        r_lnfm_a = sb("r_lnfm_a", [P, 4, T], BF16)
        r_lnfm_b = sb("r_lnfm_b", [P, 4, T], BF16)
        r_q = sb("r_q", [P, NPAIR, T], BF16)
        r_k = sb("r_k", [P, NPAIR, T], BF16)
        r_v = sb("r_v", [P, KC, H, 65], BF16)
        r_e = sb("r_e", [P, 8, D], BF16)
        r_f1 = sb("r_f1", [P, 8, D], BF16)
        r_f2 = sb("r_f2", [P, 8, D], BF16)
        r_x = sb("r_x", [P, 2, D], F32)
        r_et = sb("r_et", [P, 2, D], BF16)
        r_pt = sb("r_pt", [P, TC, P], BF16)
        r_qv = sb("r_qv", [P, T], BF16)

        c_const = sb("c_const", [P, 2], F32)
        stats = sb("stats", [P, 16], F32)
        t8 = sb("t8", [P, 8], F32)
        negm1 = sb("negm1", [P, 1], F32)
        e8b = sb("e8b", [P, 8], F32)
        e64b = sb("e64b", [P, NS], F32)
        maskb = sb("maskb", [P, NS], F32)
        s8 = sb("s8", [P, 2], F32)
        logits = sb("logits", [P, NS], F32)
        recipC = sb("recipC", [P, 8], F32)
        inv_tau = sb("inv_tau", [P, 1], F32)
        tau_t = sb("tau_t", [1, 2], F32)
        id_sb = sb("id_sb", [P, P], BF16)
        ut2_sb = sb("ut2_sb", [P, 64], BF16)
        ones_m = sb("ones_m", [1, P], F32)
        ones_pr = sb("ones_pr", [P, 2], BF16)
        ones_n = sb("ones_n", [P, 1], BF16)
        idf_sb = sb("idf_sb", [P, P], F32)
        bias_g = sb("bias_g", [1, 2 * P], F32)
        b1_row = sb("b1_row", [1, DHF], F32)
        b2_row = sb("b2_row", [1, NS], F32)
        hb_row = sb("hb_row", [1, DHF], F32)
        hbiasT = sb("hbiasT", [P, 4], F32)
        gf = sb("gf", [P, 8], F32)
        bfm = sb("bfm", [P, 8], F32)
        smf = sb("smf", [P, 8], BF16)
        slotS = sb("slotS", [NS, D], BF16)
        slotFM = sb("slotFM", [P, 8, NS], BF16)
        qsFM = sb("qsFM", [P, 2, NS], BF16)
        ksFM = sb("ksFM", [P, 2, NS], BF16)
        a2 = sb("a2", [NS, NS], BF16)
        a2T = sb("a2T", [NS, NS], BF16)
        sst = sb("sst", [NS, 8], F32)
        denA = sb("denA", [NS, 2], F32)
        gsc = sb("gsc", [NS, 8, P], F32)
        dbgav_sb = sb("dbgav", [P, 1024], BF16) if debug else None
        _sfm3 = r_qv[:, 0:1536].rearrange("p (a b c) -> p a b c", a=3, b=8, c=NS)
        siFM = _sfm3[:, 0, :, :]
        snFM = _sfm3[:, 1, :, :]
        ylnFM = _sfm3[:, 2, :, :]

        wtv = r_pt[:, :, :].rearrange("p a b -> p (a b)").rearrange(
            "p (w i c) -> p w i c", w=2, i=8, c=P)
        alphaFM = r_et[:, :, :].rearrange("p a b -> p (a b)")[0:NS, :].rearrange(
            "p (a b) -> p a b", a=TC, b=P)
        etf = r_et[:, :, :].rearrange("p a b -> p (a b)")
        f1f2 = r_f1[:, :, :].rearrange("p a b -> p (a b)")
        rptf = r_pt[:, :, :].rearrange("p a b -> p (a b)")
        siB = etf[0:NS, 0:D]
        rrB = etf[0:NS, 0:D]
        zzB = etf[0:NS, D:2 * D]
        snewB = f1f2[0:NS, 0:D]
        yyB = f1f2[0:NS, D:2 * D]
        ylnB = f1f2[0:NS, 2 * D:3 * D]
        ysqd = f1f2[0:NS, 3 * D:4 * D]
        vsB = rptf[0:NS, 0:D]
        svB = rptf[0:NS, D:2 * D]
        rlnbf = r_lnfm_b[:, :, :].rearrange("p a b -> p (a b)")
        alphaT = rlnbf[:, 0:TC * NS].rearrange("p (a b) -> p a b", a=TC, b=NS)
        qsB = rlnbf[0:NS, 1024:1024 + DQ]
        ksB = rlnbf[0:NS, 1280:1280 + DQ]
        w2N = rlnbf[:, 1536:1792].rearrange("p (a b) -> p a b", a=4, b=NS)
        bias_scr = gsc[:, :, :].rearrange("p a b -> p (a b)")[0:1, 0:D]

        f1v = r_f1[:, :, :].rearrange("p a b -> p (a b)")
        w1a_a = f1v[:, 0:2048].rearrange("p (a b) -> p a b", a=4, b=DHF)
        w1a_b = f1v[:, 2048:4096].rearrange("p (a b) -> p a b", a=4, b=DHF)
        wqN = f1v[:, 4096:6144].rearrange("p (a b) -> p a b", a=8, b=DQ)
        wkN = f1v[:, 6144:8192].rearrange("p (a b) -> p a b", a=8, b=DQ)
        f2v = r_f2[:, :, :].rearrange("p a b -> p (a b)")
        wvN = f2v[:, 0:8192].rearrange("p (a b) -> p a b", a=8, b=D)
        rkv = r_k[:, :, :].rearrange("p a b -> p (a b)")
        wvpN = rkv[:, 0:8192].rearrange("p (a b) -> p a b", a=8, b=D)
        woN = rkv[:, 8192:16384].rearrange("p (a b) -> p a b", a=8, b=D)
        rev = r_e[:, :, :].rearrange("p a b -> p (a b)")
        wstage = rev[:, 0:4096].rearrange("p (a b) -> p a b", a=4, b=D)
        ystage = rev[:, 4096:8192].rearrange("p (a b) -> p a b", a=4, b=D)
        hFM = rev[:, 0:8192].rearrange("p (a b) -> p a b", a=4, b=T)
        ln2tok = r_v[:, :, :, :].rearrange("p a b c -> p (a b c)")[:, 0:TC * D]
        ln2tok = ln2tok.rearrange("p (a b) -> p a b", a=TC, b=D)
        sq = r_et[:, :, :].rearrange("p a b -> p (a b)")
        lnt = r_pt[:, :, :].rearrange("p a b -> p (a b)")

        def lnfm(ic):
            return (r_lnfm_a if ic < 4 else r_lnfm_b)[:, ic % 4, :]

        psA = stack.enter_context(nc.psum_tensor("psA", [P, 2048], F32))
        psB1 = stack.enter_context(nc.psum_tensor("psB1", [P, 1024], F32))
        ps_tr = stack.enter_context(nc.psum_tensor("ps_tr", [P, 512], BF16))
        psC = stack.enter_context(nc.psum_tensor("psC", [P, 512], F32))

        sc0 = psA[:, 0:1024]
        sc1 = psA[:, 1024:2048]
        avA = psB1[:, 0:512].rearrange("p (a b) -> p a b", a=4, b=128)
        avB = psB1[:, 512:1024].rearrange("p (a b) -> p a b", a=4, b=128)
        y0 = psA[0:NS, 1024:2048]
        y1 = psB1[0:NS, 0:1024]
        ps_trf = psC[0:P, 128:144]

        pr = Prog(nc, stack)
        E = pr.emit
        LD, GL, TE, VE, SE, GP, SO = "LD", "GL", "TE", "VE", "SE", "GP", "SO"

        for nm, dst in [("c_id", id_sb), ("c_ut2", ut2_sb), ("c_ones_m", ones_m),
                        ("c_ones_pair", ones_pr), ("c_ones_n", ones_n), ("c_idf", idf_sb),
                        ("b1", b1_row), ("b2", b2_row)]:
            E("sync", lambda s, nm=nm, dst=dst: s.dma_start(out=dst[:], in_=dr[nm]),
              inc=LD)
        E("sync", lambda s: s.dma_start(out=tau_t[0:1, 0:1], in_=dr["tau"]), inc=LD)
        ld_consts = pr.c(LD)
        E("gpsimd", lambda g: g.memset(c_const[:, 0:1], -20.0), inc=GP)
        E("gpsimd", lambda g: g.memset(c_const[:, 1:2], 1e-5), inc=GP)
        E("gpsimd", lambda g: g.memset(r_v[:], 1.0), inc=GP)
        gp_init = pr.c(GP)
        E("gpsimd", lambda g: g.dma_start(out=slotS[:], in_=dr["slot_init"]), inc=GL)
        gl_slot = pr.c(GL)

        E("scalar", lambda s: s.activation(out=tau_t[0:1, 1:2], in_=tau_t[0:1, 0:1],
                                           func=AF.Abs),
          waits=[(LD, ld_consts)], inc=SE)
        se_tau = pr.c(SE)
        E("vector", lambda v: v.tensor_scalar_add(tau_t[0:1, 1:2], tau_t[0:1, 1:2], 0.1),
          waits=[(SE, se_tau)], inc=VE)
        E("vector", lambda v: v.reciprocal(tau_t[0:1, 0:1], tau_t[0:1, 1:2]), inc=VE, dr=True)
        ve_tau = pr.c(VE)
        E("tensor", lambda t: t.matmul(psC[0:P, 0:1], ones_m[0:1, :], tau_t[0:1, 0:1],
                                       start=True, stop=True),
          waits=[(VE, ve_tau), (LD, ld_consts)], inc=TE)
        te_tau = pr.c(TE)
        E("vector", lambda v: v.tensor_copy(inv_tau[:], psC[0:P, 0:1]),
          waits=[(TE, te_tau)], inc=VE)
        ve_invtau = pr.c(VE)

        ld_x = [0] * TC
        ve_ln = [0] * TC
        te_lntr = [0]
        tr_ev = [0]
        ve_var_prev = [0]

        def x_load(tc, slot, src):
            return lambda s: s.dma_start(out=r_x[:, slot, :],
                                         in_=src[tc * P:(tc + 1) * P, :])

        E("sync", x_load(0, 0, dr["x"]), inc=LD)
        ld_x[0] = pr.c(LD)
        se_rstd_t = [0] * TC
        ve_var_t = [0] * TC
        te_lntr_t = [0] * TC

        def ln1_stats(tc):
            sl = tc % 2
            cs = 8 * (tc % 2)
            xb = r_x[:, sl, :]
            w = [(LD, ld_x[tc])]
            if tc >= 2:
                w.append((VE, ve_var_t[tc - 2]))
            E("scalar", lambda s, xb=xb, cs=cs: s.activation(
                out=r_et[:, 0, :], in_=xb, func=AF.Square,
                accum_out=stats[:, cs:cs + 1]), waits=w, inc=SE)
            se_sq = pr.c(SE)
            E("vector", lambda v, xb=xb, cs=cs: v.reduce_sum(
                stats[:, cs + 1:cs + 2], xb, axis=AXX),
                waits=[(LD, ld_x[tc])], inc=VE)
            E("vector", lambda v, cs=cs: v.tensor_scalar_mul(
                stats[:, cs + 2:cs + 3], stats[:, cs + 1:cs + 2], 1.0 / D), inc=VE, dr=True)
            E("vector", lambda v, cs=cs: v.tensor_mul(
                stats[:, cs + 3:cs + 4], stats[:, cs + 2:cs + 3],
                stats[:, cs + 2:cs + 3]), inc=VE, dr=True)
            E("vector", lambda v, cs=cs: v.tensor_scalar(
                stats[:, cs + 4:cs + 5], stats[:, cs:cs + 1], 1.0 / D,
                None, op0=OP.mult),
                waits=[(SE, se_sq)], inc=VE)
            E("vector", lambda v, cs=cs: v.tensor_sub(
                stats[:, cs + 5:cs + 6], stats[:, cs + 4:cs + 5],
                stats[:, cs + 3:cs + 4]), inc=VE, dr=True)
            ve_var_t[tc] = pr.c(VE)
            ve_var_prev[0] = ve_var_t[tc]
            w = [(VE, ve_var_t[tc])]
            if tc == 0:
                w.append((GP, gp_init))
            E("scalar", lambda s, cs=cs: s.activation(
                out=stats[:, cs + 6:cs + 7], in_=stats[:, cs + 5:cs + 6],
                func=AF.Ln, bias=c_const[:, 1:2]), waits=w, inc=SE)
            E("scalar", lambda s, cs=cs: s.activation(
                out=stats[:, cs + 7:cs + 8], in_=stats[:, cs + 6:cs + 7],
                func=AF.Exp, scale=-0.5), inc=SE)
            se_rstd_t[tc] = pr.c(SE)

        def ln1_norm(tc):
            sl = tc % 2
            cs = 8 * (tc % 2)
            xb = r_x[:, sl, :]
            lslot = lnt[:, (tc % 2) * D:(tc % 2) * D + D]
            w = [(SE, se_rstd_t[tc])]
            if tc >= 2:
                w.append((TE, te_lntr_t[tc - 2]))
            E("vector", lambda v, xb=xb, cs=cs, lslot=lslot: v.tensor_scalar(
                lslot, xb, stats[:, cs + 2:cs + 3], stats[:, cs + 7:cs + 8],
                op0=OP.subtract, op1=OP.mult), waits=w, inc=VE)
            ve_ln[tc] = pr.c(VE)
            for g in range(2):
                w = [(VE, ve_ln[tc]), (VE, tr_ev[0])]
                for i in range(4):
                    ic = 4 * g + i
                    E("tensor", lambda t, ic=ic, i=i, lslot=lslot: t.transpose(
                        ps_tr[:, i * 128:(i + 1) * 128],
                        lslot[:, ic * 128:(ic + 1) * 128], id_sb[:]),
                        waits=w if i == 0 else [], inc=TE)
                te_lntr[0] = pr.c(TE)
                tgt = r_lnfm_a if g == 0 else r_lnfm_b
                E("vector", lambda v, tgt=tgt, tc=tc: v.tensor_copy(
                    tgt[:, 0:4, tc * P:(tc + 1) * P],
                    ps_tr[:, :]), waits=[(TE, te_lntr[0])], inc=VE)
                tr_ev[0] = pr.c(VE)
            te_lntr_t[tc] = te_lntr[0]

        for tc in range(TC):
            if tc >= 1:
                ln1_norm(tc - 1)
            if tc + 1 < TC:
                w = [(VE, ve_ln[tc - 1])] if tc >= 1 else []
                E("sync", x_load(tc + 1, (tc + 1) % 2, dr["x"]), waits=w, inc=LD)
                ld_x[tc + 1] = pr.c(LD)
            ln1_stats(tc)
        ln1_norm(TC - 1)
        ve_ln1_done = pr.c(VE)

        gl_stage = [0]
        te_stage_rd = [0]
        se_yev = [0]

        def load_stage(name, half):
            return lambda g: g.dma_start(
                out=wstage[:, :, :],
                in_=dr[name][half * 512:(half + 1) * 512, :].rearrange(
                    "(a p) b -> p a b", p=P))

        def stage_to(dst, src, half, extra_w=()):
            for c in range(4):
                for icg in range(2):
                    w = [(VE, tr_ev[0])] + list(extra_w)
                    for i in range(4):
                        ic = icg * 4 + i
                        E("tensor", lambda t, c=c, ic=ic, i=i, src=src: t.transpose(
                            ps_tr[:, i * 128:(i + 1) * 128],
                            src[:, c, ic * 128:(ic + 1) * 128], id_sb[:]),
                            waits=w if i == 0 else [], inc=TE)
                    te_tr = pr.c(TE)
                    te_stage_rd[0] = te_tr
                    E("vector", lambda v, c=c, icg=icg, half=half, dst=dst: v.tensor_copy(
                        dst[:, icg * 4:(icg + 1) * 4,
                            half * 512 + c * 128: half * 512 + (c + 1) * 128],
                        ps_tr[:, :]), waits=[(TE, te_tr)], inc=VE)
                    tr_ev[0] = pr.c(VE)

        def prep_centered(name, dst):
            for half in range(2):
                E("gpsimd", load_stage(name, half),
                  waits=[(SE, se_yev[0]), (TE, te_stage_rd[0])], inc=GL)
                gl_stage[0] = pr.c(GL)
                yev_h = {}
                for hh in range(8):
                    base = 64 * (hh % 2)
                    chunk = hh // 2
                    yps = y0 if (hh % 2 == 0) else y1
                    for sln in range(2):
                        w = []
                        if sln == 0:
                            w = [(GL, gl_stage[0])]
                            w.append((SE, yev_h[hh - 2] if hh >= 2 else se_yev[0]))
                        E("tensor", lambda t, base=base, chunk=chunk, yps=yps, sx=sln: t.matmul(
                            yps[:, sx * 512:(sx + 1) * 512],
                            ut2_sb[base:base + 64, :],
                            wstage[base:base + 64, chunk, sx * 512:(sx + 1) * 512],
                            start=True, stop=True),
                            waits=w, inc=TE)
                    te_f = pr.c(TE)
                    te_stage_rd[0] = te_f
                    E("scalar", lambda s, base=base, chunk=chunk, yps=yps: s.copy(
                        out=ystage[base:base + 64, chunk, :], in_=yps[:, :]),
                        waits=[(TE, te_f)], inc=SE)
                    se_yev[0] = pr.c(SE)
                    yev_h[hh] = se_yev[0]
                stage_to(dst, ystage, half, extra_w=[(SE, se_yev[0])])

        def prep_plain(name, dst, extra_w=()):
            for half in range(2):
                E("gpsimd", load_stage(name, half),
                  waits=[(TE, te_stage_rd[0])] + list(extra_w), inc=GL)
                gl_stage[0] = pr.c(GL)
                stage_to(dst, wstage, half, extra_w=[(GL, gl_stage[0])])

        prep_centered("Wq", r_f1)
        te_wq_prep, ve_wq_prep = pr.c(TE), pr.c(VE)
        prep_centered("Wk", r_f2)
        te_wk_prep, ve_wk_prep = pr.c(TE), pr.c(VE)

        for icg in range(2):
            w = [(GL, gl_slot), (VE, tr_ev[0])]
            for i in range(4):
                ic = icg * 4 + i
                E("tensor", lambda t, ic=ic, i=i: t.transpose(
                    ps_tr[:, i * 128: i * 128 + NS],
                    slotS[:, ic * 128:(ic + 1) * 128], id_sb[0:NS, 0:NS]),
                    waits=w if i == 0 else [], inc=TE)
            te_str = pr.c(TE)
            E("vector", lambda v, icg=icg: v.tensor_copy(
                slotFM[:, icg * 4:(icg + 1) * 4, :],
                ps_tr[:, :].rearrange("p (a b) -> p a b", a=4, b=128)[:, :, 0:NS]),
                waits=[(TE, te_str)], inc=VE)
            tr_ev[0] = pr.c(VE)
        ve_slotfm = pr.c(VE)
        for ic in range(8):
            E("vector", lambda v, ic=ic: v.reduce_sum(stats[:, 0:1], slotFM[:, ic, :], axis=AXX),
              waits=[(VE, ve_slotfm), (VE, ve_ln1_done)] if ic == 0 else [], inc=VE)
            E("vector", lambda v, ic=ic: v.tensor_scalar(smf[:, ic:ic + 1], stats[:, 0:1],
                                                         1.0 / NS, None, op0=OP.mult), inc=VE, dr=True)
        ve_smf = pr.c(VE)
        E("gpsimd", lambda g: g.dma_start(
            out=ystage[:, :, 0:DHF],
            in_=dr["w1"][D:2 * D, :].rearrange("(a p) b -> p a b", p=P)[:, 0:4, :]),
          waits=[(TE, te_stage_rd[0]), (SE, se_yev[0]), (VE, tr_ev[0])], inc=GL)
        gl_w1b_a = pr.c(GL)
        E("gpsimd", lambda g: g.dma_start(
            out=wstage[:, :, 0:DHF],
            in_=dr["w1"][D:2 * D, :].rearrange("(a p) b -> p a b", p=P)[:, 4:8, :]),
          inc=GL)
        gl_w1b_b = pr.c(GL)
        for ic in range(8):
            src = ystage if ic < 4 else wstage
            E("tensor", lambda t, src=src, ic=ic: t.matmul(
                psC[0:1, 0:DHF], smf[:, ic:ic + 1], src[:, ic % 4, 0:DHF],
                start=(ic == 0), stop=(ic == 7)),
                waits=[(VE, ve_smf), (GL, gl_w1b_b if ic >= 4 else gl_w1b_a)] if ic in (0, 4) else [],
                inc=TE)
        te_c = pr.c(TE)
        te_stage_rd[0] = te_c
        E("vector", lambda v: v.tensor_add(hb_row[:], psC[0:1, 0:DHF], b1_row[:]),
          waits=[(TE, te_c), (LD, ld_consts)], inc=VE)
        ve_hb = pr.c(VE)
        for i in range(4):
            E("tensor", lambda t, i=i: t.transpose(
                ps_trf[:, i:i + 1], hb_row[0:1, i * 128:(i + 1) * 128],
                ones_m[0:1, 0:1]),
                waits=[(VE, ve_hb)] if i == 0 else [], inc=TE)
        te_hbt = pr.c(TE)
        E("vector", lambda v: v.tensor_copy(hbiasT[:, :], ps_trf[:, 0:4]),
          waits=[(TE, te_hbt)], inc=VE)
        ve_hbt = pr.c(VE)

        E("sync", lambda s: s.dma_start(out=bias_scr[0:1, :], in_=dr["ln_g"]),
          waits=[(LD, ld_consts)], inc=LD)
        ld_g = pr.c(LD)
        for i in range(8):
            E("tensor", lambda t, i=i: t.transpose(
                ps_trf[:, 8 + i:9 + i], bias_scr[0:1, i * 128:(i + 1) * 128],
                ones_m[0:1, 0:1]),
                waits=[(LD, ld_g), (VE, ve_hbt)] if i == 0 else [], inc=TE)
        te_gt = pr.c(TE)
        E("vector", lambda v: v.tensor_copy(gf[:, :], ps_trf[:, 8:16]),
          waits=[(TE, te_gt)], inc=VE)
        ve_gf = pr.c(VE)
        E("sync", lambda s: s.dma_start(out=bias_scr[0:1, :], in_=dr["ln_b"]),
          waits=[(VE, ve_gf)], inc=LD)
        ld_b = pr.c(LD)
        for i in range(8):
            E("tensor", lambda t, i=i: t.transpose(
                ps_trf[:, 8 + i:9 + i], bias_scr[0:1, i * 128:(i + 1) * 128],
                ones_m[0:1, 0:1]),
                waits=[(LD, ld_b), (VE, ve_gf)] if i == 0 else [], inc=TE)
        te_bt = pr.c(TE)
        E("vector", lambda v: v.tensor_copy(bfm[:, :], ps_trf[:, 8:16]),
          waits=[(TE, te_bt)], inc=VE)
        ve_bfm = pr.c(VE)

        te_proj = [0]
        ve_pev = [0]
        so_scat = [0]
        se_qv = [0]
        te_qv_last = [0]

        def qk_projection(wsrc, qdst, first_waits):
            for oc in range(NPAIR):
                for halft in range(2):
                    ps = sc0 if halft == 0 else sc1
                    for ic in range(8):
                        for sln in range(2):
                            w = []
                            if ic == 0 and sln == 0:
                                w = list(first_waits) + [(VE, ve_pev[0])]
                            E("tensor", lambda t, ps=ps, oc=oc, ic=ic, sln=sln, halft=halft,
                              wsrc=wsrc: t.matmul(
                                ps[:, sln * 512:(sln + 1) * 512],
                                wsrc[:, ic, oc * 128:(oc + 1) * 128],
                                lnfm(ic)[:, halft * 1024 + sln * 512: halft * 1024 + (sln + 1) * 512],
                                start=(ic == 0), stop=(ic == 7)), waits=w, inc=TE)
                    te_proj[0] = pr.c(TE)
                    E("vector", lambda v, ps=ps, oc=oc, halft=halft, qdst=qdst: v.tensor_copy(
                        qdst[:, oc, halft * 1024:(halft + 1) * 1024], ps[:, :]),
                        waits=[(TE, te_proj[0])], inc=VE)
                    ve_pev[0] = pr.c(VE)
                E("vector", lambda v, oc=oc, qdst=qdst: v.tensor_mul(
                    sq[:, :], qdst[:, oc, :], qdst[:, oc, :]),
                    waits=[(SO, so_scat[0])], inc=VE, dr=True)
                ve_sq = pr.c(VE)
                for q4 in range(4):
                    qvp = psC[0:2, 0:512] if q4 % 2 == 0 else psB1[0:2, 0:512]
                    E("tensor", lambda t, q4=q4, qvp=qvp: t.matmul(
                        qvp[:, :], ones_pr[:, :],
                        sq[:, q4 * 512:(q4 + 1) * 512],
                        start=True, stop=True),
                        waits=[(VE, ve_sq), (SE, se_qv[0])], inc=TE)
                    te_qv = pr.c(TE)
                    te_qv_last[0] = te_qv
                    E("scalar", lambda s, q4=q4, qvp=qvp: s.activation(
                        out=r_qv[0:2, q4 * 512:(q4 + 1) * 512], in_=qvp[:, :],
                        func=AF.Copy, scale=1.0 / 63.0),
                        waits=[(TE, te_qv)], inc=SE)
                    se_qv[0] = pr.c(SE)
                E("sync", lambda s, oc=oc, qdst=qdst: s.dma_start(
                    out=qdst[63:64, oc, :], in_=r_qv[0:1, :]),
                    waits=[(SE, se_qv[0]), (VE, ve_pev[0])], inc=SO)
                E("sync", lambda s, oc=oc, qdst=qdst: s.dma_start(
                    out=qdst[127:128, oc, :], in_=r_qv[1:2, :]), inc=SO)
                so_scat[0] = pr.c(SO)

        qk_projection(r_f1, r_q,
                      [(TE, te_wq_prep), (VE, ve_wq_prep), (VE, ve_ln1_done),
                       (VE, ve_hbt), (VE, ve_bfm), (VE, ve_hb)])
        so_q_scat = so_scat[0]
        qk_projection(r_f2, r_k, [(TE, te_wk_prep), (VE, ve_wk_prep)])
        so_k_scat = so_scat[0]

        prep_plain("Wv", r_f1, extra_w=[(VE, ve_pev[0]), (SE, se_qv[0])])
        te_wv_prep, ve_wv_prep = pr.c(TE), pr.c(VE)
        if debug:
            E("sync", lambda s: s.dma_start(
                out=dbg["dbg_ln1b"],
                in_=r_lnfm_a[:, :, :].rearrange("p a b -> p (a b)")),
                waits=[(TE, te_wv_prep), (VE, ve_wv_prep)], inc=SO)
            E("sync", lambda s: s.dma_start(
                out=dbg["dbg_lnb"],
                in_=r_lnfm_b[:, :, :].rearrange("p a b -> p (a b)")), inc=SO)
            so_lnb = pr.c(SO)
        for tc in range(TC):
            ps = sc0 if tc % 2 == 0 else sc1
            for ic in range(8):
                for sln in range(2):
                    w = []
                    if ic == 0 and sln == 0:
                        w = [(TE, te_wv_prep), (VE, ve_wv_prep), (VE, ve_pev[0]),
                             (GP, gp_init)]
                        if debug and tc == 0:
                            w.append((SO, so_lnb))
                    E("tensor", lambda t, ps=ps, tc=tc, ic=ic, sln=sln: t.matmul(
                        ps[:, sln * 512:(sln + 1) * 512],
                        lnfm(ic)[:, tc * 128:(tc + 1) * 128],
                        r_f1[:, ic, sln * 512:(sln + 1) * 512],
                        start=(ic == 0), stop=(ic == 7)), waits=w, inc=TE)
            te_proj[0] = pr.c(TE)
            if debug and tc == 0:
                E("scalar", lambda s: s.activation(
                    out=dbgav_sb[:, :], in_=sc0[:, :], func=AF.Copy),
                    waits=[(TE, te_proj[0])], inc=SE)
                se_dv = pr.c(SE)
                E("sync", lambda s: s.dma_start(out=dbg["dbg_vps"], in_=dbgav_sb[:, :]),
                  waits=[(SE, se_dv)], inc=SO)
                so_dv = pr.c(SO)
                E("vector", lambda v, tc=tc, ps=ps: v.tensor_copy(
                    r_v[:, tc, :, 0:64], ps[:, :]),
                    waits=[(TE, te_proj[0]), (SE, se_dv)], inc=VE)
            else:
                E("vector", lambda v, tc=tc, ps=ps: v.tensor_copy(
                    r_v[:, tc, :, 0:64], ps[:, :]),
                    waits=[(TE, te_proj[0])], inc=VE)
            ve_pev[0] = pr.c(VE)
        ve_vhat = pr.c(VE)

        prep_plain("Wo", r_f2, extra_w=[(VE, ve_pev[0])])
        te_wo_prep, ve_wo_prep = pr.c(TE), pr.c(VE)

        if debug:
            E("sync", lambda s: s.dma_start(
                out=dbg["dbg_ln1"],
                in_=r_lnfm_a[:, :, :].rearrange("p a b -> p (a b)")),
                waits=[(VE, ve_ln1_done)], inc=SO)
            E("sync", lambda s: s.dma_start(
                out=dbg["dbg_qfm"], in_=r_q[:, :, :].rearrange("p a b -> p (a b)")),
                waits=[(SO, so_q_scat)], inc=SO)
            E("sync", lambda s: s.dma_start(
                out=dbg["dbg_kfm"], in_=r_k[:, :, :].rearrange("p a b -> p (a b)")),
                waits=[(SO, so_k_scat)], inc=SO)
            E("sync", lambda s: s.dma_start(
                out=dbg["dbg_vhat"],
                in_=r_v[:, :, :, :].rearrange("p a b c -> p (a b c)")),
                waits=[(VE, ve_vhat)], inc=SO)

        E("gpsimd", lambda g: g.nop(), waits=[(VE, ve_vhat)], inc=GP)
        gp_f1_free = pr.c(GP)
        for fn in [
            lambda g: g.dma_start(out=w1a_a[:, :, :],
                                  in_=dr["w1"][0:D, :].rearrange("(a p) b -> p a b", p=P)[:, 0:4, :]),
            lambda g: g.dma_start(out=w1a_b[:, :, :],
                                  in_=dr["w1"][0:D, :].rearrange("(a p) b -> p a b", p=P)[:, 4:8, :]),
            lambda g: g.dma_start(out=wqN[:, :, :],
                                  in_=dr["wq"].rearrange("(a p) b -> p a b", p=P)),
            lambda g: g.dma_start(out=wkN[:, :, :],
                                  in_=dr["wk"].rearrange("(a p) b -> p a b", p=P)),
        ]:
            E("gpsimd", fn, waits=[(GP, gp_f1_free)], inc=GL)
        gl_ffn1 = pr.c(GL)

        ebuf = r_et
        se_exp = {}
        te_av = {}
        ve_avev = [0]
        def emit_qk(rnd, pair, qq, kc):
            par = kc % 2
            scA = psA[:, par * 1024: par * 1024 + 512]
            scB = psA[:, par * 1024 + 512: (par + 1) * 1024]
            w = []
            if kc >= 2:
                w.append((SE, se_exp[(rnd, kc - 2)]))
            elif rnd >= 1:
                w.append((SE, se_exp[(rnd - 1, KC - 2 + kc)]))
            else:
                w += [(SO, so_q_scat), (SO, so_k_scat), (VE, ve_vhat)]
                if debug:
                    w.append((SO, pr.c(SO)))
            for hh in range(2):
                base = 64 * hh
                scp = scA if hh == 0 else scB
                E("tensor", lambda t, scp=scp, base=base, pair=pair, kc=kc,
                  qq=qq: t.matmul(
                    scp[:, :],
                    r_k[base:base + 64, pair, kc * 128:(kc + 1) * 128],
                    r_q[base:base + 64, pair, qq * 512:(qq + 1) * 512],
                    start=True, stop=True, tile_position=(base, 0)),
                    waits=w if hh == 0 else [], inc=TE)
            te_qkv = pr.c(TE)
            w = [(TE, te_qkv)]
            if kc >= 2:
                w.append((TE, te_av[(rnd, kc - 2, 1)]))
            elif rnd >= 1:
                w.append((TE, te_av[(rnd - 1, KC - 2 + kc, 1)]))
            E("scalar", lambda s, kc=kc: s.activation(
                out=ebuf[:, kc % 2, :],
                in_=psA[:, (kc % 2) * 1024:(kc % 2 + 1) * 1024], func=AF.Exp,
                scale=1.0 / 64.0, bias=c_const[:, 0:1]),
                waits=w, inc=SE)
            se_exp[(rnd, kc)] = pr.c(SE)

        def emit_av(rnd, pair, kc):
            for hh in range(2):
                avp = avA if hh == 0 else avB
                hglob = 2 * pair + hh
                w = [(SE, se_exp[(rnd, kc)])]
                if kc == 0:
                    w.append((VE, ve_avev[0]))
                for qc in range(4):
                    st = (kc == 0 and qc == 0)
                    E("tensor", lambda t, kc=kc, qc=qc, hglob=hglob, st=st,
                      avp=avp, hh=hh: t.matmul(
                        avp[:, qc, 0:65],
                        ebuf[:, kc % 2, hh * 512 + qc * 128:
                             hh * 512 + (qc + 1) * 128],
                        r_v[:, kc, hglob, :],
                        start=st, stop=(kc == KC - 1),
                        skip_group_check=True),
                        waits=w if (qc == 0 and hh == 0) else [], inc=TE)
                te_av[(rnd, kc, hh)] = pr.c(TE)

        gwt_gl = [0]
        gwt_te = [0]
        gwt_ve = [0]
        gwt_so = [0]

        gwt_te_by_step = {}
        gwt_gl_by_step = {}
        gwt_so_by_step = {}

        gwt_ve_by_step = {}

        def emit_gru_load(step):
            which, ob = step % 2, step // 2
            nm = "Wih" if which == 0 else "Whh"
            slot = step % 2
            w = [(VE, ve_vhat)] if step < 2 else [(TE, gwt_te_by_step[step - 2])]
            E("sync", lambda s, nm=nm, ob=ob, slot=slot: s.dma_start(
                out=r_x[:, slot, :],
                in_=dr[nm][ob * 128:(ob + 1) * 128, :]),
                waits=w, inc=LD)
            gwt_gl_by_step[step] = pr.c(LD)

        def emit_gru_tr(step):
            which, ob = step % 2, step // 2
            slot = step % 2
            yslot = step % 4
            for icg in range(2):
                w = [(LD, gwt_gl_by_step[step]), (VE, gwt_ve[0])]
                if step >= 4:
                    w.append((SO, gwt_so_by_step[step - 4]))
                for i in range(4):
                    ic = icg * 4 + i
                    E("tensor", lambda t, ic=ic, i=i, slot=slot: t.transpose(
                        psC[:, i * 128:(i + 1) * 128],
                        r_x[:, slot, ic * 128:(ic + 1) * 128], idf_sb[:]),
                        waits=w if i == 0 else [], inc=TE)
                te_t = pr.c(TE)
                gwt_te_by_step[step] = te_t
                E("vector", lambda v, icg=icg, yslot=yslot: v.tensor_copy(
                    ystage[:, yslot, icg * 512:(icg + 1) * 512], psC[:, :]),
                    waits=[(TE, te_t)], inc=VE)
                gwt_ve[0] = pr.c(VE)
            gwt_ve_by_step[step] = pr.c(VE)
            E("sync", lambda s, which=which, ob=ob, yslot=yslot: s.dma_start(
                out=wgT[which, ob, :, :], in_=ystage[:, yslot, :]),
                waits=[(VE, gwt_ve_by_step[step])], inc=SO)
            gwt_so_by_step[step] = pr.c(SO)
            gwt_so[0] = gwt_so_by_step[step]

        se_exp_h = {}
        pending_ptr = [None]
        te_ptr_last = [0]

        def emit_pair_tr():
            if pending_ptr[0] is None:
                return
            ppair, ve_ready = pending_ptr[0]
            pending_ptr[0] = None
            for g4 in range(4):
                w = [(VE, ve_ready), (VE, tr_ev[0])]
                for i in range(4):
                    tcl = g4 * 4 + i
                    E("tensor", lambda t, tcl=tcl, i=i: t.transpose(
                        ps_tr[:, i * 128:(i + 1) * 128],
                        r_pt[:, tcl, :], id_sb[:]),
                        waits=w if i == 0 else [], inc=TE)
                te_ptr = pr.c(TE)
                te_ptr_last[0] = te_ptr
                E("vector", lambda v, g4=g4, ppair=ppair: v.tensor_copy(
                    r_q[:, ppair, g4 * 512:(g4 + 1) * 512], ps_tr[:, :]),
                    waits=[(TE, te_ptr)], inc=VE)
                tr_ev[0] = pr.c(VE)

        NSTEP = 2 * GOB
        loads_it = iter(range(NSTEP))
        trs_it = iter(range(NSTEP))
        for pair in range(NPAIR):
            for qq in range(4):
                rnd = pair * 4 + qq
                if rnd == 2:
                    for _ in range(2):
                        st_i = next(loads_it, None)
                        if st_i is not None:
                            emit_gru_load(st_i)
                if rnd >= 3:
                    for _ in range(2):
                        st_i = next(trs_it, None)
                        if st_i is not None:
                            emit_gru_tr(st_i)
                            st_l = next(loads_it, None)
                            if st_l is not None:
                                emit_gru_load(st_l)
                emit_qk(rnd, pair, qq, 0)
                emit_qk(rnd, pair, qq, 1)
                if qq == 0:
                    emit_pair_tr()
                for kc in range(KC):
                    emit_av(rnd, pair, kc)
                    if kc + 2 < KC:
                        emit_qk(rnd, pair, qq, kc + 2)
                te_av_done = pr.c(TE)
                E("vector", lambda v: v.reciprocal(recipC[:, 0:4], avA[:, :, 64]),
                  waits=[(TE, te_av_done), (TE, te_ptr_last[0])], inc=VE)
                E("vector", lambda v: v.reciprocal(recipC[:, 4:8], avB[:, :, 64]),
                  inc=VE)
                E("vector", lambda v: v.tensor_copy(stats[:, 0:1], recipC[:, 0:1]), inc=VE)
                for hh in range(2):
                    avp = avA if hh == 0 else avB
                    for qc in range(4):
                        E("vector", lambda v, qc=qc, qq=qq, hh=hh, avp=avp: v.tensor_scalar(
                            r_pt[:, qq * 4 + qc, hh * 64:hh * 64 + 64],
                            avp[:, qc, 0:64], recipC[:, hh * 4 + qc:hh * 4 + qc + 1],
                            None, op0=OP.mult),
                            inc=VE)
                ve_avev[0] = pr.c(VE)
            pending_ptr[0] = (pair, ve_avev[0])
        for st_i in trs_it:
            emit_gru_tr(st_i)
            st_l = next(loads_it, None)
            if st_l is not None:
                emit_gru_load(st_l)
        emit_pair_tr()
        ve_attnfm = pr.c(VE)
        te_headloop = pr.c(TE)

        if debug:
            E("sync", lambda s: s.dma_start(
                out=dbg["dbg_attnfm"], in_=r_q[:, :, :].rearrange("p a b -> p (a b)")),
                waits=[(VE, ve_attnfm)], inc=SO)

        E("gpsimd", lambda g: g.nop(), waits=[(TE, te_headloop)], inc=GP)
        gp_rk_free = pr.c(GP)
        E("gpsimd", lambda g: g.dma_start(
            out=wvpN[:, :, :], in_=dr["wvp"].rearrange("(a p) b -> p a b", p=P)),
          waits=[(GP, gp_rk_free)], inc=GL)
        gl_wvp = pr.c(GL)
        E("gpsimd", lambda g: g.dma_start(
            out=woN[:, :, :], in_=dr["wo"].rearrange("(a p) b -> p a b", p=P)),
          inc=GL)
        gl_wo = pr.c(GL)

        ld_x2 = [0] * TC
        ve_x2 = [0] * TC
        ve_norm2 = [0] * TC
        so_x2 = [0] * TC
        se_sq2 = [0]
        E("sync", x_load(0, 0, dr["x"]),
          waits=[(VE, ve_avev[0]), (SO, pr.c(SO))], inc=LD)
        ld_x2[0] = pr.c(LD)

        def emit_ln2tr(tc):
            for g in range(2):
                w = [(VE, ve_norm2[tc]), (VE, tr_ev[0])]
                for i in range(4):
                    ic = 4 * g + i
                    E("tensor", lambda t, tc=tc, ic=ic, i=i: t.transpose(
                        ps_tr[:, i * 128:(i + 1) * 128],
                        ln2tok[:, tc, ic * 128:(ic + 1) * 128], id_sb[:]),
                        waits=w if i == 0 else [], inc=TE)
                te_tr2 = pr.c(TE)
                tgt = r_lnfm_a if g == 0 else r_lnfm_b
                E("vector", lambda v, tgt=tgt, tc=tc: v.tensor_copy(
                    tgt[:, 0:4, tc * P:(tc + 1) * P],
                    ps_tr[:, :]), waits=[(TE, te_tr2)], inc=VE)
                tr_ev[0] = pr.c(VE)

        for tc in range(TC):
            sl = tc % 2
            if tc + 1 < TC:
                w = []
                if tc >= 1:
                    w = [(VE, ve_norm2[tc - 1]), (SO, so_x2[tc - 1])]
                E("sync", x_load(tc + 1, (tc + 1) % 2, dr["x"]), waits=w, inc=LD)
                ld_x2[tc + 1] = pr.c(LD)
            ps = sc0 if tc % 2 == 0 else sc1
            for ic in range(8):
                for sln in range(2):
                    w = []
                    if ic == 0 and sln == 0:
                        w = [(VE, ve_attnfm), (TE, te_wo_prep), (VE, ve_wo_prep)]
                        if tc >= 2:
                            w.append((VE, ve_x2[tc - 2]))
                    E("tensor", lambda t, ps=ps, tc=tc, ic=ic, sln=sln: t.matmul(
                        ps[:, sln * 512:(sln + 1) * 512],
                        r_q[:, ic, tc * 128:(tc + 1) * 128],
                        r_f2[:, ic, sln * 512:(sln + 1) * 512],
                        start=(ic == 0), stop=(ic == 7)), waits=w, inc=TE)
            te_wo = pr.c(TE)
            if tc >= 1:
                emit_ln2tr(tc - 1)
            xb = r_x[:, sl, :]
            E("vector", lambda v, xb=xb, ps=ps: v.tensor_add(xb, ps[:, :], xb),
              waits=[(TE, te_wo), (LD, ld_x2[tc])], inc=VE)
            ve_x2[tc] = pr.c(VE)
            E("sync", lambda s, xb=xb, tc=tc: s.dma_start(
                out=x2s[tc * P:(tc + 1) * P, :], in_=xb),
                waits=[(VE, ve_x2[tc])], inc=SO)
            so_x2[tc] = pr.c(SO)
            E("scalar", lambda s, xb=xb: s.activation(
                out=sq[:, 0:D], in_=xb, func=AF.Square, accum_out=stats[:, 0:1]),
                waits=[(VE, ve_x2[tc]), (VE, ve_var_prev[0])], inc=SE)
            se_sq2[0] = pr.c(SE)
            E("vector", lambda v, xb=xb: v.reduce_sum(stats[:, 1:2], xb, axis=AXX), inc=VE, dr=True)
            E("vector", lambda v: v.tensor_scalar_mul(stats[:, 2:3], stats[:, 1:2], 1.0 / D), inc=VE, dr=True)
            E("vector", lambda v: v.tensor_mul(stats[:, 3:4], stats[:, 2:3], stats[:, 2:3]), inc=VE, dr=True)
            E("vector", lambda v: v.tensor_scalar(stats[:, 4:5], stats[:, 0:1], 1.0 / D,
                                                  None, op0=OP.mult),
              waits=[(SE, se_sq2[0])], inc=VE)
            E("vector", lambda v: v.tensor_sub(stats[:, 5:6], stats[:, 4:5], stats[:, 3:4]), inc=VE, dr=True)
            ve_var_prev[0] = pr.c(VE)
            E("scalar", lambda s: s.activation(out=stats[:, 6:7], in_=stats[:, 5:6],
                                               func=AF.Ln, bias=c_const[:, 1:2]),
              waits=[(VE, ve_var_prev[0])], inc=SE)
            E("scalar", lambda s: s.activation(out=stats[:, 7:8], in_=stats[:, 6:7],
                                               func=AF.Exp, scale=-0.5), inc=SE)
            se_rstd2 = pr.c(SE)
            E("vector", lambda v, xb=xb, tc=tc: v.tensor_scalar(
                ln2tok[:, tc, :], xb, stats[:, 2:3], stats[:, 7:8],
                op0=OP.subtract, op1=OP.mult),
                waits=[(SE, se_rstd2)], inc=VE)
            ve_norm2[tc] = pr.c(VE)
        emit_ln2tr(TC - 1)
        ve_ln2fm = pr.c(VE)
        te_p3 = pr.c(TE)

        if debug:
            E("sync", lambda s: s.dma_start(out=dbg["dbg_x2"], in_=x2s),
              waits=[(SO, so_x2[TC - 1])], inc=SO)
            E("sync", lambda s: s.dma_start(
                out=dbg["dbg_ln2fm"], in_=r_lnfm_a[:, :, :].rearrange("p a b -> p (a b)")),
                waits=[(VE, ve_ln2fm)], inc=SO)

        se_gelu = [0]
        for dhc in range(4):
            for halft in range(2):
                ps = sc0 if halft == 0 else sc1
                for ic in range(8):
                    wsrc = w1a_a if ic < 4 else w1a_b
                    for sln in range(2):
                        w = []
                        if ic == 0 and sln == 0:
                            w = [(VE, ve_ln2fm), (GL, gl_ffn1), (VE, ve_hbt)]
                            if dhc > 0 or halft > 0:
                                w.append((SE, se_gelu[0]))
                        E("tensor", lambda t, ps=ps, dhc=dhc, ic=ic, sln=sln, halft=halft,
                          wsrc=wsrc: t.matmul(
                            ps[:, sln * 512:(sln + 1) * 512],
                            wsrc[:, ic % 4, dhc * 128:(dhc + 1) * 128],
                            lnfm(ic)[:, halft * 1024 + sln * 512: halft * 1024 + (sln + 1) * 512],
                            start=(ic == 0), stop=(ic == 7)), waits=w, inc=TE)
                te_h = pr.c(TE)
                E("scalar", lambda s, ps=ps, dhc=dhc, halft=halft: s.activation(
                    out=hFM[:, dhc, halft * 1024:(halft + 1) * 1024], in_=ps[:, :],
                    func=AF.Gelu, bias=hbiasT[:, dhc:dhc + 1]),
                    waits=[(TE, te_h)], inc=SE)
                se_gelu[0] = pr.c(SE)
        se_hfm = pr.c(SE)
        if debug:
            E("sync", lambda s: s.dma_start(
                out=dbg["dbg_hfm"], in_=hFM[:, :, :].rearrange("p a b -> p (a b)")),
                waits=[(SE, se_hfm)], inc=SO)

        E("gpsimd", lambda g: g.dma_start(
            out=w2N[:, :, :], in_=dr["w2"].rearrange("(a p) b -> p a b", p=P)),
            waits=[(SE, se_hfm)], inc=GL)
        gl_w2 = pr.c(GL)
        ve_alpha = [0] * TC
        rlp = psC[0:P, 64:128]
        for tc in range(TC):
            for dhc in range(4):
                w = []
                if dhc == 0:
                    w = [(SE, se_hfm), (GL, gl_w2)]
                    if tc >= 1:
                        w.append((VE, ve_alpha[tc - 1]))
                E("tensor", lambda t, tc=tc, dhc=dhc: t.matmul(
                    rlp[:, :], hFM[:, dhc, tc * 128:(tc + 1) * 128],
                    w2N[:, dhc, :], start=(dhc == 0), stop=False,
                    skip_group_check=True), waits=w, inc=TE)
            E("tensor", lambda t: t.matmul(rlp[:, :], ones_m[0:1, :], b2_row[0:1, :],
                                           start=False, stop=True,
                                           skip_group_check=True), inc=TE)
            te_rl = pr.c(TE)
            E("vector", lambda v: v.tensor_scalar(logits[:, :], rlp[:, :], inv_tau[:, 0:1],
                                                  None, op0=OP.mult),
              waits=[(TE, te_rl), (VE, ve_invtau)], inc=VE)
            E("vector", lambda v: v.max(t8[:, :], logits[:, :]), inc=VE, dr=True)
            E("vector", lambda v: v.tensor_scalar(negm1[:, :], t8[:, 0:1], -1.0, None,
                                                  op0=OP.mult), inc=VE, dr=True)
            ve_t8 = pr.c(VE)
            E("scalar", lambda s: s.activation(out=e8b[:, :], in_=t8[:, :], func=AF.Exp,
                                               bias=negm1[:, 0:1]),
              waits=[(VE, ve_t8)], inc=SE)
            E("scalar", lambda s: s.activation(out=e64b[:, :], in_=logits[:, :],
                                               func=AF.Exp, bias=negm1[:, 0:1]), inc=SE)
            se_e = pr.c(SE)
            E("vector", lambda v: v.reduce_sum(s8[:, 0:1], e8b[:, :], axis=AXX),
              waits=[(SE, se_e)], inc=VE)
            E("vector", lambda v: v.reciprocal(s8[:, 1:2], s8[:, 0:1]), inc=VE, dr=True)
            E("vector", lambda v: v.tensor_scalar(maskb[:, :], logits[:, :], t8[:, 7:8],
                                                  None, op0=OP.is_ge), inc=VE)
            E("vector", lambda v, tc=tc: v.scalar_tensor_tensor(
                alphaT[:, tc, :], e64b[:, :], s8[:, 1:2], maskb[:, :],
                op0=OP.mult, op1=OP.mult), inc=VE)
            ve_alpha[tc] = pr.c(VE)
            if debug:
                E("sync", lambda s, tc=tc: s.dma_start(
                    out=dbg["dbg_logits"][:, tc * NS:(tc + 1) * NS], in_=logits[:, :]),
                    waits=[(VE, ve_alpha[tc])], inc=SO)
        ve_alpha_all = pr.c(VE)
        if debug:
            E("sync", lambda s: s.dma_start(
                out=dbg["dbg_alpha"], in_=alphaT[:, :, :].rearrange("p a b -> p (a b)")),
                waits=[(VE, ve_alpha_all)], inc=SO)

        si_ps = psA[0:NS, 1024:2048]
        for tc in range(TC):
            for sln in range(2):
                w = []
                if sln == 0:
                    w = [(VE, ve_alpha[tc])]
                    if tc == 0:
                        w.append((VE, ve_avev[0]))
                E("tensor", lambda t, tc=tc, sln=sln: t.matmul(
                    si_ps[:, sln * 512:(sln + 1) * 512],
                    alphaT[:, tc, :], ln2tok[:, tc, sln * 512:(sln + 1) * 512],
                    start=(tc == 0), stop=(tc == TC - 1), skip_group_check=True),
                    waits=w, inc=TE)
        te_si = pr.c(TE)
        dn_ps = psC[0:NS, 0:1]
        for tc in range(TC):
            E("tensor", lambda t, tc=tc: t.matmul(
                dn_ps[:, :], alphaT[:, tc, :], ones_n[:, :],
                start=(tc == 0), stop=(tc == TC - 1)),
                waits=[(VE, ve_alpha_all)] if tc == 0 else [], inc=TE)
        te_dn = pr.c(TE)
        E("vector", lambda v: v.tensor_scalar_add(denA[:, 0:1], dn_ps[:, :], 1e-8),
          waits=[(TE, te_dn), (TE, te_si)], inc=VE)
        E("vector", lambda v: v.reciprocal(denA[:, 1:2], denA[:, 0:1]), inc=VE, dr=True)
        E("vector", lambda v: v.tensor_copy(stats[0:NS, 0:1], denA[0:NS, 1:2]), inc=VE)
        E("vector", lambda v: v.tensor_scalar(siB[:, :], si_ps[:, :],
                                              denA[:, 1:2], None, op0=OP.mult), inc=VE)
        ve_si = pr.c(VE)


        def smajor_to_fm(srcap, dst, ready_w):
            for icg in range(2):
                w = list(ready_w) + [(VE, tr_ev[0])]
                for i in range(4):
                    ic = icg * 4 + i
                    E("tensor", lambda t, ic=ic, i=i, srcap=srcap: t.transpose(
                        ps_tr[:, i * 128: i * 128 + NS],
                        srcap[:, ic * 128:(ic + 1) * 128], id_sb[0:NS, 0:NS]),
                        waits=w if i == 0 else [], inc=TE)
                te_t = pr.c(TE)
                E("vector", lambda v, icg=icg, dst=dst: v.tensor_copy(
                    dst[:, icg * 4:(icg + 1) * 4, :],
                    ps_tr[:, :].rearrange("p (a b) -> p a b", a=4, b=128)[:, :, 0:NS]),
                    waits=[(TE, te_t)], inc=VE)
                tr_ev[0] = pr.c(VE)
            return pr.c(VE)

        ve_sifm = smajor_to_fm(siB, siFM, [(VE, ve_si)])

        gi_ps = psA[0:NS, 1024:1152]
        gh_ps = psA[0:NS, 1536:1664]
        te_gru = [0]
        ve_gru = [0]
        se_gru = [0]
        gru_tr_ev = [0]
        te_wt_done = [0, 0]
        for ob in range(GOB):
            w = []
            if ob == 0:
                w = [(VE, ve_sifm), (SE, se_hfm), (TE, te_si), (SO, gwt_so[0])]
            else:
                w = [(TE, te_gru[0])]
            E("sync", lambda s, ob=ob: s.dma_start(
                out=wtv[:, 0, :, :], in_=wgT[0, ob, :, :]), waits=w, inc=LD)
            E("sync", lambda s, ob=ob: s.dma_start(
                out=wtv[:, 1, :, :], in_=wgT[1, ob, :, :]), inc=LD)
            ld_wt = pr.c(LD)
            E("sync", lambda s, ob=ob: s.dma_start(
                out=bias_g[0:1, 0:P], in_=dr["bih"][0:1, ob * 128:(ob + 1) * 128]),
                waits=[(TE, te_gru[0])] if ob >= 1 else [], inc=LD)
            E("sync", lambda s, ob=ob: s.dma_start(
                out=bias_g[0:1, P:2 * P], in_=dr["bhh"][0:1, ob * 128:(ob + 1) * 128]), inc=LD)
            ld_gb = pr.c(LD)
            ve_wt = None
            for ic in range(8):
                w = []
                if ic == 0:
                    w = [(LD, ld_wt)]
                    if ob >= 1:
                        w += [(VE, ve_gru[0]), (SE, se_gru[0])]
                E("tensor", lambda t, ic=ic: t.matmul(
                    gi_ps[:, :], siFM[:, ic, :], wtv[:, 0, ic, :],
                    start=(ic == 0), stop=False), waits=w, inc=TE)
                E("tensor", lambda t, ic=ic: t.matmul(
                    gh_ps[:, :], slotFM[:, ic, :], wtv[:, 1, ic, :],
                    start=(ic == 0), stop=False), inc=TE)
            E("tensor", lambda t: t.matmul(gi_ps[:, :], ones_m[0:1, 0:NS], bias_g[0:1, 0:P],
                                           start=False, stop=True),
              waits=[(LD, ld_gb)], inc=TE)
            E("tensor", lambda t: t.matmul(gh_ps[:, :], ones_m[0:1, 0:NS], bias_g[0:1, P:2 * P],
                                           start=False, stop=True), inc=TE)
            te_gru[0] = pr.c(TE)
            gate = ob // 8
            dsl = (ob % 8) * 128
            if gate < 2:
                dst = rrB if gate == 0 else zzB
                E("vector", lambda v: v.tensor_copy(gsc[:, 0, :], gi_ps[:, :]),
                  waits=[(TE, te_gru[0])], inc=VE)
                E("vector", lambda v: v.tensor_add(gsc[:, 1, :],
                                                   gsc[:, 0, :], gh_ps[:, :]), inc=VE, dr=True)
                ve_g = pr.c(VE)
                E("scalar", lambda s: s.activation(out=gsc[:, 2, :],
                                                   in_=gsc[:, 1, :],
                                                   func=AF.Exp, scale=-1.0),
                  waits=[(VE, ve_g)], inc=SE)
                se_gru[0] = pr.c(SE)
                E("vector", lambda v: v.tensor_scalar_add(gsc[:, 3, :],
                                                          gsc[:, 2, :], 1.0),
                  waits=[(SE, se_gru[0])], inc=VE)
                E("vector", lambda v: v.reciprocal(gsc[:, 2, :], gsc[:, 3, :]), inc=VE, dr=True)
                E("vector", lambda v, dst=dst, dsl=dsl: v.tensor_copy(
                    dst[:, dsl:dsl + 128], gsc[:, 2, :]), inc=VE, dr=True)
                ve_gru[0] = pr.c(VE)
            else:
                E("vector", lambda v, dsl=dsl: v.tensor_mul(
                    gsc[:, 0, :], rrB[:, dsl:dsl + 128], gh_ps[:, :]),
                    waits=[(TE, te_gru[0])], inc=VE)
                E("vector", lambda v: v.tensor_add(gsc[:, 1, :],
                                                   gsc[:, 0, :], gi_ps[:, :]), inc=VE, dr=True)
                ve_g = pr.c(VE)
                E("scalar", lambda s: s.activation(out=gsc[:, 2, :],
                                                   in_=gsc[:, 1, :],
                                                   func=AF.Exp, scale=-2.0),
                  waits=[(VE, ve_g)], inc=SE)
                se_gru[0] = pr.c(SE)
                E("vector", lambda v: v.tensor_scalar_add(gsc[:, 3, :],
                                                          gsc[:, 2, :], 1.0),
                  waits=[(SE, se_gru[0])], inc=VE)
                E("vector", lambda v: v.reciprocal(gsc[:, 0, :], gsc[:, 3, :]), inc=VE, dr=True)
                E("vector", lambda v: v.tensor_scalar(gsc[:, 1, :], gsc[:, 2, :],
                                                      -1.0, 1.0, op0=OP.mult, op1=OP.add), inc=VE)
                E("vector", lambda v: v.tensor_mul(gsc[:, 1, :], gsc[:, 1, :], gsc[:, 0, :]), inc=VE, dr=True)
                E("vector", lambda v, dsl=dsl: v.tensor_sub(
                    gsc[:, 2, :], slotS[:, dsl:dsl + 128], gsc[:, 1, :]), inc=VE, dr=True)
                E("vector", lambda v, dsl=dsl: v.tensor_mul(
                    gsc[:, 3, :], zzB[:, dsl:dsl + 128], gsc[:, 2, :]), inc=VE, dr=True)
                E("vector", lambda v, dsl=dsl: v.tensor_add(
                    snewB[:, dsl:dsl + 128], gsc[:, 1, :], gsc[:, 3, :]), inc=VE, dr=True)
                ve_gru[0] = pr.c(VE)
        ve_snew = pr.c(VE)

        E("gpsimd", lambda g: g.nop(), waits=[(TE, te_p3)], inc=GP)
        gp_f2_free = pr.c(GP)
        E("gpsimd", lambda g: g.dma_start(
            out=wvN[:, :, :], in_=dr["wv"].rearrange("(a p) b -> p a b", p=P)),
          waits=[(GP, gp_f2_free)], inc=GL)
        gl_wv = pr.c(GL)

        ve_snb = ve_snew
        ve_snfm = smajor_to_fm(snewB, snFM, [(VE, ve_snew)])

        def slot_proj(dst_ps, wsrc, n_out, bias_name, first_w):
            E("sync", lambda s: s.dma_start(out=bias_scr[0:1, 0:n_out],
                                            in_=dr[bias_name]),
              waits=list(first_w) + [(TE, pr.c(TE))], inc=LD)
            ldb = pr.c(LD)
            nsl = (n_out + 511) // 512
            for ic in range(8):
                for sli in range(nsl):
                    lo, hi = sli * 512, min(n_out, (sli + 1) * 512)
                    E("tensor", lambda t, ic=ic, lo=lo, hi=hi, wsrc=wsrc: t.matmul(
                        dst_ps[:, lo:hi], snFM[:, ic, :], wsrc[:, ic, lo:hi],
                        start=(ic == 0), stop=False, skip_group_check=True),
                        waits=(first_w + [(VE, ve_snfm)]) if (ic == 0 and sli == 0) else [],
                        inc=TE)
            for sli in range(nsl):
                lo, hi = sli * 512, min(n_out, (sli + 1) * 512)
                E("tensor", lambda t, lo=lo, hi=hi: t.matmul(
                    dst_ps[:, lo:hi], ones_m[0:1, 0:NS], bias_scr[0:1, lo:hi],
                    start=False, stop=True, skip_group_check=True),
                    waits=[(LD, ldb)] if sli == 0 else [], inc=TE)
            return pr.c(TE)

        qs_ps = psC[0:NS, 0:DQ]
        ks_ps = psC[0:NS, DQ:2 * DQ]
        te_qs = slot_proj(qs_ps, wqN, DQ, "bq", [(VE, ve_snfm), (GL, gl_ffn1)])
        E("vector", lambda v: v.tensor_copy(qsB[:, :], qs_ps[:, :]),
          waits=[(TE, te_qs)], inc=VE)
        ve_qs = pr.c(VE)
        te_ks = slot_proj(ks_ps, wkN, DQ, "bk", [(VE, ve_qs)])
        E("vector", lambda v: v.tensor_copy(ksB[:, :], ks_ps[:, :]),
          waits=[(TE, te_ks)], inc=VE)
        ve_ks = pr.c(VE)
        vs_ps = psA[0:NS, 1024:2048]
        te_vs = slot_proj(vs_ps, wvN, D, "bv",
                          [(GL, gl_wv), (VE, ve_gru[0]), (SE, se_gru[0])])
        E("vector", lambda v: v.tensor_copy(vsB[:, :], vs_ps[:, :]),
          waits=[(TE, te_vs)], inc=VE)
        ve_vs = pr.c(VE)

        def tr256(src, dst, ready_w):
            w = list(ready_w) + [(VE, tr_ev[0])]
            for i in range(2):
                E("tensor", lambda t, i=i, src=src: t.transpose(
                    ps_tr[:, i * 128: i * 128 + NS],
                    src[:, i * 128:(i + 1) * 128], id_sb[0:NS, 0:NS]),
                    waits=w if i == 0 else [], inc=TE)
            te_t = pr.c(TE)
            E("vector", lambda v, dst=dst: v.tensor_copy(
                dst[:, :, :],
                ps_tr[:, 0:256].rearrange("p (a b) -> p a b", a=2, b=128)[:, :, 0:NS]),
                waits=[(TE, te_t)], inc=VE)
            tr_ev[0] = pr.c(VE)
            return pr.c(VE)

        ve_qsfm = tr256(qsB, qsFM, [(VE, ve_qs)])
        ve_ksfm = tr256(ksB, ksFM, [(VE, ve_ks)])

        s2_ps = psC[0:NS, 0:NS]
        for ic in range(2):
            E("tensor", lambda t, ic=ic: t.matmul(
                s2_ps[:, :], qsFM[:, ic, :], ksFM[:, ic, :],
                start=(ic == 0), stop=(ic == 1)),
                waits=[(VE, ve_qsfm), (VE, ve_ksfm)] if ic == 0 else [], inc=TE)
        te_s2 = pr.c(TE)
        E("vector", lambda v: v.reduce_max(sst[:, 0:1], s2_ps[:, :], axis=AXX),
          waits=[(TE, te_s2)], inc=VE)
        E("vector", lambda v: v.tensor_scalar(sst[:, 1:2], sst[:, 0:1], -0.0625, None,
                                              op0=OP.mult), inc=VE, dr=True)
        ve_m2 = pr.c(VE)
        E("scalar", lambda s: s.activation(out=a2[:, :], in_=s2_ps[:, :], func=AF.Exp,
                                           scale=0.0625, bias=sst[:, 1:2]),
          waits=[(VE, ve_m2)], inc=SE)
        se_a2 = pr.c(SE)
        E("vector", lambda v: v.reduce_sum(sst[:, 2:3], a2[:, :], axis=AXX),
          waits=[(SE, se_a2)], inc=VE)
        E("vector", lambda v: v.reciprocal(sst[:, 3:4], sst[:, 2:3]), inc=VE, dr=True)
        E("vector", lambda v: v.tensor_copy(sst[:, 4:5], sst[:, 3:4]), inc=VE)
        E("vector", lambda v: v.tensor_scalar(a2[:, :], a2[:, :], sst[:, 3:4], None,
                                              op0=OP.mult), inc=VE)
        ve_a2 = pr.c(VE)
        E("tensor", lambda t: t.transpose(ps_tr[0:NS, 0:NS], a2[:, :], id_sb[0:NS, 0:NS]),
          waits=[(VE, ve_a2), (VE, tr_ev[0])], inc=TE)
        te_a2t = pr.c(TE)
        E("vector", lambda v: v.tensor_copy(a2T[:, :], ps_tr[0:NS, 0:NS]),
          waits=[(TE, te_a2t)], inc=VE)
        ve_a2t = pr.c(VE)
        tr_ev[0] = ve_a2t
        scm_ps = psB1[0:NS, 0:1024]
        for sln in range(2):
            E("tensor", lambda t, sln=sln: t.matmul(
                scm_ps[:, sln * 512:(sln + 1) * 512], a2T[:, :],
                vsB[:, sln * 512:(sln + 1) * 512], start=True, stop=True),
                waits=[(VE, ve_a2t), (VE, ve_vs)] if sln == 0 else [], inc=TE)
        te_scm = pr.c(TE)
        E("vector", lambda v: v.tensor_add(yyB[:, :], scm_ps[:, :], snewB[:, :]),
          waits=[(TE, te_scm)], inc=VE)
        ve_y = pr.c(VE)
        E("scalar", lambda s: s.activation(out=ysqd[:, :], in_=yyB[:, :],
                                           func=AF.Square, accum_out=sst[:, 0:1]),
          waits=[(VE, ve_y)], inc=SE)
        se_ysq = pr.c(SE)
        E("vector", lambda v: v.reduce_sum(sst[:, 1:2], yyB[:, :], axis=AXX), inc=VE, dr=True)
        E("vector", lambda v: v.tensor_scalar_mul(sst[:, 2:3], sst[:, 1:2], 1.0 / D), inc=VE, dr=True)
        E("vector", lambda v: v.tensor_mul(sst[:, 3:4], sst[:, 2:3], sst[:, 2:3]), inc=VE, dr=True)
        E("vector", lambda v: v.tensor_scalar(sst[:, 4:5], sst[:, 0:1], 1.0 / D, None,
                                              op0=OP.mult),
          waits=[(SE, se_ysq)], inc=VE)
        E("vector", lambda v: v.tensor_sub(sst[:, 5:6], sst[:, 4:5], sst[:, 3:4]), inc=VE, dr=True)
        ve_yv = pr.c(VE)
        E("scalar", lambda s: s.activation(out=sst[:, 6:7], in_=sst[:, 5:6], func=AF.Ln,
                                           bias=c_const[0:NS, 1:2]),
          waits=[(VE, ve_yv)], inc=SE)
        E("scalar", lambda s: s.activation(out=sst[:, 7:8], in_=sst[:, 6:7], func=AF.Exp,
                                           scale=-0.5), inc=SE)
        se_yr = pr.c(SE)
        E("vector", lambda v: v.tensor_scalar(ylnB[:, :], yyB[:, :], sst[:, 2:3],
                                              sst[:, 7:8], op0=OP.subtract, op1=OP.mult),
          waits=[(SE, se_yr)], inc=VE)
        ve_yln = pr.c(VE)
        ve_ylnfm = smajor_to_fm(ylnB, ylnFM, [(VE, ve_yln)])

        ve_wvpg0 = [0]
        for ic in range(8):
            E("vector", lambda v, ic=ic: v.tensor_scalar(
                wvpN[:, ic, :], wvpN[:, ic, :], gf[:, ic:ic + 1], None, op0=OP.mult),
                waits=[(GL, gl_wvp), (VE, ve_gf)] if ic == 0 else [], inc=VE)
        ve_wvpg = pr.c(VE)

        ld_bw = [0]
        te_bt2 = [0]
        for half in range(2):
            for ic in range(8):
                slb = ic % 2
                E("sync", lambda s, ic=ic, slb=slb: s.dma_start(
                    out=r_x[:, slb, :], in_=dr["wvp"][ic * 128:(ic + 1) * 128, :]),
                    waits=([(VE, ve_snb), (VE, ve_yln)] if (half == 0 and ic == 0) else
                           [(TE, te_bt2[0])]), inc=LD)
                ld_bw[0] = pr.c(LD)
                E("tensor", lambda t, ic=ic, half=half, slb=slb: t.matmul(
                    psC[0:1, 0:512], bfm[:, ic:ic + 1],
                    r_x[:, slb, half * 512:(half + 1) * 512],
                    start=(ic == 0), stop=(ic == 7)),
                    waits=[(LD, ld_bw[0])] + ([(VE, ve_yv)] if (half == 0 and ic == 0) else []),
                    inc=TE)
                te_bt2[0] = pr.c(TE)
            dstrow = hb_row if half == 0 else b1_row
            E("vector", lambda v, dstrow=dstrow: v.tensor_copy(
                dstrow[0:1, 0:512], psC[0:1, 0:512]),
                waits=[(TE, te_bt2[0])], inc=VE)
        ve_btr = pr.c(VE)

        sv_ps = psA[0:NS, 1024:2048]
        E("sync", lambda s: s.dma_start(out=bias_scr[0:1, :], in_=dr["bvp"]),
          waits=[(VE, ve_btr), (TE, pr.c(TE))], inc=LD)
        ld_bvp = pr.c(LD)
        for ic in range(8):
            for sln in range(2):
                E("tensor", lambda t, ic=ic, sln=sln: t.matmul(
                    sv_ps[:, sln * 512:(sln + 1) * 512], ylnFM[:, ic, :],
                    wvpN[:, ic, sln * 512:(sln + 1) * 512],
                    start=(ic == 0), stop=False, skip_group_check=True),
                    waits=[(VE, ve_ylnfm), (VE, ve_wvpg), (VE, ve_vs)] if (ic == 0 and sln == 0) else [],
                    inc=TE)
        for sln in range(2):
            E("tensor", lambda t, sln=sln: t.matmul(
                sv_ps[:, sln * 512:(sln + 1) * 512], ones_m[0:1, 0:NS],
                bias_scr[0:1, sln * 512:(sln + 1) * 512],
                start=False, stop=False, skip_group_check=True),
                waits=[(LD, ld_bvp)] if sln == 0 else [], inc=TE)
            btsrc = hb_row if sln == 0 else b1_row
            E("tensor", lambda t, sln=sln, btsrc=btsrc: t.matmul(
                sv_ps[:, sln * 512:(sln + 1) * 512], ones_m[0:1, 0:NS],
                btsrc[0:1, 0:512],
                start=False, stop=True, skip_group_check=True),
                waits=[(VE, ve_btr)] if sln == 0 else [], inc=TE)
        te_sv = pr.c(TE)
        E("vector", lambda v: v.tensor_copy(svB[:, :], sv_ps[:, :]),
          waits=[(TE, te_sv)], inc=VE)
        ve_sv = pr.c(VE)


        for tcg in range(4):
            w = [(VE, ve_alpha_all), (VE, tr_ev[0])]
            for i in range(4):
                tc = tcg * 4 + i
                E("tensor", lambda t, tc=tc, i=i: t.transpose(
                    ps_tr[0:NS, i * 128:(i + 1) * 128],
                    alphaT[:, tc, :], id_sb[:]),
                    waits=w if i == 0 else [], inc=TE)
            te_t = pr.c(TE)
            E("vector", lambda v, tcg=tcg: v.tensor_copy(
                alphaFM[:, tcg * 4:(tcg + 1) * 4, :],
                ps_tr[0:NS, :]), waits=[(TE, te_t)], inc=VE)
            tr_ev[0] = pr.c(VE)
        ve_alphafm = pr.c(VE)
        if debug:
            E("sync", lambda s: s.dma_start(out=dbg["dbg_sv"], in_=svB[:, :]),
              waits=[(VE, ve_sv)], inc=SO)

        afm = alphaFM[:, :, :].rearrange("p a b -> p (a b)")
        ve_of = {}
        for dc in range(8):
            for sln in range(4):
                ps = sc0 if dc % 2 == 0 else sc1
                w = []
                if dc == 0 and sln == 0:
                    w = [(VE, ve_sv), (VE, ve_alphafm)]
                if (dc, sln) >= (0, 2) and (dc * 4 + sln) >= 2:
                    prevkey = (dc * 4 + sln) - 2
                    w.append((VE, ve_of[prevkey]))
                E("tensor", lambda t, dc=dc, sln=sln, ps=ps: t.matmul(
                    ps[:, (sln % 2) * 512:(sln % 2 + 1) * 512],
                    svB[:, dc * 128:(dc + 1) * 128],
                    afm[0:NS, sln * 512:(sln + 1) * 512],
                    start=True, stop=True), waits=w, inc=TE)
                te_o = pr.c(TE)
                E("vector", lambda v, dc=dc, sln=sln, ps=ps: v.tensor_copy(
                    r_q[:, dc, sln * 512:(sln + 1) * 512],
                    ps[:, (sln % 2) * 512:(sln % 2 + 1) * 512]),
                    waits=[(TE, te_o)], inc=VE)
                ve_of[dc * 4 + sln] = pr.c(VE)
        ve_outfm = pr.c(VE)

        E("sync", lambda s: s.dma_start(out=bias_scr[0:1, :], in_=dr["bo"]),
          waits=[(LD, ld_bvp), (TE, te_sv)], inc=LD)
        ld_bo = pr.c(LD)
        ld_xf = [0] * TC
        ve_fin = [0] * TC
        so_out = [0] * TC
        E("sync", x_load(0, 0, x2s), waits=[(TE, te_sv)], inc=LD)
        ld_xf[0] = pr.c(LD)
        for tc in range(TC):
            sl = tc % 2
            if tc + 1 < TC:
                w = [(TE, te_sv)]
                if tc >= 1:
                    w += [(VE, ve_fin[tc - 1]), (SO, so_out[tc - 1])]
                E("sync", x_load(tc + 1, (tc + 1) % 2, x2s), waits=w, inc=LD)
                ld_xf[tc + 1] = pr.c(LD)
            ps = sc0 if tc % 2 == 0 else sc1
            for ic in range(8):
                for sln in range(2):
                    w = []
                    if ic == 0 and sln == 0:
                        w = [(VE, ve_outfm), (GL, gl_wo)]
                        if tc >= 2:
                            w.append((VE, ve_fin[tc - 2]))
                    E("tensor", lambda t, ps=ps, tc=tc, ic=ic, sln=sln: t.matmul(
                        ps[:, sln * 512:(sln + 1) * 512],
                        r_q[:, ic, tc * 128:(tc + 1) * 128],
                        woN[:, ic, sln * 512:(sln + 1) * 512],
                        start=(ic == 0), stop=False), waits=w, inc=TE)
            for sln in range(2):
                E("tensor", lambda t, ps=ps, sln=sln: t.matmul(
                    ps[:, sln * 512:(sln + 1) * 512], ones_m[0:1, :],
                    bias_scr[0:1, sln * 512:(sln + 1) * 512],
                    start=False, stop=True),
                    waits=[(LD, ld_bo)] if sln == 0 else [], inc=TE)
            te_fin = pr.c(TE)
            xb = r_x[:, sl, :]
            E("vector", lambda v, xb=xb, ps=ps: v.tensor_add(xb, ps[:, :], xb),
              waits=[(TE, te_fin), (LD, ld_xf[tc])], inc=VE)
            ve_fin[tc] = pr.c(VE)
            E("sync", lambda s, xb=xb, tc=tc: s.dma_start(
                out=out[tc * P:(tc + 1) * P, :], in_=xb),
                waits=[(VE, ve_fin[tc])], inc=SO)
            so_out[tc] = pr.c(SO)
        so_done = pr.c(SO)
        if _TAPS:
            tap_pairs = [("tap_hfm", hFM[:, :, :].rearrange("p a b -> p (a b)")),
                           ("tap_alpha", alphaT[:, :, :].rearrange("p a b -> p (a b)")),
                           ("tap_sv", svB[:, :]), ("tap_snew", snewB[:, :]),
                           ("tap_yln", ylnB[:, :]), ("tap_vs", vsB[:, :]),
                           ("tap_sifm", r_qv[:, 0:512]),
                           ("tap_bt", None), ("tap_wvp", rkv[:, 0:8192]),
                           ("tap_ylnfm", r_qv[:, 1024:1536]),
                           ("tap_outfm", r_q[:, :, :].rearrange("p a b -> p (a b)"))]
            for nm, ap in tap_pairs:
                if ap is None:
                    continue
                E("sync", lambda s, nm=nm, ap=ap: s.dma_start(out=dbg[nm], in_=ap),
                  waits=[(SO, so_done)], inc=SO)
            E("sync", lambda s: s.dma_start(out=dbg["tap_bt"][0:1, 0:512],
                                            in_=hb_row[0:1, :]),
              waits=[(SO, so_done)], inc=SO)
            E("sync", lambda s: s.dma_start(out=dbg["tap_bt"][0:1, 512:1024],
                                            in_=b1_row[0:1, :]), inc=SO)
            so_done = pr.c(SO)
        E("sync", lambda s: s.nop(), waits=[(SO, so_done)], inc=None)

        pr.finalize()
        block = stack.enter_context(nc.Block())

        @block.sync
        def _(eng):
            pr.run_engine("sync", eng)

        @block.gpsimd
        def _(eng):
            pr.run_engine("gpsimd", eng)

        @block.tensor
        def _(eng):
            pr.run_engine("tensor", eng)

        @block.scalar
        def _(eng):
            pr.run_engine("scalar", eng)

        @block.vector
        def _(eng):
            pr.run_engine("vector", eng)

    return nc


_NC_CACHE = {}


def kernel(**inputs):
    consts = _np_consts()
    if "main" not in _NC_CACHE:
        _NC_CACHE["main"] = build_nc(debug=_DEBUG)
    nc = _NC_CACHE["main"]

    x = np.ascontiguousarray(np.asarray(inputs["L0_attn_x"], dtype=np.float32))
    names = {
        "Wq": "L0_attn_Wq", "Wk": "L0_attn_Wk", "Wv": "L0_attn_Wv",
        "Wo": "L0_attn_Wo", "slot_init": "L0_ffn_slot_init", "w1": "L0_ffn_w1",
        "w2": "L0_ffn_w2", "Wih": "L0_ffn_Wih", "Whh": "L0_ffn_Whh",
        "wq": "L0_ffn_wq", "wk": "L0_ffn_wk", "wv": "L0_ffn_wv",
        "wvp": "L0_ffn_wvp", "wo": "L0_ffn_wo",
    }
    rows = {
        "b1": "L0_ffn_b1", "b2": "L0_ffn_b2", "bih": "L0_ffn_bih",
        "bhh": "L0_ffn_bhh", "bq": "L0_ffn_bq", "bk": "L0_ffn_bk",
        "bv": "L0_ffn_bv", "ln_g": "L0_ffn_ln_g", "ln_b": "L0_ffn_ln_b",
        "bvp": "L0_ffn_bvp", "bo": "L0_ffn_bo",
    }
    shared = {}
    for k, src in names.items():
        shared[k] = np.ascontiguousarray(np.asarray(inputs[src], dtype=np.float32))
    for k, src in rows.items():
        shared[k] = np.ascontiguousarray(
            np.asarray(inputs[src], dtype=np.float32).reshape(1, -1))
    shared["tau"] = np.ascontiguousarray(
        np.asarray(inputs["L0_ffn_tau"], dtype=np.float32).reshape(1, 1))
    shared.update(consts)

    in_maps = []
    for b in range(8):
        m = dict(shared)
        m["x"] = np.ascontiguousarray(x[b])
        in_maps.append(m)

    res = run_bass_kernel_spmd(nc, in_maps, core_ids=list(range(8)))
    return np.stack([res.results[b]["out"] for b in range(8)], axis=0).astype(np.float32)


# version 14
# speedup vs baseline: 1.1258x; 1.0979x over previous
import numpy as np
import ml_dtypes
from contextlib import ExitStack

import concourse.bass as bass
import concourse.mybir as mybir
from concourse.bass_utils import run_bass_kernel_spmd
from concourse.alu_op_type import AluOpType as OP

F32 = mybir.dt.float32
BF16 = mybir.dt.bfloat16
AF = mybir.ActivationFunctionType
AXX = mybir.AxisListType.X

P = 128
T = 2048
TC = 16
D = 1024
H = 16
NPAIR = 8
KC = 16
NS = 64
DQ = 256
DHF = 512
GOB = 24

_DEBUG = False
_X2OUT = False
_TAPS = False


def _np_consts():
    h = np.array([[1.0]])
    while h.shape[0] < 64:
        h = np.block([[h, h], [h, -h]])
    u = h[1:, :] / 8.0
    ut = np.zeros((64, 64), np.float32)
    ut[:, :63] = u.T
    ut2 = np.tile(ut, (2, 1))
    ones_pair = np.zeros((128, 2), np.float32)
    ones_pair[0:64, 0] = 1.0
    ones_pair[64:128, 1] = 1.0
    return {
        "c_ut2": ut2.astype(ml_dtypes.bfloat16),
        "c_id": np.eye(128, dtype=np.float32).astype(ml_dtypes.bfloat16),
        "c_ones_m": np.ones((1, 128), np.float32),
        "c_ones_pair": ones_pair.astype(ml_dtypes.bfloat16),
        "c_ones_n": np.ones((128, 1), np.float32).astype(ml_dtypes.bfloat16),
        "c_idf": np.eye(128, dtype=np.float32),
    }


class Prog:
    def __init__(self, nc, stack):
        self.nc = nc
        self.steps = []
        self.sems = {}
        self.cnt = {}
        for name in ("LD", "GL", "TE", "VE", "SE", "GP", "SO"):
            self.sems[name] = stack.enter_context(nc.semaphore(name))
            self.cnt[name] = 0

    def emit(self, engine, fn, waits=(), inc=None, n=None, dr=False):
        if n is None:
            n = 16 if inc in ("LD", "GL", "SO") else 1
        for s, v in waits:
            assert v <= self.cnt[s], (engine, s, v, self.cnt[s])
        if inc is not None:
            self.cnt[inc] += n
        self.steps.append((engine, fn, tuple(waits), inc, n, dr))
        return self.cnt[inc] if inc is not None else None

    def c(self, sem):
        return self.cnt[sem]

    def finalize(self):
        waited = {s: set() for s in self.sems}
        for _, _, waits, _, _, _ in self.steps:
            for s, v in waits:
                waited[s].add(v)
        cnt = {s: 0 for s in self.sems}
        newcnt = {s: 0 for s in self.sems}
        remap = {s: {0: 0} for s in self.sems}
        new_steps = []
        for e, fn, waits, inc, n, dr in self.steps:
            nw = tuple((s, remap[s][v]) for s, v in waits)
            if inc is not None:
                cnt[inc] += n
                if inc in ("GL", "LD", "SO") or cnt[inc] in waited[inc]:
                    newcnt[inc] += n
                    remap[inc][cnt[inc]] = newcnt[inc]
                    new_steps.append((e, fn, nw, inc, n, dr))
                else:
                    remap[inc][cnt[inc]] = newcnt[inc]
                    new_steps.append((e, fn, nw, None, n, dr))
            else:
                new_steps.append((e, fn, nw, None, n, dr))
        self.steps = new_steps

    def run_engine(self, name, eng):
        for e, fn, waits, inc, n, dr in self.steps:
            if e != name:
                continue
            for s, v in waits:
                eng.wait_ge(self.sems[s], v)
            if dr:
                eng.drain()
            instr = fn(eng)
            if inc is not None:
                instr.then_inc(self.sems[inc], n)


def build_nc(debug=False):
    nc = bass.Bass("TRN2", target_bir_lowering=False,
                   dynamic_dma_scratch_size=8192)

    dr = {}

    def din(name, shape, dt=F32):
        dr[name] = nc.dram_tensor(name, list(shape), dt, kind="ExternalInput").ap()

    din("x", (T, D))
    for nm, shp in [
        ("Wq", (D, D)), ("Wk", (D, D)), ("Wv", (D, D)), ("Wo", (D, D)),
        ("slot_init", (NS, D)), ("w1", (2 * D, DHF)), ("b1", (1, DHF)),
        ("w2", (DHF, NS)), ("b2", (1, NS)), ("tau", (1, 1)),
        ("Wih", (3 * D, D)), ("Whh", (3 * D, D)),
        ("bih", (1, 3 * D)), ("bhh", (1, 3 * D)),
        ("wq", (D, DQ)), ("bq", (1, DQ)), ("wk", (D, DQ)), ("bk", (1, DQ)),
        ("wv", (D, D)), ("bv", (1, D)), ("ln_g", (1, D)), ("ln_b", (1, D)),
        ("wvp", (D, D)), ("bvp", (1, D)), ("wo", (D, D)), ("bo", (1, D)),
    ]:
        din(nm, shp)
    din("c_ut2", (P, 64), BF16)
    din("c_id", (P, P), BF16)
    din("c_ones_m", (1, P))
    din("c_ones_pair", (P, 2), BF16)
    din("c_ones_n", (P, 1), BF16)
    din("c_idf", (P, P))

    out = nc.dram_tensor("out", [T, D], F32, kind="ExternalOutput").ap()
    x2s = nc.dram_tensor("x2s", [T, D], F32,
                         kind="ExternalOutput" if _X2OUT else "Internal").ap()
    wgT = nc.dram_tensor("wgT", [2, GOB, P, 8 * P], BF16, kind="Internal").ap()
    dbg = {}
    if _TAPS:
        for nm, shp, dt in [
            ("tap_hfm", (P, 4 * T), BF16), ("tap_alpha", (P, TC * NS), BF16),
            ("tap_sv", (NS, D), BF16), ("tap_snew", (NS, D), BF16),
            ("tap_yln", (NS, D), BF16), ("tap_vs", (NS, D), BF16),
            ("tap_sifm", (P, 8 * NS), BF16), ("tap_outfm", (P, 8 * T), BF16),
            ("tap_bt", (1, D), F32), ("tap_wvp", (P, 8 * D), BF16),
            ("tap_ylnfm", (P, 8 * NS), BF16),
        ]:
            dbg[nm] = nc.dram_tensor(nm, list(shp), dt, kind="ExternalOutput").ap()
    if debug:
        for nm, shp, dt in [
            ("dbg_ln1", (P, 4 * T), BF16), ("dbg_qfm", (P, 8 * T), BF16),
            ("dbg_kfm", (P, 8 * T), BF16), ("dbg_vhat", (P, KC * H * 65), BF16),
            ("dbg_attnfm", (P, 8 * T), BF16), ("dbg_x2", (T, D), F32),
            ("dbg_hfm", (P, 4 * T), BF16), ("dbg_alpha", (P, TC * NS), BF16),
            ("dbg_sv", (NS, D), BF16), ("dbg_ln2fm", (P, 4 * T), BF16),
            ("dbg_logits", (P, TC * NS), F32), ("dbg_e0", (P, D), BF16),
            ("dbg_ln1b", (P, 4 * T), BF16), ("dbg_lnb", (P, 4 * T), BF16),
            ("dbg_vps", (P, D), BF16),
            ("dbg_av0", (P, 8 * 128), BF16), ("dbg_pt0", (P, TC * P), BF16),
        ]:
            dbg[nm] = nc.dram_tensor(nm, list(shp), dt, kind="ExternalOutput").ap()

    with ExitStack() as stack:
        def sb(name, shape, dt):
            return stack.enter_context(nc.sbuf_tensor(name, shape, dt))

        r_lnfm_a = sb("r_lnfm_a", [P, 4, T], BF16)
        r_lnfm_b = sb("r_lnfm_b", [P, 4, T], BF16)
        r_q = sb("r_q", [P, NPAIR, T], BF16)
        r_k = sb("r_k", [P, NPAIR, T], BF16)
        r_v = sb("r_v", [P, KC, H, 65], BF16)
        r_e = sb("r_e", [P, 8, D], BF16)
        r_f1 = sb("r_f1", [P, 8, D], BF16)
        r_f2 = sb("r_f2", [P, 8, D], BF16)
        r_x = sb("r_x", [P, 2, D], F32)
        r_et = sb("r_et", [P, 2, D], BF16)
        r_pt = sb("r_pt", [P, TC, P], BF16)
        r_qv = sb("r_qv", [P, T], BF16)

        c_const = sb("c_const", [P, 2], F32)
        stats = sb("stats", [P, 16], F32)
        t8 = sb("t8", [P, 8], F32)
        negm1 = sb("negm1", [P, 1], F32)
        e8b = sb("e8b", [P, 8], F32)
        e64b = sb("e64b", [P, NS], F32)
        maskb = sb("maskb", [P, NS], F32)
        s8 = sb("s8", [P, 2], F32)
        logits = sb("logits", [P, NS], F32)
        recipC = sb("recipC", [P, 8], F32)
        inv_tau = sb("inv_tau", [P, 1], F32)
        tau_t = sb("tau_t", [1, 2], F32)
        id_sb = sb("id_sb", [P, P], BF16)
        ut2_sb = sb("ut2_sb", [P, 64], BF16)
        ones_m = sb("ones_m", [1, P], F32)
        ones_pr = sb("ones_pr", [P, 2], BF16)
        ones_n = sb("ones_n", [P, 1], BF16)
        idf_sb = sb("idf_sb", [P, P], F32)
        bias_g = sb("bias_g", [1, 2 * P], F32)
        b1_row = sb("b1_row", [1, DHF], F32)
        b2_row = sb("b2_row", [1, NS], F32)
        hb_row = sb("hb_row", [1, DHF], F32)
        hbiasT = sb("hbiasT", [P, 4], F32)
        gf = sb("gf", [P, 8], F32)
        bfm = sb("bfm", [P, 8], F32)
        smf = sb("smf", [P, 8], BF16)
        slotS = sb("slotS", [NS, D], BF16)
        slotFM = sb("slotFM", [P, 8, NS], BF16)
        qsFM = sb("qsFM", [P, 2, NS], BF16)
        ksFM = sb("ksFM", [P, 2, NS], BF16)
        a2 = sb("a2", [NS, NS], BF16)
        a2T = sb("a2T", [NS, NS], BF16)
        sst = sb("sst", [NS, 8], F32)
        denA = sb("denA", [NS, 2], F32)
        gsc = sb("gsc", [NS, 8, P], F32)
        dbgav_sb = sb("dbgav", [P, 1024], BF16) if debug else None
        _sfm3 = r_qv[:, 0:1536].rearrange("p (a b c) -> p a b c", a=3, b=8, c=NS)
        siFM = _sfm3[:, 0, :, :]
        snFM = _sfm3[:, 1, :, :]
        ylnFM = _sfm3[:, 2, :, :]

        wtv = r_pt[:, :, :].rearrange("p a b -> p (a b)").rearrange(
            "p (w i c) -> p w i c", w=2, i=8, c=P)
        alphaFM = r_et[:, :, :].rearrange("p a b -> p (a b)")[0:NS, :].rearrange(
            "p (a b) -> p a b", a=TC, b=P)
        etf = r_et[:, :, :].rearrange("p a b -> p (a b)")
        f1f2 = r_f1[:, :, :].rearrange("p a b -> p (a b)")
        rptf = r_pt[:, :, :].rearrange("p a b -> p (a b)")
        siB = etf[0:NS, 0:D]
        rrB = etf[0:NS, 0:D]
        zzB = etf[0:NS, D:2 * D]
        snewB = f1f2[0:NS, 0:D]
        yyB = f1f2[0:NS, D:2 * D]
        ylnB = f1f2[0:NS, 2 * D:3 * D]
        ysqd = f1f2[0:NS, 3 * D:4 * D]
        vsB = rptf[0:NS, 0:D]
        svB = rptf[0:NS, D:2 * D]
        rlnbf = r_lnfm_b[:, :, :].rearrange("p a b -> p (a b)")
        alphaT = rlnbf[:, 0:TC * NS].rearrange("p (a b) -> p a b", a=TC, b=NS)
        qsB = rlnbf[0:NS, 1024:1024 + DQ]
        ksB = rlnbf[0:NS, 1280:1280 + DQ]
        w2N = rlnbf[:, 1536:1792].rearrange("p (a b) -> p a b", a=4, b=NS)
        bias_scr = gsc[:, :, :].rearrange("p a b -> p (a b)")[0:1, 0:D]

        f1v = r_f1[:, :, :].rearrange("p a b -> p (a b)")
        w1a_a = f1v[:, 0:2048].rearrange("p (a b) -> p a b", a=4, b=DHF)
        w1a_b = f1v[:, 2048:4096].rearrange("p (a b) -> p a b", a=4, b=DHF)
        wqN = f1v[:, 4096:6144].rearrange("p (a b) -> p a b", a=8, b=DQ)
        wkN = f1v[:, 6144:8192].rearrange("p (a b) -> p a b", a=8, b=DQ)
        f2v = r_f2[:, :, :].rearrange("p a b -> p (a b)")
        wvN = f2v[:, 0:8192].rearrange("p (a b) -> p a b", a=8, b=D)
        rkv = r_k[:, :, :].rearrange("p a b -> p (a b)")
        wvpN = rkv[:, 0:8192].rearrange("p (a b) -> p a b", a=8, b=D)
        woN = rkv[:, 8192:16384].rearrange("p (a b) -> p a b", a=8, b=D)
        rev = r_e[:, :, :].rearrange("p a b -> p (a b)")
        wstage = rev[:, 0:4096].rearrange("p (a b) -> p a b", a=4, b=D)
        ystage = rev[:, 4096:8192].rearrange("p (a b) -> p a b", a=4, b=D)
        hFM = rev[:, 0:8192].rearrange("p (a b) -> p a b", a=4, b=T)
        ln2tok = r_v[:, :, :, :].rearrange("p a b c -> p (a b c)")[:, 0:TC * D]
        ln2tok = ln2tok.rearrange("p (a b) -> p a b", a=TC, b=D)
        sq = r_et[:, :, :].rearrange("p a b -> p (a b)")
        lnt = r_pt[:, :, :].rearrange("p a b -> p (a b)")

        def lnfm(ic):
            return (r_lnfm_a if ic < 4 else r_lnfm_b)[:, ic % 4, :]

        psA = stack.enter_context(nc.psum_tensor("psA", [P, 2048], F32))
        psB1 = stack.enter_context(nc.psum_tensor("psB1", [P, 1024], F32))
        ps_tr = stack.enter_context(nc.psum_tensor("ps_tr", [P, 512], BF16))
        psC = stack.enter_context(nc.psum_tensor("psC", [P, 512], F32))

        sc0 = psA[:, 0:1024]
        sc1 = psA[:, 1024:2048]
        avA = psB1[:, 0:512].rearrange("p (a b) -> p a b", a=4, b=128)
        avB = psB1[:, 512:1024].rearrange("p (a b) -> p a b", a=4, b=128)
        y0 = psA[0:NS, 1024:2048]
        y1 = psB1[0:NS, 0:1024]
        ps_trf = psC[0:P, 128:144]

        pr = Prog(nc, stack)
        E = pr.emit
        LD, GL, TE, VE, SE, GP, SO = "LD", "GL", "TE", "VE", "SE", "GP", "SO"

        for nm, dst in [("c_id", id_sb), ("c_ut2", ut2_sb), ("c_ones_m", ones_m),
                        ("c_ones_pair", ones_pr), ("c_ones_n", ones_n), ("c_idf", idf_sb),
                        ("b1", b1_row), ("b2", b2_row)]:
            E("sync", lambda s, nm=nm, dst=dst: s.dma_start(out=dst[:], in_=dr[nm]),
              inc=LD)
        E("sync", lambda s: s.dma_start(out=tau_t[0:1, 0:1], in_=dr["tau"]), inc=LD)
        ld_consts = pr.c(LD)
        E("gpsimd", lambda g: g.memset(c_const[:, 0:1], -20.0), inc=GP)
        E("gpsimd", lambda g: g.memset(c_const[:, 1:2], 1e-5), inc=GP)
        E("gpsimd", lambda g: g.memset(r_v[:], 1.0), inc=GP)
        gp_init = pr.c(GP)
        E("gpsimd", lambda g: g.dma_start(out=slotS[:], in_=dr["slot_init"]), inc=GL)
        gl_slot = pr.c(GL)

        E("scalar", lambda s: s.activation(out=tau_t[0:1, 1:2], in_=tau_t[0:1, 0:1],
                                           func=AF.Abs),
          waits=[(LD, ld_consts)], inc=SE)
        se_tau = pr.c(SE)
        E("vector", lambda v: v.tensor_scalar_add(tau_t[0:1, 1:2], tau_t[0:1, 1:2], 0.1),
          waits=[(SE, se_tau)], inc=VE)
        E("vector", lambda v: v.reciprocal(tau_t[0:1, 0:1], tau_t[0:1, 1:2]), inc=VE, dr=True)
        ve_tau = pr.c(VE)
        E("tensor", lambda t: t.matmul(psC[0:P, 0:1], ones_m[0:1, :], tau_t[0:1, 0:1],
                                       start=True, stop=True),
          waits=[(VE, ve_tau), (LD, ld_consts)], inc=TE)
        te_tau = pr.c(TE)
        E("vector", lambda v: v.tensor_copy(inv_tau[:], psC[0:P, 0:1]),
          waits=[(TE, te_tau)], inc=VE)
        ve_invtau = pr.c(VE)

        ld_x = [0] * TC
        ve_ln = [0] * TC
        te_lntr = [0]
        tr_ev = [0]
        ve_var_prev = [0]

        def x_load(tc, slot, src):
            return lambda s: s.dma_start(out=r_x[:, slot, :],
                                         in_=src[tc * P:(tc + 1) * P, :])

        E("sync", x_load(0, 0, dr["x"]), inc=LD)
        ld_x[0] = pr.c(LD)
        se_rstd_t = [0] * TC
        ve_var_t = [0] * TC
        te_lntr_t = [0] * TC

        def ln1_stats(tc):
            sl = tc % 2
            cs = 8 * (tc % 2)
            xb = r_x[:, sl, :]
            w = [(LD, ld_x[tc])]
            if tc >= 2:
                w.append((VE, ve_var_t[tc - 2]))
            E("scalar", lambda s, xb=xb, cs=cs: s.activation(
                out=r_et[:, 0, :], in_=xb, func=AF.Square,
                accum_out=stats[:, cs:cs + 1]), waits=w, inc=SE)
            se_sq = pr.c(SE)
            E("vector", lambda v, xb=xb, cs=cs: v.reduce_sum(
                stats[:, cs + 1:cs + 2], xb, axis=AXX),
                waits=[(LD, ld_x[tc])], inc=VE)
            E("vector", lambda v, cs=cs: v.tensor_scalar_mul(
                stats[:, cs + 2:cs + 3], stats[:, cs + 1:cs + 2], 1.0 / D), inc=VE, dr=True)
            E("vector", lambda v, cs=cs: v.tensor_mul(
                stats[:, cs + 3:cs + 4], stats[:, cs + 2:cs + 3],
                stats[:, cs + 2:cs + 3]), inc=VE, dr=True)
            E("vector", lambda v, cs=cs: v.tensor_scalar(
                stats[:, cs + 4:cs + 5], stats[:, cs:cs + 1], 1.0 / D,
                None, op0=OP.mult),
                waits=[(SE, se_sq)], inc=VE)
            E("vector", lambda v, cs=cs: v.tensor_sub(
                stats[:, cs + 5:cs + 6], stats[:, cs + 4:cs + 5],
                stats[:, cs + 3:cs + 4]), inc=VE, dr=True)
            ve_var_t[tc] = pr.c(VE)
            ve_var_prev[0] = ve_var_t[tc]
            w = [(VE, ve_var_t[tc])]
            if tc == 0:
                w.append((GP, gp_init))
            E("scalar", lambda s, cs=cs: s.activation(
                out=stats[:, cs + 6:cs + 7], in_=stats[:, cs + 5:cs + 6],
                func=AF.Ln, bias=c_const[:, 1:2]), waits=w, inc=SE)
            E("scalar", lambda s, cs=cs: s.activation(
                out=stats[:, cs + 7:cs + 8], in_=stats[:, cs + 6:cs + 7],
                func=AF.Exp, scale=-0.5), inc=SE)
            se_rstd_t[tc] = pr.c(SE)

        def ln1_norm(tc):
            sl = tc % 2
            cs = 8 * (tc % 2)
            xb = r_x[:, sl, :]
            lslot = lnt[:, (tc % 2) * D:(tc % 2) * D + D]
            w = [(SE, se_rstd_t[tc])]
            if tc >= 2:
                w.append((TE, te_lntr_t[tc - 2]))
            E("vector", lambda v, xb=xb, cs=cs, lslot=lslot: v.tensor_scalar(
                lslot, xb, stats[:, cs + 2:cs + 3], stats[:, cs + 7:cs + 8],
                op0=OP.subtract, op1=OP.mult), waits=w, inc=VE)
            ve_ln[tc] = pr.c(VE)
            for g in range(2):
                w = [(VE, ve_ln[tc]), (VE, tr_ev[0])]
                for i in range(4):
                    ic = 4 * g + i
                    E("tensor", lambda t, ic=ic, i=i, lslot=lslot: t.transpose(
                        ps_tr[:, i * 128:(i + 1) * 128],
                        lslot[:, ic * 128:(ic + 1) * 128], id_sb[:]),
                        waits=w if i == 0 else [], inc=TE)
                te_lntr[0] = pr.c(TE)
                tgt = r_lnfm_a if g == 0 else r_lnfm_b
                E("vector", lambda v, tgt=tgt, tc=tc: v.tensor_copy(
                    tgt[:, 0:4, tc * P:(tc + 1) * P],
                    ps_tr[:, :]), waits=[(TE, te_lntr[0])], inc=VE)
                tr_ev[0] = pr.c(VE)
            te_lntr_t[tc] = te_lntr[0]

        for tc in range(TC):
            if tc >= 1:
                ln1_norm(tc - 1)
            if tc + 1 < TC:
                w = [(VE, ve_ln[tc - 1])] if tc >= 1 else []
                E("sync", x_load(tc + 1, (tc + 1) % 2, dr["x"]), waits=w, inc=LD)
                ld_x[tc + 1] = pr.c(LD)
            ln1_stats(tc)
        ln1_norm(TC - 1)
        ve_ln1_done = pr.c(VE)

        gl_stage = [0]
        te_stage_rd = [0]
        se_yev = [0]

        def load_stage(name, half):
            return lambda g: g.dma_start(
                out=wstage[:, :, :],
                in_=dr[name][half * 512:(half + 1) * 512, :].rearrange(
                    "(a p) b -> p a b", p=P))

        def stage_to(dst, src, half, extra_w=()):
            for c in range(4):
                for icg in range(2):
                    w = [(VE, tr_ev[0])] + list(extra_w)
                    for i in range(4):
                        ic = icg * 4 + i
                        E("tensor", lambda t, c=c, ic=ic, i=i, src=src: t.transpose(
                            ps_tr[:, i * 128:(i + 1) * 128],
                            src[:, c, ic * 128:(ic + 1) * 128], id_sb[:]),
                            waits=w if i == 0 else [], inc=TE)
                    te_tr = pr.c(TE)
                    te_stage_rd[0] = te_tr
                    E("vector", lambda v, c=c, icg=icg, half=half, dst=dst: v.tensor_copy(
                        dst[:, icg * 4:(icg + 1) * 4,
                            half * 512 + c * 128: half * 512 + (c + 1) * 128],
                        ps_tr[:, :]), waits=[(TE, te_tr)], inc=VE)
                    tr_ev[0] = pr.c(VE)

        def prep_centered(name, dst):
            for half in range(2):
                E("gpsimd", load_stage(name, half),
                  waits=[(SE, se_yev[0]), (TE, te_stage_rd[0])], inc=GL)
                gl_stage[0] = pr.c(GL)
                yev_h = {}
                for hh in range(8):
                    base = 64 * (hh % 2)
                    chunk = hh // 2
                    yps = y0 if (hh % 2 == 0) else y1
                    for sln in range(2):
                        w = []
                        if sln == 0:
                            w = [(GL, gl_stage[0])]
                            w.append((SE, yev_h[hh - 2] if hh >= 2 else se_yev[0]))
                        E("tensor", lambda t, base=base, chunk=chunk, yps=yps, sx=sln: t.matmul(
                            yps[:, sx * 512:(sx + 1) * 512],
                            ut2_sb[base:base + 64, :],
                            wstage[base:base + 64, chunk, sx * 512:(sx + 1) * 512],
                            start=True, stop=True),
                            waits=w, inc=TE)
                    te_f = pr.c(TE)
                    te_stage_rd[0] = te_f
                    E("scalar", lambda s, base=base, chunk=chunk, yps=yps: s.copy(
                        out=ystage[base:base + 64, chunk, :], in_=yps[:, :]),
                        waits=[(TE, te_f)], inc=SE)
                    se_yev[0] = pr.c(SE)
                    yev_h[hh] = se_yev[0]
                stage_to(dst, ystage, half, extra_w=[(SE, se_yev[0])])

        def prep_plain(name, dst, extra_w=()):
            for half in range(2):
                E("gpsimd", load_stage(name, half),
                  waits=[(TE, te_stage_rd[0])] + list(extra_w), inc=GL)
                gl_stage[0] = pr.c(GL)
                stage_to(dst, wstage, half, extra_w=[(GL, gl_stage[0])])

        prep_centered("Wq", r_f1)
        te_wq_prep, ve_wq_prep = pr.c(TE), pr.c(VE)
        prep_centered("Wk", r_f2)
        te_wk_prep, ve_wk_prep = pr.c(TE), pr.c(VE)

        for icg in range(2):
            w = [(GL, gl_slot), (VE, tr_ev[0])]
            for i in range(4):
                ic = icg * 4 + i
                E("tensor", lambda t, ic=ic, i=i: t.transpose(
                    ps_tr[:, i * 128: i * 128 + NS],
                    slotS[:, ic * 128:(ic + 1) * 128], id_sb[0:NS, 0:NS]),
                    waits=w if i == 0 else [], inc=TE)
            te_str = pr.c(TE)
            E("vector", lambda v, icg=icg: v.tensor_copy(
                slotFM[:, icg * 4:(icg + 1) * 4, :],
                ps_tr[:, :].rearrange("p (a b) -> p a b", a=4, b=128)[:, :, 0:NS]),
                waits=[(TE, te_str)], inc=VE)
            tr_ev[0] = pr.c(VE)
        ve_slotfm = pr.c(VE)
        for ic in range(8):
            E("vector", lambda v, ic=ic: v.reduce_sum(stats[:, 0:1], slotFM[:, ic, :], axis=AXX),
              waits=[(VE, ve_slotfm), (VE, ve_ln1_done)] if ic == 0 else [], inc=VE)
            E("vector", lambda v, ic=ic: v.tensor_scalar(smf[:, ic:ic + 1], stats[:, 0:1],
                                                         1.0 / NS, None, op0=OP.mult), inc=VE, dr=True)
        ve_smf = pr.c(VE)
        E("gpsimd", lambda g: g.dma_start(
            out=ystage[:, :, 0:DHF],
            in_=dr["w1"][D:2 * D, :].rearrange("(a p) b -> p a b", p=P)[:, 0:4, :]),
          waits=[(TE, te_stage_rd[0]), (SE, se_yev[0]), (VE, tr_ev[0])], inc=GL)
        gl_w1b_a = pr.c(GL)
        E("gpsimd", lambda g: g.dma_start(
            out=wstage[:, :, 0:DHF],
            in_=dr["w1"][D:2 * D, :].rearrange("(a p) b -> p a b", p=P)[:, 4:8, :]),
          inc=GL)
        gl_w1b_b = pr.c(GL)
        for ic in range(8):
            src = ystage if ic < 4 else wstage
            E("tensor", lambda t, src=src, ic=ic: t.matmul(
                psC[0:1, 0:DHF], smf[:, ic:ic + 1], src[:, ic % 4, 0:DHF],
                start=(ic == 0), stop=(ic == 7)),
                waits=[(VE, ve_smf), (GL, gl_w1b_b if ic >= 4 else gl_w1b_a)] if ic in (0, 4) else [],
                inc=TE)
        te_c = pr.c(TE)
        te_stage_rd[0] = te_c
        E("vector", lambda v: v.tensor_add(hb_row[:], psC[0:1, 0:DHF], b1_row[:]),
          waits=[(TE, te_c), (LD, ld_consts)], inc=VE)
        ve_hb = pr.c(VE)
        for i in range(4):
            E("tensor", lambda t, i=i: t.transpose(
                ps_trf[:, i:i + 1], hb_row[0:1, i * 128:(i + 1) * 128],
                ones_m[0:1, 0:1]),
                waits=[(VE, ve_hb)] if i == 0 else [], inc=TE)
        te_hbt = pr.c(TE)
        E("vector", lambda v: v.tensor_copy(hbiasT[:, :], ps_trf[:, 0:4]),
          waits=[(TE, te_hbt)], inc=VE)
        ve_hbt = pr.c(VE)

        E("sync", lambda s: s.dma_start(out=bias_scr[0:1, :], in_=dr["ln_g"]),
          waits=[(LD, ld_consts)], inc=LD)
        ld_g = pr.c(LD)
        for i in range(8):
            E("tensor", lambda t, i=i: t.transpose(
                ps_trf[:, 8 + i:9 + i], bias_scr[0:1, i * 128:(i + 1) * 128],
                ones_m[0:1, 0:1]),
                waits=[(LD, ld_g), (VE, ve_hbt)] if i == 0 else [], inc=TE)
        te_gt = pr.c(TE)
        E("vector", lambda v: v.tensor_copy(gf[:, :], ps_trf[:, 8:16]),
          waits=[(TE, te_gt)], inc=VE)
        ve_gf = pr.c(VE)
        E("sync", lambda s: s.dma_start(out=bias_scr[0:1, :], in_=dr["ln_b"]),
          waits=[(VE, ve_gf)], inc=LD)
        ld_b = pr.c(LD)
        for i in range(8):
            E("tensor", lambda t, i=i: t.transpose(
                ps_trf[:, 8 + i:9 + i], bias_scr[0:1, i * 128:(i + 1) * 128],
                ones_m[0:1, 0:1]),
                waits=[(LD, ld_b), (VE, ve_gf)] if i == 0 else [], inc=TE)
        te_bt = pr.c(TE)
        E("vector", lambda v: v.tensor_copy(bfm[:, :], ps_trf[:, 8:16]),
          waits=[(TE, te_bt)], inc=VE)
        ve_bfm = pr.c(VE)

        te_proj = [0]
        ve_pev = [0]
        so_scat = [0]
        se_qv = [0]
        te_qv_last = [0]

        def qk_projection(wsrc, qdst, first_waits):
            for oc in range(NPAIR):
                for halft in range(2):
                    ps = sc0 if halft == 0 else sc1
                    for ic in range(8):
                        for sln in range(2):
                            w = []
                            if ic == 0 and sln == 0:
                                w = list(first_waits) + [(VE, ve_pev[0])]
                            E("tensor", lambda t, ps=ps, oc=oc, ic=ic, sln=sln, halft=halft,
                              wsrc=wsrc: t.matmul(
                                ps[:, sln * 512:(sln + 1) * 512],
                                wsrc[:, ic, oc * 128:(oc + 1) * 128],
                                lnfm(ic)[:, halft * 1024 + sln * 512: halft * 1024 + (sln + 1) * 512],
                                start=(ic == 0), stop=(ic == 7)), waits=w, inc=TE)
                    te_proj[0] = pr.c(TE)
                    E("vector", lambda v, ps=ps, oc=oc, halft=halft, qdst=qdst: v.tensor_copy(
                        qdst[:, oc, halft * 1024:(halft + 1) * 1024], ps[:, :]),
                        waits=[(TE, te_proj[0])], inc=VE)
                    ve_pev[0] = pr.c(VE)
                E("vector", lambda v, oc=oc, qdst=qdst: v.tensor_mul(
                    sq[:, :], qdst[:, oc, :], qdst[:, oc, :]),
                    waits=[(SO, so_scat[0])], inc=VE, dr=True)
                ve_sq = pr.c(VE)
                for q4 in range(4):
                    qvp = psC[0:2, 0:512] if q4 % 2 == 0 else psB1[0:2, 0:512]
                    E("tensor", lambda t, q4=q4, qvp=qvp: t.matmul(
                        qvp[:, :], ones_pr[:, :],
                        sq[:, q4 * 512:(q4 + 1) * 512],
                        start=True, stop=True),
                        waits=[(VE, ve_sq), (SE, se_qv[0])], inc=TE)
                    te_qv = pr.c(TE)
                    te_qv_last[0] = te_qv
                    E("scalar", lambda s, q4=q4, qvp=qvp: s.activation(
                        out=r_qv[0:2, q4 * 512:(q4 + 1) * 512], in_=qvp[:, :],
                        func=AF.Copy, scale=1.0 / 63.0),
                        waits=[(TE, te_qv)], inc=SE)
                    se_qv[0] = pr.c(SE)
                E("sync", lambda s, oc=oc, qdst=qdst: s.dma_start(
                    out=qdst[63:64, oc, :], in_=r_qv[0:1, :]),
                    waits=[(SE, se_qv[0]), (VE, ve_pev[0])], inc=SO)
                E("sync", lambda s, oc=oc, qdst=qdst: s.dma_start(
                    out=qdst[127:128, oc, :], in_=r_qv[1:2, :]), inc=SO)
                so_scat[0] = pr.c(SO)

        qk_projection(r_f1, r_q,
                      [(TE, te_wq_prep), (VE, ve_wq_prep), (VE, ve_ln1_done),
                       (VE, ve_hbt), (VE, ve_bfm), (VE, ve_hb)])
        so_q_scat = so_scat[0]
        qk_projection(r_f2, r_k, [(TE, te_wk_prep), (VE, ve_wk_prep)])
        so_k_scat = so_scat[0]

        prep_plain("Wv", r_f1, extra_w=[(VE, ve_pev[0]), (SE, se_qv[0])])
        te_wv_prep, ve_wv_prep = pr.c(TE), pr.c(VE)
        if debug:
            E("sync", lambda s: s.dma_start(
                out=dbg["dbg_ln1b"],
                in_=r_lnfm_a[:, :, :].rearrange("p a b -> p (a b)")),
                waits=[(TE, te_wv_prep), (VE, ve_wv_prep)], inc=SO)
            E("sync", lambda s: s.dma_start(
                out=dbg["dbg_lnb"],
                in_=r_lnfm_b[:, :, :].rearrange("p a b -> p (a b)")), inc=SO)
            so_lnb = pr.c(SO)
        for tc in range(TC):
            ps = sc0 if tc % 2 == 0 else sc1
            for ic in range(8):
                for sln in range(2):
                    w = []
                    if ic == 0 and sln == 0:
                        w = [(TE, te_wv_prep), (VE, ve_wv_prep), (VE, ve_pev[0]),
                             (GP, gp_init)]
                        if debug and tc == 0:
                            w.append((SO, so_lnb))
                    E("tensor", lambda t, ps=ps, tc=tc, ic=ic, sln=sln: t.matmul(
                        ps[:, sln * 512:(sln + 1) * 512],
                        lnfm(ic)[:, tc * 128:(tc + 1) * 128],
                        r_f1[:, ic, sln * 512:(sln + 1) * 512],
                        start=(ic == 0), stop=(ic == 7)), waits=w, inc=TE)
            te_proj[0] = pr.c(TE)
            if debug and tc == 0:
                E("scalar", lambda s: s.activation(
                    out=dbgav_sb[:, :], in_=sc0[:, :], func=AF.Copy),
                    waits=[(TE, te_proj[0])], inc=SE)
                se_dv = pr.c(SE)
                E("sync", lambda s: s.dma_start(out=dbg["dbg_vps"], in_=dbgav_sb[:, :]),
                  waits=[(SE, se_dv)], inc=SO)
                so_dv = pr.c(SO)
                E("vector", lambda v, tc=tc, ps=ps: v.tensor_copy(
                    r_v[:, tc, :, 0:64], ps[:, :]),
                    waits=[(TE, te_proj[0]), (SE, se_dv)], inc=VE)
            else:
                E("vector", lambda v, tc=tc, ps=ps: v.tensor_copy(
                    r_v[:, tc, :, 0:64], ps[:, :]),
                    waits=[(TE, te_proj[0])], inc=VE)
            ve_pev[0] = pr.c(VE)
        ve_vhat = pr.c(VE)

        prep_plain("Wo", r_f2, extra_w=[(VE, ve_pev[0])])
        te_wo_prep, ve_wo_prep = pr.c(TE), pr.c(VE)

        if debug:
            E("sync", lambda s: s.dma_start(
                out=dbg["dbg_ln1"],
                in_=r_lnfm_a[:, :, :].rearrange("p a b -> p (a b)")),
                waits=[(VE, ve_ln1_done)], inc=SO)
            E("sync", lambda s: s.dma_start(
                out=dbg["dbg_qfm"], in_=r_q[:, :, :].rearrange("p a b -> p (a b)")),
                waits=[(SO, so_q_scat)], inc=SO)
            E("sync", lambda s: s.dma_start(
                out=dbg["dbg_kfm"], in_=r_k[:, :, :].rearrange("p a b -> p (a b)")),
                waits=[(SO, so_k_scat)], inc=SO)
            E("sync", lambda s: s.dma_start(
                out=dbg["dbg_vhat"],
                in_=r_v[:, :, :, :].rearrange("p a b c -> p (a b c)")),
                waits=[(VE, ve_vhat)], inc=SO)

        E("gpsimd", lambda g: g.nop(), waits=[(VE, ve_vhat)], inc=GP)
        gp_f1_free = pr.c(GP)
        for fn in [
            lambda g: g.dma_start(out=w1a_a[:, :, :],
                                  in_=dr["w1"][0:D, :].rearrange("(a p) b -> p a b", p=P)[:, 0:4, :]),
            lambda g: g.dma_start(out=w1a_b[:, :, :],
                                  in_=dr["w1"][0:D, :].rearrange("(a p) b -> p a b", p=P)[:, 4:8, :]),
            lambda g: g.dma_start(out=wqN[:, :, :],
                                  in_=dr["wq"].rearrange("(a p) b -> p a b", p=P)),
            lambda g: g.dma_start(out=wkN[:, :, :],
                                  in_=dr["wk"].rearrange("(a p) b -> p a b", p=P)),
        ]:
            E("gpsimd", fn, waits=[(GP, gp_f1_free)], inc=GL)
        gl_ffn1 = pr.c(GL)

        ebuf = r_et
        se_exp = {}
        te_av = {}
        ve_avev = [0]
        def emit_qk(rnd, pair, qq, kc):
            par = kc % 2
            scA = psA[:, par * 1024: par * 1024 + 512]
            scB = psA[:, par * 1024 + 512: (par + 1) * 1024]
            w = []
            if kc >= 2:
                w.append((SE, se_exp[(rnd, kc - 2)]))
            elif rnd >= 1:
                w.append((SE, se_exp[(rnd - 1, KC - 2 + kc)]))
            else:
                w += [(SO, so_q_scat), (SO, so_k_scat), (VE, ve_vhat)]
                if debug:
                    w.append((SO, pr.c(SO)))
            for hh in range(2):
                base = 64 * hh
                scp = scA if hh == 0 else scB
                E("tensor", lambda t, scp=scp, base=base, pair=pair, kc=kc,
                  qq=qq: t.matmul(
                    scp[:, :],
                    r_k[base:base + 64, pair, kc * 128:(kc + 1) * 128],
                    r_q[base:base + 64, pair, qq * 512:(qq + 1) * 512],
                    start=True, stop=True, tile_position=(base, 0)),
                    waits=w if hh == 0 else [], inc=TE)
            te_qkv = pr.c(TE)
            w = [(TE, te_qkv)]
            if kc >= 2:
                w.append((TE, te_av[(rnd, kc - 2, 1)]))
            elif rnd >= 1:
                w.append((TE, te_av[(rnd - 1, KC - 2 + kc, 1)]))
            E("scalar", lambda s, kc=kc: s.activation(
                out=ebuf[:, kc % 2, :],
                in_=psA[:, (kc % 2) * 1024:(kc % 2 + 1) * 1024], func=AF.Exp,
                scale=1.0 / 64.0, bias=c_const[:, 0:1]),
                waits=w, inc=SE)
            se_exp[(rnd, kc)] = pr.c(SE)

        def emit_av(rnd, pair, kc):
            for hh in range(2):
                avp = avA if hh == 0 else avB
                hglob = 2 * pair + hh
                w = [(SE, se_exp[(rnd, kc)])]
                if kc == 0:
                    w.append((VE, ve_avev[0]))
                for qc in range(4):
                    st = (kc == 0 and qc == 0)
                    E("tensor", lambda t, kc=kc, qc=qc, hglob=hglob, st=st,
                      avp=avp, hh=hh: t.matmul(
                        avp[:, qc, 0:65],
                        ebuf[:, kc % 2, hh * 512 + qc * 128:
                             hh * 512 + (qc + 1) * 128],
                        r_v[:, kc, hglob, :],
                        start=st, stop=(kc == KC - 1),
                        skip_group_check=True),
                        waits=w if (qc == 0 and hh == 0) else [], inc=TE)
                te_av[(rnd, kc, hh)] = pr.c(TE)

        gwt_so = [0]
        gwt_ld = {}
        gwt_cast = {}
        gwt_trg = {}
        gwt_evac = {}
        gwt_out = {}
        gstage = r_lnfm_a[:, :, :].rearrange("p a b -> p (a b)")

        def q_load(step):
            which, ob = step % 2, step // 2
            nm = "Wih" if which == 0 else "Whh"
            slot = step % 2
            w = [(VE, ve_vhat)] if step < 2 else [(VE, gwt_cast[step - 2])]
            E("sync", lambda s, nm=nm, ob=ob, slot=slot: s.dma_start(
                out=r_x[:, slot, :],
                in_=dr[nm][ob * 128:(ob + 1) * 128, :]),
                waits=w, inc=LD)
            gwt_ld[step] = pr.c(LD)

        def q_cast(step):
            slot = step % 2
            w = [(LD, gwt_ld[step])]
            if step >= 2:
                w.append((TE, gwt_trg[(step - 2, 1)]))
            E("vector", lambda v, slot=slot: v.tensor_copy(
                gstage[:, slot * 1024:(slot + 1) * 1024], r_x[:, slot, :]),
                waits=w, inc=VE)
            gwt_cast[step] = pr.c(VE)

        def q_trg(step, icg):
            slot = step % 2
            yslot = step % 4
            w = [(VE, gwt_cast[step]), (VE, tr_ev[0])]
            for i in range(4):
                ic = icg * 4 + i
                E("tensor", lambda t, ic=ic, i=i, slot=slot: t.transpose(
                    ps_tr[:, i * 128:(i + 1) * 128],
                    gstage[:, slot * 1024 + ic * 128:slot * 1024 + (ic + 1) * 128],
                    id_sb[:]),
                    waits=w if i == 0 else [], inc=TE)
            te_t = pr.c(TE)
            gwt_trg[(step, icg)] = te_t
            we = [(TE, te_t)]
            if icg == 0 and step >= 4:
                we.append((SO, gwt_out[step - 4]))
            E("vector", lambda v, icg=icg, yslot=yslot: v.tensor_copy(
                ystage[:, yslot, icg * 512:(icg + 1) * 512], ps_tr[:, :]),
                waits=we, inc=VE)
            gwt_evac[(step, icg)] = pr.c(VE)
            tr_ev[0] = pr.c(VE)

        def q_out(step):
            which, ob = step % 2, step // 2
            yslot = step % 4
            E("sync", lambda s, which=which, ob=ob, yslot=yslot: s.dma_start(
                out=wgT[which, ob, :, :], in_=ystage[:, yslot, :]),
                waits=[(VE, gwt_evac[(step, 1)])], inc=SO)
            gwt_out[step] = pr.c(SO)
            gwt_so[0] = gwt_out[step]

        NSTEP_GRU = 2 * GOB
        gru_quanta = []
        gru_quanta.append(lambda: q_load(0))
        gru_quanta.append(lambda: q_load(1))
        for s_ in range(NSTEP_GRU):
            gru_quanta.append(lambda s_=s_: q_cast(s_))
            gru_quanta.append(lambda s_=s_: q_trg(s_, 0))
            gru_quanta.append(lambda s_=s_: q_trg(s_, 1))
            gru_quanta.append(lambda s_=s_: q_out(s_))
            if s_ + 2 < NSTEP_GRU:
                gru_quanta.append(lambda s_=s_: q_load(s_ + 2))
        gru_quanta_it = iter(gru_quanta)

        se_exp_h = {}
        pending_ptr = [None]
        te_ptr_last = [0]

        def emit_pair_tr():
            if pending_ptr[0] is None:
                return
            ppair, ve_ready = pending_ptr[0]
            pending_ptr[0] = None
            for g4 in range(4):
                w = [(VE, ve_ready), (VE, tr_ev[0])]
                for i in range(4):
                    tcl = g4 * 4 + i
                    E("tensor", lambda t, tcl=tcl, i=i: t.transpose(
                        ps_tr[:, i * 128:(i + 1) * 128],
                        r_pt[:, tcl, :], id_sb[:]),
                        waits=w if i == 0 else [], inc=TE)
                te_ptr = pr.c(TE)
                te_ptr_last[0] = te_ptr
                E("vector", lambda v, g4=g4, ppair=ppair: v.tensor_copy(
                    r_q[:, ppair, g4 * 512:(g4 + 1) * 512], ps_tr[:, :]),
                    waits=[(TE, te_ptr)], inc=VE)
                tr_ev[0] = pr.c(VE)

        for pair in range(NPAIR):
            for qq in range(4):
                rnd = pair * 4 + qq
                emit_qk(rnd, pair, qq, 0)
                emit_qk(rnd, pair, qq, 1)
                if qq == 0:
                    emit_pair_tr()
                for kc in range(KC):
                    emit_av(rnd, pair, kc)
                    if kc + 2 < KC:
                        emit_qk(rnd, pair, qq, kc + 2)
                    if rnd >= 2:
                        qk_q = next(gru_quanta_it, None)
                        if qk_q is not None:
                            qk_q()
                te_av_done = pr.c(TE)
                E("vector", lambda v: v.reciprocal(recipC[:, 0:4], avA[:, :, 64]),
                  waits=[(TE, te_av_done), (TE, te_ptr_last[0])], inc=VE)
                E("vector", lambda v: v.reciprocal(recipC[:, 4:8], avB[:, :, 64]),
                  inc=VE)
                E("vector", lambda v: v.tensor_copy(stats[:, 0:1], recipC[:, 0:1]), inc=VE)
                for hh in range(2):
                    avp = avA if hh == 0 else avB
                    for qc in range(4):
                        E("vector", lambda v, qc=qc, qq=qq, hh=hh, avp=avp: v.tensor_scalar(
                            r_pt[:, qq * 4 + qc, hh * 64:hh * 64 + 64],
                            avp[:, qc, 0:64], recipC[:, hh * 4 + qc:hh * 4 + qc + 1],
                            None, op0=OP.mult),
                            inc=VE)
                ve_avev[0] = pr.c(VE)
            pending_ptr[0] = (pair, ve_avev[0])
        for qk_q in gru_quanta_it:
            qk_q()
        emit_pair_tr()
        ve_attnfm = pr.c(VE)
        te_headloop = pr.c(TE)

        if debug:
            E("sync", lambda s: s.dma_start(
                out=dbg["dbg_attnfm"], in_=r_q[:, :, :].rearrange("p a b -> p (a b)")),
                waits=[(VE, ve_attnfm)], inc=SO)

        E("gpsimd", lambda g: g.nop(), waits=[(TE, te_headloop)], inc=GP)
        gp_rk_free = pr.c(GP)
        E("gpsimd", lambda g: g.dma_start(
            out=wvpN[:, :, :], in_=dr["wvp"].rearrange("(a p) b -> p a b", p=P)),
          waits=[(GP, gp_rk_free)], inc=GL)
        gl_wvp = pr.c(GL)
        E("gpsimd", lambda g: g.dma_start(
            out=woN[:, :, :], in_=dr["wo"].rearrange("(a p) b -> p a b", p=P)),
          inc=GL)
        gl_wo = pr.c(GL)

        ld_x2 = [0] * TC
        ve_x2 = [0] * TC
        ve_norm2 = [0] * TC
        so_x2 = [0] * TC
        se_sq2 = [0]
        E("sync", x_load(0, 0, dr["x"]),
          waits=[(VE, ve_avev[0]), (SO, pr.c(SO))], inc=LD)
        ld_x2[0] = pr.c(LD)

        def emit_ln2tr(tc):
            for g in range(2):
                w = [(VE, ve_norm2[tc]), (VE, tr_ev[0])]
                for i in range(4):
                    ic = 4 * g + i
                    E("tensor", lambda t, tc=tc, ic=ic, i=i: t.transpose(
                        ps_tr[:, i * 128:(i + 1) * 128],
                        ln2tok[:, tc, ic * 128:(ic + 1) * 128], id_sb[:]),
                        waits=w if i == 0 else [], inc=TE)
                te_tr2 = pr.c(TE)
                tgt = r_lnfm_a if g == 0 else r_lnfm_b
                E("vector", lambda v, tgt=tgt, tc=tc: v.tensor_copy(
                    tgt[:, 0:4, tc * P:(tc + 1) * P],
                    ps_tr[:, :]), waits=[(TE, te_tr2)], inc=VE)
                tr_ev[0] = pr.c(VE)

        for tc in range(TC):
            sl = tc % 2
            if tc + 1 < TC:
                w = []
                if tc >= 1:
                    w = [(VE, ve_norm2[tc - 1]), (SO, so_x2[tc - 1])]
                E("sync", x_load(tc + 1, (tc + 1) % 2, dr["x"]), waits=w, inc=LD)
                ld_x2[tc + 1] = pr.c(LD)
            ps = sc0 if tc % 2 == 0 else sc1
            for ic in range(8):
                for sln in range(2):
                    w = []
                    if ic == 0 and sln == 0:
                        w = [(VE, ve_attnfm), (TE, te_wo_prep), (VE, ve_wo_prep)]
                        if tc >= 2:
                            w.append((VE, ve_x2[tc - 2]))
                    E("tensor", lambda t, ps=ps, tc=tc, ic=ic, sln=sln: t.matmul(
                        ps[:, sln * 512:(sln + 1) * 512],
                        r_q[:, ic, tc * 128:(tc + 1) * 128],
                        r_f2[:, ic, sln * 512:(sln + 1) * 512],
                        start=(ic == 0), stop=(ic == 7)), waits=w, inc=TE)
            te_wo = pr.c(TE)
            if tc >= 1:
                emit_ln2tr(tc - 1)
            xb = r_x[:, sl, :]
            E("vector", lambda v, xb=xb, ps=ps: v.tensor_add(xb, ps[:, :], xb),
              waits=[(TE, te_wo), (LD, ld_x2[tc])], inc=VE)
            ve_x2[tc] = pr.c(VE)
            E("sync", lambda s, xb=xb, tc=tc: s.dma_start(
                out=x2s[tc * P:(tc + 1) * P, :], in_=xb),
                waits=[(VE, ve_x2[tc])], inc=SO)
            so_x2[tc] = pr.c(SO)
            E("scalar", lambda s, xb=xb: s.activation(
                out=sq[:, 0:D], in_=xb, func=AF.Square, accum_out=stats[:, 0:1]),
                waits=[(VE, ve_x2[tc]), (VE, ve_var_prev[0])], inc=SE)
            se_sq2[0] = pr.c(SE)
            E("vector", lambda v, xb=xb: v.reduce_sum(stats[:, 1:2], xb, axis=AXX), inc=VE, dr=True)
            E("vector", lambda v: v.tensor_scalar_mul(stats[:, 2:3], stats[:, 1:2], 1.0 / D), inc=VE, dr=True)
            E("vector", lambda v: v.tensor_mul(stats[:, 3:4], stats[:, 2:3], stats[:, 2:3]), inc=VE, dr=True)
            E("vector", lambda v: v.tensor_scalar(stats[:, 4:5], stats[:, 0:1], 1.0 / D,
                                                  None, op0=OP.mult),
              waits=[(SE, se_sq2[0])], inc=VE)
            E("vector", lambda v: v.tensor_sub(stats[:, 5:6], stats[:, 4:5], stats[:, 3:4]), inc=VE, dr=True)
            ve_var_prev[0] = pr.c(VE)
            E("scalar", lambda s: s.activation(out=stats[:, 6:7], in_=stats[:, 5:6],
                                               func=AF.Ln, bias=c_const[:, 1:2]),
              waits=[(VE, ve_var_prev[0])], inc=SE)
            E("scalar", lambda s: s.activation(out=stats[:, 7:8], in_=stats[:, 6:7],
                                               func=AF.Exp, scale=-0.5), inc=SE)
            se_rstd2 = pr.c(SE)
            E("vector", lambda v, xb=xb, tc=tc: v.tensor_scalar(
                ln2tok[:, tc, :], xb, stats[:, 2:3], stats[:, 7:8],
                op0=OP.subtract, op1=OP.mult),
                waits=[(SE, se_rstd2)], inc=VE)
            ve_norm2[tc] = pr.c(VE)
        emit_ln2tr(TC - 1)
        ve_ln2fm = pr.c(VE)
        te_p3 = pr.c(TE)

        if debug:
            E("sync", lambda s: s.dma_start(out=dbg["dbg_x2"], in_=x2s),
              waits=[(SO, so_x2[TC - 1])], inc=SO)
            E("sync", lambda s: s.dma_start(
                out=dbg["dbg_ln2fm"], in_=r_lnfm_a[:, :, :].rearrange("p a b -> p (a b)")),
                waits=[(VE, ve_ln2fm)], inc=SO)

        se_gelu = [0]
        for dhc in range(4):
            for halft in range(2):
                ps = sc0 if halft == 0 else sc1
                for ic in range(8):
                    wsrc = w1a_a if ic < 4 else w1a_b
                    for sln in range(2):
                        w = []
                        if ic == 0 and sln == 0:
                            w = [(VE, ve_ln2fm), (GL, gl_ffn1), (VE, ve_hbt)]
                            if dhc > 0 or halft > 0:
                                w.append((SE, se_gelu[0]))
                        E("tensor", lambda t, ps=ps, dhc=dhc, ic=ic, sln=sln, halft=halft,
                          wsrc=wsrc: t.matmul(
                            ps[:, sln * 512:(sln + 1) * 512],
                            wsrc[:, ic % 4, dhc * 128:(dhc + 1) * 128],
                            lnfm(ic)[:, halft * 1024 + sln * 512: halft * 1024 + (sln + 1) * 512],
                            start=(ic == 0), stop=(ic == 7)), waits=w, inc=TE)
                te_h = pr.c(TE)
                E("scalar", lambda s, ps=ps, dhc=dhc, halft=halft: s.activation(
                    out=hFM[:, dhc, halft * 1024:(halft + 1) * 1024], in_=ps[:, :],
                    func=AF.Gelu, bias=hbiasT[:, dhc:dhc + 1]),
                    waits=[(TE, te_h)], inc=SE)
                se_gelu[0] = pr.c(SE)
        se_hfm = pr.c(SE)
        if debug:
            E("sync", lambda s: s.dma_start(
                out=dbg["dbg_hfm"], in_=hFM[:, :, :].rearrange("p a b -> p (a b)")),
                waits=[(SE, se_hfm)], inc=SO)

        E("gpsimd", lambda g: g.dma_start(
            out=w2N[:, :, :], in_=dr["w2"].rearrange("(a p) b -> p a b", p=P)),
            waits=[(SE, se_hfm)], inc=GL)
        gl_w2 = pr.c(GL)
        ve_alpha = [0] * TC
        rlp = psC[0:P, 64:128]
        for tc in range(TC):
            for dhc in range(4):
                w = []
                if dhc == 0:
                    w = [(SE, se_hfm), (GL, gl_w2)]
                    if tc >= 1:
                        w.append((VE, ve_alpha[tc - 1]))
                E("tensor", lambda t, tc=tc, dhc=dhc: t.matmul(
                    rlp[:, :], hFM[:, dhc, tc * 128:(tc + 1) * 128],
                    w2N[:, dhc, :], start=(dhc == 0), stop=False,
                    skip_group_check=True), waits=w, inc=TE)
            E("tensor", lambda t: t.matmul(rlp[:, :], ones_m[0:1, :], b2_row[0:1, :],
                                           start=False, stop=True,
                                           skip_group_check=True), inc=TE)
            te_rl = pr.c(TE)
            E("vector", lambda v: v.tensor_scalar(logits[:, :], rlp[:, :], inv_tau[:, 0:1],
                                                  None, op0=OP.mult),
              waits=[(TE, te_rl), (VE, ve_invtau)], inc=VE)
            E("vector", lambda v: v.max(t8[:, :], logits[:, :]), inc=VE, dr=True)
            E("vector", lambda v: v.tensor_scalar(negm1[:, :], t8[:, 0:1], -1.0, None,
                                                  op0=OP.mult), inc=VE, dr=True)
            ve_t8 = pr.c(VE)
            E("scalar", lambda s: s.activation(out=e8b[:, :], in_=t8[:, :], func=AF.Exp,
                                               bias=negm1[:, 0:1]),
              waits=[(VE, ve_t8)], inc=SE)
            E("scalar", lambda s: s.activation(out=e64b[:, :], in_=logits[:, :],
                                               func=AF.Exp, bias=negm1[:, 0:1]), inc=SE)
            se_e = pr.c(SE)
            E("vector", lambda v: v.reduce_sum(s8[:, 0:1], e8b[:, :], axis=AXX),
              waits=[(SE, se_e)], inc=VE)
            E("vector", lambda v: v.reciprocal(s8[:, 1:2], s8[:, 0:1]), inc=VE, dr=True)
            E("vector", lambda v: v.tensor_scalar(maskb[:, :], logits[:, :], t8[:, 7:8],
                                                  None, op0=OP.is_ge), inc=VE)
            E("vector", lambda v, tc=tc: v.scalar_tensor_tensor(
                alphaT[:, tc, :], e64b[:, :], s8[:, 1:2], maskb[:, :],
                op0=OP.mult, op1=OP.mult), inc=VE)
            ve_alpha[tc] = pr.c(VE)
            if debug:
                E("sync", lambda s, tc=tc: s.dma_start(
                    out=dbg["dbg_logits"][:, tc * NS:(tc + 1) * NS], in_=logits[:, :]),
                    waits=[(VE, ve_alpha[tc])], inc=SO)
        ve_alpha_all = pr.c(VE)
        if debug:
            E("sync", lambda s: s.dma_start(
                out=dbg["dbg_alpha"], in_=alphaT[:, :, :].rearrange("p a b -> p (a b)")),
                waits=[(VE, ve_alpha_all)], inc=SO)

        si_ps = psA[0:NS, 1024:2048]
        for tc in range(TC):
            for sln in range(2):
                w = []
                if sln == 0:
                    w = [(VE, ve_alpha[tc])]
                    if tc == 0:
                        w.append((VE, ve_avev[0]))
                E("tensor", lambda t, tc=tc, sln=sln: t.matmul(
                    si_ps[:, sln * 512:(sln + 1) * 512],
                    alphaT[:, tc, :], ln2tok[:, tc, sln * 512:(sln + 1) * 512],
                    start=(tc == 0), stop=(tc == TC - 1), skip_group_check=True),
                    waits=w, inc=TE)
        te_si = pr.c(TE)
        dn_ps = psC[0:NS, 0:1]
        for tc in range(TC):
            E("tensor", lambda t, tc=tc: t.matmul(
                dn_ps[:, :], alphaT[:, tc, :], ones_n[:, :],
                start=(tc == 0), stop=(tc == TC - 1)),
                waits=[(VE, ve_alpha_all)] if tc == 0 else [], inc=TE)
        te_dn = pr.c(TE)
        E("vector", lambda v: v.tensor_scalar_add(denA[:, 0:1], dn_ps[:, :], 1e-8),
          waits=[(TE, te_dn), (TE, te_si)], inc=VE)
        E("vector", lambda v: v.reciprocal(denA[:, 1:2], denA[:, 0:1]), inc=VE, dr=True)
        E("vector", lambda v: v.tensor_copy(stats[0:NS, 0:1], denA[0:NS, 1:2]), inc=VE)
        E("vector", lambda v: v.tensor_scalar(siB[:, :], si_ps[:, :],
                                              denA[:, 1:2], None, op0=OP.mult), inc=VE)
        ve_si = pr.c(VE)


        def smajor_to_fm(srcap, dst, ready_w):
            for icg in range(2):
                w = list(ready_w) + [(VE, tr_ev[0])]
                for i in range(4):
                    ic = icg * 4 + i
                    E("tensor", lambda t, ic=ic, i=i, srcap=srcap: t.transpose(
                        ps_tr[:, i * 128: i * 128 + NS],
                        srcap[:, ic * 128:(ic + 1) * 128], id_sb[0:NS, 0:NS]),
                        waits=w if i == 0 else [], inc=TE)
                te_t = pr.c(TE)
                E("vector", lambda v, icg=icg, dst=dst: v.tensor_copy(
                    dst[:, icg * 4:(icg + 1) * 4, :],
                    ps_tr[:, :].rearrange("p (a b) -> p a b", a=4, b=128)[:, :, 0:NS]),
                    waits=[(TE, te_t)], inc=VE)
                tr_ev[0] = pr.c(VE)
            return pr.c(VE)

        ve_sifm = smajor_to_fm(siB, siFM, [(VE, ve_si)])

        gi_ps = psA[0:NS, 1024:1152]
        gh_ps = psA[0:NS, 1536:1664]
        te_gru = [0]
        ve_gru = [0]
        se_gru = [0]
        gru_tr_ev = [0]
        te_wt_done = [0, 0]
        for ob in range(GOB):
            w = []
            if ob == 0:
                w = [(VE, ve_sifm), (SE, se_hfm), (TE, te_si), (SO, gwt_so[0])]
            else:
                w = [(TE, te_gru[0])]
            E("sync", lambda s, ob=ob: s.dma_start(
                out=wtv[:, 0, :, :], in_=wgT[0, ob, :, :]), waits=w, inc=LD)
            E("sync", lambda s, ob=ob: s.dma_start(
                out=wtv[:, 1, :, :], in_=wgT[1, ob, :, :]), inc=LD)
            ld_wt = pr.c(LD)
            E("sync", lambda s, ob=ob: s.dma_start(
                out=bias_g[0:1, 0:P], in_=dr["bih"][0:1, ob * 128:(ob + 1) * 128]),
                waits=[(TE, te_gru[0])] if ob >= 1 else [], inc=LD)
            E("sync", lambda s, ob=ob: s.dma_start(
                out=bias_g[0:1, P:2 * P], in_=dr["bhh"][0:1, ob * 128:(ob + 1) * 128]), inc=LD)
            ld_gb = pr.c(LD)
            ve_wt = None
            for ic in range(8):
                w = []
                if ic == 0:
                    w = [(LD, ld_wt)]
                    if ob >= 1:
                        w += [(VE, ve_gru[0]), (SE, se_gru[0])]
                E("tensor", lambda t, ic=ic: t.matmul(
                    gi_ps[:, :], siFM[:, ic, :], wtv[:, 0, ic, :],
                    start=(ic == 0), stop=False), waits=w, inc=TE)
                E("tensor", lambda t, ic=ic: t.matmul(
                    gh_ps[:, :], slotFM[:, ic, :], wtv[:, 1, ic, :],
                    start=(ic == 0), stop=False), inc=TE)
            E("tensor", lambda t: t.matmul(gi_ps[:, :], ones_m[0:1, 0:NS], bias_g[0:1, 0:P],
                                           start=False, stop=True),
              waits=[(LD, ld_gb)], inc=TE)
            E("tensor", lambda t: t.matmul(gh_ps[:, :], ones_m[0:1, 0:NS], bias_g[0:1, P:2 * P],
                                           start=False, stop=True), inc=TE)
            te_gru[0] = pr.c(TE)
            gate = ob // 8
            dsl = (ob % 8) * 128
            if gate < 2:
                dst = rrB if gate == 0 else zzB
                E("vector", lambda v: v.tensor_copy(gsc[:, 0, :], gi_ps[:, :]),
                  waits=[(TE, te_gru[0])], inc=VE)
                E("vector", lambda v: v.tensor_add(gsc[:, 1, :],
                                                   gsc[:, 0, :], gh_ps[:, :]), inc=VE, dr=True)
                ve_g = pr.c(VE)
                E("scalar", lambda s: s.activation(out=gsc[:, 2, :],
                                                   in_=gsc[:, 1, :],
                                                   func=AF.Exp, scale=-1.0),
                  waits=[(VE, ve_g)], inc=SE)
                se_gru[0] = pr.c(SE)
                E("vector", lambda v: v.tensor_scalar_add(gsc[:, 3, :],
                                                          gsc[:, 2, :], 1.0),
                  waits=[(SE, se_gru[0])], inc=VE)
                E("vector", lambda v: v.reciprocal(gsc[:, 2, :], gsc[:, 3, :]), inc=VE, dr=True)
                E("vector", lambda v, dst=dst, dsl=dsl: v.tensor_copy(
                    dst[:, dsl:dsl + 128], gsc[:, 2, :]), inc=VE, dr=True)
                ve_gru[0] = pr.c(VE)
            else:
                E("vector", lambda v, dsl=dsl: v.tensor_mul(
                    gsc[:, 0, :], rrB[:, dsl:dsl + 128], gh_ps[:, :]),
                    waits=[(TE, te_gru[0])], inc=VE)
                E("vector", lambda v: v.tensor_add(gsc[:, 1, :],
                                                   gsc[:, 0, :], gi_ps[:, :]), inc=VE, dr=True)
                ve_g = pr.c(VE)
                E("scalar", lambda s: s.activation(out=gsc[:, 2, :],
                                                   in_=gsc[:, 1, :],
                                                   func=AF.Exp, scale=-2.0),
                  waits=[(VE, ve_g)], inc=SE)
                se_gru[0] = pr.c(SE)
                E("vector", lambda v: v.tensor_scalar_add(gsc[:, 3, :],
                                                          gsc[:, 2, :], 1.0),
                  waits=[(SE, se_gru[0])], inc=VE)
                E("vector", lambda v: v.reciprocal(gsc[:, 0, :], gsc[:, 3, :]), inc=VE, dr=True)
                E("vector", lambda v: v.tensor_scalar(gsc[:, 1, :], gsc[:, 2, :],
                                                      -1.0, 1.0, op0=OP.mult, op1=OP.add), inc=VE)
                E("vector", lambda v: v.tensor_mul(gsc[:, 1, :], gsc[:, 1, :], gsc[:, 0, :]), inc=VE, dr=True)
                E("vector", lambda v, dsl=dsl: v.tensor_sub(
                    gsc[:, 2, :], slotS[:, dsl:dsl + 128], gsc[:, 1, :]), inc=VE, dr=True)
                E("vector", lambda v, dsl=dsl: v.tensor_mul(
                    gsc[:, 3, :], zzB[:, dsl:dsl + 128], gsc[:, 2, :]), inc=VE, dr=True)
                E("vector", lambda v, dsl=dsl: v.tensor_add(
                    snewB[:, dsl:dsl + 128], gsc[:, 1, :], gsc[:, 3, :]), inc=VE, dr=True)
                ve_gru[0] = pr.c(VE)
        ve_snew = pr.c(VE)

        E("gpsimd", lambda g: g.nop(), waits=[(TE, te_p3)], inc=GP)
        gp_f2_free = pr.c(GP)
        E("gpsimd", lambda g: g.dma_start(
            out=wvN[:, :, :], in_=dr["wv"].rearrange("(a p) b -> p a b", p=P)),
          waits=[(GP, gp_f2_free)], inc=GL)
        gl_wv = pr.c(GL)

        ve_snb = ve_snew
        ve_snfm = smajor_to_fm(snewB, snFM, [(VE, ve_snew)])

        def slot_proj(dst_ps, wsrc, n_out, bias_name, first_w):
            E("sync", lambda s: s.dma_start(out=bias_scr[0:1, 0:n_out],
                                            in_=dr[bias_name]),
              waits=list(first_w) + [(TE, pr.c(TE))], inc=LD)
            ldb = pr.c(LD)
            nsl = (n_out + 511) // 512
            for ic in range(8):
                for sli in range(nsl):
                    lo, hi = sli * 512, min(n_out, (sli + 1) * 512)
                    E("tensor", lambda t, ic=ic, lo=lo, hi=hi, wsrc=wsrc: t.matmul(
                        dst_ps[:, lo:hi], snFM[:, ic, :], wsrc[:, ic, lo:hi],
                        start=(ic == 0), stop=False, skip_group_check=True),
                        waits=(first_w + [(VE, ve_snfm)]) if (ic == 0 and sli == 0) else [],
                        inc=TE)
            for sli in range(nsl):
                lo, hi = sli * 512, min(n_out, (sli + 1) * 512)
                E("tensor", lambda t, lo=lo, hi=hi: t.matmul(
                    dst_ps[:, lo:hi], ones_m[0:1, 0:NS], bias_scr[0:1, lo:hi],
                    start=False, stop=True, skip_group_check=True),
                    waits=[(LD, ldb)] if sli == 0 else [], inc=TE)
            return pr.c(TE)

        qs_ps = psC[0:NS, 0:DQ]
        ks_ps = psC[0:NS, DQ:2 * DQ]
        te_qs = slot_proj(qs_ps, wqN, DQ, "bq", [(VE, ve_snfm), (GL, gl_ffn1)])
        E("vector", lambda v: v.tensor_copy(qsB[:, :], qs_ps[:, :]),
          waits=[(TE, te_qs)], inc=VE)
        ve_qs = pr.c(VE)
        te_ks = slot_proj(ks_ps, wkN, DQ, "bk", [(VE, ve_qs)])
        E("vector", lambda v: v.tensor_copy(ksB[:, :], ks_ps[:, :]),
          waits=[(TE, te_ks)], inc=VE)
        ve_ks = pr.c(VE)
        vs_ps = psA[0:NS, 1024:2048]
        te_vs = slot_proj(vs_ps, wvN, D, "bv",
                          [(GL, gl_wv), (VE, ve_gru[0]), (SE, se_gru[0])])
        E("vector", lambda v: v.tensor_copy(vsB[:, :], vs_ps[:, :]),
          waits=[(TE, te_vs)], inc=VE)
        ve_vs = pr.c(VE)

        def tr256(src, dst, ready_w):
            w = list(ready_w) + [(VE, tr_ev[0])]
            for i in range(2):
                E("tensor", lambda t, i=i, src=src: t.transpose(
                    ps_tr[:, i * 128: i * 128 + NS],
                    src[:, i * 128:(i + 1) * 128], id_sb[0:NS, 0:NS]),
                    waits=w if i == 0 else [], inc=TE)
            te_t = pr.c(TE)
            E("vector", lambda v, dst=dst: v.tensor_copy(
                dst[:, :, :],
                ps_tr[:, 0:256].rearrange("p (a b) -> p a b", a=2, b=128)[:, :, 0:NS]),
                waits=[(TE, te_t)], inc=VE)
            tr_ev[0] = pr.c(VE)
            return pr.c(VE)

        ve_qsfm = tr256(qsB, qsFM, [(VE, ve_qs)])
        ve_ksfm = tr256(ksB, ksFM, [(VE, ve_ks)])

        s2_ps = psC[0:NS, 0:NS]
        for ic in range(2):
            E("tensor", lambda t, ic=ic: t.matmul(
                s2_ps[:, :], qsFM[:, ic, :], ksFM[:, ic, :],
                start=(ic == 0), stop=(ic == 1)),
                waits=[(VE, ve_qsfm), (VE, ve_ksfm)] if ic == 0 else [], inc=TE)
        te_s2 = pr.c(TE)
        E("vector", lambda v: v.reduce_max(sst[:, 0:1], s2_ps[:, :], axis=AXX),
          waits=[(TE, te_s2)], inc=VE)
        E("vector", lambda v: v.tensor_scalar(sst[:, 1:2], sst[:, 0:1], -0.0625, None,
                                              op0=OP.mult), inc=VE, dr=True)
        ve_m2 = pr.c(VE)
        E("scalar", lambda s: s.activation(out=a2[:, :], in_=s2_ps[:, :], func=AF.Exp,
                                           scale=0.0625, bias=sst[:, 1:2]),
          waits=[(VE, ve_m2)], inc=SE)
        se_a2 = pr.c(SE)
        E("vector", lambda v: v.reduce_sum(sst[:, 2:3], a2[:, :], axis=AXX),
          waits=[(SE, se_a2)], inc=VE)
        E("vector", lambda v: v.reciprocal(sst[:, 3:4], sst[:, 2:3]), inc=VE, dr=True)
        E("vector", lambda v: v.tensor_copy(sst[:, 4:5], sst[:, 3:4]), inc=VE)
        E("vector", lambda v: v.tensor_scalar(a2[:, :], a2[:, :], sst[:, 3:4], None,
                                              op0=OP.mult), inc=VE)
        ve_a2 = pr.c(VE)
        E("tensor", lambda t: t.transpose(ps_tr[0:NS, 0:NS], a2[:, :], id_sb[0:NS, 0:NS]),
          waits=[(VE, ve_a2), (VE, tr_ev[0])], inc=TE)
        te_a2t = pr.c(TE)
        E("vector", lambda v: v.tensor_copy(a2T[:, :], ps_tr[0:NS, 0:NS]),
          waits=[(TE, te_a2t)], inc=VE)
        ve_a2t = pr.c(VE)
        tr_ev[0] = ve_a2t
        scm_ps = psB1[0:NS, 0:1024]
        for sln in range(2):
            E("tensor", lambda t, sln=sln: t.matmul(
                scm_ps[:, sln * 512:(sln + 1) * 512], a2T[:, :],
                vsB[:, sln * 512:(sln + 1) * 512], start=True, stop=True),
                waits=[(VE, ve_a2t), (VE, ve_vs)] if sln == 0 else [], inc=TE)
        te_scm = pr.c(TE)
        E("vector", lambda v: v.tensor_add(yyB[:, :], scm_ps[:, :], snewB[:, :]),
          waits=[(TE, te_scm)], inc=VE)
        ve_y = pr.c(VE)
        E("scalar", lambda s: s.activation(out=ysqd[:, :], in_=yyB[:, :],
                                           func=AF.Square, accum_out=sst[:, 0:1]),
          waits=[(VE, ve_y)], inc=SE)
        se_ysq = pr.c(SE)
        E("vector", lambda v: v.reduce_sum(sst[:, 1:2], yyB[:, :], axis=AXX), inc=VE, dr=True)
        E("vector", lambda v: v.tensor_scalar_mul(sst[:, 2:3], sst[:, 1:2], 1.0 / D), inc=VE, dr=True)
        E("vector", lambda v: v.tensor_mul(sst[:, 3:4], sst[:, 2:3], sst[:, 2:3]), inc=VE, dr=True)
        E("vector", lambda v: v.tensor_scalar(sst[:, 4:5], sst[:, 0:1], 1.0 / D, None,
                                              op0=OP.mult),
          waits=[(SE, se_ysq)], inc=VE)
        E("vector", lambda v: v.tensor_sub(sst[:, 5:6], sst[:, 4:5], sst[:, 3:4]), inc=VE, dr=True)
        ve_yv = pr.c(VE)
        E("scalar", lambda s: s.activation(out=sst[:, 6:7], in_=sst[:, 5:6], func=AF.Ln,
                                           bias=c_const[0:NS, 1:2]),
          waits=[(VE, ve_yv)], inc=SE)
        E("scalar", lambda s: s.activation(out=sst[:, 7:8], in_=sst[:, 6:7], func=AF.Exp,
                                           scale=-0.5), inc=SE)
        se_yr = pr.c(SE)
        E("vector", lambda v: v.tensor_scalar(ylnB[:, :], yyB[:, :], sst[:, 2:3],
                                              sst[:, 7:8], op0=OP.subtract, op1=OP.mult),
          waits=[(SE, se_yr)], inc=VE)
        ve_yln = pr.c(VE)
        ve_ylnfm = smajor_to_fm(ylnB, ylnFM, [(VE, ve_yln)])

        ve_wvpg0 = [0]
        for ic in range(8):
            E("vector", lambda v, ic=ic: v.tensor_scalar(
                wvpN[:, ic, :], wvpN[:, ic, :], gf[:, ic:ic + 1], None, op0=OP.mult),
                waits=[(GL, gl_wvp), (VE, ve_gf)] if ic == 0 else [], inc=VE)
        ve_wvpg = pr.c(VE)

        ld_bw = [0]
        te_bt2 = [0]
        for half in range(2):
            for ic in range(8):
                slb = ic % 2
                E("sync", lambda s, ic=ic, slb=slb: s.dma_start(
                    out=r_x[:, slb, :], in_=dr["wvp"][ic * 128:(ic + 1) * 128, :]),
                    waits=([(VE, ve_snb), (VE, ve_yln)] if (half == 0 and ic == 0) else
                           [(TE, te_bt2[0])]), inc=LD)
                ld_bw[0] = pr.c(LD)
                E("tensor", lambda t, ic=ic, half=half, slb=slb: t.matmul(
                    psC[0:1, 0:512], bfm[:, ic:ic + 1],
                    r_x[:, slb, half * 512:(half + 1) * 512],
                    start=(ic == 0), stop=(ic == 7)),
                    waits=[(LD, ld_bw[0])] + ([(VE, ve_yv)] if (half == 0 and ic == 0) else []),
                    inc=TE)
                te_bt2[0] = pr.c(TE)
            dstrow = hb_row if half == 0 else b1_row
            E("vector", lambda v, dstrow=dstrow: v.tensor_copy(
                dstrow[0:1, 0:512], psC[0:1, 0:512]),
                waits=[(TE, te_bt2[0])], inc=VE)
        ve_btr = pr.c(VE)

        sv_ps = psA[0:NS, 1024:2048]
        E("sync", lambda s: s.dma_start(out=bias_scr[0:1, :], in_=dr["bvp"]),
          waits=[(VE, ve_btr), (TE, pr.c(TE))], inc=LD)
        ld_bvp = pr.c(LD)
        for ic in range(8):
            for sln in range(2):
                E("tensor", lambda t, ic=ic, sln=sln: t.matmul(
                    sv_ps[:, sln * 512:(sln + 1) * 512], ylnFM[:, ic, :],
                    wvpN[:, ic, sln * 512:(sln + 1) * 512],
                    start=(ic == 0), stop=False, skip_group_check=True),
                    waits=[(VE, ve_ylnfm), (VE, ve_wvpg), (VE, ve_vs)] if (ic == 0 and sln == 0) else [],
                    inc=TE)
        for sln in range(2):
            E("tensor", lambda t, sln=sln: t.matmul(
                sv_ps[:, sln * 512:(sln + 1) * 512], ones_m[0:1, 0:NS],
                bias_scr[0:1, sln * 512:(sln + 1) * 512],
                start=False, stop=False, skip_group_check=True),
                waits=[(LD, ld_bvp)] if sln == 0 else [], inc=TE)
            btsrc = hb_row if sln == 0 else b1_row
            E("tensor", lambda t, sln=sln, btsrc=btsrc: t.matmul(
                sv_ps[:, sln * 512:(sln + 1) * 512], ones_m[0:1, 0:NS],
                btsrc[0:1, 0:512],
                start=False, stop=True, skip_group_check=True),
                waits=[(VE, ve_btr)] if sln == 0 else [], inc=TE)
        te_sv = pr.c(TE)
        E("vector", lambda v: v.tensor_copy(svB[:, :], sv_ps[:, :]),
          waits=[(TE, te_sv)], inc=VE)
        ve_sv = pr.c(VE)


        for tcg in range(4):
            w = [(VE, ve_alpha_all), (VE, tr_ev[0])]
            for i in range(4):
                tc = tcg * 4 + i
                E("tensor", lambda t, tc=tc, i=i: t.transpose(
                    ps_tr[0:NS, i * 128:(i + 1) * 128],
                    alphaT[:, tc, :], id_sb[:]),
                    waits=w if i == 0 else [], inc=TE)
            te_t = pr.c(TE)
            E("vector", lambda v, tcg=tcg: v.tensor_copy(
                alphaFM[:, tcg * 4:(tcg + 1) * 4, :],
                ps_tr[0:NS, :]), waits=[(TE, te_t)], inc=VE)
            tr_ev[0] = pr.c(VE)
        ve_alphafm = pr.c(VE)
        if debug:
            E("sync", lambda s: s.dma_start(out=dbg["dbg_sv"], in_=svB[:, :]),
              waits=[(VE, ve_sv)], inc=SO)

        afm = alphaFM[:, :, :].rearrange("p a b -> p (a b)")
        ve_of = {}
        for dc in range(8):
            for sln in range(4):
                ps = sc0 if dc % 2 == 0 else sc1
                w = []
                if dc == 0 and sln == 0:
                    w = [(VE, ve_sv), (VE, ve_alphafm)]
                if (dc, sln) >= (0, 2) and (dc * 4 + sln) >= 2:
                    prevkey = (dc * 4 + sln) - 2
                    w.append((VE, ve_of[prevkey]))
                E("tensor", lambda t, dc=dc, sln=sln, ps=ps: t.matmul(
                    ps[:, (sln % 2) * 512:(sln % 2 + 1) * 512],
                    svB[:, dc * 128:(dc + 1) * 128],
                    afm[0:NS, sln * 512:(sln + 1) * 512],
                    start=True, stop=True), waits=w, inc=TE)
                te_o = pr.c(TE)
                E("vector", lambda v, dc=dc, sln=sln, ps=ps: v.tensor_copy(
                    r_q[:, dc, sln * 512:(sln + 1) * 512],
                    ps[:, (sln % 2) * 512:(sln % 2 + 1) * 512]),
                    waits=[(TE, te_o)], inc=VE)
                ve_of[dc * 4 + sln] = pr.c(VE)
        ve_outfm = pr.c(VE)

        E("sync", lambda s: s.dma_start(out=bias_scr[0:1, :], in_=dr["bo"]),
          waits=[(LD, ld_bvp), (TE, te_sv)], inc=LD)
        ld_bo = pr.c(LD)
        ld_xf = [0] * TC
        ve_fin = [0] * TC
        so_out = [0] * TC
        E("sync", x_load(0, 0, x2s), waits=[(TE, te_sv)], inc=LD)
        ld_xf[0] = pr.c(LD)
        for tc in range(TC):
            sl = tc % 2
            if tc + 1 < TC:
                w = [(TE, te_sv)]
                if tc >= 1:
                    w += [(VE, ve_fin[tc - 1]), (SO, so_out[tc - 1])]
                E("sync", x_load(tc + 1, (tc + 1) % 2, x2s), waits=w, inc=LD)
                ld_xf[tc + 1] = pr.c(LD)
            ps = sc0 if tc % 2 == 0 else sc1
            for ic in range(8):
                for sln in range(2):
                    w = []
                    if ic == 0 and sln == 0:
                        w = [(VE, ve_outfm), (GL, gl_wo)]
                        if tc >= 2:
                            w.append((VE, ve_fin[tc - 2]))
                    E("tensor", lambda t, ps=ps, tc=tc, ic=ic, sln=sln: t.matmul(
                        ps[:, sln * 512:(sln + 1) * 512],
                        r_q[:, ic, tc * 128:(tc + 1) * 128],
                        woN[:, ic, sln * 512:(sln + 1) * 512],
                        start=(ic == 0), stop=False), waits=w, inc=TE)
            for sln in range(2):
                E("tensor", lambda t, ps=ps, sln=sln: t.matmul(
                    ps[:, sln * 512:(sln + 1) * 512], ones_m[0:1, :],
                    bias_scr[0:1, sln * 512:(sln + 1) * 512],
                    start=False, stop=True),
                    waits=[(LD, ld_bo)] if sln == 0 else [], inc=TE)
            te_fin = pr.c(TE)
            xb = r_x[:, sl, :]
            E("vector", lambda v, xb=xb, ps=ps: v.tensor_add(xb, ps[:, :], xb),
              waits=[(TE, te_fin), (LD, ld_xf[tc])], inc=VE)
            ve_fin[tc] = pr.c(VE)
            E("sync", lambda s, xb=xb, tc=tc: s.dma_start(
                out=out[tc * P:(tc + 1) * P, :], in_=xb),
                waits=[(VE, ve_fin[tc])], inc=SO)
            so_out[tc] = pr.c(SO)
        so_done = pr.c(SO)
        if _TAPS:
            tap_pairs = [("tap_hfm", hFM[:, :, :].rearrange("p a b -> p (a b)")),
                           ("tap_alpha", alphaT[:, :, :].rearrange("p a b -> p (a b)")),
                           ("tap_sv", svB[:, :]), ("tap_snew", snewB[:, :]),
                           ("tap_yln", ylnB[:, :]), ("tap_vs", vsB[:, :]),
                           ("tap_sifm", r_qv[:, 0:512]),
                           ("tap_bt", None), ("tap_wvp", rkv[:, 0:8192]),
                           ("tap_ylnfm", r_qv[:, 1024:1536]),
                           ("tap_outfm", r_q[:, :, :].rearrange("p a b -> p (a b)"))]
            for nm, ap in tap_pairs:
                if ap is None:
                    continue
                E("sync", lambda s, nm=nm, ap=ap: s.dma_start(out=dbg[nm], in_=ap),
                  waits=[(SO, so_done)], inc=SO)
            E("sync", lambda s: s.dma_start(out=dbg["tap_bt"][0:1, 0:512],
                                            in_=hb_row[0:1, :]),
              waits=[(SO, so_done)], inc=SO)
            E("sync", lambda s: s.dma_start(out=dbg["tap_bt"][0:1, 512:1024],
                                            in_=b1_row[0:1, :]), inc=SO)
            so_done = pr.c(SO)
        E("sync", lambda s: s.nop(), waits=[(SO, so_done)], inc=None)

        pr.finalize()
        block = stack.enter_context(nc.Block())

        @block.sync
        def _(eng):
            pr.run_engine("sync", eng)

        @block.gpsimd
        def _(eng):
            pr.run_engine("gpsimd", eng)

        @block.tensor
        def _(eng):
            pr.run_engine("tensor", eng)

        @block.scalar
        def _(eng):
            pr.run_engine("scalar", eng)

        @block.vector
        def _(eng):
            pr.run_engine("vector", eng)

    return nc


_NC_CACHE = {}


def kernel(**inputs):
    consts = _np_consts()
    if "main" not in _NC_CACHE:
        _NC_CACHE["main"] = build_nc(debug=_DEBUG)
    nc = _NC_CACHE["main"]

    x = np.ascontiguousarray(np.asarray(inputs["L0_attn_x"], dtype=np.float32))
    names = {
        "Wq": "L0_attn_Wq", "Wk": "L0_attn_Wk", "Wv": "L0_attn_Wv",
        "Wo": "L0_attn_Wo", "slot_init": "L0_ffn_slot_init", "w1": "L0_ffn_w1",
        "w2": "L0_ffn_w2", "Wih": "L0_ffn_Wih", "Whh": "L0_ffn_Whh",
        "wq": "L0_ffn_wq", "wk": "L0_ffn_wk", "wv": "L0_ffn_wv",
        "wvp": "L0_ffn_wvp", "wo": "L0_ffn_wo",
    }
    rows = {
        "b1": "L0_ffn_b1", "b2": "L0_ffn_b2", "bih": "L0_ffn_bih",
        "bhh": "L0_ffn_bhh", "bq": "L0_ffn_bq", "bk": "L0_ffn_bk",
        "bv": "L0_ffn_bv", "ln_g": "L0_ffn_ln_g", "ln_b": "L0_ffn_ln_b",
        "bvp": "L0_ffn_bvp", "bo": "L0_ffn_bo",
    }
    shared = {}
    for k, src in names.items():
        shared[k] = np.ascontiguousarray(np.asarray(inputs[src], dtype=np.float32))
    for k, src in rows.items():
        shared[k] = np.ascontiguousarray(
            np.asarray(inputs[src], dtype=np.float32).reshape(1, -1))
    shared["tau"] = np.ascontiguousarray(
        np.asarray(inputs["L0_ffn_tau"], dtype=np.float32).reshape(1, 1))
    shared.update(consts)

    in_maps = []
    for b in range(8):
        m = dict(shared)
        m["x"] = np.ascontiguousarray(x[b])
        in_maps.append(m)

    res = run_bass_kernel_spmd(nc, in_maps, core_ids=list(range(8)))
    return np.stack([res.results[b]["out"] for b in range(8)], axis=0).astype(np.float32)


# version 15
# speedup vs baseline: 1.1448x; 1.0169x over previous
import numpy as np
import ml_dtypes
from contextlib import ExitStack

import concourse.bass as bass
import concourse.mybir as mybir
from concourse.bass_utils import run_bass_kernel_spmd
from concourse.alu_op_type import AluOpType as OP

F32 = mybir.dt.float32
BF16 = mybir.dt.bfloat16
AF = mybir.ActivationFunctionType
AXX = mybir.AxisListType.X

P = 128
T = 2048
TC = 16
D = 1024
H = 16
NPAIR = 8
KC = 16
NS = 64
DQ = 256
DHF = 512
GOB = 24

_DEBUG = False
_X2OUT = False
_TAPS = False


def _np_consts():
    h = np.array([[1.0]])
    while h.shape[0] < 64:
        h = np.block([[h, h], [h, -h]])
    u = h[1:, :] / 8.0
    ut = np.zeros((64, 64), np.float32)
    ut[:, :63] = u.T
    ut2 = np.tile(ut, (2, 1))
    ones_pair = np.zeros((128, 2), np.float32)
    ones_pair[0:64, 0] = 1.0
    ones_pair[64:128, 1] = 1.0
    return {
        "c_ut2": ut2.astype(ml_dtypes.bfloat16),
        "c_id": np.eye(128, dtype=np.float32).astype(ml_dtypes.bfloat16),
        "c_ones_m": np.ones((1, 128), np.float32),
        "c_ones_pair": ones_pair.astype(ml_dtypes.bfloat16),
        "c_ones_n": np.ones((128, 1), np.float32).astype(ml_dtypes.bfloat16),
        "c_idf": np.eye(128, dtype=np.float32),
    }


class Prog:
    def __init__(self, nc, stack):
        self.nc = nc
        self.steps = []
        self.sems = {}
        self.cnt = {}
        for name in ("LD", "GL", "TE", "VE", "SE", "GP", "SO"):
            self.sems[name] = stack.enter_context(nc.semaphore(name))
            self.cnt[name] = 0

    def emit(self, engine, fn, waits=(), inc=None, n=None, dr=False):
        if n is None:
            n = 16 if inc in ("LD", "GL", "SO") else 1
        for s, v in waits:
            assert v <= self.cnt[s], (engine, s, v, self.cnt[s])
        if inc is not None:
            self.cnt[inc] += n
        self.steps.append((engine, fn, tuple(waits), inc, n, dr))
        return self.cnt[inc] if inc is not None else None

    def c(self, sem):
        return self.cnt[sem]

    def finalize(self):
        waited = {s: set() for s in self.sems}
        for _, _, waits, _, _, _ in self.steps:
            for s, v in waits:
                waited[s].add(v)
        cnt = {s: 0 for s in self.sems}
        newcnt = {s: 0 for s in self.sems}
        remap = {s: {0: 0} for s in self.sems}
        new_steps = []
        for e, fn, waits, inc, n, dr in self.steps:
            nw = tuple((s, remap[s][v]) for s, v in waits)
            if inc is not None:
                cnt[inc] += n
                if inc in ("GL", "LD", "SO") or cnt[inc] in waited[inc]:
                    newcnt[inc] += n
                    remap[inc][cnt[inc]] = newcnt[inc]
                    new_steps.append((e, fn, nw, inc, n, dr))
                else:
                    remap[inc][cnt[inc]] = newcnt[inc]
                    new_steps.append((e, fn, nw, None, n, dr))
            else:
                new_steps.append((e, fn, nw, None, n, dr))
        self.steps = new_steps

    def run_engine(self, name, eng):
        for e, fn, waits, inc, n, dr in self.steps:
            if e != name:
                continue
            for s, v in waits:
                eng.wait_ge(self.sems[s], v)
            if dr:
                eng.drain()
            instr = fn(eng)
            if inc is not None:
                instr.then_inc(self.sems[inc], n)


def build_nc(debug=False):
    nc = bass.Bass("TRN2", target_bir_lowering=False,
                   dynamic_dma_scratch_size=8192)

    dr = {}

    def din(name, shape, dt=F32):
        dr[name] = nc.dram_tensor(name, list(shape), dt, kind="ExternalInput").ap()

    din("x", (T, D))
    for nm, shp in [
        ("Wq", (D, D)), ("Wk", (D, D)), ("Wv", (D, D)), ("Wo", (D, D)),
        ("slot_init", (NS, D)), ("w1", (2 * D, DHF)), ("b1", (1, DHF)),
        ("w2", (DHF, NS)), ("b2", (1, NS)), ("tau", (1, 1)),
        ("Wih", (3 * D, D)), ("Whh", (3 * D, D)),
        ("bih", (1, 3 * D)), ("bhh", (1, 3 * D)),
        ("wq", (D, DQ)), ("bq", (1, DQ)), ("wk", (D, DQ)), ("bk", (1, DQ)),
        ("wv", (D, D)), ("bv", (1, D)), ("ln_g", (1, D)), ("ln_b", (1, D)),
        ("wvp", (D, D)), ("bvp", (1, D)), ("wo", (D, D)), ("bo", (1, D)),
    ]:
        din(nm, shp)
    din("c_ut2", (P, 64), BF16)
    din("c_id", (P, P), BF16)
    din("c_ones_m", (1, P))
    din("c_ones_pair", (P, 2), BF16)
    din("c_ones_n", (P, 1), BF16)
    din("c_idf", (P, P))

    out = nc.dram_tensor("out", [T, D], F32, kind="ExternalOutput").ap()
    x2s = nc.dram_tensor("x2s", [T, D], F32,
                         kind="ExternalOutput" if _X2OUT else "Internal").ap()
    wgT = nc.dram_tensor("wgT", [2, GOB, P, 8 * P], BF16, kind="Internal").ap()
    dbg = {}
    if _TAPS:
        for nm, shp, dt in [
            ("tap_hfm", (P, 4 * T), BF16), ("tap_alpha", (P, TC * NS), BF16),
            ("tap_sv", (NS, D), BF16), ("tap_snew", (NS, D), BF16),
            ("tap_yln", (NS, D), BF16), ("tap_vs", (NS, D), BF16),
            ("tap_sifm", (P, 8 * NS), BF16), ("tap_outfm", (P, 8 * T), BF16),
            ("tap_bt", (1, D), F32), ("tap_wvp", (P, 8 * D), BF16),
            ("tap_ylnfm", (P, 8 * NS), BF16),
        ]:
            dbg[nm] = nc.dram_tensor(nm, list(shp), dt, kind="ExternalOutput").ap()
    if debug:
        for nm, shp, dt in [
            ("dbg_ln1", (P, 4 * T), BF16), ("dbg_qfm", (P, 8 * T), BF16),
            ("dbg_kfm", (P, 8 * T), BF16), ("dbg_vhat", (P, KC * H * 65), BF16),
            ("dbg_attnfm", (P, 8 * T), BF16), ("dbg_x2", (T, D), F32),
            ("dbg_hfm", (P, 4 * T), BF16), ("dbg_alpha", (P, TC * NS), BF16),
            ("dbg_sv", (NS, D), BF16), ("dbg_ln2fm", (P, 4 * T), BF16),
            ("dbg_logits", (P, TC * NS), F32), ("dbg_e0", (P, D), BF16),
            ("dbg_ln1b", (P, 4 * T), BF16), ("dbg_lnb", (P, 4 * T), BF16),
            ("dbg_vps", (P, D), BF16),
            ("dbg_av0", (P, 8 * 128), BF16), ("dbg_pt0", (P, TC * P), BF16),
        ]:
            dbg[nm] = nc.dram_tensor(nm, list(shp), dt, kind="ExternalOutput").ap()

    with ExitStack() as stack:
        def sb(name, shape, dt):
            return stack.enter_context(nc.sbuf_tensor(name, shape, dt))

        r_lnfm_a = sb("r_lnfm_a", [P, 4, T], BF16)
        r_lnfm_b = sb("r_lnfm_b", [P, 4, T], BF16)
        r_q = sb("r_q", [P, NPAIR, T], BF16)
        r_k = sb("r_k", [P, NPAIR, T], BF16)
        r_v = sb("r_v", [P, KC, H, 65], BF16)
        r_e = sb("r_e", [P, 8, D], BF16)
        r_f1 = sb("r_f1", [P, 8, D], BF16)
        r_f2 = sb("r_f2", [P, 8, D], BF16)
        r_x = sb("r_x", [P, 2, D], F32)
        r_et = sb("r_et", [P, 2, D], BF16)
        r_pt = sb("r_pt", [P, TC, P], BF16)
        r_qv = sb("r_qv", [P, T], BF16)

        c_const = sb("c_const", [P, 2], F32)
        stats = sb("stats", [P, 16], F32)
        t8 = sb("t8", [P, 8], F32)
        negm1 = sb("negm1", [P, 1], F32)
        e8b = sb("e8b", [P, 8], F32)
        e64b = sb("e64b", [P, NS], F32)
        maskb = sb("maskb", [P, NS], F32)
        s8 = sb("s8", [P, 2], F32)
        logits = sb("logits", [P, NS], F32)
        recipC = sb("recipC", [P, 8], F32)
        inv_tau = sb("inv_tau", [P, 1], F32)
        tau_t = sb("tau_t", [1, 2], F32)
        id_sb = sb("id_sb", [P, P], BF16)
        ut2_sb = sb("ut2_sb", [P, 64], BF16)
        ones_m = sb("ones_m", [1, P], F32)
        ones_pr = sb("ones_pr", [P, 2], BF16)
        ones_n = sb("ones_n", [P, 1], BF16)
        idf_sb = sb("idf_sb", [P, P], F32)
        bias_g = sb("bias_g", [1, 2 * P], F32)
        b1_row = sb("b1_row", [1, DHF], F32)
        b2_row = sb("b2_row", [1, NS], F32)
        hb_row = sb("hb_row", [1, DHF], F32)
        hbiasT = sb("hbiasT", [P, 4], F32)
        gf = sb("gf", [P, 8], F32)
        bfm = sb("bfm", [P, 8], F32)
        smf = sb("smf", [P, 8], BF16)
        slotS = sb("slotS", [NS, D], BF16)
        slotFM = sb("slotFM", [P, 8, NS], BF16)
        qsFM = sb("qsFM", [P, 2, NS], BF16)
        ksFM = sb("ksFM", [P, 2, NS], BF16)
        a2 = sb("a2", [NS, NS], BF16)
        a2T = sb("a2T", [NS, NS], BF16)
        sst = sb("sst", [NS, 8], F32)
        denA = sb("denA", [NS, 2], F32)
        gsc = sb("gsc", [NS, 8, P], F32)
        dbgav_sb = sb("dbgav", [P, 1024], BF16) if debug else None
        _sfm3 = r_qv[:, 0:1536].rearrange("p (a b c) -> p a b c", a=3, b=8, c=NS)
        siFM = _sfm3[:, 0, :, :]
        snFM = _sfm3[:, 1, :, :]
        ylnFM = _sfm3[:, 2, :, :]

        wtv = r_pt[:, :, :].rearrange("p a b -> p (a b)").rearrange(
            "p (w i c) -> p w i c", w=2, i=8, c=P)
        alphaFM = r_et[:, :, :].rearrange("p a b -> p (a b)")[0:NS, :].rearrange(
            "p (a b) -> p a b", a=TC, b=P)
        etf = r_et[:, :, :].rearrange("p a b -> p (a b)")
        f1f2 = r_f1[:, :, :].rearrange("p a b -> p (a b)")
        rptf = r_pt[:, :, :].rearrange("p a b -> p (a b)")
        siB = etf[0:NS, 0:D]
        rrB = etf[0:NS, 0:D]
        zzB = etf[0:NS, D:2 * D]
        snewB = f1f2[0:NS, 0:D]
        yyB = f1f2[0:NS, D:2 * D]
        ylnB = f1f2[0:NS, 2 * D:3 * D]
        ysqd = f1f2[0:NS, 3 * D:4 * D]
        vsB = rptf[0:NS, 0:D]
        svB = rptf[0:NS, D:2 * D]
        rlnbf = r_lnfm_b[:, :, :].rearrange("p a b -> p (a b)")
        alphaT = rlnbf[:, 0:TC * NS].rearrange("p (a b) -> p a b", a=TC, b=NS)
        qsB = rlnbf[0:NS, 1024:1024 + DQ]
        ksB = rlnbf[0:NS, 1280:1280 + DQ]
        w2N = rlnbf[:, 1536:1792].rearrange("p (a b) -> p a b", a=4, b=NS)
        bias_scr = gsc[:, :, :].rearrange("p a b -> p (a b)")[0:1, 0:D]

        f1v = r_f1[:, :, :].rearrange("p a b -> p (a b)")
        w1a_a = f1v[:, 0:2048].rearrange("p (a b) -> p a b", a=4, b=DHF)
        w1a_b = f1v[:, 2048:4096].rearrange("p (a b) -> p a b", a=4, b=DHF)
        wqN = f1v[:, 4096:6144].rearrange("p (a b) -> p a b", a=8, b=DQ)
        wkN = f1v[:, 6144:8192].rearrange("p (a b) -> p a b", a=8, b=DQ)
        f2v = r_f2[:, :, :].rearrange("p a b -> p (a b)")
        wvN = f2v[:, 0:8192].rearrange("p (a b) -> p a b", a=8, b=D)
        rkv = r_k[:, :, :].rearrange("p a b -> p (a b)")
        wvpN = rkv[:, 0:8192].rearrange("p (a b) -> p a b", a=8, b=D)
        woN = rkv[:, 8192:16384].rearrange("p (a b) -> p a b", a=8, b=D)
        rev = r_e[:, :, :].rearrange("p a b -> p (a b)")
        wstage = rev[:, 0:4096].rearrange("p (a b) -> p a b", a=4, b=D)
        ystage = rev[:, 4096:8192].rearrange("p (a b) -> p a b", a=4, b=D)
        hFM = rev[:, 0:8192].rearrange("p (a b) -> p a b", a=4, b=T)
        ln2tok = r_v[:, :, :, :].rearrange("p a b c -> p (a b c)")[:, 0:TC * D]
        ln2tok = ln2tok.rearrange("p (a b) -> p a b", a=TC, b=D)
        sq = r_et[:, :, :].rearrange("p a b -> p (a b)")
        lnt = r_pt[:, :, :].rearrange("p a b -> p (a b)")

        def lnfm(ic):
            return (r_lnfm_a if ic < 4 else r_lnfm_b)[:, ic % 4, :]

        psA = stack.enter_context(nc.psum_tensor("psA", [P, 2048], F32))
        psB1 = stack.enter_context(nc.psum_tensor("psB1", [P, 1024], F32))
        ps_tr = stack.enter_context(nc.psum_tensor("ps_tr", [P, 512], BF16))
        psC = stack.enter_context(nc.psum_tensor("psC", [P, 512], F32))

        sc0 = psA[:, 0:1024]
        sc1 = psA[:, 1024:2048]
        avA = psB1[:, 0:512].rearrange("p (a b) -> p a b", a=4, b=128)
        avB = psB1[:, 512:1024].rearrange("p (a b) -> p a b", a=4, b=128)
        y0 = psA[0:NS, 1024:2048]
        y1 = psB1[0:NS, 0:1024]
        ps_trf = psC[0:P, 128:144]

        pr = Prog(nc, stack)
        E = pr.emit
        LD, GL, TE, VE, SE, GP, SO = "LD", "GL", "TE", "VE", "SE", "GP", "SO"

        for nm, dst in [("c_id", id_sb), ("c_ut2", ut2_sb), ("c_ones_m", ones_m),
                        ("c_ones_pair", ones_pr), ("c_ones_n", ones_n), ("c_idf", idf_sb),
                        ("b1", b1_row), ("b2", b2_row)]:
            E("sync", lambda s, nm=nm, dst=dst: s.dma_start(out=dst[:], in_=dr[nm]),
              inc=LD)
        E("sync", lambda s: s.dma_start(out=tau_t[0:1, 0:1], in_=dr["tau"]), inc=LD)
        ld_consts = pr.c(LD)
        E("gpsimd", lambda g: g.memset(c_const[:, 0:1], -20.0), inc=GP)
        E("gpsimd", lambda g: g.memset(c_const[:, 1:2], 1e-5), inc=GP)
        E("gpsimd", lambda g: g.memset(r_v[:], 1.0), inc=GP)
        gp_init = pr.c(GP)
        E("gpsimd", lambda g: g.dma_start(out=slotS[:], in_=dr["slot_init"]), inc=GL)
        gl_slot = pr.c(GL)

        E("scalar", lambda s: s.activation(out=tau_t[0:1, 1:2], in_=tau_t[0:1, 0:1],
                                           func=AF.Abs),
          waits=[(LD, ld_consts)], inc=SE)
        se_tau = pr.c(SE)
        E("vector", lambda v: v.tensor_scalar_add(tau_t[0:1, 1:2], tau_t[0:1, 1:2], 0.1),
          waits=[(SE, se_tau)], inc=VE)
        E("vector", lambda v: v.reciprocal(tau_t[0:1, 0:1], tau_t[0:1, 1:2]), inc=VE, dr=True)
        ve_tau = pr.c(VE)
        E("tensor", lambda t: t.matmul(psC[0:P, 0:1], ones_m[0:1, :], tau_t[0:1, 0:1],
                                       start=True, stop=True),
          waits=[(VE, ve_tau), (LD, ld_consts)], inc=TE)
        te_tau = pr.c(TE)
        E("vector", lambda v: v.tensor_copy(inv_tau[:], psC[0:P, 0:1]),
          waits=[(TE, te_tau)], inc=VE)
        ve_invtau = pr.c(VE)

        ld_x = [0] * TC
        ve_ln = [0] * TC
        te_lntr = [0]
        tr_ev = [0]
        ve_var_prev = [0]

        def x_load(tc, slot, src):
            return lambda s: s.dma_start(out=r_x[:, slot, :],
                                         in_=src[tc * P:(tc + 1) * P, :])

        E("sync", x_load(0, 0, dr["x"]), inc=LD)
        ld_x[0] = pr.c(LD)
        se_rstd_t = [0] * TC
        ve_var_t = [0] * TC
        te_lntr_t = [0] * TC

        def ln1_stats(tc):
            sl = tc % 2
            cs = 8 * (tc % 2)
            xb = r_x[:, sl, :]
            w = [(LD, ld_x[tc])]
            if tc >= 2:
                w.append((VE, ve_var_t[tc - 2]))
            E("scalar", lambda s, xb=xb, cs=cs: s.activation(
                out=r_et[:, 0, :], in_=xb, func=AF.Square,
                accum_out=stats[:, cs:cs + 1]), waits=w, inc=SE)
            se_sq = pr.c(SE)
            E("vector", lambda v, xb=xb, cs=cs: v.reduce_sum(
                stats[:, cs + 1:cs + 2], xb, axis=AXX),
                waits=[(LD, ld_x[tc])], inc=VE)
            E("vector", lambda v, cs=cs: v.tensor_scalar_mul(
                stats[:, cs + 2:cs + 3], stats[:, cs + 1:cs + 2], 1.0 / D), inc=VE, dr=True)
            E("vector", lambda v, cs=cs: v.tensor_mul(
                stats[:, cs + 3:cs + 4], stats[:, cs + 2:cs + 3],
                stats[:, cs + 2:cs + 3]), inc=VE, dr=True)
            E("vector", lambda v, cs=cs: v.tensor_scalar(
                stats[:, cs + 4:cs + 5], stats[:, cs:cs + 1], 1.0 / D,
                None, op0=OP.mult),
                waits=[(SE, se_sq)], inc=VE)
            E("vector", lambda v, cs=cs: v.tensor_sub(
                stats[:, cs + 5:cs + 6], stats[:, cs + 4:cs + 5],
                stats[:, cs + 3:cs + 4]), inc=VE, dr=True)
            ve_var_t[tc] = pr.c(VE)
            ve_var_prev[0] = ve_var_t[tc]
            w = [(VE, ve_var_t[tc])]
            if tc == 0:
                w.append((GP, gp_init))
            E("scalar", lambda s, cs=cs: s.activation(
                out=stats[:, cs + 6:cs + 7], in_=stats[:, cs + 5:cs + 6],
                func=AF.Ln, bias=c_const[:, 1:2]), waits=w, inc=SE)
            E("scalar", lambda s, cs=cs: s.activation(
                out=stats[:, cs + 7:cs + 8], in_=stats[:, cs + 6:cs + 7],
                func=AF.Exp, scale=-0.5), inc=SE)
            se_rstd_t[tc] = pr.c(SE)

        def ln1_norm(tc):
            sl = tc % 2
            cs = 8 * (tc % 2)
            xb = r_x[:, sl, :]
            lslot = lnt[:, (tc % 2) * D:(tc % 2) * D + D]
            w = [(SE, se_rstd_t[tc])]
            if tc >= 2:
                w.append((TE, te_lntr_t[tc - 2]))
            E("vector", lambda v, xb=xb, cs=cs, lslot=lslot: v.tensor_scalar(
                lslot, xb, stats[:, cs + 2:cs + 3], stats[:, cs + 7:cs + 8],
                op0=OP.subtract, op1=OP.mult), waits=w, inc=VE)
            ve_ln[tc] = pr.c(VE)
            for g in range(2):
                w = [(VE, ve_ln[tc]), (VE, tr_ev[0])]
                for i in range(4):
                    ic = 4 * g + i
                    E("tensor", lambda t, ic=ic, i=i, lslot=lslot: t.transpose(
                        ps_tr[:, i * 128:(i + 1) * 128],
                        lslot[:, ic * 128:(ic + 1) * 128], id_sb[:]),
                        waits=w if i == 0 else [], inc=TE)
                te_lntr[0] = pr.c(TE)
                tgt = r_lnfm_a if g == 0 else r_lnfm_b
                E("vector", lambda v, tgt=tgt, tc=tc: v.tensor_copy(
                    tgt[:, 0:4, tc * P:(tc + 1) * P],
                    ps_tr[:, :]), waits=[(TE, te_lntr[0])], inc=VE)
                tr_ev[0] = pr.c(VE)
            te_lntr_t[tc] = te_lntr[0]

        for tc in range(TC):
            if tc >= 1:
                ln1_norm(tc - 1)
            if tc + 1 < TC:
                w = [(VE, ve_ln[tc - 1])] if tc >= 1 else []
                E("sync", x_load(tc + 1, (tc + 1) % 2, dr["x"]), waits=w, inc=LD)
                ld_x[tc + 1] = pr.c(LD)
            ln1_stats(tc)
        ln1_norm(TC - 1)
        ve_ln1_done = pr.c(VE)

        gl_stage = [0]
        te_stage_rd = [0]
        se_yev = [0]

        def load_stage(name, half):
            return lambda g: g.dma_start(
                out=wstage[:, :, :],
                in_=dr[name][half * 512:(half + 1) * 512, :].rearrange(
                    "(a p) b -> p a b", p=P))

        def stage_to(dst, src, half, extra_w=()):
            for c in range(4):
                for icg in range(2):
                    w = [(VE, tr_ev[0])] + list(extra_w)
                    for i in range(4):
                        ic = icg * 4 + i
                        E("tensor", lambda t, c=c, ic=ic, i=i, src=src: t.transpose(
                            ps_tr[:, i * 128:(i + 1) * 128],
                            src[:, c, ic * 128:(ic + 1) * 128], id_sb[:]),
                            waits=w if i == 0 else [], inc=TE)
                    te_tr = pr.c(TE)
                    te_stage_rd[0] = te_tr
                    E("vector", lambda v, c=c, icg=icg, half=half, dst=dst: v.tensor_copy(
                        dst[:, icg * 4:(icg + 1) * 4,
                            half * 512 + c * 128: half * 512 + (c + 1) * 128],
                        ps_tr[:, :]), waits=[(TE, te_tr)], inc=VE)
                    tr_ev[0] = pr.c(VE)

        def prep_centered(name, dst):
            for half in range(2):
                E("gpsimd", load_stage(name, half),
                  waits=[(SE, se_yev[0]), (TE, te_stage_rd[0])], inc=GL)
                gl_stage[0] = pr.c(GL)
                yev_h = {}
                for hh in range(8):
                    base = 64 * (hh % 2)
                    chunk = hh // 2
                    yps = y0 if (hh % 2 == 0) else y1
                    for sln in range(2):
                        w = []
                        if sln == 0:
                            w = [(GL, gl_stage[0])]
                            w.append((SE, yev_h[hh - 2] if hh >= 2 else se_yev[0]))
                        E("tensor", lambda t, base=base, chunk=chunk, yps=yps, sx=sln: t.matmul(
                            yps[:, sx * 512:(sx + 1) * 512],
                            ut2_sb[base:base + 64, :],
                            wstage[base:base + 64, chunk, sx * 512:(sx + 1) * 512],
                            start=True, stop=True),
                            waits=w, inc=TE)
                    te_f = pr.c(TE)
                    te_stage_rd[0] = te_f
                    E("scalar", lambda s, base=base, chunk=chunk, yps=yps: s.copy(
                        out=ystage[base:base + 64, chunk, :], in_=yps[:, :]),
                        waits=[(TE, te_f)], inc=SE)
                    se_yev[0] = pr.c(SE)
                    yev_h[hh] = se_yev[0]
                stage_to(dst, ystage, half, extra_w=[(SE, se_yev[0])])

        def prep_plain(name, dst, extra_w=()):
            for half in range(2):
                E("gpsimd", load_stage(name, half),
                  waits=[(TE, te_stage_rd[0])] + list(extra_w), inc=GL)
                gl_stage[0] = pr.c(GL)
                stage_to(dst, wstage, half, extra_w=[(GL, gl_stage[0])])

        prep_centered("Wq", r_f1)
        te_wq_prep, ve_wq_prep = pr.c(TE), pr.c(VE)
        prep_centered("Wk", r_f2)
        te_wk_prep, ve_wk_prep = pr.c(TE), pr.c(VE)

        for icg in range(2):
            w = [(GL, gl_slot), (VE, tr_ev[0])]
            for i in range(4):
                ic = icg * 4 + i
                E("tensor", lambda t, ic=ic, i=i: t.transpose(
                    ps_tr[:, i * 128: i * 128 + NS],
                    slotS[:, ic * 128:(ic + 1) * 128], id_sb[0:NS, 0:NS]),
                    waits=w if i == 0 else [], inc=TE)
            te_str = pr.c(TE)
            E("vector", lambda v, icg=icg: v.tensor_copy(
                slotFM[:, icg * 4:(icg + 1) * 4, :],
                ps_tr[:, :].rearrange("p (a b) -> p a b", a=4, b=128)[:, :, 0:NS]),
                waits=[(TE, te_str)], inc=VE)
            tr_ev[0] = pr.c(VE)
        ve_slotfm = pr.c(VE)
        for ic in range(8):
            E("vector", lambda v, ic=ic: v.reduce_sum(stats[:, 0:1], slotFM[:, ic, :], axis=AXX),
              waits=[(VE, ve_slotfm), (VE, ve_ln1_done)] if ic == 0 else [], inc=VE)
            E("vector", lambda v, ic=ic: v.tensor_scalar(smf[:, ic:ic + 1], stats[:, 0:1],
                                                         1.0 / NS, None, op0=OP.mult), inc=VE, dr=True)
        ve_smf = pr.c(VE)
        E("gpsimd", lambda g: g.dma_start(
            out=ystage[:, :, 0:DHF],
            in_=dr["w1"][D:2 * D, :].rearrange("(a p) b -> p a b", p=P)[:, 0:4, :]),
          waits=[(TE, te_stage_rd[0]), (SE, se_yev[0]), (VE, tr_ev[0])], inc=GL)
        gl_w1b_a = pr.c(GL)
        E("gpsimd", lambda g: g.dma_start(
            out=wstage[:, :, 0:DHF],
            in_=dr["w1"][D:2 * D, :].rearrange("(a p) b -> p a b", p=P)[:, 4:8, :]),
          inc=GL)
        gl_w1b_b = pr.c(GL)
        for ic in range(8):
            src = ystage if ic < 4 else wstage
            E("tensor", lambda t, src=src, ic=ic: t.matmul(
                psC[0:1, 0:DHF], smf[:, ic:ic + 1], src[:, ic % 4, 0:DHF],
                start=(ic == 0), stop=(ic == 7)),
                waits=[(VE, ve_smf), (GL, gl_w1b_b if ic >= 4 else gl_w1b_a)] if ic in (0, 4) else [],
                inc=TE)
        te_c = pr.c(TE)
        te_stage_rd[0] = te_c
        E("vector", lambda v: v.tensor_add(hb_row[:], psC[0:1, 0:DHF], b1_row[:]),
          waits=[(TE, te_c), (LD, ld_consts)], inc=VE)
        ve_hb = pr.c(VE)
        for i in range(4):
            E("tensor", lambda t, i=i: t.transpose(
                ps_trf[:, i:i + 1], hb_row[0:1, i * 128:(i + 1) * 128],
                ones_m[0:1, 0:1]),
                waits=[(VE, ve_hb)] if i == 0 else [], inc=TE)
        te_hbt = pr.c(TE)
        E("vector", lambda v: v.tensor_copy(hbiasT[:, :], ps_trf[:, 0:4]),
          waits=[(TE, te_hbt)], inc=VE)
        ve_hbt = pr.c(VE)

        E("sync", lambda s: s.dma_start(out=bias_scr[0:1, :], in_=dr["ln_g"]),
          waits=[(LD, ld_consts)], inc=LD)
        ld_g = pr.c(LD)
        for i in range(8):
            E("tensor", lambda t, i=i: t.transpose(
                ps_trf[:, 8 + i:9 + i], bias_scr[0:1, i * 128:(i + 1) * 128],
                ones_m[0:1, 0:1]),
                waits=[(LD, ld_g), (VE, ve_hbt)] if i == 0 else [], inc=TE)
        te_gt = pr.c(TE)
        E("vector", lambda v: v.tensor_copy(gf[:, :], ps_trf[:, 8:16]),
          waits=[(TE, te_gt)], inc=VE)
        ve_gf = pr.c(VE)
        E("sync", lambda s: s.dma_start(out=bias_scr[0:1, :], in_=dr["ln_b"]),
          waits=[(VE, ve_gf)], inc=LD)
        ld_b = pr.c(LD)
        for i in range(8):
            E("tensor", lambda t, i=i: t.transpose(
                ps_trf[:, 8 + i:9 + i], bias_scr[0:1, i * 128:(i + 1) * 128],
                ones_m[0:1, 0:1]),
                waits=[(LD, ld_b), (VE, ve_gf)] if i == 0 else [], inc=TE)
        te_bt = pr.c(TE)
        E("vector", lambda v: v.tensor_copy(bfm[:, :], ps_trf[:, 8:16]),
          waits=[(TE, te_bt)], inc=VE)
        ve_bfm = pr.c(VE)

        te_proj = [0]
        ve_pev = [0]
        so_scat = [0]
        se_qv = [0]
        te_qv_last = [0]

        def qk_projection(wsrc, qdst, first_waits):
            for oc in range(NPAIR):
                for halft in range(2):
                    ps = sc0 if halft == 0 else sc1
                    for ic in range(8):
                        for sln in range(2):
                            w = []
                            if ic == 0 and sln == 0:
                                w = list(first_waits) + [(VE, ve_pev[0])]
                            E("tensor", lambda t, ps=ps, oc=oc, ic=ic, sln=sln, halft=halft,
                              wsrc=wsrc: t.matmul(
                                ps[:, sln * 512:(sln + 1) * 512],
                                wsrc[:, ic, oc * 128:(oc + 1) * 128],
                                lnfm(ic)[:, halft * 1024 + sln * 512: halft * 1024 + (sln + 1) * 512],
                                start=(ic == 0), stop=(ic == 7)), waits=w, inc=TE)
                    te_proj[0] = pr.c(TE)
                    E("vector", lambda v, ps=ps, oc=oc, halft=halft, qdst=qdst: v.tensor_copy(
                        qdst[:, oc, halft * 1024:(halft + 1) * 1024], ps[:, :]),
                        waits=[(TE, te_proj[0])], inc=VE)
                    ve_pev[0] = pr.c(VE)
                E("vector", lambda v, oc=oc, qdst=qdst: v.tensor_mul(
                    sq[:, :], qdst[:, oc, :], qdst[:, oc, :]),
                    waits=[(SO, so_scat[0])], inc=VE, dr=True)
                ve_sq = pr.c(VE)
                for q4 in range(4):
                    qvp = psC[0:2, 0:512] if q4 % 2 == 0 else psB1[0:2, 0:512]
                    E("tensor", lambda t, q4=q4, qvp=qvp: t.matmul(
                        qvp[:, :], ones_pr[:, :],
                        sq[:, q4 * 512:(q4 + 1) * 512],
                        start=True, stop=True),
                        waits=[(VE, ve_sq), (SE, se_qv[0])], inc=TE)
                    te_qv = pr.c(TE)
                    te_qv_last[0] = te_qv
                    E("scalar", lambda s, q4=q4, qvp=qvp: s.activation(
                        out=r_qv[0:2, q4 * 512:(q4 + 1) * 512], in_=qvp[:, :],
                        func=AF.Copy, scale=1.0 / 63.0),
                        waits=[(TE, te_qv)], inc=SE)
                    se_qv[0] = pr.c(SE)
                E("sync", lambda s, oc=oc, qdst=qdst: s.dma_start(
                    out=qdst[63:64, oc, :], in_=r_qv[0:1, :]),
                    waits=[(SE, se_qv[0]), (VE, ve_pev[0])], inc=SO)
                E("sync", lambda s, oc=oc, qdst=qdst: s.dma_start(
                    out=qdst[127:128, oc, :], in_=r_qv[1:2, :]), inc=SO)
                so_scat[0] = pr.c(SO)

        qk_projection(r_f1, r_q,
                      [(TE, te_wq_prep), (VE, ve_wq_prep), (VE, ve_ln1_done),
                       (VE, ve_hbt), (VE, ve_bfm), (VE, ve_hb)])
        so_q_scat = so_scat[0]
        qk_projection(r_f2, r_k, [(TE, te_wk_prep), (VE, ve_wk_prep)])
        so_k_scat = so_scat[0]

        prep_plain("Wv", r_f1, extra_w=[(VE, ve_pev[0]), (SE, se_qv[0])])
        te_wv_prep, ve_wv_prep = pr.c(TE), pr.c(VE)
        if debug:
            E("sync", lambda s: s.dma_start(
                out=dbg["dbg_ln1b"],
                in_=r_lnfm_a[:, :, :].rearrange("p a b -> p (a b)")),
                waits=[(TE, te_wv_prep), (VE, ve_wv_prep)], inc=SO)
            E("sync", lambda s: s.dma_start(
                out=dbg["dbg_lnb"],
                in_=r_lnfm_b[:, :, :].rearrange("p a b -> p (a b)")), inc=SO)
            so_lnb = pr.c(SO)
        for tc in range(TC):
            ps = sc0 if tc % 2 == 0 else sc1
            for ic in range(8):
                for sln in range(2):
                    w = []
                    if ic == 0 and sln == 0:
                        w = [(TE, te_wv_prep), (VE, ve_wv_prep), (VE, ve_pev[0]),
                             (GP, gp_init)]
                        if debug and tc == 0:
                            w.append((SO, so_lnb))
                    E("tensor", lambda t, ps=ps, tc=tc, ic=ic, sln=sln: t.matmul(
                        ps[:, sln * 512:(sln + 1) * 512],
                        lnfm(ic)[:, tc * 128:(tc + 1) * 128],
                        r_f1[:, ic, sln * 512:(sln + 1) * 512],
                        start=(ic == 0), stop=(ic == 7)), waits=w, inc=TE)
            te_proj[0] = pr.c(TE)
            if debug and tc == 0:
                E("scalar", lambda s: s.activation(
                    out=dbgav_sb[:, :], in_=sc0[:, :], func=AF.Copy),
                    waits=[(TE, te_proj[0])], inc=SE)
                se_dv = pr.c(SE)
                E("sync", lambda s: s.dma_start(out=dbg["dbg_vps"], in_=dbgav_sb[:, :]),
                  waits=[(SE, se_dv)], inc=SO)
                so_dv = pr.c(SO)
                E("vector", lambda v, tc=tc, ps=ps: v.tensor_copy(
                    r_v[:, tc, :, 0:64], ps[:, :]),
                    waits=[(TE, te_proj[0]), (SE, se_dv)], inc=VE)
            else:
                E("vector", lambda v, tc=tc, ps=ps: v.tensor_copy(
                    r_v[:, tc, :, 0:64], ps[:, :]),
                    waits=[(TE, te_proj[0])], inc=VE)
            ve_pev[0] = pr.c(VE)
        ve_vhat = pr.c(VE)

        prep_plain("Wo", r_f2, extra_w=[(VE, ve_pev[0])])
        te_wo_prep, ve_wo_prep = pr.c(TE), pr.c(VE)

        if debug:
            E("sync", lambda s: s.dma_start(
                out=dbg["dbg_ln1"],
                in_=r_lnfm_a[:, :, :].rearrange("p a b -> p (a b)")),
                waits=[(VE, ve_ln1_done)], inc=SO)
            E("sync", lambda s: s.dma_start(
                out=dbg["dbg_qfm"], in_=r_q[:, :, :].rearrange("p a b -> p (a b)")),
                waits=[(SO, so_q_scat)], inc=SO)
            E("sync", lambda s: s.dma_start(
                out=dbg["dbg_kfm"], in_=r_k[:, :, :].rearrange("p a b -> p (a b)")),
                waits=[(SO, so_k_scat)], inc=SO)
            E("sync", lambda s: s.dma_start(
                out=dbg["dbg_vhat"],
                in_=r_v[:, :, :, :].rearrange("p a b c -> p (a b c)")),
                waits=[(VE, ve_vhat)], inc=SO)

        E("gpsimd", lambda g: g.nop(), waits=[(VE, ve_vhat)], inc=GP)
        gp_f1_free = pr.c(GP)
        for fn in [
            lambda g: g.dma_start(out=w1a_a[:, :, :],
                                  in_=dr["w1"][0:D, :].rearrange("(a p) b -> p a b", p=P)[:, 0:4, :]),
            lambda g: g.dma_start(out=w1a_b[:, :, :],
                                  in_=dr["w1"][0:D, :].rearrange("(a p) b -> p a b", p=P)[:, 4:8, :]),
            lambda g: g.dma_start(out=wqN[:, :, :],
                                  in_=dr["wq"].rearrange("(a p) b -> p a b", p=P)),
            lambda g: g.dma_start(out=wkN[:, :, :],
                                  in_=dr["wk"].rearrange("(a p) b -> p a b", p=P)),
        ]:
            E("gpsimd", fn, waits=[(GP, gp_f1_free)], inc=GL)
        gl_ffn1 = pr.c(GL)

        ebuf = r_et
        se_exp = {}
        te_av = {}
        ve_avev = [0]
        def emit_qk(rnd, pair, qq, kc):
            par = kc % 2
            scA = psA[:, par * 1024: par * 1024 + 512]
            scB = psA[:, par * 1024 + 512: (par + 1) * 1024]
            w = []
            if kc >= 2:
                w.append((SE, se_exp[(rnd, kc - 2)]))
            elif rnd >= 1:
                w.append((SE, se_exp[(rnd - 1, KC - 2 + kc)]))
            else:
                w += [(SO, so_q_scat), (SO, so_k_scat), (VE, ve_vhat)]
                if debug:
                    w.append((SO, pr.c(SO)))
            for hh in range(2):
                base = 64 * hh
                scp = scA if hh == 0 else scB
                E("tensor", lambda t, scp=scp, base=base, pair=pair, kc=kc,
                  qq=qq: t.matmul(
                    scp[:, :],
                    r_k[base:base + 64, pair, kc * 128:(kc + 1) * 128],
                    r_q[base:base + 64, pair, qq * 512:(qq + 1) * 512],
                    start=True, stop=True, tile_position=(base, 0)),
                    waits=w if hh == 0 else [], inc=TE)
            te_qkv = pr.c(TE)
            w = [(TE, te_qkv)]
            if kc >= 2:
                w.append((TE, te_av[(rnd, kc - 2, 1)]))
            elif rnd >= 1:
                w.append((TE, te_av[(rnd - 1, KC - 2 + kc, 1)]))
            E("scalar", lambda s, kc=kc: s.activation(
                out=ebuf[:, kc % 2, :],
                in_=psA[:, (kc % 2) * 1024:(kc % 2 + 1) * 1024], func=AF.Exp,
                scale=1.0 / 64.0, bias=c_const[:, 0:1]),
                waits=w, inc=SE)
            se_exp[(rnd, kc)] = pr.c(SE)

        def emit_av(rnd, pair, kc):
            for hh in range(2):
                avp = avA if hh == 0 else avB
                hglob = 2 * pair + hh
                w = [(SE, se_exp[(rnd, kc)])]
                if kc == 0:
                    w.append((VE, ve_avev[0]))
                for qc in range(4):
                    st = (kc == 0 and qc == 0)
                    E("tensor", lambda t, kc=kc, qc=qc, hglob=hglob, st=st,
                      avp=avp, hh=hh: t.matmul(
                        avp[:, qc, 0:65],
                        ebuf[:, kc % 2, hh * 512 + qc * 128:
                             hh * 512 + (qc + 1) * 128],
                        r_v[:, kc, hglob, :],
                        start=st, stop=(kc == KC - 1),
                        skip_group_check=True),
                        waits=w if (qc == 0 and hh == 0) else [], inc=TE)
                te_av[(rnd, kc, hh)] = pr.c(TE)

        gwt_so = [0]
        gwt_ld = {}
        gwt_cast = {}
        gwt_trg = {}
        gwt_evac = {}
        gwt_out = {}
        gstage = r_lnfm_a[:, :, :].rearrange("p a b -> p (a b)")

        def q_load(step):
            which, ob = step % 2, step // 2
            nm = "Wih" if which == 0 else "Whh"
            slot = step % 2
            w = [(VE, ve_vhat)] if step < 2 else [(VE, gwt_cast[step - 2])]
            E("sync", lambda s, nm=nm, ob=ob, slot=slot: s.dma_start(
                out=r_x[:, slot, :],
                in_=dr[nm][ob * 128:(ob + 1) * 128, :]),
                waits=w, inc=LD)
            gwt_ld[step] = pr.c(LD)

        def q_cast(step):
            slot = step % 2
            w = [(LD, gwt_ld[step])]
            if step >= 2:
                w.append((TE, gwt_trg[(step - 2, 1)]))
            E("vector", lambda v, slot=slot: v.tensor_copy(
                gstage[:, slot * 1024:(slot + 1) * 1024], r_x[:, slot, :]),
                waits=w, inc=VE)
            gwt_cast[step] = pr.c(VE)

        def q_trg(step, icg):
            slot = step % 2
            yslot = step % 4
            w = [(VE, gwt_cast[step]), (VE, tr_ev[0])]
            for i in range(4):
                ic = icg * 4 + i
                E("tensor", lambda t, ic=ic, i=i, slot=slot: t.transpose(
                    ps_tr[:, i * 128:(i + 1) * 128],
                    gstage[:, slot * 1024 + ic * 128:slot * 1024 + (ic + 1) * 128],
                    id_sb[:]),
                    waits=w if i == 0 else [], inc=TE)
            te_t = pr.c(TE)
            gwt_trg[(step, icg)] = te_t
            we = [(TE, te_t)]
            if icg == 0 and step >= 4:
                we.append((SO, gwt_out[step - 4]))
            E("vector", lambda v, icg=icg, yslot=yslot: v.tensor_copy(
                ystage[:, yslot, icg * 512:(icg + 1) * 512], ps_tr[:, :]),
                waits=we, inc=VE)
            gwt_evac[(step, icg)] = pr.c(VE)
            tr_ev[0] = pr.c(VE)

        def q_out(step):
            which, ob = step % 2, step // 2
            yslot = step % 4
            E("sync", lambda s, which=which, ob=ob, yslot=yslot: s.dma_start(
                out=wgT[which, ob, :, :], in_=ystage[:, yslot, :]),
                waits=[(VE, gwt_evac[(step, 1)])], inc=SO)
            gwt_out[step] = pr.c(SO)
            gwt_so[0] = gwt_out[step]

        NSTEP_GRU = 2 * GOB
        gru_quanta = []
        gru_quanta.append(lambda: q_load(0))
        gru_quanta.append(lambda: q_load(1))
        for s_ in range(NSTEP_GRU):
            gru_quanta.append(lambda s_=s_: q_cast(s_))
            gru_quanta.append(lambda s_=s_: q_trg(s_, 0))
            gru_quanta.append(lambda s_=s_: q_trg(s_, 1))
            gru_quanta.append(lambda s_=s_: q_out(s_))
            if s_ + 2 < NSTEP_GRU:
                gru_quanta.append(lambda s_=s_: q_load(s_ + 2))
        gru_quanta_it = iter(gru_quanta)

        se_exp_h = {}
        pending_ptr = [None]
        te_ptr_last = [0]

        def emit_pair_tr():
            if pending_ptr[0] is None:
                return
            ppair, ve_ready = pending_ptr[0]
            pending_ptr[0] = None
            for g4 in range(4):
                w = [(VE, ve_ready), (VE, tr_ev[0])]
                for i in range(4):
                    tcl = g4 * 4 + i
                    E("tensor", lambda t, tcl=tcl, i=i: t.transpose(
                        ps_tr[:, i * 128:(i + 1) * 128],
                        r_pt[:, tcl, :], id_sb[:]),
                        waits=w if i == 0 else [], inc=TE)
                te_ptr = pr.c(TE)
                te_ptr_last[0] = te_ptr
                E("vector", lambda v, g4=g4, ppair=ppair: v.tensor_copy(
                    r_q[:, ppair, g4 * 512:(g4 + 1) * 512], ps_tr[:, :]),
                    waits=[(TE, te_ptr)], inc=VE)
                tr_ev[0] = pr.c(VE)

        for pair in range(NPAIR):
            for qq in range(4):
                rnd = pair * 4 + qq
                emit_qk(rnd, pair, qq, 0)
                emit_qk(rnd, pair, qq, 1)
                if qq == 0:
                    emit_pair_tr()
                for kc in range(KC):
                    emit_av(rnd, pair, kc)
                    if kc + 2 < KC:
                        emit_qk(rnd, pair, qq, kc + 2)
                    if rnd >= 2 and kc % 2 == 0:
                        qk_q = next(gru_quanta_it, None)
                        if qk_q is not None:
                            qk_q()
                te_av_done = pr.c(TE)
                E("vector", lambda v: v.reciprocal(recipC[:, 0:4], avA[:, :, 64]),
                  waits=[(TE, te_av_done), (TE, te_ptr_last[0])], inc=VE)
                E("vector", lambda v: v.reciprocal(recipC[:, 4:8], avB[:, :, 64]),
                  inc=VE)
                E("vector", lambda v: v.tensor_copy(stats[:, 0:1], recipC[:, 0:1]), inc=VE)
                for hh in range(2):
                    avp = avA if hh == 0 else avB
                    for qc in range(4):
                        E("vector", lambda v, qc=qc, qq=qq, hh=hh, avp=avp: v.tensor_scalar(
                            r_pt[:, qq * 4 + qc, hh * 64:hh * 64 + 64],
                            avp[:, qc, 0:64], recipC[:, hh * 4 + qc:hh * 4 + qc + 1],
                            None, op0=OP.mult),
                            inc=VE)
                ve_avev[0] = pr.c(VE)
            pending_ptr[0] = (pair, ve_avev[0])
        for qk_q in gru_quanta_it:
            qk_q()
        emit_pair_tr()
        ve_attnfm = pr.c(VE)
        te_headloop = pr.c(TE)

        if debug:
            E("sync", lambda s: s.dma_start(
                out=dbg["dbg_attnfm"], in_=r_q[:, :, :].rearrange("p a b -> p (a b)")),
                waits=[(VE, ve_attnfm)], inc=SO)

        E("gpsimd", lambda g: g.nop(), waits=[(TE, te_headloop)], inc=GP)
        gp_rk_free = pr.c(GP)
        E("gpsimd", lambda g: g.dma_start(
            out=wvpN[:, :, :], in_=dr["wvp"].rearrange("(a p) b -> p a b", p=P)),
          waits=[(GP, gp_rk_free)], inc=GL)
        gl_wvp = pr.c(GL)
        E("gpsimd", lambda g: g.dma_start(
            out=woN[:, :, :], in_=dr["wo"].rearrange("(a p) b -> p a b", p=P)),
          inc=GL)
        gl_wo = pr.c(GL)

        ld_x2 = [0] * TC
        ve_x2 = [0] * TC
        ve_norm2 = [0] * TC
        so_x2 = [0] * TC
        se_sq2 = [0]
        E("sync", x_load(0, 0, dr["x"]),
          waits=[(VE, ve_avev[0]), (SO, pr.c(SO))], inc=LD)
        ld_x2[0] = pr.c(LD)

        def emit_ln2tr(tc):
            for g in range(2):
                w = [(VE, ve_norm2[tc]), (VE, tr_ev[0])]
                for i in range(4):
                    ic = 4 * g + i
                    E("tensor", lambda t, tc=tc, ic=ic, i=i: t.transpose(
                        ps_tr[:, i * 128:(i + 1) * 128],
                        ln2tok[:, tc, ic * 128:(ic + 1) * 128], id_sb[:]),
                        waits=w if i == 0 else [], inc=TE)
                te_tr2 = pr.c(TE)
                tgt = r_lnfm_a if g == 0 else r_lnfm_b
                E("vector", lambda v, tgt=tgt, tc=tc: v.tensor_copy(
                    tgt[:, 0:4, tc * P:(tc + 1) * P],
                    ps_tr[:, :]), waits=[(TE, te_tr2)], inc=VE)
                tr_ev[0] = pr.c(VE)

        for tc in range(TC):
            sl = tc % 2
            if tc + 1 < TC:
                w = []
                if tc >= 1:
                    w = [(VE, ve_norm2[tc - 1]), (SO, so_x2[tc - 1])]
                E("sync", x_load(tc + 1, (tc + 1) % 2, dr["x"]), waits=w, inc=LD)
                ld_x2[tc + 1] = pr.c(LD)
            ps = sc0 if tc % 2 == 0 else sc1
            for ic in range(8):
                for sln in range(2):
                    w = []
                    if ic == 0 and sln == 0:
                        w = [(VE, ve_attnfm), (TE, te_wo_prep), (VE, ve_wo_prep)]
                        if tc >= 2:
                            w.append((VE, ve_x2[tc - 2]))
                    E("tensor", lambda t, ps=ps, tc=tc, ic=ic, sln=sln: t.matmul(
                        ps[:, sln * 512:(sln + 1) * 512],
                        r_q[:, ic, tc * 128:(tc + 1) * 128],
                        r_f2[:, ic, sln * 512:(sln + 1) * 512],
                        start=(ic == 0), stop=(ic == 7)), waits=w, inc=TE)
            te_wo = pr.c(TE)
            if tc >= 1:
                emit_ln2tr(tc - 1)
            xb = r_x[:, sl, :]
            E("vector", lambda v, xb=xb, ps=ps: v.tensor_add(xb, ps[:, :], xb),
              waits=[(TE, te_wo), (LD, ld_x2[tc])], inc=VE)
            ve_x2[tc] = pr.c(VE)
            E("sync", lambda s, xb=xb, tc=tc: s.dma_start(
                out=x2s[tc * P:(tc + 1) * P, :], in_=xb),
                waits=[(VE, ve_x2[tc])], inc=SO)
            so_x2[tc] = pr.c(SO)
            E("scalar", lambda s, xb=xb: s.activation(
                out=sq[:, 0:D], in_=xb, func=AF.Square, accum_out=stats[:, 0:1]),
                waits=[(VE, ve_x2[tc]), (VE, ve_var_prev[0])], inc=SE)
            se_sq2[0] = pr.c(SE)
            E("vector", lambda v, xb=xb: v.reduce_sum(stats[:, 1:2], xb, axis=AXX), inc=VE, dr=True)
            E("vector", lambda v: v.tensor_scalar_mul(stats[:, 2:3], stats[:, 1:2], 1.0 / D), inc=VE, dr=True)
            E("vector", lambda v: v.tensor_mul(stats[:, 3:4], stats[:, 2:3], stats[:, 2:3]), inc=VE, dr=True)
            E("vector", lambda v: v.tensor_scalar(stats[:, 4:5], stats[:, 0:1], 1.0 / D,
                                                  None, op0=OP.mult),
              waits=[(SE, se_sq2[0])], inc=VE)
            E("vector", lambda v: v.tensor_sub(stats[:, 5:6], stats[:, 4:5], stats[:, 3:4]), inc=VE, dr=True)
            ve_var_prev[0] = pr.c(VE)
            E("scalar", lambda s: s.activation(out=stats[:, 6:7], in_=stats[:, 5:6],
                                               func=AF.Ln, bias=c_const[:, 1:2]),
              waits=[(VE, ve_var_prev[0])], inc=SE)
            E("scalar", lambda s: s.activation(out=stats[:, 7:8], in_=stats[:, 6:7],
                                               func=AF.Exp, scale=-0.5), inc=SE)
            se_rstd2 = pr.c(SE)
            E("vector", lambda v, xb=xb, tc=tc: v.tensor_scalar(
                ln2tok[:, tc, :], xb, stats[:, 2:3], stats[:, 7:8],
                op0=OP.subtract, op1=OP.mult),
                waits=[(SE, se_rstd2)], inc=VE)
            ve_norm2[tc] = pr.c(VE)
        emit_ln2tr(TC - 1)
        ve_ln2fm = pr.c(VE)
        te_p3 = pr.c(TE)

        if debug:
            E("sync", lambda s: s.dma_start(out=dbg["dbg_x2"], in_=x2s),
              waits=[(SO, so_x2[TC - 1])], inc=SO)
            E("sync", lambda s: s.dma_start(
                out=dbg["dbg_ln2fm"], in_=r_lnfm_a[:, :, :].rearrange("p a b -> p (a b)")),
                waits=[(VE, ve_ln2fm)], inc=SO)

        se_gelu = [0]
        for dhc in range(4):
            for halft in range(2):
                ps = sc0 if halft == 0 else sc1
                for ic in range(8):
                    wsrc = w1a_a if ic < 4 else w1a_b
                    for sln in range(2):
                        w = []
                        if ic == 0 and sln == 0:
                            w = [(VE, ve_ln2fm), (GL, gl_ffn1), (VE, ve_hbt)]
                            if dhc > 0 or halft > 0:
                                w.append((SE, se_gelu[0]))
                        E("tensor", lambda t, ps=ps, dhc=dhc, ic=ic, sln=sln, halft=halft,
                          wsrc=wsrc: t.matmul(
                            ps[:, sln * 512:(sln + 1) * 512],
                            wsrc[:, ic % 4, dhc * 128:(dhc + 1) * 128],
                            lnfm(ic)[:, halft * 1024 + sln * 512: halft * 1024 + (sln + 1) * 512],
                            start=(ic == 0), stop=(ic == 7)), waits=w, inc=TE)
                te_h = pr.c(TE)
                E("scalar", lambda s, ps=ps, dhc=dhc, halft=halft: s.activation(
                    out=hFM[:, dhc, halft * 1024:(halft + 1) * 1024], in_=ps[:, :],
                    func=AF.Gelu, bias=hbiasT[:, dhc:dhc + 1]),
                    waits=[(TE, te_h)], inc=SE)
                se_gelu[0] = pr.c(SE)
        se_hfm = pr.c(SE)
        if debug:
            E("sync", lambda s: s.dma_start(
                out=dbg["dbg_hfm"], in_=hFM[:, :, :].rearrange("p a b -> p (a b)")),
                waits=[(SE, se_hfm)], inc=SO)

        E("gpsimd", lambda g: g.dma_start(
            out=w2N[:, :, :], in_=dr["w2"].rearrange("(a p) b -> p a b", p=P)),
            waits=[(SE, se_hfm)], inc=GL)
        gl_w2 = pr.c(GL)
        ve_alpha = [0] * TC
        rlp = psC[0:P, 64:128]
        for tc in range(TC):
            for dhc in range(4):
                w = []
                if dhc == 0:
                    w = [(SE, se_hfm), (GL, gl_w2)]
                    if tc >= 1:
                        w.append((VE, ve_alpha[tc - 1]))
                E("tensor", lambda t, tc=tc, dhc=dhc: t.matmul(
                    rlp[:, :], hFM[:, dhc, tc * 128:(tc + 1) * 128],
                    w2N[:, dhc, :], start=(dhc == 0), stop=False,
                    skip_group_check=True), waits=w, inc=TE)
            E("tensor", lambda t: t.matmul(rlp[:, :], ones_m[0:1, :], b2_row[0:1, :],
                                           start=False, stop=True,
                                           skip_group_check=True), inc=TE)
            te_rl = pr.c(TE)
            E("vector", lambda v: v.tensor_scalar(logits[:, :], rlp[:, :], inv_tau[:, 0:1],
                                                  None, op0=OP.mult),
              waits=[(TE, te_rl), (VE, ve_invtau)], inc=VE)
            E("vector", lambda v: v.max(t8[:, :], logits[:, :]), inc=VE, dr=True)
            E("vector", lambda v: v.tensor_scalar(negm1[:, :], t8[:, 0:1], -1.0, None,
                                                  op0=OP.mult), inc=VE, dr=True)
            ve_t8 = pr.c(VE)
            E("scalar", lambda s: s.activation(out=e8b[:, :], in_=t8[:, :], func=AF.Exp,
                                               bias=negm1[:, 0:1]),
              waits=[(VE, ve_t8)], inc=SE)
            E("scalar", lambda s: s.activation(out=e64b[:, :], in_=logits[:, :],
                                               func=AF.Exp, bias=negm1[:, 0:1]), inc=SE)
            se_e = pr.c(SE)
            E("vector", lambda v: v.reduce_sum(s8[:, 0:1], e8b[:, :], axis=AXX),
              waits=[(SE, se_e)], inc=VE)
            E("vector", lambda v: v.reciprocal(s8[:, 1:2], s8[:, 0:1]), inc=VE, dr=True)
            E("vector", lambda v: v.tensor_scalar(maskb[:, :], logits[:, :], t8[:, 7:8],
                                                  None, op0=OP.is_ge), inc=VE)
            E("vector", lambda v, tc=tc: v.scalar_tensor_tensor(
                alphaT[:, tc, :], e64b[:, :], s8[:, 1:2], maskb[:, :],
                op0=OP.mult, op1=OP.mult), inc=VE)
            ve_alpha[tc] = pr.c(VE)
            if debug:
                E("sync", lambda s, tc=tc: s.dma_start(
                    out=dbg["dbg_logits"][:, tc * NS:(tc + 1) * NS], in_=logits[:, :]),
                    waits=[(VE, ve_alpha[tc])], inc=SO)
        ve_alpha_all = pr.c(VE)
        if debug:
            E("sync", lambda s: s.dma_start(
                out=dbg["dbg_alpha"], in_=alphaT[:, :, :].rearrange("p a b -> p (a b)")),
                waits=[(VE, ve_alpha_all)], inc=SO)

        si_ps = psA[0:NS, 1024:2048]
        for tc in range(TC):
            for sln in range(2):
                w = []
                if sln == 0:
                    w = [(VE, ve_alpha[tc])]
                    if tc == 0:
                        w.append((VE, ve_avev[0]))
                E("tensor", lambda t, tc=tc, sln=sln: t.matmul(
                    si_ps[:, sln * 512:(sln + 1) * 512],
                    alphaT[:, tc, :], ln2tok[:, tc, sln * 512:(sln + 1) * 512],
                    start=(tc == 0), stop=(tc == TC - 1), skip_group_check=True),
                    waits=w, inc=TE)
        te_si = pr.c(TE)
        dn_ps = psC[0:NS, 0:1]
        for tc in range(TC):
            E("tensor", lambda t, tc=tc: t.matmul(
                dn_ps[:, :], alphaT[:, tc, :], ones_n[:, :],
                start=(tc == 0), stop=(tc == TC - 1)),
                waits=[(VE, ve_alpha_all)] if tc == 0 else [], inc=TE)
        te_dn = pr.c(TE)
        E("vector", lambda v: v.tensor_scalar_add(denA[:, 0:1], dn_ps[:, :], 1e-8),
          waits=[(TE, te_dn), (TE, te_si)], inc=VE)
        E("vector", lambda v: v.reciprocal(denA[:, 1:2], denA[:, 0:1]), inc=VE, dr=True)
        E("vector", lambda v: v.tensor_copy(stats[0:NS, 0:1], denA[0:NS, 1:2]), inc=VE)
        E("vector", lambda v: v.tensor_scalar(siB[:, :], si_ps[:, :],
                                              denA[:, 1:2], None, op0=OP.mult), inc=VE)
        ve_si = pr.c(VE)


        def smajor_to_fm(srcap, dst, ready_w):
            for icg in range(2):
                w = list(ready_w) + [(VE, tr_ev[0])]
                for i in range(4):
                    ic = icg * 4 + i
                    E("tensor", lambda t, ic=ic, i=i, srcap=srcap: t.transpose(
                        ps_tr[:, i * 128: i * 128 + NS],
                        srcap[:, ic * 128:(ic + 1) * 128], id_sb[0:NS, 0:NS]),
                        waits=w if i == 0 else [], inc=TE)
                te_t = pr.c(TE)
                E("vector", lambda v, icg=icg, dst=dst: v.tensor_copy(
                    dst[:, icg * 4:(icg + 1) * 4, :],
                    ps_tr[:, :].rearrange("p (a b) -> p a b", a=4, b=128)[:, :, 0:NS]),
                    waits=[(TE, te_t)], inc=VE)
                tr_ev[0] = pr.c(VE)
            return pr.c(VE)

        ve_sifm = smajor_to_fm(siB, siFM, [(VE, ve_si)])

        gi_ps = psA[0:NS, 1024:1152]
        gh_ps = psA[0:NS, 1536:1664]
        te_gru = [0]
        ve_gru = [0]
        se_gru = [0]
        gru_tr_ev = [0]
        te_wt_done = [0, 0]
        for ob in range(GOB):
            w = []
            if ob == 0:
                w = [(VE, ve_sifm), (SE, se_hfm), (TE, te_si), (SO, gwt_so[0])]
            else:
                w = [(TE, te_gru[0])]
            E("sync", lambda s, ob=ob: s.dma_start(
                out=wtv[:, 0, :, :], in_=wgT[0, ob, :, :]), waits=w, inc=LD)
            E("sync", lambda s, ob=ob: s.dma_start(
                out=wtv[:, 1, :, :], in_=wgT[1, ob, :, :]), inc=LD)
            ld_wt = pr.c(LD)
            E("sync", lambda s, ob=ob: s.dma_start(
                out=bias_g[0:1, 0:P], in_=dr["bih"][0:1, ob * 128:(ob + 1) * 128]),
                waits=[(TE, te_gru[0])] if ob >= 1 else [], inc=LD)
            E("sync", lambda s, ob=ob: s.dma_start(
                out=bias_g[0:1, P:2 * P], in_=dr["bhh"][0:1, ob * 128:(ob + 1) * 128]), inc=LD)
            ld_gb = pr.c(LD)
            ve_wt = None
            for ic in range(8):
                w = []
                if ic == 0:
                    w = [(LD, ld_wt)]
                    if ob >= 1:
                        w += [(VE, ve_gru[0]), (SE, se_gru[0])]
                E("tensor", lambda t, ic=ic: t.matmul(
                    gi_ps[:, :], siFM[:, ic, :], wtv[:, 0, ic, :],
                    start=(ic == 0), stop=False), waits=w, inc=TE)
                E("tensor", lambda t, ic=ic: t.matmul(
                    gh_ps[:, :], slotFM[:, ic, :], wtv[:, 1, ic, :],
                    start=(ic == 0), stop=False), inc=TE)
            E("tensor", lambda t: t.matmul(gi_ps[:, :], ones_m[0:1, 0:NS], bias_g[0:1, 0:P],
                                           start=False, stop=True),
              waits=[(LD, ld_gb)], inc=TE)
            E("tensor", lambda t: t.matmul(gh_ps[:, :], ones_m[0:1, 0:NS], bias_g[0:1, P:2 * P],
                                           start=False, stop=True), inc=TE)
            te_gru[0] = pr.c(TE)
            gate = ob // 8
            dsl = (ob % 8) * 128
            if gate < 2:
                dst = rrB if gate == 0 else zzB
                E("vector", lambda v: v.tensor_copy(gsc[:, 0, :], gi_ps[:, :]),
                  waits=[(TE, te_gru[0])], inc=VE)
                E("vector", lambda v: v.tensor_add(gsc[:, 1, :],
                                                   gsc[:, 0, :], gh_ps[:, :]), inc=VE, dr=True)
                ve_g = pr.c(VE)
                E("scalar", lambda s: s.activation(out=gsc[:, 2, :],
                                                   in_=gsc[:, 1, :],
                                                   func=AF.Exp, scale=-1.0),
                  waits=[(VE, ve_g)], inc=SE)
                se_gru[0] = pr.c(SE)
                E("vector", lambda v: v.tensor_scalar_add(gsc[:, 3, :],
                                                          gsc[:, 2, :], 1.0),
                  waits=[(SE, se_gru[0])], inc=VE)
                E("vector", lambda v: v.reciprocal(gsc[:, 2, :], gsc[:, 3, :]), inc=VE, dr=True)
                E("vector", lambda v, dst=dst, dsl=dsl: v.tensor_copy(
                    dst[:, dsl:dsl + 128], gsc[:, 2, :]), inc=VE, dr=True)
                ve_gru[0] = pr.c(VE)
            else:
                E("vector", lambda v, dsl=dsl: v.tensor_mul(
                    gsc[:, 0, :], rrB[:, dsl:dsl + 128], gh_ps[:, :]),
                    waits=[(TE, te_gru[0])], inc=VE)
                E("vector", lambda v: v.tensor_add(gsc[:, 1, :],
                                                   gsc[:, 0, :], gi_ps[:, :]), inc=VE, dr=True)
                ve_g = pr.c(VE)
                E("scalar", lambda s: s.activation(out=gsc[:, 2, :],
                                                   in_=gsc[:, 1, :],
                                                   func=AF.Exp, scale=-2.0),
                  waits=[(VE, ve_g)], inc=SE)
                se_gru[0] = pr.c(SE)
                E("vector", lambda v: v.tensor_scalar_add(gsc[:, 3, :],
                                                          gsc[:, 2, :], 1.0),
                  waits=[(SE, se_gru[0])], inc=VE)
                E("vector", lambda v: v.reciprocal(gsc[:, 0, :], gsc[:, 3, :]), inc=VE, dr=True)
                E("vector", lambda v: v.tensor_scalar(gsc[:, 1, :], gsc[:, 2, :],
                                                      -1.0, 1.0, op0=OP.mult, op1=OP.add), inc=VE)
                E("vector", lambda v: v.tensor_mul(gsc[:, 1, :], gsc[:, 1, :], gsc[:, 0, :]), inc=VE, dr=True)
                E("vector", lambda v, dsl=dsl: v.tensor_sub(
                    gsc[:, 2, :], slotS[:, dsl:dsl + 128], gsc[:, 1, :]), inc=VE, dr=True)
                E("vector", lambda v, dsl=dsl: v.tensor_mul(
                    gsc[:, 3, :], zzB[:, dsl:dsl + 128], gsc[:, 2, :]), inc=VE, dr=True)
                E("vector", lambda v, dsl=dsl: v.tensor_add(
                    snewB[:, dsl:dsl + 128], gsc[:, 1, :], gsc[:, 3, :]), inc=VE, dr=True)
                ve_gru[0] = pr.c(VE)
        ve_snew = pr.c(VE)

        E("gpsimd", lambda g: g.nop(), waits=[(TE, te_p3)], inc=GP)
        gp_f2_free = pr.c(GP)
        E("gpsimd", lambda g: g.dma_start(
            out=wvN[:, :, :], in_=dr["wv"].rearrange("(a p) b -> p a b", p=P)),
          waits=[(GP, gp_f2_free)], inc=GL)
        gl_wv = pr.c(GL)

        ve_snb = ve_snew
        ve_snfm = smajor_to_fm(snewB, snFM, [(VE, ve_snew)])

        def slot_proj(dst_ps, wsrc, n_out, bias_name, first_w):
            E("sync", lambda s: s.dma_start(out=bias_scr[0:1, 0:n_out],
                                            in_=dr[bias_name]),
              waits=list(first_w) + [(TE, pr.c(TE))], inc=LD)
            ldb = pr.c(LD)
            nsl = (n_out + 511) // 512
            for ic in range(8):
                for sli in range(nsl):
                    lo, hi = sli * 512, min(n_out, (sli + 1) * 512)
                    E("tensor", lambda t, ic=ic, lo=lo, hi=hi, wsrc=wsrc: t.matmul(
                        dst_ps[:, lo:hi], snFM[:, ic, :], wsrc[:, ic, lo:hi],
                        start=(ic == 0), stop=False, skip_group_check=True),
                        waits=(first_w + [(VE, ve_snfm)]) if (ic == 0 and sli == 0) else [],
                        inc=TE)
            for sli in range(nsl):
                lo, hi = sli * 512, min(n_out, (sli + 1) * 512)
                E("tensor", lambda t, lo=lo, hi=hi: t.matmul(
                    dst_ps[:, lo:hi], ones_m[0:1, 0:NS], bias_scr[0:1, lo:hi],
                    start=False, stop=True, skip_group_check=True),
                    waits=[(LD, ldb)] if sli == 0 else [], inc=TE)
            return pr.c(TE)

        qs_ps = psC[0:NS, 0:DQ]
        ks_ps = psC[0:NS, DQ:2 * DQ]
        te_qs = slot_proj(qs_ps, wqN, DQ, "bq", [(VE, ve_snfm), (GL, gl_ffn1)])
        E("vector", lambda v: v.tensor_copy(qsB[:, :], qs_ps[:, :]),
          waits=[(TE, te_qs)], inc=VE)
        ve_qs = pr.c(VE)
        te_ks = slot_proj(ks_ps, wkN, DQ, "bk", [(VE, ve_qs)])
        E("vector", lambda v: v.tensor_copy(ksB[:, :], ks_ps[:, :]),
          waits=[(TE, te_ks)], inc=VE)
        ve_ks = pr.c(VE)
        vs_ps = psA[0:NS, 1024:2048]
        te_vs = slot_proj(vs_ps, wvN, D, "bv",
                          [(GL, gl_wv), (VE, ve_gru[0]), (SE, se_gru[0])])
        E("vector", lambda v: v.tensor_copy(vsB[:, :], vs_ps[:, :]),
          waits=[(TE, te_vs)], inc=VE)
        ve_vs = pr.c(VE)

        def tr256(src, dst, ready_w):
            w = list(ready_w) + [(VE, tr_ev[0])]
            for i in range(2):
                E("tensor", lambda t, i=i, src=src: t.transpose(
                    ps_tr[:, i * 128: i * 128 + NS],
                    src[:, i * 128:(i + 1) * 128], id_sb[0:NS, 0:NS]),
                    waits=w if i == 0 else [], inc=TE)
            te_t = pr.c(TE)
            E("vector", lambda v, dst=dst: v.tensor_copy(
                dst[:, :, :],
                ps_tr[:, 0:256].rearrange("p (a b) -> p a b", a=2, b=128)[:, :, 0:NS]),
                waits=[(TE, te_t)], inc=VE)
            tr_ev[0] = pr.c(VE)
            return pr.c(VE)

        ve_qsfm = tr256(qsB, qsFM, [(VE, ve_qs)])
        ve_ksfm = tr256(ksB, ksFM, [(VE, ve_ks)])

        s2_ps = psC[0:NS, 0:NS]
        for ic in range(2):
            E("tensor", lambda t, ic=ic: t.matmul(
                s2_ps[:, :], qsFM[:, ic, :], ksFM[:, ic, :],
                start=(ic == 0), stop=(ic == 1)),
                waits=[(VE, ve_qsfm), (VE, ve_ksfm)] if ic == 0 else [], inc=TE)
        te_s2 = pr.c(TE)
        E("vector", lambda v: v.reduce_max(sst[:, 0:1], s2_ps[:, :], axis=AXX),
          waits=[(TE, te_s2)], inc=VE)
        E("vector", lambda v: v.tensor_scalar(sst[:, 1:2], sst[:, 0:1], -0.0625, None,
                                              op0=OP.mult), inc=VE, dr=True)
        ve_m2 = pr.c(VE)
        E("scalar", lambda s: s.activation(out=a2[:, :], in_=s2_ps[:, :], func=AF.Exp,
                                           scale=0.0625, bias=sst[:, 1:2]),
          waits=[(VE, ve_m2)], inc=SE)
        se_a2 = pr.c(SE)
        E("vector", lambda v: v.reduce_sum(sst[:, 2:3], a2[:, :], axis=AXX),
          waits=[(SE, se_a2)], inc=VE)
        E("vector", lambda v: v.reciprocal(sst[:, 3:4], sst[:, 2:3]), inc=VE, dr=True)
        E("vector", lambda v: v.tensor_copy(sst[:, 4:5], sst[:, 3:4]), inc=VE)
        E("vector", lambda v: v.tensor_scalar(a2[:, :], a2[:, :], sst[:, 3:4], None,
                                              op0=OP.mult), inc=VE)
        ve_a2 = pr.c(VE)
        E("tensor", lambda t: t.transpose(ps_tr[0:NS, 0:NS], a2[:, :], id_sb[0:NS, 0:NS]),
          waits=[(VE, ve_a2), (VE, tr_ev[0])], inc=TE)
        te_a2t = pr.c(TE)
        E("vector", lambda v: v.tensor_copy(a2T[:, :], ps_tr[0:NS, 0:NS]),
          waits=[(TE, te_a2t)], inc=VE)
        ve_a2t = pr.c(VE)
        tr_ev[0] = ve_a2t
        scm_ps = psB1[0:NS, 0:1024]
        for sln in range(2):
            E("tensor", lambda t, sln=sln: t.matmul(
                scm_ps[:, sln * 512:(sln + 1) * 512], a2T[:, :],
                vsB[:, sln * 512:(sln + 1) * 512], start=True, stop=True),
                waits=[(VE, ve_a2t), (VE, ve_vs)] if sln == 0 else [], inc=TE)
        te_scm = pr.c(TE)
        E("vector", lambda v: v.tensor_add(yyB[:, :], scm_ps[:, :], snewB[:, :]),
          waits=[(TE, te_scm)], inc=VE)
        ve_y = pr.c(VE)
        E("scalar", lambda s: s.activation(out=ysqd[:, :], in_=yyB[:, :],
                                           func=AF.Square, accum_out=sst[:, 0:1]),
          waits=[(VE, ve_y)], inc=SE)
        se_ysq = pr.c(SE)
        E("vector", lambda v: v.reduce_sum(sst[:, 1:2], yyB[:, :], axis=AXX), inc=VE, dr=True)
        E("vector", lambda v: v.tensor_scalar_mul(sst[:, 2:3], sst[:, 1:2], 1.0 / D), inc=VE, dr=True)
        E("vector", lambda v: v.tensor_mul(sst[:, 3:4], sst[:, 2:3], sst[:, 2:3]), inc=VE, dr=True)
        E("vector", lambda v: v.tensor_scalar(sst[:, 4:5], sst[:, 0:1], 1.0 / D, None,
                                              op0=OP.mult),
          waits=[(SE, se_ysq)], inc=VE)
        E("vector", lambda v: v.tensor_sub(sst[:, 5:6], sst[:, 4:5], sst[:, 3:4]), inc=VE, dr=True)
        ve_yv = pr.c(VE)
        E("scalar", lambda s: s.activation(out=sst[:, 6:7], in_=sst[:, 5:6], func=AF.Ln,
                                           bias=c_const[0:NS, 1:2]),
          waits=[(VE, ve_yv)], inc=SE)
        E("scalar", lambda s: s.activation(out=sst[:, 7:8], in_=sst[:, 6:7], func=AF.Exp,
                                           scale=-0.5), inc=SE)
        se_yr = pr.c(SE)
        E("vector", lambda v: v.tensor_scalar(ylnB[:, :], yyB[:, :], sst[:, 2:3],
                                              sst[:, 7:8], op0=OP.subtract, op1=OP.mult),
          waits=[(SE, se_yr)], inc=VE)
        ve_yln = pr.c(VE)
        ve_ylnfm = smajor_to_fm(ylnB, ylnFM, [(VE, ve_yln)])

        ve_wvpg0 = [0]
        for ic in range(8):
            E("vector", lambda v, ic=ic: v.tensor_scalar(
                wvpN[:, ic, :], wvpN[:, ic, :], gf[:, ic:ic + 1], None, op0=OP.mult),
                waits=[(GL, gl_wvp), (VE, ve_gf)] if ic == 0 else [], inc=VE)
        ve_wvpg = pr.c(VE)

        ld_bw = [0]
        te_bt2 = [0]
        for half in range(2):
            for ic in range(8):
                slb = ic % 2
                E("sync", lambda s, ic=ic, slb=slb: s.dma_start(
                    out=r_x[:, slb, :], in_=dr["wvp"][ic * 128:(ic + 1) * 128, :]),
                    waits=([(VE, ve_snb), (VE, ve_yln)] if (half == 0 and ic == 0) else
                           [(TE, te_bt2[0])]), inc=LD)
                ld_bw[0] = pr.c(LD)
                E("tensor", lambda t, ic=ic, half=half, slb=slb: t.matmul(
                    psC[0:1, 0:512], bfm[:, ic:ic + 1],
                    r_x[:, slb, half * 512:(half + 1) * 512],
                    start=(ic == 0), stop=(ic == 7)),
                    waits=[(LD, ld_bw[0])] + ([(VE, ve_yv)] if (half == 0 and ic == 0) else []),
                    inc=TE)
                te_bt2[0] = pr.c(TE)
            dstrow = hb_row if half == 0 else b1_row
            E("vector", lambda v, dstrow=dstrow: v.tensor_copy(
                dstrow[0:1, 0:512], psC[0:1, 0:512]),
                waits=[(TE, te_bt2[0])], inc=VE)
        ve_btr = pr.c(VE)

        sv_ps = psA[0:NS, 1024:2048]
        E("sync", lambda s: s.dma_start(out=bias_scr[0:1, :], in_=dr["bvp"]),
          waits=[(VE, ve_btr), (TE, pr.c(TE))], inc=LD)
        ld_bvp = pr.c(LD)
        for ic in range(8):
            for sln in range(2):
                E("tensor", lambda t, ic=ic, sln=sln: t.matmul(
                    sv_ps[:, sln * 512:(sln + 1) * 512], ylnFM[:, ic, :],
                    wvpN[:, ic, sln * 512:(sln + 1) * 512],
                    start=(ic == 0), stop=False, skip_group_check=True),
                    waits=[(VE, ve_ylnfm), (VE, ve_wvpg), (VE, ve_vs)] if (ic == 0 and sln == 0) else [],
                    inc=TE)
        for sln in range(2):
            E("tensor", lambda t, sln=sln: t.matmul(
                sv_ps[:, sln * 512:(sln + 1) * 512], ones_m[0:1, 0:NS],
                bias_scr[0:1, sln * 512:(sln + 1) * 512],
                start=False, stop=False, skip_group_check=True),
                waits=[(LD, ld_bvp)] if sln == 0 else [], inc=TE)
            btsrc = hb_row if sln == 0 else b1_row
            E("tensor", lambda t, sln=sln, btsrc=btsrc: t.matmul(
                sv_ps[:, sln * 512:(sln + 1) * 512], ones_m[0:1, 0:NS],
                btsrc[0:1, 0:512],
                start=False, stop=True, skip_group_check=True),
                waits=[(VE, ve_btr)] if sln == 0 else [], inc=TE)
        te_sv = pr.c(TE)
        E("vector", lambda v: v.tensor_copy(svB[:, :], sv_ps[:, :]),
          waits=[(TE, te_sv)], inc=VE)
        ve_sv = pr.c(VE)


        for tcg in range(4):
            w = [(VE, ve_alpha_all), (VE, tr_ev[0])]
            for i in range(4):
                tc = tcg * 4 + i
                E("tensor", lambda t, tc=tc, i=i: t.transpose(
                    ps_tr[0:NS, i * 128:(i + 1) * 128],
                    alphaT[:, tc, :], id_sb[:]),
                    waits=w if i == 0 else [], inc=TE)
            te_t = pr.c(TE)
            E("vector", lambda v, tcg=tcg: v.tensor_copy(
                alphaFM[:, tcg * 4:(tcg + 1) * 4, :],
                ps_tr[0:NS, :]), waits=[(TE, te_t)], inc=VE)
            tr_ev[0] = pr.c(VE)
        ve_alphafm = pr.c(VE)
        if debug:
            E("sync", lambda s: s.dma_start(out=dbg["dbg_sv"], in_=svB[:, :]),
              waits=[(VE, ve_sv)], inc=SO)

        afm = alphaFM[:, :, :].rearrange("p a b -> p (a b)")
        ve_of = {}
        for dc in range(8):
            for sln in range(4):
                ps = sc0 if dc % 2 == 0 else sc1
                w = []
                if dc == 0 and sln == 0:
                    w = [(VE, ve_sv), (VE, ve_alphafm)]
                if (dc, sln) >= (0, 2) and (dc * 4 + sln) >= 2:
                    prevkey = (dc * 4 + sln) - 2
                    w.append((VE, ve_of[prevkey]))
                E("tensor", lambda t, dc=dc, sln=sln, ps=ps: t.matmul(
                    ps[:, (sln % 2) * 512:(sln % 2 + 1) * 512],
                    svB[:, dc * 128:(dc + 1) * 128],
                    afm[0:NS, sln * 512:(sln + 1) * 512],
                    start=True, stop=True), waits=w, inc=TE)
                te_o = pr.c(TE)
                E("vector", lambda v, dc=dc, sln=sln, ps=ps: v.tensor_copy(
                    r_q[:, dc, sln * 512:(sln + 1) * 512],
                    ps[:, (sln % 2) * 512:(sln % 2 + 1) * 512]),
                    waits=[(TE, te_o)], inc=VE)
                ve_of[dc * 4 + sln] = pr.c(VE)
        ve_outfm = pr.c(VE)

        E("sync", lambda s: s.dma_start(out=bias_scr[0:1, :], in_=dr["bo"]),
          waits=[(LD, ld_bvp), (TE, te_sv)], inc=LD)
        ld_bo = pr.c(LD)
        ld_xf = [0] * TC
        ve_fin = [0] * TC
        so_out = [0] * TC
        E("sync", x_load(0, 0, x2s), waits=[(TE, te_sv)], inc=LD)
        ld_xf[0] = pr.c(LD)
        for tc in range(TC):
            sl = tc % 2
            if tc + 1 < TC:
                w = [(TE, te_sv)]
                if tc >= 1:
                    w += [(VE, ve_fin[tc - 1]), (SO, so_out[tc - 1])]
                E("sync", x_load(tc + 1, (tc + 1) % 2, x2s), waits=w, inc=LD)
                ld_xf[tc + 1] = pr.c(LD)
            ps = sc0 if tc % 2 == 0 else sc1
            for ic in range(8):
                for sln in range(2):
                    w = []
                    if ic == 0 and sln == 0:
                        w = [(VE, ve_outfm), (GL, gl_wo)]
                        if tc >= 2:
                            w.append((VE, ve_fin[tc - 2]))
                    E("tensor", lambda t, ps=ps, tc=tc, ic=ic, sln=sln: t.matmul(
                        ps[:, sln * 512:(sln + 1) * 512],
                        r_q[:, ic, tc * 128:(tc + 1) * 128],
                        woN[:, ic, sln * 512:(sln + 1) * 512],
                        start=(ic == 0), stop=False), waits=w, inc=TE)
            for sln in range(2):
                E("tensor", lambda t, ps=ps, sln=sln: t.matmul(
                    ps[:, sln * 512:(sln + 1) * 512], ones_m[0:1, :],
                    bias_scr[0:1, sln * 512:(sln + 1) * 512],
                    start=False, stop=True),
                    waits=[(LD, ld_bo)] if sln == 0 else [], inc=TE)
            te_fin = pr.c(TE)
            xb = r_x[:, sl, :]
            E("vector", lambda v, xb=xb, ps=ps: v.tensor_add(xb, ps[:, :], xb),
              waits=[(TE, te_fin), (LD, ld_xf[tc])], inc=VE)
            ve_fin[tc] = pr.c(VE)
            E("sync", lambda s, xb=xb, tc=tc: s.dma_start(
                out=out[tc * P:(tc + 1) * P, :], in_=xb),
                waits=[(VE, ve_fin[tc])], inc=SO)
            so_out[tc] = pr.c(SO)
        so_done = pr.c(SO)
        if _TAPS:
            tap_pairs = [("tap_hfm", hFM[:, :, :].rearrange("p a b -> p (a b)")),
                           ("tap_alpha", alphaT[:, :, :].rearrange("p a b -> p (a b)")),
                           ("tap_sv", svB[:, :]), ("tap_snew", snewB[:, :]),
                           ("tap_yln", ylnB[:, :]), ("tap_vs", vsB[:, :]),
                           ("tap_sifm", r_qv[:, 0:512]),
                           ("tap_bt", None), ("tap_wvp", rkv[:, 0:8192]),
                           ("tap_ylnfm", r_qv[:, 1024:1536]),
                           ("tap_outfm", r_q[:, :, :].rearrange("p a b -> p (a b)"))]
            for nm, ap in tap_pairs:
                if ap is None:
                    continue
                E("sync", lambda s, nm=nm, ap=ap: s.dma_start(out=dbg[nm], in_=ap),
                  waits=[(SO, so_done)], inc=SO)
            E("sync", lambda s: s.dma_start(out=dbg["tap_bt"][0:1, 0:512],
                                            in_=hb_row[0:1, :]),
              waits=[(SO, so_done)], inc=SO)
            E("sync", lambda s: s.dma_start(out=dbg["tap_bt"][0:1, 512:1024],
                                            in_=b1_row[0:1, :]), inc=SO)
            so_done = pr.c(SO)
        E("sync", lambda s: s.nop(), waits=[(SO, so_done)], inc=None)

        pr.finalize()
        block = stack.enter_context(nc.Block())

        @block.sync
        def _(eng):
            pr.run_engine("sync", eng)

        @block.gpsimd
        def _(eng):
            pr.run_engine("gpsimd", eng)

        @block.tensor
        def _(eng):
            pr.run_engine("tensor", eng)

        @block.scalar
        def _(eng):
            pr.run_engine("scalar", eng)

        @block.vector
        def _(eng):
            pr.run_engine("vector", eng)

    return nc


_NC_CACHE = {}


def kernel(**inputs):
    consts = _np_consts()
    if "main" not in _NC_CACHE:
        _NC_CACHE["main"] = build_nc(debug=_DEBUG)
    nc = _NC_CACHE["main"]

    x = np.ascontiguousarray(np.asarray(inputs["L0_attn_x"], dtype=np.float32))
    names = {
        "Wq": "L0_attn_Wq", "Wk": "L0_attn_Wk", "Wv": "L0_attn_Wv",
        "Wo": "L0_attn_Wo", "slot_init": "L0_ffn_slot_init", "w1": "L0_ffn_w1",
        "w2": "L0_ffn_w2", "Wih": "L0_ffn_Wih", "Whh": "L0_ffn_Whh",
        "wq": "L0_ffn_wq", "wk": "L0_ffn_wk", "wv": "L0_ffn_wv",
        "wvp": "L0_ffn_wvp", "wo": "L0_ffn_wo",
    }
    rows = {
        "b1": "L0_ffn_b1", "b2": "L0_ffn_b2", "bih": "L0_ffn_bih",
        "bhh": "L0_ffn_bhh", "bq": "L0_ffn_bq", "bk": "L0_ffn_bk",
        "bv": "L0_ffn_bv", "ln_g": "L0_ffn_ln_g", "ln_b": "L0_ffn_ln_b",
        "bvp": "L0_ffn_bvp", "bo": "L0_ffn_bo",
    }
    shared = {}
    for k, src in names.items():
        shared[k] = np.ascontiguousarray(np.asarray(inputs[src], dtype=np.float32))
    for k, src in rows.items():
        shared[k] = np.ascontiguousarray(
            np.asarray(inputs[src], dtype=np.float32).reshape(1, -1))
    shared["tau"] = np.ascontiguousarray(
        np.asarray(inputs["L0_ffn_tau"], dtype=np.float32).reshape(1, 1))
    shared.update(consts)

    in_maps = []
    for b in range(8):
        m = dict(shared)
        m["x"] = np.ascontiguousarray(x[b])
        in_maps.append(m)

    res = run_bass_kernel_spmd(nc, in_maps, core_ids=list(range(8)))
    return np.stack([res.results[b]["out"] for b in range(8)], axis=0).astype(np.float32)


# version 17
# speedup vs baseline: 1.1466x; 1.0016x over previous
import numpy as np
import ml_dtypes
from contextlib import ExitStack

import concourse.bass as bass
import concourse.mybir as mybir
from concourse.bass_utils import run_bass_kernel_spmd
from concourse.alu_op_type import AluOpType as OP

F32 = mybir.dt.float32
BF16 = mybir.dt.bfloat16
AF = mybir.ActivationFunctionType
AXX = mybir.AxisListType.X

P = 128
T = 2048
TC = 16
D = 1024
H = 16
NPAIR = 8
KC = 16
NS = 64
DQ = 256
DHF = 512
GOB = 24

_DEBUG = False
_X2OUT = False
_TAPS = False


def _np_consts():
    h = np.array([[1.0]])
    while h.shape[0] < 64:
        h = np.block([[h, h], [h, -h]])
    u = h[1:, :] / 8.0
    ut = np.zeros((64, 64), np.float32)
    ut[:, :63] = u.T
    ut2 = np.tile(ut, (2, 1))
    ones_pair = np.zeros((128, 2), np.float32)
    ones_pair[0:64, 0] = 1.0
    ones_pair[64:128, 1] = 1.0
    return {
        "c_ut2": ut2.astype(ml_dtypes.bfloat16),
        "c_id": np.eye(128, dtype=np.float32).astype(ml_dtypes.bfloat16),
        "c_ones_m": np.ones((1, 128), np.float32),
        "c_ones_pair": ones_pair.astype(ml_dtypes.bfloat16),
        "c_ones_n": np.ones((128, 1), np.float32).astype(ml_dtypes.bfloat16),
        "c_idf": np.eye(128, dtype=np.float32),
    }


class Prog:
    def __init__(self, nc, stack):
        self.nc = nc
        self.steps = []
        self.sems = {}
        self.cnt = {}
        for name in ("LD", "GL", "TE", "VE", "SE", "GP", "SO"):
            self.sems[name] = stack.enter_context(nc.semaphore(name))
            self.cnt[name] = 0

    def emit(self, engine, fn, waits=(), inc=None, n=None, dr=False):
        if n is None:
            n = 16 if inc in ("LD", "GL", "SO") else 1
        for s, v in waits:
            assert v <= self.cnt[s], (engine, s, v, self.cnt[s])
        if inc is not None:
            self.cnt[inc] += n
        self.steps.append((engine, fn, tuple(waits), inc, n, dr))
        return self.cnt[inc] if inc is not None else None

    def c(self, sem):
        return self.cnt[sem]

    def finalize(self):
        waited = {s: set() for s in self.sems}
        for _, _, waits, _, _, _ in self.steps:
            for s, v in waits:
                waited[s].add(v)
        cnt = {s: 0 for s in self.sems}
        newcnt = {s: 0 for s in self.sems}
        remap = {s: {0: 0} for s in self.sems}
        new_steps = []
        for e, fn, waits, inc, n, dr in self.steps:
            nw = tuple((s, remap[s][v]) for s, v in waits)
            if inc is not None:
                cnt[inc] += n
                if inc in ("GL", "LD", "SO") or cnt[inc] in waited[inc]:
                    newcnt[inc] += n
                    remap[inc][cnt[inc]] = newcnt[inc]
                    new_steps.append((e, fn, nw, inc, n, dr))
                else:
                    remap[inc][cnt[inc]] = newcnt[inc]
                    new_steps.append((e, fn, nw, None, n, dr))
            else:
                new_steps.append((e, fn, nw, None, n, dr))
        self.steps = new_steps

    def run_engine(self, name, eng):
        for e, fn, waits, inc, n, dr in self.steps:
            if e != name:
                continue
            for s, v in waits:
                eng.wait_ge(self.sems[s], v)
            if dr:
                eng.drain()
            instr = fn(eng)
            if inc is not None:
                instr.then_inc(self.sems[inc], n)


def build_nc(debug=False):
    nc = bass.Bass("TRN2", target_bir_lowering=False,
                   dynamic_dma_scratch_size=8192)

    dr = {}

    def din(name, shape, dt=F32):
        dr[name] = nc.dram_tensor(name, list(shape), dt, kind="ExternalInput").ap()

    din("x", (T, D))
    for nm, shp in [
        ("Wq", (D, D)), ("Wk", (D, D)), ("Wv", (D, D)), ("Wo", (D, D)),
        ("slot_init", (NS, D)), ("w1", (2 * D, DHF)), ("b1", (1, DHF)),
        ("w2", (DHF, NS)), ("b2", (1, NS)), ("tau", (1, 1)),
        ("Wih", (3 * D, D)), ("Whh", (3 * D, D)),
        ("bih", (1, 3 * D)), ("bhh", (1, 3 * D)),
        ("wq", (D, DQ)), ("bq", (1, DQ)), ("wk", (D, DQ)), ("bk", (1, DQ)),
        ("wv", (D, D)), ("bv", (1, D)), ("ln_g", (1, D)), ("ln_b", (1, D)),
        ("wvp", (D, D)), ("bvp", (1, D)), ("wo", (D, D)), ("bo", (1, D)),
    ]:
        din(nm, shp)
    din("c_ut2", (P, 64), BF16)
    din("c_id", (P, P), BF16)
    din("c_ones_m", (1, P))
    din("c_ones_pair", (P, 2), BF16)
    din("c_ones_n", (P, 1), BF16)
    din("c_idf", (P, P))

    out = nc.dram_tensor("out", [T, D], F32, kind="ExternalOutput").ap()
    x2s = nc.dram_tensor("x2s", [T, D], F32,
                         kind="ExternalOutput" if _X2OUT else "Internal").ap()
    wgT = nc.dram_tensor("wgT", [2, GOB, P, 8 * P], BF16, kind="Internal").ap()
    dbg = {}
    if _TAPS:
        for nm, shp, dt in [
            ("tap_hfm", (P, 4 * T), BF16), ("tap_alpha", (P, TC * NS), BF16),
            ("tap_sv", (NS, D), BF16), ("tap_snew", (NS, D), BF16),
            ("tap_yln", (NS, D), BF16), ("tap_vs", (NS, D), BF16),
            ("tap_sifm", (P, 8 * NS), BF16), ("tap_outfm", (P, 8 * T), BF16),
            ("tap_bt", (1, D), F32), ("tap_wvp", (P, 8 * D), BF16),
            ("tap_ylnfm", (P, 8 * NS), BF16),
        ]:
            dbg[nm] = nc.dram_tensor(nm, list(shp), dt, kind="ExternalOutput").ap()
    if debug:
        for nm, shp, dt in [
            ("dbg_ln1", (P, 4 * T), BF16), ("dbg_qfm", (P, 8 * T), BF16),
            ("dbg_kfm", (P, 8 * T), BF16), ("dbg_vhat", (P, KC * H * 65), BF16),
            ("dbg_attnfm", (P, 8 * T), BF16), ("dbg_x2", (T, D), F32),
            ("dbg_hfm", (P, 4 * T), BF16), ("dbg_alpha", (P, TC * NS), BF16),
            ("dbg_sv", (NS, D), BF16), ("dbg_ln2fm", (P, 4 * T), BF16),
            ("dbg_logits", (P, TC * NS), F32), ("dbg_e0", (P, D), BF16),
            ("dbg_ln1b", (P, 4 * T), BF16), ("dbg_lnb", (P, 4 * T), BF16),
            ("dbg_vps", (P, D), BF16),
            ("dbg_av0", (P, 8 * 128), BF16), ("dbg_pt0", (P, TC * P), BF16),
        ]:
            dbg[nm] = nc.dram_tensor(nm, list(shp), dt, kind="ExternalOutput").ap()

    with ExitStack() as stack:
        def sb(name, shape, dt):
            return stack.enter_context(nc.sbuf_tensor(name, shape, dt))

        r_lnfm_a = sb("r_lnfm_a", [P, 4, T], BF16)
        r_lnfm_b = sb("r_lnfm_b", [P, 4, T], BF16)
        r_q = sb("r_q", [P, NPAIR, T], BF16)
        r_k = sb("r_k", [P, NPAIR, T], BF16)
        r_v = sb("r_v", [P, KC, H, 65], BF16)
        r_e = sb("r_e", [P, 8, D], BF16)
        r_f1 = sb("r_f1", [P, 8, D], BF16)
        r_f2 = sb("r_f2", [P, 8, D], BF16)
        r_x = sb("r_x", [P, 2, D], F32)
        r_et = sb("r_et", [P, 2, D], BF16)
        r_pt = sb("r_pt", [P, TC, P], BF16)
        r_qv = sb("r_qv", [P, T], BF16)

        c_const = sb("c_const", [P, 2], F32)
        stats = sb("stats", [P, 16], F32)
        t8 = sb("t8", [P, 8], F32)
        negm1 = sb("negm1", [P, 1], F32)
        e8b = sb("e8b", [P, 8], F32)
        e64b = sb("e64b", [P, NS], F32)
        maskb = sb("maskb", [P, NS], F32)
        s8 = sb("s8", [P, 2], F32)
        logits = sb("logits", [P, NS], F32)
        recipC = sb("recipC", [P, 8], F32)
        inv_tau = sb("inv_tau", [P, 1], F32)
        tau_t = sb("tau_t", [1, 2], F32)
        id_sb = sb("id_sb", [P, P], BF16)
        ut2_sb = sb("ut2_sb", [P, 64], BF16)
        ones_m = sb("ones_m", [1, P], F32)
        ones_pr = sb("ones_pr", [P, 2], BF16)
        ones_n = sb("ones_n", [P, 1], BF16)
        idf_sb = sb("idf_sb", [P, P], F32)
        bias_g = sb("bias_g", [1, 2 * P], F32)
        b1_row = sb("b1_row", [1, DHF], F32)
        b2_row = sb("b2_row", [1, NS], F32)
        hb_row = sb("hb_row", [1, DHF], F32)
        hbiasT = sb("hbiasT", [P, 4], F32)
        gf = sb("gf", [P, 8], F32)
        bfm = sb("bfm", [P, 8], F32)
        smf = sb("smf", [P, 8], BF16)
        slotS = sb("slotS", [NS, D], BF16)
        slotFM = sb("slotFM", [P, 8, NS], BF16)
        qsFM = sb("qsFM", [P, 2, NS], BF16)
        ksFM = sb("ksFM", [P, 2, NS], BF16)
        a2 = sb("a2", [NS, NS], BF16)
        a2T = sb("a2T", [NS, NS], BF16)
        sst = sb("sst", [NS, 8], F32)
        denA = sb("denA", [NS, 2], F32)
        gsc = sb("gsc", [NS, 8, P], F32)
        dbgav_sb = sb("dbgav", [P, 1024], BF16) if debug else None
        _sfm3 = r_qv[:, 0:1536].rearrange("p (a b c) -> p a b c", a=3, b=8, c=NS)
        siFM = _sfm3[:, 0, :, :]
        snFM = _sfm3[:, 1, :, :]
        ylnFM = _sfm3[:, 2, :, :]

        wtv = r_pt[:, :, :].rearrange("p a b -> p (a b)").rearrange(
            "p (w i c) -> p w i c", w=2, i=8, c=P)
        alphaFM = r_et[:, :, :].rearrange("p a b -> p (a b)")[0:NS, :].rearrange(
            "p (a b) -> p a b", a=TC, b=P)
        etf = r_et[:, :, :].rearrange("p a b -> p (a b)")
        f1f2 = r_f1[:, :, :].rearrange("p a b -> p (a b)")
        rptf = r_pt[:, :, :].rearrange("p a b -> p (a b)")
        siB = etf[0:NS, 0:D]
        rrB = etf[0:NS, 0:D]
        zzB = etf[0:NS, D:2 * D]
        snewB = f1f2[0:NS, 0:D]
        yyB = f1f2[0:NS, D:2 * D]
        ylnB = f1f2[0:NS, 2 * D:3 * D]
        ysqd = f1f2[0:NS, 3 * D:4 * D]
        vsB = rptf[0:NS, 0:D]
        svB = rptf[0:NS, D:2 * D]
        rlnbf = r_lnfm_b[:, :, :].rearrange("p a b -> p (a b)")
        alphaT = rlnbf[:, 0:TC * NS].rearrange("p (a b) -> p a b", a=TC, b=NS)
        qsB = rlnbf[0:NS, 1024:1024 + DQ]
        ksB = rlnbf[0:NS, 1280:1280 + DQ]
        w2N = rlnbf[:, 1536:1792].rearrange("p (a b) -> p a b", a=4, b=NS)
        bias_scr = gsc[:, :, :].rearrange("p a b -> p (a b)")[0:1, 0:D]

        f1v = r_f1[:, :, :].rearrange("p a b -> p (a b)")
        w1a_a = f1v[:, 0:2048].rearrange("p (a b) -> p a b", a=4, b=DHF)
        w1a_b = f1v[:, 2048:4096].rearrange("p (a b) -> p a b", a=4, b=DHF)
        wqN = f1v[:, 4096:6144].rearrange("p (a b) -> p a b", a=8, b=DQ)
        wkN = f1v[:, 6144:8192].rearrange("p (a b) -> p a b", a=8, b=DQ)
        f2v = r_f2[:, :, :].rearrange("p a b -> p (a b)")
        wvN = f2v[:, 0:8192].rearrange("p (a b) -> p a b", a=8, b=D)
        rkv = r_k[:, :, :].rearrange("p a b -> p (a b)")
        wvpN = rkv[:, 0:8192].rearrange("p (a b) -> p a b", a=8, b=D)
        woN = rkv[:, 8192:16384].rearrange("p (a b) -> p a b", a=8, b=D)
        rev = r_e[:, :, :].rearrange("p a b -> p (a b)")
        wstage = rev[:, 0:4096].rearrange("p (a b) -> p a b", a=4, b=D)
        ystage = rev[:, 4096:8192].rearrange("p (a b) -> p a b", a=4, b=D)
        hFM = rev[:, 0:8192].rearrange("p (a b) -> p a b", a=4, b=T)
        ln2tok = r_v[:, :, :, :].rearrange("p a b c -> p (a b c)")[:, 0:TC * D]
        ln2tok = ln2tok.rearrange("p (a b) -> p a b", a=TC, b=D)
        sq = r_et[:, :, :].rearrange("p a b -> p (a b)")
        lnt = r_pt[:, :, :].rearrange("p a b -> p (a b)")

        def lnfm(ic):
            return (r_lnfm_a if ic < 4 else r_lnfm_b)[:, ic % 4, :]

        psA = stack.enter_context(nc.psum_tensor("psA", [P, 2048], F32))
        psB1 = stack.enter_context(nc.psum_tensor("psB1", [P, 1024], F32))
        ps_tr = stack.enter_context(nc.psum_tensor("ps_tr", [P, 512], BF16))
        psC = stack.enter_context(nc.psum_tensor("psC", [P, 512], F32))

        sc0 = psA[:, 0:1024]
        sc1 = psA[:, 1024:2048]
        avA = psB1[:, 0:512].rearrange("p (a b) -> p a b", a=4, b=128)
        avB = psB1[:, 512:1024].rearrange("p (a b) -> p a b", a=4, b=128)
        y0 = psA[0:NS, 1024:2048]
        y1 = psB1[0:NS, 0:1024]
        ps_trf = psC[0:P, 128:144]

        pr = Prog(nc, stack)
        E = pr.emit
        LD, GL, TE, VE, SE, GP, SO = "LD", "GL", "TE", "VE", "SE", "GP", "SO"

        for nm, dst in [("c_id", id_sb), ("c_ut2", ut2_sb), ("c_ones_m", ones_m),
                        ("c_ones_pair", ones_pr), ("c_ones_n", ones_n), ("c_idf", idf_sb),
                        ("b1", b1_row), ("b2", b2_row)]:
            E("sync", lambda s, nm=nm, dst=dst: s.dma_start(out=dst[:], in_=dr[nm]),
              inc=LD)
        E("sync", lambda s: s.dma_start(out=tau_t[0:1, 0:1], in_=dr["tau"]), inc=LD)
        ld_consts = pr.c(LD)
        E("gpsimd", lambda g: g.memset(c_const[:, 0:1], -20.0), inc=GP)
        E("gpsimd", lambda g: g.memset(c_const[:, 1:2], 1e-5), inc=GP)
        E("gpsimd", lambda g: g.memset(r_v[:], 1.0), inc=GP)
        gp_init = pr.c(GP)
        E("gpsimd", lambda g: g.dma_start(out=slotS[:], in_=dr["slot_init"]), inc=GL)
        gl_slot = pr.c(GL)

        E("scalar", lambda s: s.activation(out=tau_t[0:1, 1:2], in_=tau_t[0:1, 0:1],
                                           func=AF.Abs),
          waits=[(LD, ld_consts)], inc=SE)
        se_tau = pr.c(SE)
        E("vector", lambda v: v.tensor_scalar_add(tau_t[0:1, 1:2], tau_t[0:1, 1:2], 0.1),
          waits=[(SE, se_tau)], inc=VE)
        E("vector", lambda v: v.reciprocal(tau_t[0:1, 0:1], tau_t[0:1, 1:2]), inc=VE, dr=True)
        ve_tau = pr.c(VE)
        E("tensor", lambda t: t.matmul(psC[0:P, 0:1], ones_m[0:1, :], tau_t[0:1, 0:1],
                                       start=True, stop=True),
          waits=[(VE, ve_tau), (LD, ld_consts)], inc=TE)
        te_tau = pr.c(TE)
        E("vector", lambda v: v.tensor_copy(inv_tau[:], psC[0:P, 0:1]),
          waits=[(TE, te_tau)], inc=VE)
        ve_invtau = pr.c(VE)

        ld_x = [0] * TC
        ve_ln = [0] * TC
        te_lntr = [0]
        tr_ev = [0]
        ve_var_prev = [0]

        def x_load(tc, slot, src):
            return lambda s: s.dma_start(out=r_x[:, slot, :],
                                         in_=src[tc * P:(tc + 1) * P, :])

        E("sync", x_load(0, 0, dr["x"]), inc=LD)
        ld_x[0] = pr.c(LD)
        se_rstd_t = [0] * TC
        ve_var_t = [0] * TC
        te_lntr_t = [0] * TC

        def ln1_stats(tc):
            sl = tc % 2
            cs = 8 * (tc % 2)
            xb = r_x[:, sl, :]
            w = [(LD, ld_x[tc])]
            if tc >= 2:
                w.append((VE, ve_var_t[tc - 2]))
            E("scalar", lambda s, xb=xb, cs=cs: s.activation(
                out=r_et[:, 0, :], in_=xb, func=AF.Square,
                accum_out=stats[:, cs:cs + 1]), waits=w, inc=SE)
            se_sq = pr.c(SE)
            E("vector", lambda v, xb=xb, cs=cs: v.reduce_sum(
                stats[:, cs + 1:cs + 2], xb, axis=AXX),
                waits=[(LD, ld_x[tc])], inc=VE)
            E("vector", lambda v, cs=cs: v.tensor_scalar_mul(
                stats[:, cs + 2:cs + 3], stats[:, cs + 1:cs + 2], 1.0 / D), inc=VE, dr=True)
            E("vector", lambda v, cs=cs: v.tensor_mul(
                stats[:, cs + 3:cs + 4], stats[:, cs + 2:cs + 3],
                stats[:, cs + 2:cs + 3]), inc=VE, dr=True)
            E("vector", lambda v, cs=cs: v.tensor_scalar(
                stats[:, cs + 4:cs + 5], stats[:, cs:cs + 1], 1.0 / D,
                None, op0=OP.mult),
                waits=[(SE, se_sq)], inc=VE)
            E("vector", lambda v, cs=cs: v.tensor_sub(
                stats[:, cs + 5:cs + 6], stats[:, cs + 4:cs + 5],
                stats[:, cs + 3:cs + 4]), inc=VE, dr=True)
            ve_var_t[tc] = pr.c(VE)
            ve_var_prev[0] = ve_var_t[tc]
            w = [(VE, ve_var_t[tc])]
            if tc == 0:
                w.append((GP, gp_init))
            E("scalar", lambda s, cs=cs: s.activation(
                out=stats[:, cs + 6:cs + 7], in_=stats[:, cs + 5:cs + 6],
                func=AF.Ln, bias=c_const[:, 1:2]), waits=w, inc=SE)
            E("scalar", lambda s, cs=cs: s.activation(
                out=stats[:, cs + 7:cs + 8], in_=stats[:, cs + 6:cs + 7],
                func=AF.Exp, scale=-0.5), inc=SE)
            se_rstd_t[tc] = pr.c(SE)

        def ln1_norm(tc):
            sl = tc % 2
            cs = 8 * (tc % 2)
            xb = r_x[:, sl, :]
            lslot = lnt[:, (tc % 2) * D:(tc % 2) * D + D]
            w = [(SE, se_rstd_t[tc])]
            if tc >= 2:
                w.append((TE, te_lntr_t[tc - 2]))
            E("vector", lambda v, xb=xb, cs=cs, lslot=lslot: v.tensor_scalar(
                lslot, xb, stats[:, cs + 2:cs + 3], stats[:, cs + 7:cs + 8],
                op0=OP.subtract, op1=OP.mult), waits=w, inc=VE)
            ve_ln[tc] = pr.c(VE)
            for g in range(2):
                w = [(VE, ve_ln[tc]), (VE, tr_ev[0])]
                for i in range(4):
                    ic = 4 * g + i
                    E("tensor", lambda t, ic=ic, i=i, lslot=lslot: t.transpose(
                        ps_tr[:, i * 128:(i + 1) * 128],
                        lslot[:, ic * 128:(ic + 1) * 128], id_sb[:]),
                        waits=w if i == 0 else [], inc=TE)
                te_lntr[0] = pr.c(TE)
                tgt = r_lnfm_a if g == 0 else r_lnfm_b
                E("vector", lambda v, tgt=tgt, tc=tc: v.tensor_copy(
                    tgt[:, 0:4, tc * P:(tc + 1) * P],
                    ps_tr[:, :]), waits=[(TE, te_lntr[0])], inc=VE)
                tr_ev[0] = pr.c(VE)
            te_lntr_t[tc] = te_lntr[0]

        for tc in range(TC):
            if tc >= 1:
                ln1_norm(tc - 1)
            if tc + 1 < TC:
                w = [(VE, ve_ln[tc - 1])] if tc >= 1 else []
                E("sync", x_load(tc + 1, (tc + 1) % 2, dr["x"]), waits=w, inc=LD)
                ld_x[tc + 1] = pr.c(LD)
            ln1_stats(tc)
        ln1_norm(TC - 1)
        ve_ln1_done = pr.c(VE)

        gl_stage = [0]
        te_stage_rd = [0]
        se_yev = [0]

        def load_stage(name, half):
            return lambda g: g.dma_start(
                out=wstage[:, :, :],
                in_=dr[name][half * 512:(half + 1) * 512, :].rearrange(
                    "(a p) b -> p a b", p=P))

        def stage_to(dst, src, half, extra_w=()):
            for c in range(4):
                for icg in range(2):
                    w = [(VE, tr_ev[0])] + list(extra_w)
                    for i in range(4):
                        ic = icg * 4 + i
                        E("tensor", lambda t, c=c, ic=ic, i=i, src=src: t.transpose(
                            ps_tr[:, i * 128:(i + 1) * 128],
                            src[:, c, ic * 128:(ic + 1) * 128], id_sb[:]),
                            waits=w if i == 0 else [], inc=TE)
                    te_tr = pr.c(TE)
                    te_stage_rd[0] = te_tr
                    E("vector", lambda v, c=c, icg=icg, half=half, dst=dst: v.tensor_copy(
                        dst[:, icg * 4:(icg + 1) * 4,
                            half * 512 + c * 128: half * 512 + (c + 1) * 128],
                        ps_tr[:, :]), waits=[(TE, te_tr)], inc=VE)
                    tr_ev[0] = pr.c(VE)

        def prep_centered(name, dst):
            for half in range(2):
                E("gpsimd", load_stage(name, half),
                  waits=[(SE, se_yev[0]), (TE, te_stage_rd[0])], inc=GL)
                gl_stage[0] = pr.c(GL)
                yev_h = {}
                for hh in range(8):
                    base = 64 * (hh % 2)
                    chunk = hh // 2
                    yps = y0 if (hh % 2 == 0) else y1
                    for sln in range(2):
                        w = []
                        if sln == 0:
                            w = [(GL, gl_stage[0])]
                            w.append((SE, yev_h[hh - 2] if hh >= 2 else se_yev[0]))
                        E("tensor", lambda t, base=base, chunk=chunk, yps=yps, sx=sln: t.matmul(
                            yps[:, sx * 512:(sx + 1) * 512],
                            ut2_sb[base:base + 64, :],
                            wstage[base:base + 64, chunk, sx * 512:(sx + 1) * 512],
                            start=True, stop=True),
                            waits=w, inc=TE)
                    te_f = pr.c(TE)
                    te_stage_rd[0] = te_f
                    E("scalar", lambda s, base=base, chunk=chunk, yps=yps: s.copy(
                        out=ystage[base:base + 64, chunk, :], in_=yps[:, :]),
                        waits=[(TE, te_f)], inc=SE)
                    se_yev[0] = pr.c(SE)
                    yev_h[hh] = se_yev[0]
                stage_to(dst, ystage, half, extra_w=[(SE, se_yev[0])])

        def prep_plain(name, dst, extra_w=()):
            for half in range(2):
                E("gpsimd", load_stage(name, half),
                  waits=[(TE, te_stage_rd[0])] + list(extra_w), inc=GL)
                gl_stage[0] = pr.c(GL)
                stage_to(dst, wstage, half, extra_w=[(GL, gl_stage[0])])

        prep_centered("Wq", r_f1)
        te_wq_prep, ve_wq_prep = pr.c(TE), pr.c(VE)
        prep_centered("Wk", r_f2)
        te_wk_prep, ve_wk_prep = pr.c(TE), pr.c(VE)

        for icg in range(2):
            w = [(GL, gl_slot), (VE, tr_ev[0])]
            for i in range(4):
                ic = icg * 4 + i
                E("tensor", lambda t, ic=ic, i=i: t.transpose(
                    ps_tr[:, i * 128: i * 128 + NS],
                    slotS[:, ic * 128:(ic + 1) * 128], id_sb[0:NS, 0:NS]),
                    waits=w if i == 0 else [], inc=TE)
            te_str = pr.c(TE)
            E("vector", lambda v, icg=icg: v.tensor_copy(
                slotFM[:, icg * 4:(icg + 1) * 4, :],
                ps_tr[:, :].rearrange("p (a b) -> p a b", a=4, b=128)[:, :, 0:NS]),
                waits=[(TE, te_str)], inc=VE)
            tr_ev[0] = pr.c(VE)
        ve_slotfm = pr.c(VE)
        for ic in range(8):
            E("vector", lambda v, ic=ic: v.reduce_sum(stats[:, 0:1], slotFM[:, ic, :], axis=AXX),
              waits=[(VE, ve_slotfm), (VE, ve_ln1_done)] if ic == 0 else [], inc=VE)
            E("vector", lambda v, ic=ic: v.tensor_scalar(smf[:, ic:ic + 1], stats[:, 0:1],
                                                         1.0 / NS, None, op0=OP.mult), inc=VE, dr=True)
        ve_smf = pr.c(VE)
        E("gpsimd", lambda g: g.dma_start(
            out=ystage[:, :, 0:DHF],
            in_=dr["w1"][D:2 * D, :].rearrange("(a p) b -> p a b", p=P)[:, 0:4, :]),
          waits=[(TE, te_stage_rd[0]), (SE, se_yev[0]), (VE, tr_ev[0])], inc=GL)
        gl_w1b_a = pr.c(GL)
        E("gpsimd", lambda g: g.dma_start(
            out=wstage[:, :, 0:DHF],
            in_=dr["w1"][D:2 * D, :].rearrange("(a p) b -> p a b", p=P)[:, 4:8, :]),
          inc=GL)
        gl_w1b_b = pr.c(GL)
        for ic in range(8):
            src = ystage if ic < 4 else wstage
            E("tensor", lambda t, src=src, ic=ic: t.matmul(
                psC[0:1, 0:DHF], smf[:, ic:ic + 1], src[:, ic % 4, 0:DHF],
                start=(ic == 0), stop=(ic == 7)),
                waits=[(VE, ve_smf), (GL, gl_w1b_b if ic >= 4 else gl_w1b_a)] if ic in (0, 4) else [],
                inc=TE)
        te_c = pr.c(TE)
        te_stage_rd[0] = te_c
        E("vector", lambda v: v.tensor_add(hb_row[:], psC[0:1, 0:DHF], b1_row[:]),
          waits=[(TE, te_c), (LD, ld_consts)], inc=VE)
        ve_hb = pr.c(VE)
        for i in range(4):
            E("tensor", lambda t, i=i: t.transpose(
                ps_trf[:, i:i + 1], hb_row[0:1, i * 128:(i + 1) * 128],
                ones_m[0:1, 0:1]),
                waits=[(VE, ve_hb)] if i == 0 else [], inc=TE)
        te_hbt = pr.c(TE)
        E("vector", lambda v: v.tensor_copy(hbiasT[:, :], ps_trf[:, 0:4]),
          waits=[(TE, te_hbt)], inc=VE)
        ve_hbt = pr.c(VE)

        E("sync", lambda s: s.dma_start(out=bias_scr[0:1, :], in_=dr["ln_g"]),
          waits=[(LD, ld_consts)], inc=LD)
        ld_g = pr.c(LD)
        for i in range(8):
            E("tensor", lambda t, i=i: t.transpose(
                ps_trf[:, 8 + i:9 + i], bias_scr[0:1, i * 128:(i + 1) * 128],
                ones_m[0:1, 0:1]),
                waits=[(LD, ld_g), (VE, ve_hbt)] if i == 0 else [], inc=TE)
        te_gt = pr.c(TE)
        E("vector", lambda v: v.tensor_copy(gf[:, :], ps_trf[:, 8:16]),
          waits=[(TE, te_gt)], inc=VE)
        ve_gf = pr.c(VE)
        E("sync", lambda s: s.dma_start(out=bias_scr[0:1, :], in_=dr["ln_b"]),
          waits=[(VE, ve_gf)], inc=LD)
        ld_b = pr.c(LD)
        for i in range(8):
            E("tensor", lambda t, i=i: t.transpose(
                ps_trf[:, 8 + i:9 + i], bias_scr[0:1, i * 128:(i + 1) * 128],
                ones_m[0:1, 0:1]),
                waits=[(LD, ld_b), (VE, ve_gf)] if i == 0 else [], inc=TE)
        te_bt = pr.c(TE)
        E("vector", lambda v: v.tensor_copy(bfm[:, :], ps_trf[:, 8:16]),
          waits=[(TE, te_bt)], inc=VE)
        ve_bfm = pr.c(VE)

        te_proj = [0]
        ve_pev = [0]
        so_scat = [0]
        se_qv = [0]
        te_qv_last = [0]

        def qk_projection(wsrc, qdst, first_waits):
            for oc in range(NPAIR):
                for halft in range(2):
                    ps = sc0 if halft == 0 else sc1
                    for ic in range(8):
                        for sln in range(2):
                            w = []
                            if ic == 0 and sln == 0:
                                w = list(first_waits) + [(VE, ve_pev[0])]
                            E("tensor", lambda t, ps=ps, oc=oc, ic=ic, sln=sln, halft=halft,
                              wsrc=wsrc: t.matmul(
                                ps[:, sln * 512:(sln + 1) * 512],
                                wsrc[:, ic, oc * 128:(oc + 1) * 128],
                                lnfm(ic)[:, halft * 1024 + sln * 512: halft * 1024 + (sln + 1) * 512],
                                start=(ic == 0), stop=(ic == 7)), waits=w, inc=TE)
                    te_proj[0] = pr.c(TE)
                    E("vector", lambda v, ps=ps, oc=oc, halft=halft, qdst=qdst: v.tensor_copy(
                        qdst[:, oc, halft * 1024:(halft + 1) * 1024], ps[:, :]),
                        waits=[(TE, te_proj[0])], inc=VE)
                    ve_pev[0] = pr.c(VE)
                E("vector", lambda v, oc=oc, qdst=qdst: v.tensor_mul(
                    sq[:, :], qdst[:, oc, :], qdst[:, oc, :]),
                    waits=[(SO, so_scat[0])], inc=VE, dr=True)
                ve_sq = pr.c(VE)
                for q4 in range(4):
                    qvp = psC[0:2, 0:512] if q4 % 2 == 0 else psB1[0:2, 0:512]
                    E("tensor", lambda t, q4=q4, qvp=qvp: t.matmul(
                        qvp[:, :], ones_pr[:, :],
                        sq[:, q4 * 512:(q4 + 1) * 512],
                        start=True, stop=True),
                        waits=[(VE, ve_sq), (SE, se_qv[0])], inc=TE)
                    te_qv = pr.c(TE)
                    te_qv_last[0] = te_qv
                    E("scalar", lambda s, q4=q4, qvp=qvp: s.activation(
                        out=r_qv[0:2, q4 * 512:(q4 + 1) * 512], in_=qvp[:, :],
                        func=AF.Copy, scale=1.0 / 63.0),
                        waits=[(TE, te_qv)], inc=SE)
                    se_qv[0] = pr.c(SE)
                E("sync", lambda s, oc=oc, qdst=qdst: s.dma_start(
                    out=qdst[63:64, oc, :], in_=r_qv[0:1, :]),
                    waits=[(SE, se_qv[0]), (VE, ve_pev[0])], inc=SO)
                E("sync", lambda s, oc=oc, qdst=qdst: s.dma_start(
                    out=qdst[127:128, oc, :], in_=r_qv[1:2, :]), inc=SO)
                so_scat[0] = pr.c(SO)

        qk_projection(r_f1, r_q,
                      [(TE, te_wq_prep), (VE, ve_wq_prep), (VE, ve_ln1_done),
                       (VE, ve_hbt), (VE, ve_bfm), (VE, ve_hb)])
        so_q_scat = so_scat[0]
        qk_projection(r_f2, r_k, [(TE, te_wk_prep), (VE, ve_wk_prep)])
        so_k_scat = so_scat[0]

        prep_plain("Wv", r_f1, extra_w=[(VE, ve_pev[0]), (SE, se_qv[0])])
        te_wv_prep, ve_wv_prep = pr.c(TE), pr.c(VE)
        if debug:
            E("sync", lambda s: s.dma_start(
                out=dbg["dbg_ln1b"],
                in_=r_lnfm_a[:, :, :].rearrange("p a b -> p (a b)")),
                waits=[(TE, te_wv_prep), (VE, ve_wv_prep)], inc=SO)
            E("sync", lambda s: s.dma_start(
                out=dbg["dbg_lnb"],
                in_=r_lnfm_b[:, :, :].rearrange("p a b -> p (a b)")), inc=SO)
            so_lnb = pr.c(SO)
        for tc in range(TC):
            ps = sc0 if tc % 2 == 0 else sc1
            for ic in range(8):
                for sln in range(2):
                    w = []
                    if ic == 0 and sln == 0:
                        w = [(TE, te_wv_prep), (VE, ve_wv_prep), (VE, ve_pev[0]),
                             (GP, gp_init)]
                        if debug and tc == 0:
                            w.append((SO, so_lnb))
                    E("tensor", lambda t, ps=ps, tc=tc, ic=ic, sln=sln: t.matmul(
                        ps[:, sln * 512:(sln + 1) * 512],
                        lnfm(ic)[:, tc * 128:(tc + 1) * 128],
                        r_f1[:, ic, sln * 512:(sln + 1) * 512],
                        start=(ic == 0), stop=(ic == 7)), waits=w, inc=TE)
            te_proj[0] = pr.c(TE)
            if debug and tc == 0:
                E("scalar", lambda s: s.activation(
                    out=dbgav_sb[:, :], in_=sc0[:, :], func=AF.Copy),
                    waits=[(TE, te_proj[0])], inc=SE)
                se_dv = pr.c(SE)
                E("sync", lambda s: s.dma_start(out=dbg["dbg_vps"], in_=dbgav_sb[:, :]),
                  waits=[(SE, se_dv)], inc=SO)
                so_dv = pr.c(SO)
                E("vector", lambda v, tc=tc, ps=ps: v.tensor_copy(
                    r_v[:, tc, :, 0:64], ps[:, :]),
                    waits=[(TE, te_proj[0]), (SE, se_dv)], inc=VE)
            else:
                E("vector", lambda v, tc=tc, ps=ps: v.tensor_copy(
                    r_v[:, tc, :, 0:64], ps[:, :]),
                    waits=[(TE, te_proj[0])], inc=VE)
            ve_pev[0] = pr.c(VE)
        ve_vhat = pr.c(VE)

        prep_plain("Wo", r_f2, extra_w=[(VE, ve_pev[0])])
        te_wo_prep, ve_wo_prep = pr.c(TE), pr.c(VE)

        if debug:
            E("sync", lambda s: s.dma_start(
                out=dbg["dbg_ln1"],
                in_=r_lnfm_a[:, :, :].rearrange("p a b -> p (a b)")),
                waits=[(VE, ve_ln1_done)], inc=SO)
            E("sync", lambda s: s.dma_start(
                out=dbg["dbg_qfm"], in_=r_q[:, :, :].rearrange("p a b -> p (a b)")),
                waits=[(SO, so_q_scat)], inc=SO)
            E("sync", lambda s: s.dma_start(
                out=dbg["dbg_kfm"], in_=r_k[:, :, :].rearrange("p a b -> p (a b)")),
                waits=[(SO, so_k_scat)], inc=SO)
            E("sync", lambda s: s.dma_start(
                out=dbg["dbg_vhat"],
                in_=r_v[:, :, :, :].rearrange("p a b c -> p (a b c)")),
                waits=[(VE, ve_vhat)], inc=SO)

        E("gpsimd", lambda g: g.nop(), waits=[(VE, ve_vhat)], inc=GP)
        gp_f1_free = pr.c(GP)
        for fn in [
            lambda g: g.dma_start(out=w1a_a[:, :, :],
                                  in_=dr["w1"][0:D, :].rearrange("(a p) b -> p a b", p=P)[:, 0:4, :]),
            lambda g: g.dma_start(out=w1a_b[:, :, :],
                                  in_=dr["w1"][0:D, :].rearrange("(a p) b -> p a b", p=P)[:, 4:8, :]),
            lambda g: g.dma_start(out=wqN[:, :, :],
                                  in_=dr["wq"].rearrange("(a p) b -> p a b", p=P)),
            lambda g: g.dma_start(out=wkN[:, :, :],
                                  in_=dr["wk"].rearrange("(a p) b -> p a b", p=P)),
        ]:
            E("gpsimd", fn, waits=[(GP, gp_f1_free)], inc=GL)
        gl_ffn1 = pr.c(GL)

        ebuf = r_et
        se_exp = {}
        te_av = {}
        ve_avev = [0]
        te_qkv_d = {}

        def emit_qk_mm(rnd, pair, qq, kc):
            par = kc % 2
            scA = psA[:, par * 1024: par * 1024 + 512]
            scB = psA[:, par * 1024 + 512: (par + 1) * 1024]
            w = []
            if kc >= 2:
                w.append((SE, se_exp[(rnd, kc - 2)]))
            elif rnd >= 1:
                w.append((SE, se_exp[(rnd - 1, KC - 2 + kc)]))
            else:
                w += [(SO, so_q_scat), (SO, so_k_scat), (VE, ve_vhat)]
                if debug:
                    w.append((SO, pr.c(SO)))
            for hh in range(2):
                base = 64 * hh
                scp = scA if hh == 0 else scB
                E("tensor", lambda t, scp=scp, base=base, pair=pair, kc=kc,
                  qq=qq: t.matmul(
                    scp[:, :],
                    r_k[base:base + 64, pair, kc * 128:(kc + 1) * 128],
                    r_q[base:base + 64, pair, qq * 512:(qq + 1) * 512],
                    start=True, stop=True, tile_position=(base, 0)),
                    waits=w if hh == 0 else [], inc=TE)
            te_qkv_d[(rnd, kc)] = pr.c(TE)

        def emit_exp(rnd, kc):
            w = [(TE, te_qkv_d[(rnd, kc)])]
            if kc >= 2:
                w.append((TE, te_av[(rnd, kc - 2, 1)]))
            elif rnd >= 1:
                w.append((TE, te_av[(rnd - 1, KC - 2 + kc, 1)]))
            E("scalar", lambda s, kc=kc: s.activation(
                out=ebuf[:, kc % 2, :],
                in_=psA[:, (kc % 2) * 1024:(kc % 2 + 1) * 1024], func=AF.Exp,
                scale=1.0 / 64.0, bias=c_const[:, 0:1]),
                waits=w, inc=SE)
            se_exp[(rnd, kc)] = pr.c(SE)

        def emit_qk(rnd, pair, qq, kc):
            emit_qk_mm(rnd, pair, qq, kc)
            emit_exp(rnd, kc)

        def emit_av(rnd, pair, kc):
            for hh in range(2):
                avp = avA if hh == 0 else avB
                hglob = 2 * pair + hh
                w = [(SE, se_exp[(rnd, kc)])]
                if kc == 0:
                    w.append((VE, ve_avev[0]))
                for qc in range(4):
                    st = (kc == 0 and qc == 0)
                    E("tensor", lambda t, kc=kc, qc=qc, hglob=hglob, st=st,
                      avp=avp, hh=hh: t.matmul(
                        avp[:, qc, 0:65],
                        ebuf[:, kc % 2, hh * 512 + qc * 128:
                             hh * 512 + (qc + 1) * 128],
                        r_v[:, kc, hglob, :],
                        start=st, stop=(kc == KC - 1),
                        skip_group_check=True),
                        waits=w if (qc == 0 and hh == 0) else [], inc=TE)
                te_av[(rnd, kc, hh)] = pr.c(TE)

        gwt_so = [0]
        gwt_ld = {}
        gwt_cast = {}
        gwt_trg = {}
        gwt_evac = {}
        gwt_out = {}
        gstage = r_lnfm_a[:, :, :].rearrange("p a b -> p (a b)")

        def q_load(step):
            which, ob = step % 2, step // 2
            nm = "Wih" if which == 0 else "Whh"
            slot = step % 2
            w = [(VE, ve_vhat)] if step < 2 else [(VE, gwt_cast[step - 2])]
            E("sync", lambda s, nm=nm, ob=ob, slot=slot: s.dma_start(
                out=r_x[:, slot, :],
                in_=dr[nm][ob * 128:(ob + 1) * 128, :]),
                waits=w, inc=LD)
            gwt_ld[step] = pr.c(LD)

        def q_cast(step):
            slot = step % 2
            w = [(LD, gwt_ld[step])]
            if step >= 2:
                w.append((TE, gwt_trg[(step - 2, 1)]))
            E("vector", lambda v, slot=slot: v.tensor_copy(
                gstage[:, slot * 1024:(slot + 1) * 1024], r_x[:, slot, :]),
                waits=w, inc=VE)
            gwt_cast[step] = pr.c(VE)

        def q_trg(step, icg):
            slot = step % 2
            yslot = step % 4
            w = [(VE, gwt_cast[step]), (VE, tr_ev[0])]
            for i in range(4):
                ic = icg * 4 + i
                E("tensor", lambda t, ic=ic, i=i, slot=slot: t.transpose(
                    ps_tr[:, i * 128:(i + 1) * 128],
                    gstage[:, slot * 1024 + ic * 128:slot * 1024 + (ic + 1) * 128],
                    id_sb[:]),
                    waits=w if i == 0 else [], inc=TE)
            te_t = pr.c(TE)
            gwt_trg[(step, icg)] = te_t
            we = [(TE, te_t)]
            if icg == 0 and step >= 4:
                we.append((SO, gwt_out[step - 4]))
            E("vector", lambda v, icg=icg, yslot=yslot: v.tensor_copy(
                ystage[:, yslot, icg * 512:(icg + 1) * 512], ps_tr[:, :]),
                waits=we, inc=VE)
            gwt_evac[(step, icg)] = pr.c(VE)
            tr_ev[0] = pr.c(VE)

        def q_out(step):
            which, ob = step % 2, step // 2
            yslot = step % 4
            E("sync", lambda s, which=which, ob=ob, yslot=yslot: s.dma_start(
                out=wgT[which, ob, :, :], in_=ystage[:, yslot, :]),
                waits=[(VE, gwt_evac[(step, 1)])], inc=SO)
            gwt_out[step] = pr.c(SO)
            gwt_so[0] = gwt_out[step]

        NSTEP_GRU = 2 * GOB
        gru_quanta = []
        gru_quanta.append(lambda: q_load(0))
        gru_quanta.append(lambda: q_load(1))
        for s_ in range(NSTEP_GRU):
            gru_quanta.append(lambda s_=s_: q_cast(s_))
            gru_quanta.append(lambda s_=s_: q_trg(s_, 0))
            gru_quanta.append(lambda s_=s_: q_trg(s_, 1))
            gru_quanta.append(lambda s_=s_: q_out(s_))
            if s_ + 2 < NSTEP_GRU:
                gru_quanta.append(lambda s_=s_: q_load(s_ + 2))
        gru_quanta_it = iter(gru_quanta)

        se_exp_h = {}
        pending_ptr = [None]
        te_ptr_last = [0]

        def emit_pair_tr():
            if pending_ptr[0] is None:
                return
            ppair, ve_ready = pending_ptr[0]
            pending_ptr[0] = None
            for g4 in range(4):
                w = [(VE, ve_ready), (VE, tr_ev[0])]
                for i in range(4):
                    tcl = g4 * 4 + i
                    E("tensor", lambda t, tcl=tcl, i=i: t.transpose(
                        ps_tr[:, i * 128:(i + 1) * 128],
                        r_pt[:, tcl, :], id_sb[:]),
                        waits=w if i == 0 else [], inc=TE)
                te_ptr = pr.c(TE)
                te_ptr_last[0] = te_ptr
                E("vector", lambda v, g4=g4, ppair=ppair: v.tensor_copy(
                    r_q[:, ppair, g4 * 512:(g4 + 1) * 512], ps_tr[:, :]),
                    waits=[(TE, te_ptr)], inc=VE)
                tr_ev[0] = pr.c(VE)

        rounds = [(p_, q_) for p_ in range(NPAIR) for q_ in range(4)]
        emit_qk(0, 0, 0, 0)
        emit_qk(0, 0, 0, 1)
        for ri, (pair, qq) in enumerate(rounds):
            rnd = ri
            if True:
                if qq == 0:
                    emit_pair_tr()
                for kc in range(KC):
                    nxt = None
                    if kc + 2 < KC:
                        nxt = (rnd, pair, qq, kc + 2)
                    elif ri + 1 < len(rounds):
                        npr, nqq = rounds[ri + 1]
                        nxt = (ri + 1, npr, nqq, kc + 2 - KC)
                    if nxt is not None:
                        emit_qk_mm(*nxt)
                    emit_av(rnd, pair, kc)
                    if nxt is not None:
                        emit_exp(nxt[0], nxt[3])
                    if rnd >= 2 and kc % 2 == 0:
                        qk_q = next(gru_quanta_it, None)
                        if qk_q is not None:
                            qk_q()
                te_av_done = pr.c(TE)
                E("vector", lambda v: v.reciprocal(recipC[:, 0:4], avA[:, :, 64]),
                  waits=[(TE, te_av_done), (TE, te_ptr_last[0])], inc=VE)
                E("vector", lambda v: v.reciprocal(recipC[:, 4:8], avB[:, :, 64]),
                  inc=VE)
                E("vector", lambda v: v.tensor_copy(stats[:, 0:1], recipC[:, 0:1]), inc=VE)
                for hh in range(2):
                    avp = avA if hh == 0 else avB
                    for qc in range(4):
                        E("vector", lambda v, qc=qc, qq=qq, hh=hh, avp=avp: v.tensor_scalar(
                            r_pt[:, qq * 4 + qc, hh * 64:hh * 64 + 64],
                            avp[:, qc, 0:64], recipC[:, hh * 4 + qc:hh * 4 + qc + 1],
                            None, op0=OP.mult),
                            inc=VE)
                ve_avev[0] = pr.c(VE)
            pending_ptr[0] = (pair, ve_avev[0])
        for qk_q in gru_quanta_it:
            qk_q()
        emit_pair_tr()
        ve_attnfm = pr.c(VE)
        te_headloop = pr.c(TE)

        if debug:
            E("sync", lambda s: s.dma_start(
                out=dbg["dbg_attnfm"], in_=r_q[:, :, :].rearrange("p a b -> p (a b)")),
                waits=[(VE, ve_attnfm)], inc=SO)

        E("gpsimd", lambda g: g.nop(), waits=[(TE, te_headloop)], inc=GP)
        gp_rk_free = pr.c(GP)
        E("gpsimd", lambda g: g.dma_start(
            out=wvpN[:, :, :], in_=dr["wvp"].rearrange("(a p) b -> p a b", p=P)),
          waits=[(GP, gp_rk_free)], inc=GL)
        gl_wvp = pr.c(GL)
        E("gpsimd", lambda g: g.dma_start(
            out=woN[:, :, :], in_=dr["wo"].rearrange("(a p) b -> p a b", p=P)),
          inc=GL)
        gl_wo = pr.c(GL)

        ld_x2 = [0] * TC
        ve_x2 = [0] * TC
        ve_norm2 = [0] * TC
        so_x2 = [0] * TC
        se_sq2 = [0]
        E("sync", x_load(0, 0, dr["x"]),
          waits=[(VE, ve_avev[0]), (SO, pr.c(SO))], inc=LD)
        ld_x2[0] = pr.c(LD)

        def emit_ln2tr(tc):
            for g in range(2):
                w = [(VE, ve_norm2[tc]), (VE, tr_ev[0])]
                for i in range(4):
                    ic = 4 * g + i
                    E("tensor", lambda t, tc=tc, ic=ic, i=i: t.transpose(
                        ps_tr[:, i * 128:(i + 1) * 128],
                        ln2tok[:, tc, ic * 128:(ic + 1) * 128], id_sb[:]),
                        waits=w if i == 0 else [], inc=TE)
                te_tr2 = pr.c(TE)
                tgt = r_lnfm_a if g == 0 else r_lnfm_b
                E("vector", lambda v, tgt=tgt, tc=tc: v.tensor_copy(
                    tgt[:, 0:4, tc * P:(tc + 1) * P],
                    ps_tr[:, :]), waits=[(TE, te_tr2)], inc=VE)
                tr_ev[0] = pr.c(VE)

        for tc in range(TC):
            sl = tc % 2
            if tc + 1 < TC:
                w = []
                if tc >= 1:
                    w = [(VE, ve_norm2[tc - 1]), (SO, so_x2[tc - 1])]
                E("sync", x_load(tc + 1, (tc + 1) % 2, dr["x"]), waits=w, inc=LD)
                ld_x2[tc + 1] = pr.c(LD)
            ps = sc0 if tc % 2 == 0 else sc1
            for ic in range(8):
                for sln in range(2):
                    w = []
                    if ic == 0 and sln == 0:
                        w = [(VE, ve_attnfm), (TE, te_wo_prep), (VE, ve_wo_prep)]
                        if tc >= 2:
                            w.append((VE, ve_x2[tc - 2]))
                    E("tensor", lambda t, ps=ps, tc=tc, ic=ic, sln=sln: t.matmul(
                        ps[:, sln * 512:(sln + 1) * 512],
                        r_q[:, ic, tc * 128:(tc + 1) * 128],
                        r_f2[:, ic, sln * 512:(sln + 1) * 512],
                        start=(ic == 0), stop=(ic == 7)), waits=w, inc=TE)
            te_wo = pr.c(TE)
            if tc >= 1:
                emit_ln2tr(tc - 1)
            xb = r_x[:, sl, :]
            E("vector", lambda v, xb=xb, ps=ps: v.tensor_add(xb, ps[:, :], xb),
              waits=[(TE, te_wo), (LD, ld_x2[tc])], inc=VE)
            ve_x2[tc] = pr.c(VE)
            E("sync", lambda s, xb=xb, tc=tc: s.dma_start(
                out=x2s[tc * P:(tc + 1) * P, :], in_=xb),
                waits=[(VE, ve_x2[tc])], inc=SO)
            so_x2[tc] = pr.c(SO)
            E("scalar", lambda s, xb=xb: s.activation(
                out=sq[:, 0:D], in_=xb, func=AF.Square, accum_out=stats[:, 0:1]),
                waits=[(VE, ve_x2[tc]), (VE, ve_var_prev[0])], inc=SE)
            se_sq2[0] = pr.c(SE)
            E("vector", lambda v, xb=xb: v.reduce_sum(stats[:, 1:2], xb, axis=AXX), inc=VE, dr=True)
            E("vector", lambda v: v.tensor_scalar_mul(stats[:, 2:3], stats[:, 1:2], 1.0 / D), inc=VE, dr=True)
            E("vector", lambda v: v.tensor_mul(stats[:, 3:4], stats[:, 2:3], stats[:, 2:3]), inc=VE, dr=True)
            E("vector", lambda v: v.tensor_scalar(stats[:, 4:5], stats[:, 0:1], 1.0 / D,
                                                  None, op0=OP.mult),
              waits=[(SE, se_sq2[0])], inc=VE)
            E("vector", lambda v: v.tensor_sub(stats[:, 5:6], stats[:, 4:5], stats[:, 3:4]), inc=VE, dr=True)
            ve_var_prev[0] = pr.c(VE)
            E("scalar", lambda s: s.activation(out=stats[:, 6:7], in_=stats[:, 5:6],
                                               func=AF.Ln, bias=c_const[:, 1:2]),
              waits=[(VE, ve_var_prev[0])], inc=SE)
            E("scalar", lambda s: s.activation(out=stats[:, 7:8], in_=stats[:, 6:7],
                                               func=AF.Exp, scale=-0.5), inc=SE)
            se_rstd2 = pr.c(SE)
            E("vector", lambda v, xb=xb, tc=tc: v.tensor_scalar(
                ln2tok[:, tc, :], xb, stats[:, 2:3], stats[:, 7:8],
                op0=OP.subtract, op1=OP.mult),
                waits=[(SE, se_rstd2)], inc=VE)
            ve_norm2[tc] = pr.c(VE)
        emit_ln2tr(TC - 1)
        ve_ln2fm = pr.c(VE)
        te_p3 = pr.c(TE)

        if debug:
            E("sync", lambda s: s.dma_start(out=dbg["dbg_x2"], in_=x2s),
              waits=[(SO, so_x2[TC - 1])], inc=SO)
            E("sync", lambda s: s.dma_start(
                out=dbg["dbg_ln2fm"], in_=r_lnfm_a[:, :, :].rearrange("p a b -> p (a b)")),
                waits=[(VE, ve_ln2fm)], inc=SO)

        se_gelu = [0]
        for dhc in range(4):
            for halft in range(2):
                ps = sc0 if halft == 0 else sc1
                for ic in range(8):
                    wsrc = w1a_a if ic < 4 else w1a_b
                    for sln in range(2):
                        w = []
                        if ic == 0 and sln == 0:
                            w = [(VE, ve_ln2fm), (GL, gl_ffn1), (VE, ve_hbt)]
                            if dhc > 0 or halft > 0:
                                w.append((SE, se_gelu[0]))
                        E("tensor", lambda t, ps=ps, dhc=dhc, ic=ic, sln=sln, halft=halft,
                          wsrc=wsrc: t.matmul(
                            ps[:, sln * 512:(sln + 1) * 512],
                            wsrc[:, ic % 4, dhc * 128:(dhc + 1) * 128],
                            lnfm(ic)[:, halft * 1024 + sln * 512: halft * 1024 + (sln + 1) * 512],
                            start=(ic == 0), stop=(ic == 7)), waits=w, inc=TE)
                te_h = pr.c(TE)
                E("scalar", lambda s, ps=ps, dhc=dhc, halft=halft: s.activation(
                    out=hFM[:, dhc, halft * 1024:(halft + 1) * 1024], in_=ps[:, :],
                    func=AF.Gelu, bias=hbiasT[:, dhc:dhc + 1]),
                    waits=[(TE, te_h)], inc=SE)
                se_gelu[0] = pr.c(SE)
        se_hfm = pr.c(SE)
        if debug:
            E("sync", lambda s: s.dma_start(
                out=dbg["dbg_hfm"], in_=hFM[:, :, :].rearrange("p a b -> p (a b)")),
                waits=[(SE, se_hfm)], inc=SO)

        E("gpsimd", lambda g: g.dma_start(
            out=w2N[:, :, :], in_=dr["w2"].rearrange("(a p) b -> p a b", p=P)),
            waits=[(SE, se_hfm)], inc=GL)
        gl_w2 = pr.c(GL)
        ve_alpha = [0] * TC
        rlp = psC[0:P, 64:128]
        for tc in range(TC):
            for dhc in range(4):
                w = []
                if dhc == 0:
                    w = [(SE, se_hfm), (GL, gl_w2)]
                    if tc >= 1:
                        w.append((VE, ve_alpha[tc - 1]))
                E("tensor", lambda t, tc=tc, dhc=dhc: t.matmul(
                    rlp[:, :], hFM[:, dhc, tc * 128:(tc + 1) * 128],
                    w2N[:, dhc, :], start=(dhc == 0), stop=False,
                    skip_group_check=True), waits=w, inc=TE)
            E("tensor", lambda t: t.matmul(rlp[:, :], ones_m[0:1, :], b2_row[0:1, :],
                                           start=False, stop=True,
                                           skip_group_check=True), inc=TE)
            te_rl = pr.c(TE)
            E("vector", lambda v: v.tensor_scalar(logits[:, :], rlp[:, :], inv_tau[:, 0:1],
                                                  None, op0=OP.mult),
              waits=[(TE, te_rl), (VE, ve_invtau)], inc=VE)
            E("vector", lambda v: v.max(t8[:, :], logits[:, :]), inc=VE, dr=True)
            E("vector", lambda v: v.tensor_scalar(negm1[:, :], t8[:, 0:1], -1.0, None,
                                                  op0=OP.mult), inc=VE, dr=True)
            ve_t8 = pr.c(VE)
            E("scalar", lambda s: s.activation(out=e8b[:, :], in_=t8[:, :], func=AF.Exp,
                                               bias=negm1[:, 0:1]),
              waits=[(VE, ve_t8)], inc=SE)
            E("scalar", lambda s: s.activation(out=e64b[:, :], in_=logits[:, :],
                                               func=AF.Exp, bias=negm1[:, 0:1]), inc=SE)
            se_e = pr.c(SE)
            E("vector", lambda v: v.reduce_sum(s8[:, 0:1], e8b[:, :], axis=AXX),
              waits=[(SE, se_e)], inc=VE)
            E("vector", lambda v: v.reciprocal(s8[:, 1:2], s8[:, 0:1]), inc=VE, dr=True)
            E("vector", lambda v: v.tensor_scalar(maskb[:, :], logits[:, :], t8[:, 7:8],
                                                  None, op0=OP.is_ge), inc=VE)
            E("vector", lambda v, tc=tc: v.scalar_tensor_tensor(
                alphaT[:, tc, :], e64b[:, :], s8[:, 1:2], maskb[:, :],
                op0=OP.mult, op1=OP.mult), inc=VE)
            ve_alpha[tc] = pr.c(VE)
            if debug:
                E("sync", lambda s, tc=tc: s.dma_start(
                    out=dbg["dbg_logits"][:, tc * NS:(tc + 1) * NS], in_=logits[:, :]),
                    waits=[(VE, ve_alpha[tc])], inc=SO)
        ve_alpha_all = pr.c(VE)
        if debug:
            E("sync", lambda s: s.dma_start(
                out=dbg["dbg_alpha"], in_=alphaT[:, :, :].rearrange("p a b -> p (a b)")),
                waits=[(VE, ve_alpha_all)], inc=SO)

        si_ps = psA[0:NS, 1024:2048]
        for tc in range(TC):
            for sln in range(2):
                w = []
                if sln == 0:
                    w = [(VE, ve_alpha[tc])]
                    if tc == 0:
                        w.append((VE, ve_avev[0]))
                E("tensor", lambda t, tc=tc, sln=sln: t.matmul(
                    si_ps[:, sln * 512:(sln + 1) * 512],
                    alphaT[:, tc, :], ln2tok[:, tc, sln * 512:(sln + 1) * 512],
                    start=(tc == 0), stop=(tc == TC - 1), skip_group_check=True),
                    waits=w, inc=TE)
        te_si = pr.c(TE)
        dn_ps = psC[0:NS, 0:1]
        for tc in range(TC):
            E("tensor", lambda t, tc=tc: t.matmul(
                dn_ps[:, :], alphaT[:, tc, :], ones_n[:, :],
                start=(tc == 0), stop=(tc == TC - 1)),
                waits=[(VE, ve_alpha_all)] if tc == 0 else [], inc=TE)
        te_dn = pr.c(TE)
        E("vector", lambda v: v.tensor_scalar_add(denA[:, 0:1], dn_ps[:, :], 1e-8),
          waits=[(TE, te_dn), (TE, te_si)], inc=VE)
        E("vector", lambda v: v.reciprocal(denA[:, 1:2], denA[:, 0:1]), inc=VE, dr=True)
        E("vector", lambda v: v.tensor_copy(stats[0:NS, 0:1], denA[0:NS, 1:2]), inc=VE)
        E("vector", lambda v: v.tensor_scalar(siB[:, :], si_ps[:, :],
                                              denA[:, 1:2], None, op0=OP.mult), inc=VE)
        ve_si = pr.c(VE)


        def smajor_to_fm(srcap, dst, ready_w):
            for icg in range(2):
                w = list(ready_w) + [(VE, tr_ev[0])]
                for i in range(4):
                    ic = icg * 4 + i
                    E("tensor", lambda t, ic=ic, i=i, srcap=srcap: t.transpose(
                        ps_tr[:, i * 128: i * 128 + NS],
                        srcap[:, ic * 128:(ic + 1) * 128], id_sb[0:NS, 0:NS]),
                        waits=w if i == 0 else [], inc=TE)
                te_t = pr.c(TE)
                E("vector", lambda v, icg=icg, dst=dst: v.tensor_copy(
                    dst[:, icg * 4:(icg + 1) * 4, :],
                    ps_tr[:, :].rearrange("p (a b) -> p a b", a=4, b=128)[:, :, 0:NS]),
                    waits=[(TE, te_t)], inc=VE)
                tr_ev[0] = pr.c(VE)
            return pr.c(VE)

        ve_sifm = smajor_to_fm(siB, siFM, [(VE, ve_si)])

        gi_ps = psA[0:NS, 1024:1152]
        gh_ps = psA[0:NS, 1536:1664]
        te_gru = [0]
        ve_gru = [0]
        se_gru = [0]
        gru_tr_ev = [0]
        te_wt_done = [0, 0]
        for ob in range(GOB):
            w = []
            if ob == 0:
                w = [(VE, ve_sifm), (SE, se_hfm), (TE, te_si), (SO, gwt_so[0])]
            else:
                w = [(TE, te_gru[0])]
            E("sync", lambda s, ob=ob: s.dma_start(
                out=wtv[:, 0, :, :], in_=wgT[0, ob, :, :]), waits=w, inc=LD)
            E("sync", lambda s, ob=ob: s.dma_start(
                out=wtv[:, 1, :, :], in_=wgT[1, ob, :, :]), inc=LD)
            ld_wt = pr.c(LD)
            E("sync", lambda s, ob=ob: s.dma_start(
                out=bias_g[0:1, 0:P], in_=dr["bih"][0:1, ob * 128:(ob + 1) * 128]),
                waits=[(TE, te_gru[0])] if ob >= 1 else [], inc=LD)
            E("sync", lambda s, ob=ob: s.dma_start(
                out=bias_g[0:1, P:2 * P], in_=dr["bhh"][0:1, ob * 128:(ob + 1) * 128]), inc=LD)
            ld_gb = pr.c(LD)
            ve_wt = None
            for ic in range(8):
                w = []
                if ic == 0:
                    w = [(LD, ld_wt)]
                    if ob >= 1:
                        w += [(VE, ve_gru[0]), (SE, se_gru[0])]
                E("tensor", lambda t, ic=ic: t.matmul(
                    gi_ps[:, :], siFM[:, ic, :], wtv[:, 0, ic, :],
                    start=(ic == 0), stop=False), waits=w, inc=TE)
                E("tensor", lambda t, ic=ic: t.matmul(
                    gh_ps[:, :], slotFM[:, ic, :], wtv[:, 1, ic, :],
                    start=(ic == 0), stop=False), inc=TE)
            E("tensor", lambda t: t.matmul(gi_ps[:, :], ones_m[0:1, 0:NS], bias_g[0:1, 0:P],
                                           start=False, stop=True),
              waits=[(LD, ld_gb)], inc=TE)
            E("tensor", lambda t: t.matmul(gh_ps[:, :], ones_m[0:1, 0:NS], bias_g[0:1, P:2 * P],
                                           start=False, stop=True), inc=TE)
            te_gru[0] = pr.c(TE)
            gate = ob // 8
            dsl = (ob % 8) * 128
            if gate < 2:
                dst = rrB if gate == 0 else zzB
                E("vector", lambda v: v.tensor_copy(gsc[:, 0, :], gi_ps[:, :]),
                  waits=[(TE, te_gru[0])], inc=VE)
                E("vector", lambda v: v.tensor_add(gsc[:, 1, :],
                                                   gsc[:, 0, :], gh_ps[:, :]), inc=VE, dr=True)
                ve_g = pr.c(VE)
                E("scalar", lambda s: s.activation(out=gsc[:, 2, :],
                                                   in_=gsc[:, 1, :],
                                                   func=AF.Exp, scale=-1.0),
                  waits=[(VE, ve_g)], inc=SE)
                se_gru[0] = pr.c(SE)
                E("vector", lambda v: v.tensor_scalar_add(gsc[:, 3, :],
                                                          gsc[:, 2, :], 1.0),
                  waits=[(SE, se_gru[0])], inc=VE)
                E("vector", lambda v: v.reciprocal(gsc[:, 2, :], gsc[:, 3, :]), inc=VE, dr=True)
                E("vector", lambda v, dst=dst, dsl=dsl: v.tensor_copy(
                    dst[:, dsl:dsl + 128], gsc[:, 2, :]), inc=VE, dr=True)
                ve_gru[0] = pr.c(VE)
            else:
                E("vector", lambda v, dsl=dsl: v.tensor_mul(
                    gsc[:, 0, :], rrB[:, dsl:dsl + 128], gh_ps[:, :]),
                    waits=[(TE, te_gru[0])], inc=VE)
                E("vector", lambda v: v.tensor_add(gsc[:, 1, :],
                                                   gsc[:, 0, :], gi_ps[:, :]), inc=VE, dr=True)
                ve_g = pr.c(VE)
                E("scalar", lambda s: s.activation(out=gsc[:, 2, :],
                                                   in_=gsc[:, 1, :],
                                                   func=AF.Exp, scale=-2.0),
                  waits=[(VE, ve_g)], inc=SE)
                se_gru[0] = pr.c(SE)
                E("vector", lambda v: v.tensor_scalar_add(gsc[:, 3, :],
                                                          gsc[:, 2, :], 1.0),
                  waits=[(SE, se_gru[0])], inc=VE)
                E("vector", lambda v: v.reciprocal(gsc[:, 0, :], gsc[:, 3, :]), inc=VE, dr=True)
                E("vector", lambda v: v.tensor_scalar(gsc[:, 1, :], gsc[:, 2, :],
                                                      -1.0, 1.0, op0=OP.mult, op1=OP.add), inc=VE)
                E("vector", lambda v: v.tensor_mul(gsc[:, 1, :], gsc[:, 1, :], gsc[:, 0, :]), inc=VE, dr=True)
                E("vector", lambda v, dsl=dsl: v.tensor_sub(
                    gsc[:, 2, :], slotS[:, dsl:dsl + 128], gsc[:, 1, :]), inc=VE, dr=True)
                E("vector", lambda v, dsl=dsl: v.tensor_mul(
                    gsc[:, 3, :], zzB[:, dsl:dsl + 128], gsc[:, 2, :]), inc=VE, dr=True)
                E("vector", lambda v, dsl=dsl: v.tensor_add(
                    snewB[:, dsl:dsl + 128], gsc[:, 1, :], gsc[:, 3, :]), inc=VE, dr=True)
                ve_gru[0] = pr.c(VE)
        ve_snew = pr.c(VE)

        E("gpsimd", lambda g: g.nop(), waits=[(TE, te_p3)], inc=GP)
        gp_f2_free = pr.c(GP)
        E("gpsimd", lambda g: g.dma_start(
            out=wvN[:, :, :], in_=dr["wv"].rearrange("(a p) b -> p a b", p=P)),
          waits=[(GP, gp_f2_free)], inc=GL)
        gl_wv = pr.c(GL)

        ve_snb = ve_snew
        ve_snfm = smajor_to_fm(snewB, snFM, [(VE, ve_snew)])

        def slot_proj(dst_ps, wsrc, n_out, bias_name, first_w):
            E("sync", lambda s: s.dma_start(out=bias_scr[0:1, 0:n_out],
                                            in_=dr[bias_name]),
              waits=list(first_w) + [(TE, pr.c(TE))], inc=LD)
            ldb = pr.c(LD)
            nsl = (n_out + 511) // 512
            for ic in range(8):
                for sli in range(nsl):
                    lo, hi = sli * 512, min(n_out, (sli + 1) * 512)
                    E("tensor", lambda t, ic=ic, lo=lo, hi=hi, wsrc=wsrc: t.matmul(
                        dst_ps[:, lo:hi], snFM[:, ic, :], wsrc[:, ic, lo:hi],
                        start=(ic == 0), stop=False, skip_group_check=True),
                        waits=(first_w + [(VE, ve_snfm)]) if (ic == 0 and sli == 0) else [],
                        inc=TE)
            for sli in range(nsl):
                lo, hi = sli * 512, min(n_out, (sli + 1) * 512)
                E("tensor", lambda t, lo=lo, hi=hi: t.matmul(
                    dst_ps[:, lo:hi], ones_m[0:1, 0:NS], bias_scr[0:1, lo:hi],
                    start=False, stop=True, skip_group_check=True),
                    waits=[(LD, ldb)] if sli == 0 else [], inc=TE)
            return pr.c(TE)

        qs_ps = psC[0:NS, 0:DQ]
        ks_ps = psC[0:NS, DQ:2 * DQ]
        te_qs = slot_proj(qs_ps, wqN, DQ, "bq", [(VE, ve_snfm), (GL, gl_ffn1)])
        E("vector", lambda v: v.tensor_copy(qsB[:, :], qs_ps[:, :]),
          waits=[(TE, te_qs)], inc=VE)
        ve_qs = pr.c(VE)
        te_ks = slot_proj(ks_ps, wkN, DQ, "bk", [(VE, ve_qs)])
        E("vector", lambda v: v.tensor_copy(ksB[:, :], ks_ps[:, :]),
          waits=[(TE, te_ks)], inc=VE)
        ve_ks = pr.c(VE)
        vs_ps = psA[0:NS, 1024:2048]
        te_vs = slot_proj(vs_ps, wvN, D, "bv",
                          [(GL, gl_wv), (VE, ve_gru[0]), (SE, se_gru[0])])
        E("vector", lambda v: v.tensor_copy(vsB[:, :], vs_ps[:, :]),
          waits=[(TE, te_vs)], inc=VE)
        ve_vs = pr.c(VE)

        def tr256(src, dst, ready_w):
            w = list(ready_w) + [(VE, tr_ev[0])]
            for i in range(2):
                E("tensor", lambda t, i=i, src=src: t.transpose(
                    ps_tr[:, i * 128: i * 128 + NS],
                    src[:, i * 128:(i + 1) * 128], id_sb[0:NS, 0:NS]),
                    waits=w if i == 0 else [], inc=TE)
            te_t = pr.c(TE)
            E("vector", lambda v, dst=dst: v.tensor_copy(
                dst[:, :, :],
                ps_tr[:, 0:256].rearrange("p (a b) -> p a b", a=2, b=128)[:, :, 0:NS]),
                waits=[(TE, te_t)], inc=VE)
            tr_ev[0] = pr.c(VE)
            return pr.c(VE)

        ve_qsfm = tr256(qsB, qsFM, [(VE, ve_qs)])
        ve_ksfm = tr256(ksB, ksFM, [(VE, ve_ks)])

        s2_ps = psC[0:NS, 0:NS]
        for ic in range(2):
            E("tensor", lambda t, ic=ic: t.matmul(
                s2_ps[:, :], qsFM[:, ic, :], ksFM[:, ic, :],
                start=(ic == 0), stop=(ic == 1)),
                waits=[(VE, ve_qsfm), (VE, ve_ksfm)] if ic == 0 else [], inc=TE)
        te_s2 = pr.c(TE)
        E("vector", lambda v: v.reduce_max(sst[:, 0:1], s2_ps[:, :], axis=AXX),
          waits=[(TE, te_s2)], inc=VE)
        E("vector", lambda v: v.tensor_scalar(sst[:, 1:2], sst[:, 0:1], -0.0625, None,
                                              op0=OP.mult), inc=VE, dr=True)
        ve_m2 = pr.c(VE)
        E("scalar", lambda s: s.activation(out=a2[:, :], in_=s2_ps[:, :], func=AF.Exp,
                                           scale=0.0625, bias=sst[:, 1:2]),
          waits=[(VE, ve_m2)], inc=SE)
        se_a2 = pr.c(SE)
        E("vector", lambda v: v.reduce_sum(sst[:, 2:3], a2[:, :], axis=AXX),
          waits=[(SE, se_a2)], inc=VE)
        E("vector", lambda v: v.reciprocal(sst[:, 3:4], sst[:, 2:3]), inc=VE, dr=True)
        E("vector", lambda v: v.tensor_copy(sst[:, 4:5], sst[:, 3:4]), inc=VE)
        E("vector", lambda v: v.tensor_scalar(a2[:, :], a2[:, :], sst[:, 3:4], None,
                                              op0=OP.mult), inc=VE)
        ve_a2 = pr.c(VE)
        E("tensor", lambda t: t.transpose(ps_tr[0:NS, 0:NS], a2[:, :], id_sb[0:NS, 0:NS]),
          waits=[(VE, ve_a2), (VE, tr_ev[0])], inc=TE)
        te_a2t = pr.c(TE)
        E("vector", lambda v: v.tensor_copy(a2T[:, :], ps_tr[0:NS, 0:NS]),
          waits=[(TE, te_a2t)], inc=VE)
        ve_a2t = pr.c(VE)
        tr_ev[0] = ve_a2t
        scm_ps = psB1[0:NS, 0:1024]
        for sln in range(2):
            E("tensor", lambda t, sln=sln: t.matmul(
                scm_ps[:, sln * 512:(sln + 1) * 512], a2T[:, :],
                vsB[:, sln * 512:(sln + 1) * 512], start=True, stop=True),
                waits=[(VE, ve_a2t), (VE, ve_vs)] if sln == 0 else [], inc=TE)
        te_scm = pr.c(TE)
        E("vector", lambda v: v.tensor_add(yyB[:, :], scm_ps[:, :], snewB[:, :]),
          waits=[(TE, te_scm)], inc=VE)
        ve_y = pr.c(VE)
        E("scalar", lambda s: s.activation(out=ysqd[:, :], in_=yyB[:, :],
                                           func=AF.Square, accum_out=sst[:, 0:1]),
          waits=[(VE, ve_y)], inc=SE)
        se_ysq = pr.c(SE)
        E("vector", lambda v: v.reduce_sum(sst[:, 1:2], yyB[:, :], axis=AXX), inc=VE, dr=True)
        E("vector", lambda v: v.tensor_scalar_mul(sst[:, 2:3], sst[:, 1:2], 1.0 / D), inc=VE, dr=True)
        E("vector", lambda v: v.tensor_mul(sst[:, 3:4], sst[:, 2:3], sst[:, 2:3]), inc=VE, dr=True)
        E("vector", lambda v: v.tensor_scalar(sst[:, 4:5], sst[:, 0:1], 1.0 / D, None,
                                              op0=OP.mult),
          waits=[(SE, se_ysq)], inc=VE)
        E("vector", lambda v: v.tensor_sub(sst[:, 5:6], sst[:, 4:5], sst[:, 3:4]), inc=VE, dr=True)
        ve_yv = pr.c(VE)
        E("scalar", lambda s: s.activation(out=sst[:, 6:7], in_=sst[:, 5:6], func=AF.Ln,
                                           bias=c_const[0:NS, 1:2]),
          waits=[(VE, ve_yv)], inc=SE)
        E("scalar", lambda s: s.activation(out=sst[:, 7:8], in_=sst[:, 6:7], func=AF.Exp,
                                           scale=-0.5), inc=SE)
        se_yr = pr.c(SE)
        E("vector", lambda v: v.tensor_scalar(ylnB[:, :], yyB[:, :], sst[:, 2:3],
                                              sst[:, 7:8], op0=OP.subtract, op1=OP.mult),
          waits=[(SE, se_yr)], inc=VE)
        ve_yln = pr.c(VE)
        ve_ylnfm = smajor_to_fm(ylnB, ylnFM, [(VE, ve_yln)])

        ve_wvpg0 = [0]
        for ic in range(8):
            E("vector", lambda v, ic=ic: v.tensor_scalar(
                wvpN[:, ic, :], wvpN[:, ic, :], gf[:, ic:ic + 1], None, op0=OP.mult),
                waits=[(GL, gl_wvp), (VE, ve_gf)] if ic == 0 else [], inc=VE)
        ve_wvpg = pr.c(VE)

        ld_bw = [0]
        te_bt2 = [0]
        for half in range(2):
            for ic in range(8):
                slb = ic % 2
                E("sync", lambda s, ic=ic, slb=slb: s.dma_start(
                    out=r_x[:, slb, :], in_=dr["wvp"][ic * 128:(ic + 1) * 128, :]),
                    waits=([(VE, ve_snb), (VE, ve_yln)] if (half == 0 and ic == 0) else
                           [(TE, te_bt2[0])]), inc=LD)
                ld_bw[0] = pr.c(LD)
                E("tensor", lambda t, ic=ic, half=half, slb=slb: t.matmul(
                    psC[0:1, 0:512], bfm[:, ic:ic + 1],
                    r_x[:, slb, half * 512:(half + 1) * 512],
                    start=(ic == 0), stop=(ic == 7)),
                    waits=[(LD, ld_bw[0])] + ([(VE, ve_yv)] if (half == 0 and ic == 0) else []),
                    inc=TE)
                te_bt2[0] = pr.c(TE)
            dstrow = hb_row if half == 0 else b1_row
            E("vector", lambda v, dstrow=dstrow: v.tensor_copy(
                dstrow[0:1, 0:512], psC[0:1, 0:512]),
                waits=[(TE, te_bt2[0])], inc=VE)
        ve_btr = pr.c(VE)

        sv_ps = psA[0:NS, 1024:2048]
        E("sync", lambda s: s.dma_start(out=bias_scr[0:1, :], in_=dr["bvp"]),
          waits=[(VE, ve_btr), (TE, pr.c(TE))], inc=LD)
        ld_bvp = pr.c(LD)
        for ic in range(8):
            for sln in range(2):
                E("tensor", lambda t, ic=ic, sln=sln: t.matmul(
                    sv_ps[:, sln * 512:(sln + 1) * 512], ylnFM[:, ic, :],
                    wvpN[:, ic, sln * 512:(sln + 1) * 512],
                    start=(ic == 0), stop=False, skip_group_check=True),
                    waits=[(VE, ve_ylnfm), (VE, ve_wvpg), (VE, ve_vs)] if (ic == 0 and sln == 0) else [],
                    inc=TE)
        for sln in range(2):
            E("tensor", lambda t, sln=sln: t.matmul(
                sv_ps[:, sln * 512:(sln + 1) * 512], ones_m[0:1, 0:NS],
                bias_scr[0:1, sln * 512:(sln + 1) * 512],
                start=False, stop=False, skip_group_check=True),
                waits=[(LD, ld_bvp)] if sln == 0 else [], inc=TE)
            btsrc = hb_row if sln == 0 else b1_row
            E("tensor", lambda t, sln=sln, btsrc=btsrc: t.matmul(
                sv_ps[:, sln * 512:(sln + 1) * 512], ones_m[0:1, 0:NS],
                btsrc[0:1, 0:512],
                start=False, stop=True, skip_group_check=True),
                waits=[(VE, ve_btr)] if sln == 0 else [], inc=TE)
        te_sv = pr.c(TE)
        E("vector", lambda v: v.tensor_copy(svB[:, :], sv_ps[:, :]),
          waits=[(TE, te_sv)], inc=VE)
        ve_sv = pr.c(VE)


        for tcg in range(4):
            w = [(VE, ve_alpha_all), (VE, tr_ev[0])]
            for i in range(4):
                tc = tcg * 4 + i
                E("tensor", lambda t, tc=tc, i=i: t.transpose(
                    ps_tr[0:NS, i * 128:(i + 1) * 128],
                    alphaT[:, tc, :], id_sb[:]),
                    waits=w if i == 0 else [], inc=TE)
            te_t = pr.c(TE)
            E("vector", lambda v, tcg=tcg: v.tensor_copy(
                alphaFM[:, tcg * 4:(tcg + 1) * 4, :],
                ps_tr[0:NS, :]), waits=[(TE, te_t)], inc=VE)
            tr_ev[0] = pr.c(VE)
        ve_alphafm = pr.c(VE)
        if debug:
            E("sync", lambda s: s.dma_start(out=dbg["dbg_sv"], in_=svB[:, :]),
              waits=[(VE, ve_sv)], inc=SO)

        afm = alphaFM[:, :, :].rearrange("p a b -> p (a b)")
        ve_of = {}
        for dc in range(8):
            for sln in range(4):
                ps = sc0 if dc % 2 == 0 else sc1
                w = []
                if dc == 0 and sln == 0:
                    w = [(VE, ve_sv), (VE, ve_alphafm)]
                if (dc, sln) >= (0, 2) and (dc * 4 + sln) >= 2:
                    prevkey = (dc * 4 + sln) - 2
                    w.append((VE, ve_of[prevkey]))
                E("tensor", lambda t, dc=dc, sln=sln, ps=ps: t.matmul(
                    ps[:, (sln % 2) * 512:(sln % 2 + 1) * 512],
                    svB[:, dc * 128:(dc + 1) * 128],
                    afm[0:NS, sln * 512:(sln + 1) * 512],
                    start=True, stop=True), waits=w, inc=TE)
                te_o = pr.c(TE)
                E("vector", lambda v, dc=dc, sln=sln, ps=ps: v.tensor_copy(
                    r_q[:, dc, sln * 512:(sln + 1) * 512],
                    ps[:, (sln % 2) * 512:(sln % 2 + 1) * 512]),
                    waits=[(TE, te_o)], inc=VE)
                ve_of[dc * 4 + sln] = pr.c(VE)
        ve_outfm = pr.c(VE)

        E("sync", lambda s: s.dma_start(out=bias_scr[0:1, :], in_=dr["bo"]),
          waits=[(LD, ld_bvp), (TE, te_sv)], inc=LD)
        ld_bo = pr.c(LD)
        ld_xf = [0] * TC
        ve_fin = [0] * TC
        so_out = [0] * TC
        E("sync", x_load(0, 0, x2s), waits=[(TE, te_sv)], inc=LD)
        ld_xf[0] = pr.c(LD)
        for tc in range(TC):
            sl = tc % 2
            if tc + 1 < TC:
                w = [(TE, te_sv)]
                if tc >= 1:
                    w += [(VE, ve_fin[tc - 1]), (SO, so_out[tc - 1])]
                E("sync", x_load(tc + 1, (tc + 1) % 2, x2s), waits=w, inc=LD)
                ld_xf[tc + 1] = pr.c(LD)
            ps = sc0 if tc % 2 == 0 else sc1
            for ic in range(8):
                for sln in range(2):
                    w = []
                    if ic == 0 and sln == 0:
                        w = [(VE, ve_outfm), (GL, gl_wo)]
                        if tc >= 2:
                            w.append((VE, ve_fin[tc - 2]))
                    E("tensor", lambda t, ps=ps, tc=tc, ic=ic, sln=sln: t.matmul(
                        ps[:, sln * 512:(sln + 1) * 512],
                        r_q[:, ic, tc * 128:(tc + 1) * 128],
                        woN[:, ic, sln * 512:(sln + 1) * 512],
                        start=(ic == 0), stop=False), waits=w, inc=TE)
            for sln in range(2):
                E("tensor", lambda t, ps=ps, sln=sln: t.matmul(
                    ps[:, sln * 512:(sln + 1) * 512], ones_m[0:1, :],
                    bias_scr[0:1, sln * 512:(sln + 1) * 512],
                    start=False, stop=True),
                    waits=[(LD, ld_bo)] if sln == 0 else [], inc=TE)
            te_fin = pr.c(TE)
            xb = r_x[:, sl, :]
            E("vector", lambda v, xb=xb, ps=ps: v.tensor_add(xb, ps[:, :], xb),
              waits=[(TE, te_fin), (LD, ld_xf[tc])], inc=VE)
            ve_fin[tc] = pr.c(VE)
            E("sync", lambda s, xb=xb, tc=tc: s.dma_start(
                out=out[tc * P:(tc + 1) * P, :], in_=xb),
                waits=[(VE, ve_fin[tc])], inc=SO)
            so_out[tc] = pr.c(SO)
        so_done = pr.c(SO)
        if _TAPS:
            tap_pairs = [("tap_hfm", hFM[:, :, :].rearrange("p a b -> p (a b)")),
                           ("tap_alpha", alphaT[:, :, :].rearrange("p a b -> p (a b)")),
                           ("tap_sv", svB[:, :]), ("tap_snew", snewB[:, :]),
                           ("tap_yln", ylnB[:, :]), ("tap_vs", vsB[:, :]),
                           ("tap_sifm", r_qv[:, 0:512]),
                           ("tap_bt", None), ("tap_wvp", rkv[:, 0:8192]),
                           ("tap_ylnfm", r_qv[:, 1024:1536]),
                           ("tap_outfm", r_q[:, :, :].rearrange("p a b -> p (a b)"))]
            for nm, ap in tap_pairs:
                if ap is None:
                    continue
                E("sync", lambda s, nm=nm, ap=ap: s.dma_start(out=dbg[nm], in_=ap),
                  waits=[(SO, so_done)], inc=SO)
            E("sync", lambda s: s.dma_start(out=dbg["tap_bt"][0:1, 0:512],
                                            in_=hb_row[0:1, :]),
              waits=[(SO, so_done)], inc=SO)
            E("sync", lambda s: s.dma_start(out=dbg["tap_bt"][0:1, 512:1024],
                                            in_=b1_row[0:1, :]), inc=SO)
            so_done = pr.c(SO)
        E("sync", lambda s: s.nop(), waits=[(SO, so_done)], inc=None)

        pr.finalize()
        block = stack.enter_context(nc.Block())

        @block.sync
        def _(eng):
            pr.run_engine("sync", eng)

        @block.gpsimd
        def _(eng):
            pr.run_engine("gpsimd", eng)

        @block.tensor
        def _(eng):
            pr.run_engine("tensor", eng)

        @block.scalar
        def _(eng):
            pr.run_engine("scalar", eng)

        @block.vector
        def _(eng):
            pr.run_engine("vector", eng)

    return nc


_NC_CACHE = {}


def kernel(**inputs):
    consts = _np_consts()
    if "main" not in _NC_CACHE:
        _NC_CACHE["main"] = build_nc(debug=_DEBUG)
    nc = _NC_CACHE["main"]

    x = np.ascontiguousarray(np.asarray(inputs["L0_attn_x"], dtype=np.float32))
    names = {
        "Wq": "L0_attn_Wq", "Wk": "L0_attn_Wk", "Wv": "L0_attn_Wv",
        "Wo": "L0_attn_Wo", "slot_init": "L0_ffn_slot_init", "w1": "L0_ffn_w1",
        "w2": "L0_ffn_w2", "Wih": "L0_ffn_Wih", "Whh": "L0_ffn_Whh",
        "wq": "L0_ffn_wq", "wk": "L0_ffn_wk", "wv": "L0_ffn_wv",
        "wvp": "L0_ffn_wvp", "wo": "L0_ffn_wo",
    }
    rows = {
        "b1": "L0_ffn_b1", "b2": "L0_ffn_b2", "bih": "L0_ffn_bih",
        "bhh": "L0_ffn_bhh", "bq": "L0_ffn_bq", "bk": "L0_ffn_bk",
        "bv": "L0_ffn_bv", "ln_g": "L0_ffn_ln_g", "ln_b": "L0_ffn_ln_b",
        "bvp": "L0_ffn_bvp", "bo": "L0_ffn_bo",
    }
    shared = {}
    for k, src in names.items():
        shared[k] = np.ascontiguousarray(np.asarray(inputs[src], dtype=np.float32))
    for k, src in rows.items():
        shared[k] = np.ascontiguousarray(
            np.asarray(inputs[src], dtype=np.float32).reshape(1, -1))
    shared["tau"] = np.ascontiguousarray(
        np.asarray(inputs["L0_ffn_tau"], dtype=np.float32).reshape(1, 1))
    shared.update(consts)

    in_maps = []
    for b in range(8):
        m = dict(shared)
        m["x"] = np.ascontiguousarray(x[b])
        in_maps.append(m)

    res = run_bass_kernel_spmd(nc, in_maps, core_ids=list(range(8)))
    return np.stack([res.results[b]["out"] for b in range(8)], axis=0).astype(np.float32)
